# version 01
# baseline (speedup 1.0000x reference)
import contextlib
import numpy as np
import ml_dtypes
import concourse.bass as bass
import concourse.mybir as mybir
from concourse.bass_utils import run_bass_kernel_spmd

F32 = mybir.dt.float32
BF16 = mybir.dt.bfloat16
AF = mybir.ActivationFunctionType
ALU = mybir.AluOpType
AX = mybir.AxisListType
BF = ml_dtypes.bfloat16

B, L, D = 4, 4096, 1024
TC = 2048
DI = 2048
NST = 16
NH, NKV, DH = 4, 2, 256
FH = 2731
FFH = 4096
IN2 = 2048
DTR = 64
E = 8
F2 = 4096
FF1 = 8192
LN_EPS = 1e-5
RMS_EPS = 1e-6
MASK_NEG = -1e30
DEBUG_L1 = False
DEBUG_L4 = False
DEBUG_L5 = False
L6_PHASES = 3



ENGINES = ("sync", "gpsimd", "tensor", "vector", "scalar")


class Prog:
    def __init__(self, name="prog"):
        self.nc = bass.Bass("TRN2", target_bir_lowering=False)
        self.ops = []
        self.name = name
        self._es = None
        self.groups = {}

    def dma_group(self, name, stream=False):
        if name not in self.groups:
            self.groups[name] = {"stream": stream, "ops": []}
        return name

    def op(self, engine, fn, deps=(), dma=False, group=None):
        gid = None
        if dma:
            assert group is not None, "dma ops must declare a group"
            gid = group
        i = len(self.ops)
        self.ops.append([engine, fn, list(deps), 16 if dma else 1, gid])
        if gid is not None:
            self.groups[gid]["ops"].append(i)
        return i

    def sbuf(self, name, shape, dtype):
        return self._es.enter_context(self.nc.sbuf_tensor(name, shape, dtype))

    def psum(self, name, shape, dtype=F32):
        return self._es.enter_context(self.nc.psum_tensor(name, shape, dtype))

    def dram_in(self, name, shape, dtype=F32):
        return self.nc.dram_tensor(name, shape, dtype, kind="ExternalInput").ap()

    def dram_out(self, name, shape, dtype=F32):
        return self.nc.dram_tensor(name, shape, dtype, kind="ExternalOutput").ap()

    def build(self, body):
        with contextlib.ExitStack() as es:
            self._es = es
            eng_sems = {e: es.enter_context(self.nc.semaphore(f"ch_{e}")) for e in ENGINES}
            body(self)
            grp_sems = {g: es.enter_context(self.nc.semaphore(f"g_{g}")) for g in self.groups}
            nums = sorted([s.num for s in eng_sems.values()] + [s.num for s in grp_sems.values()])
            rng = range(min(nums), max(nums) + 1)
            self.nc.gpsimd.dma_reset(rng)
            self.nc.gpsimd.sem_clear(rng)
            self.nc.all_engine_barrier()

            n = len(self.ops)
            needed = [False] * n
            per_eng = {e: [] for e in ENGINES}
            grp_pos = {}
            for g, info in self.groups.items():
                for j, i in enumerate(info["ops"]):
                    grp_pos[i] = (g, j + 1)
            for i, (e, fn, deps, inc, gid) in enumerate(self.ops):
                per_eng[e].append(i)
                if gid is not None:
                    needed[i] = True
                for d in deps:
                    needed[d] = True
                    if self.ops[d][0] == e and self.ops[d][4] is None and gid is None:
                        assert d < i, f"same-engine dep must precede: {d} -> {i}"
            eng_cnt = {e: 0 for e in ENGINES}
            idx = [0] * n
            for i, (e, fn, deps, inc, gid) in enumerate(self.ops):
                if gid is None and needed[i]:
                    eng_cnt[e] += 1
                idx[i] = eng_cnt[e]

            def wait_val(d):
                e, _, _, inc, gid = self.ops[d]
                if gid is None:
                    return (eng_sems[e], idx[d])
                g, pos = grp_pos[d]
                if self.groups[g]["stream"]:
                    return (grp_sems[g], 16 * pos)
                return (grp_sems[g], 16 * len(self.groups[g]["ops"]))

            blk = es.enter_context(self.nc.Block())
            for e in ENGINES:
                ids = per_eng[e]
                if not ids:
                    continue

                def section(eng, ids=ids, e=e):
                    hi = {}
                    for i in ids:
                        _, fn, deps, inc, gid = self.ops[i]
                        want = {}
                        for d in deps:
                            sem, v = wait_val(d)
                            want[sem.num] = (sem, max(want[sem.num][1] if sem.num in want else 0, v))
                        for _, (sem, v) in sorted(want.items()):
                            if hi.get(sem.num, -1) < v:
                                eng.wait_ge(sem, v)
                                hi[sem.num] = v
                        instr = fn(eng)
                        if needed[i]:
                            assert instr is not None, f"op {i} on {e} returned None"
                            if gid is not None:
                                instr.then_inc(grp_sems[gid], 16)
                            else:
                                instr.then_inc(eng_sems[e], 1)

                getattr(blk, e)(section)
        return self.nc



def fblocks(n, step=128):
    out = []
    off = 0
    while off < n:
        out.append((off, min(step, n - off)))
        off += step
    return out


def np_bf16(x):
    return np.ascontiguousarray(x, dtype=BF)


def npT(x):
    return np.ascontiguousarray(np.swapaxes(np.asarray(x), -1, -2))


IDENT = np.eye(128, dtype=BF)
ONES128 = np.ones((128, 1), np.float32)
ONESBF = np.ones((128, 1), BF)


def ln_stats_ops(p, sc_sq, x_ap, st, rows, eps, mean_out=True):
    raise NotImplementedError





def emit_ln_tile(pr, x_ap, p, sb, deps, eps_col, sub_mean=True):
    s_sq, s_st, s_ln = sb["sq"], sb["st"], sb["ln"]
    d_sq = pr.op("scalar", lambda sc: sc.activation(s_sq[0:p, :], x_ap, AF.Square), deps=deps)
    d_r1 = pr.op("vector", lambda v: v.tensor_reduce(s_st[0:p, 0:1], x_ap, axis=AX.X, op=ALU.add), deps=deps)
    d_r2 = pr.op("vector", lambda v: v.tensor_reduce(s_st[0:p, 1:2], s_sq[0:p, :], axis=AX.X, op=ALU.add), deps=[d_sq])
    d_m = pr.op("vector", lambda v: v.tensor_scalar(s_st[0:p, 0:1], s_st[0:p, 0:1], 1.0 / D, None, op0=ALU.mult), deps=[d_r1])
    d_v = pr.op("vector", lambda v: v.tensor_scalar(s_st[0:p, 1:2], s_st[0:p, 1:2], 1.0 / D, None, op0=ALU.mult), deps=[d_r2])
    d_m2 = pr.op("vector", lambda v: v.tensor_tensor(s_st[0:p, 2:3], s_st[0:p, 0:1], s_st[0:p, 0:1], op=ALU.mult), deps=[d_m])
    d_var = pr.op("vector", lambda v: v.tensor_tensor(s_st[0:p, 3:4], s_st[0:p, 1:2], s_st[0:p, 2:3], op=ALU.subtract), deps=[d_v, d_m2])
    d_rs = pr.op("scalar", lambda sc: sc.activation(s_st[0:p, 4:5], s_st[0:p, 3:4], AF.Ln, bias=eps_col), deps=[d_var])
    d_rs2 = pr.op("scalar", lambda sc: sc.activation(s_st[0:p, 4:5], s_st[0:p, 4:5], AF.Exp, scale=-0.5), deps=[d_rs])
    d_ln = pr.op("vector", lambda v: v.tensor_scalar(
        s_ln[0:p, :], x_ap, s_st[0:p, 0:1], s_st[0:p, 4:5], op0=ALU.subtract, op1=ALU.mult), deps=[d_rs2, d_m])
    return d_ln


def emit_rms_tile(pr, x_ap, out_ap, p, sb, deps, eps_col):
    s_sq, s_st = sb["sq"], sb["st"]
    d_sq = pr.op("vector", lambda v: v.tensor_tensor(s_sq[0:p, :], x_ap, x_ap, op=ALU.mult), deps=deps)
    d_r = pr.op("vector", lambda v: v.tensor_reduce(s_st[0:p, 5:6], s_sq[0:p, :], axis=AX.X, op=ALU.add), deps=[d_sq])
    d_mm = pr.op("vector", lambda v: v.tensor_scalar(s_st[0:p, 5:6], s_st[0:p, 5:6], 1.0 / D, None, op0=ALU.mult), deps=[d_r])
    d_l = pr.op("scalar", lambda sc: sc.activation(s_st[0:p, 6:7], s_st[0:p, 5:6], AF.Ln, bias=eps_col), deps=[d_mm])
    d_e = pr.op("scalar", lambda sc: sc.activation(s_st[0:p, 6:7], s_st[0:p, 6:7], AF.Exp, scale=-0.5), deps=[d_l])
    d_o = pr.op("vector", lambda v: v.tensor_scalar(out_ap, x_ap, s_st[0:p, 6:7], None, op0=ALU.mult), deps=[d_e] + deps)
    return d_o


def emit_transpose_cols(pr, src_bf_ap_fn, dstT, p, col_off, p_tr, s_id, deps, wcolT=None,
                        nblk=8, prev_evac=None):
    evacs = []
    prev = prev_evac
    for j in range(nblk):
        d_tp = pr.op("tensor", lambda t, j=j: t.transpose(
            p_tr[:, 0:p], src_bf_ap_fn(j), s_id[0:p, 0:p]),
            deps=deps + ([prev] if prev is not None else []))
        if wcolT is not None:
            d_ev = pr.op("vector", lambda v, j=j: v.tensor_scalar(
                dstT[j][:, col_off:col_off + p], p_tr[:, 0:p], wcolT[:, j:j + 1], None,
                op0=ALU.mult), deps=[d_tp])
        else:
            d_ev = pr.op("vector", lambda v, j=j: v.tensor_copy(
                dstT[j][:, col_off:col_off + p], p_tr[:, 0:p]), deps=[d_tp])
        evacs.append(d_ev)
        prev = d_ev
    return evacs, prev



def build_l1():
    pr = Prog("l1")
    TP = TC + 3
    x_pad = pr.dram_in("x_pad", [TP, D])
    w_norm = pr.dram_in("w_norm", [1, D])
    w_normT = pr.dram_in("w_normT", [128, D // 128])
    in_projT = pr.dram_in("in_projT", [D, 2 * DI])
    conv_ck = pr.dram_in("conv_ck", [DI, 4])
    conv_bk = pr.dram_in("conv_bk", [128, DI // 128])
    x_projT = pr.dram_in("x_projT", [DI, 33])
    ident = pr.dram_in("ident", [128, 128], BF16)
    o_xconv = pr.dram_out("o_xconv", [DI, TC], BF16)
    o_ssp = pr.dram_out("o_ssp", [33, TC])
    o_z = pr.dram_out("o_z", [DI, TC], BF16)
    o_dbg = pr.dram_out("o_dbg", [8 * 128, TC + 3], BF16) if DEBUG_L1 else None

    def body(pr):
        nc = pr.nc
        s_x = [pr.sbuf(f"s_x{i}", [128, D], F32) for i in range(3)]
        s_ln = pr.sbuf("s_ln", [128, D], F32)
        s_sq = pr.sbuf("s_sq", [128, D], F32)
        s_xnbf = pr.sbuf("s_xnbf", [128, D], BF16)
        s_st = pr.sbuf("s_st", [128, 8], F32)
        s_cst = pr.sbuf("s_cst", [128, 2], F32)
        s_wnT = pr.sbuf("s_wnT", [128, 8], F32)
        s_id = pr.sbuf("s_id", [128, 128], BF16)
        s_xnT = [pr.sbuf(f"s_xnT{j}", [128, TP], BF16) for j in range(8)]
        s_w = [pr.sbuf(f"s_w{k}", [128, 1024], BF16) for k in range(2)]
        s_fb = pr.sbuf("s_fb", [128, TP], F32)
        s_conv = pr.sbuf("s_conv", [128, TC], F32)
        s_cw = pr.sbuf("s_cw", [128, 4 * 16], F32)
        s_cb = pr.sbuf("s_cb", [128, 16], F32)
        s_xc = [pr.sbuf(f"s_xc{j}", [128, TC], BF16) for j in range(16)]
        s_zb = pr.sbuf("s_zb", [128, TC], BF16)
        s_xp = pr.sbuf("s_xp", [128, 33 * 16], BF16)
        s_ssp = pr.sbuf("s_ssp", [33, TC], F32)
        p_tr = pr.psum("p_tr", [128, 128], BF16)
        p_mm = [pr.psum(f"p_mm{i}", [128, 512]) for i in range(2)]
        p_sp = pr.psum("p_sp", [33, 512])

        tiles = [(0, 3)] + [(3 + 128 * i, 128) for i in range(16)]
        prev_gate = []
        x_consumed = {}

        d_c0s = pr.op("vector", lambda v: v.memset(s_cst[:, 0:1], LN_EPS))
        d_c1s = pr.op("vector", lambda v: v.memset(s_cst[:, 1:2], RMS_EPS))
        ld_wnT = pr.op("sync", lambda s: s.dma_start(
            out=s_wnT[:, :], in_=w_normT[:, :]), dma=True, group=pr.dma_group("consts"))
        ld_id = pr.op("sync", lambda s: s.dma_start(out=s_id[:], in_=ident[:]), dma=True, group=pr.dma_group("consts"))
        ld_cw = pr.op("sync", lambda s: s.dma_start(
            out=s_cw[:, :].rearrange("p (a k) -> p a k", a=16),
            in_=conv_ck[:, :].rearrange("(a p) k -> p a k", p=128)), dma=True, group=pr.dma_group("consts"))
        ld_cb = pr.op("sync", lambda s: s.dma_start(out=s_cb[:, :], in_=conv_bk[:, :]), dma=True, group=pr.dma_group("consts"))
        ld_xp = pr.op("gpsimd", lambda g: g.dma_start(
            out=s_xp[:, :].rearrange("p (a c) -> p a c", a=16),
            in_=x_projT[:, :].rearrange("(a p) c -> p a c", p=128)), dma=True, group=pr.dma_group("gconsts"))

        tr_done = []
        for i, (off, p) in enumerate(tiles):
            xb = s_x[i % 3]
            ld_xi = pr.op("sync", lambda s, xb=xb, off=off, p=p: s.dma_start(
                out=xb[0:p, :], in_=x_pad[off:off + p, :]),
                deps=[x_consumed[i - 3]] if (i - 3) in x_consumed else [], dma=True,
                group=pr.dma_group(f"x{i % 3}", stream=True))
            d_sq = pr.op("scalar", lambda sc, xb=xb, p=p: sc.activation(
                s_sq[0:p, :], xb[0:p, :], AF.Square), deps=[ld_xi] + prev_gate)
            d_r1 = pr.op("vector", lambda v, p=p, xb=xb: v.tensor_reduce(
                s_st[0:p, 0:1], xb[0:p, :], axis=AX.X, op=ALU.add), deps=[ld_xi])
            d_r2 = pr.op("vector", lambda v, p=p: v.tensor_reduce(
                s_st[0:p, 1:2], s_sq[0:p, :], axis=AX.X, op=ALU.add), deps=[d_sq])
            d_m = pr.op("vector", lambda v, p=p: v.tensor_scalar(
                s_st[0:p, 0:1], s_st[0:p, 0:1], 1.0 / D, None, op0=ALU.mult), deps=[d_r1])
            d_v = pr.op("vector", lambda v, p=p: v.tensor_scalar(
                s_st[0:p, 1:2], s_st[0:p, 1:2], 1.0 / D, None, op0=ALU.mult), deps=[d_r2])
            d_m2 = pr.op("vector", lambda v, p=p: v.tensor_tensor(
                s_st[0:p, 2:3], s_st[0:p, 0:1], s_st[0:p, 0:1], op=ALU.mult), deps=[d_m])
            d_var = pr.op("vector", lambda v, p=p: v.tensor_tensor(
                s_st[0:p, 3:4], s_st[0:p, 1:2], s_st[0:p, 2:3], op=ALU.subtract),
                deps=[d_v, d_m2])
            d_rs = pr.op("scalar", lambda sc, p=p: sc.activation(
                s_st[0:p, 4:5], s_st[0:p, 3:4], AF.Ln, bias=s_cst[0:p, 0:1]), deps=[d_var, d_c0s])
            d_rs2 = pr.op("scalar", lambda sc, p=p: sc.activation(
                s_st[0:p, 4:5], s_st[0:p, 4:5], AF.Exp, scale=-0.5), deps=[d_rs])
            d_ln = pr.op("vector", lambda v, p=p, xb=xb: v.tensor_scalar(
                s_ln[0:p, :], xb[0:p, :], s_st[0:p, 0:1], s_st[0:p, 4:5],
                op0=ALU.subtract, op1=ALU.mult), deps=[d_rs2, d_m])
            x_consumed[i] = d_ln
            d_sq2 = pr.op("vector", lambda v, p=p: v.tensor_tensor(
                s_sq[0:p, :], s_ln[0:p, :], s_ln[0:p, :], op=ALU.mult), deps=[d_ln])
            d_r3 = pr.op("vector", lambda v, p=p: v.tensor_reduce(
                s_st[0:p, 5:6], s_sq[0:p, :], axis=AX.X, op=ALU.add), deps=[d_sq2])
            d_mm = pr.op("vector", lambda v, p=p: v.tensor_scalar(
                s_st[0:p, 5:6], s_st[0:p, 5:6], 1.0 / D, None, op0=ALU.mult), deps=[d_r3])
            d_l2 = pr.op("scalar", lambda sc, p=p: sc.activation(
                s_st[0:p, 6:7], s_st[0:p, 5:6], AF.Ln, bias=s_cst[0:p, 1:2]), deps=[d_mm, d_c1s])
            d_e2 = pr.op("scalar", lambda sc, p=p: sc.activation(
                s_st[0:p, 6:7], s_st[0:p, 6:7], AF.Exp, scale=-0.5), deps=[d_l2])
            d_xn = pr.op("vector", lambda v, p=p: v.tensor_scalar(
                s_xnbf[0:p, :], s_ln[0:p, :], s_st[0:p, 6:7], None, op0=ALU.mult),
                deps=[d_e2, d_ln] + prev_gate)
            prev_evacs = []
            for j in range(8):
                d_tp = pr.op("tensor", lambda t, p=p, j=j: t.transpose(
                    p_tr[:, 0:p], s_xnbf[0:p, j * 128:(j + 1) * 128], s_id[0:p, 0:p]),
                    deps=[d_xn, ld_id] + tr_done[-1:])
                d_ev = pr.op("vector", lambda v, p=p, j=j, off=off: v.tensor_scalar(
                    s_xnT[j][:, off:off + p], p_tr[:, 0:p], s_wnT[:, j:j + 1], None,
                    op0=ALU.mult), deps=[d_tp, ld_wnT])
                tr_done.append(d_ev)
            prev_gate = [d_xn, d_tp]
        d_xnT_all = tr_done[-1]

        if DEBUG_L1:
            dbg_st = [pr.op("sync", lambda s, j=j: s.dma_start(
                out=o_dbg[j * 128:(j + 1) * 128, :], in_=s_xnT[j][:, :]),
                deps=[d_xnT_all], dma=True, group=pr.dma_group("st_dbg")) for j in range(8)]

        TCH = [(0, 512), (512, 512), (1024, 512), (1536, 512), (2048, 3)]
        d_wload = {}
        conv_ops = []
        last_mm_on_bank = [None, None]
        prev_fb_scalar_reader = None
        prev_si = None
        prev_zst = None
        for f in range(32):
            wb = s_w[f % 2]
            dep_prev_use = d_wload.get(f - 2, [])
            d_w = pr.op("gpsimd", lambda g, f=f, wb=wb: g.dma_start(
                out=wb[:, :].rearrange("p (a c) -> p a c", a=8),
                in_=in_projT[:, f * 128:(f + 1) * 128].rearrange("(a p) c -> p a c", p=128)),
                deps=dep_prev_use, dma=True, group=pr.dma_group(f"w{f % 2}", stream=True))
            fb_parts = []
            for ci, (co, cw) in enumerate(TCH):
                bank = ci % 2
                mm_deps = [d_w, d_xnT_all]
                if last_mm_on_bank[bank] is not None:
                    mm_deps.append(last_mm_on_bank[bank][1])
                d_mmlast = None
                for k in range(8):
                    d_mmlast = pr.op("tensor", lambda t, wb=wb, k=k, co=co, cw=cw, bank=bank, first=(k == 0): t.matmul(
                        p_mm[bank][:, 0:cw], wb[:, k * 128:(k + 1) * 128],
                        s_xnT[k][:, co:co + cw], start=first, stop=(k == 7)),
                        deps=(mm_deps if k == 0 else [d_mmlast]))
                ev_deps = [d_mmlast]
                if ci == 0 and prev_fb_scalar_reader is not None:
                    ev_deps.append(prev_fb_scalar_reader)
                d_ev = pr.op("vector", lambda v, co=co, cw=cw, bank=bank: v.tensor_copy(
                    s_fb[:, co:co + cw], p_mm[bank][:, 0:cw]), deps=ev_deps)
                last_mm_on_bank[bank] = (d_mmlast, d_ev)
                fb_parts.append(d_ev)
            d_wload[f] = [last_mm_on_bank[0][0], last_mm_on_bank[1][0]]
            if f < 16:
                d_c0 = pr.op("vector", lambda v, f=f: v.tensor_scalar(
                    s_conv[:, :], s_fb[:, 0:TC], s_cw[:, f * 4:f * 4 + 1], None,
                    op0=ALU.mult), deps=fb_parts + [ld_cw] + ([prev_si] if prev_si else []))
                d_ck = d_c0
                for k in range(1, 4):
                    d_ck = pr.op("vector", lambda v, f=f, k=k: v.scalar_tensor_tensor(
                        s_conv[:, :], s_fb[:, k:k + TC], s_cw[:, f * 4 + k:f * 4 + k + 1],
                        s_conv[:, :], op0=ALU.mult, op1=ALU.add), deps=[d_ck])
                d_si = pr.op("scalar", lambda sc, f=f: sc.activation(
                    s_xc[f][:, :], s_conv[:, :], AF.Silu, bias=s_cb[:, f:f + 1]),
                    deps=[d_ck, ld_cb])
                prev_si = d_si
                d_st = pr.op("sync", lambda s, f=f: s.dma_start(
                    out=o_xconv[f * 128:(f + 1) * 128, :], in_=s_xc[f][:, :]),
                    deps=[d_si], dma=True, group=pr.dma_group("st_xc"))
                conv_ops.append((d_si, d_st))
            else:
                d_zb = pr.op("scalar", lambda sc: sc.activation(
                    s_zb[:, :], s_fb[:, 3:3 + TC], AF.Identity),
                    deps=fb_parts + ([prev_zst] if prev_zst else []))
                prev_fb_scalar_reader = d_zb
                d_st = pr.op("sync", lambda s, f=f: s.dma_start(
                    out=o_z[(f - 16) * 128:(f - 15) * 128, :], in_=s_zb[:, :]),
                    deps=[d_zb], dma=True, group=pr.dma_group("zst", stream=True))
                prev_zst = d_st
                conv_ops.append((d_zb, d_st))

        sp_evac = []
        for ci, (co, cw) in enumerate(fblocks(TC, 512)):
            d_mmlast = None
            for k in range(16):
                d_mmlast = pr.op("tensor", lambda t, k=k, co=co, cw=cw, first=(k == 0): t.matmul(
                    p_sp[:, 0:cw], s_xp[:, k * 33:(k + 1) * 33], s_xc[k][:, co:co + cw],
                    start=first, stop=(k == 15)),
                    deps=([ld_xp, conv_ops[15][0]] + sp_evac[-1:]) if k == 0 else [d_mmlast])
            d_ev = pr.op("vector", lambda v, co=co, cw=cw: v.tensor_copy(
                s_ssp[:, co:co + cw], p_sp[:, 0:cw]), deps=[d_mmlast])
            sp_evac.append(d_ev)
        d_spst = pr.op("sync", lambda s: s.dma_start(out=o_ssp[:, :], in_=s_ssp[:, :]),
                       deps=sp_evac, dma=True, group=pr.dma_group("st_sp"))

        all_st = [d_spst] + [st for (_x, st) in conv_ops] + (dbg_st if DEBUG_L1 else [])
        pr.op("vector", lambda v: v.engine_nop(), deps=all_st)

    nc = pr.build(body)
    return nc



def _run(nc, in_maps):
    res = run_bass_kernel_spmd(nc, in_maps, core_ids=list(range(8)))
    return res.results


def _l1_inmaps(inputs):
    x = np.asarray(inputs["L0_attn_x"])
    wT = {
        "w_norm": np.asarray(inputs["L0_attn_norm_m_w"]).reshape(1, D),
        "w_normT": npT(np.asarray(inputs["L0_attn_norm_m_w"]).reshape(D // 128, 128)),
        "in_projT": npT(inputs["L0_attn_in_proj_w"]),
        "conv_ck": np.ascontiguousarray(np.asarray(inputs["L0_attn_conv_w"])[:, 0, :]),
        "conv_bk": npT(np.asarray(inputs["L0_attn_conv_b"]).reshape(DI // 128, 128)),
        "x_projT": npT(inputs["L0_attn_x_proj_w"]),
        "ident": IDENT,
    }
    maps = []
    for c in range(8):
        b, h = c // 2, c % 2
        t0 = h * TC
        if h == 0:
            xp = np.concatenate([np.zeros((3, D), np.float32), x[b, 0:TC]], axis=0)
        else:
            xp = x[b, t0 - 3:t0 + TC]
        maps.append({"x_pad": np.ascontiguousarray(xp), **wT})
    return maps


def run_l1(inputs):
    nc = build_l1()
    res = _run(nc, _l1_inmaps(inputs))
    xconv = np.stack([np.concatenate([res[2 * b]["o_xconv"], res[2 * b + 1]["o_xconv"]], axis=1) for b in range(B)])
    ssp = np.stack([np.concatenate([res[2 * b]["o_ssp"], res[2 * b + 1]["o_ssp"]], axis=1) for b in range(B)])
    z = np.stack([np.concatenate([res[2 * b]["o_z"], res[2 * b + 1]["o_z"]], axis=1) for b in range(B)])
    return xconv, ssp, z





def build_l2(nrows=1024, seqlen=L, dstate=16, tchunk=256, delta_input=False):
    pr = Prog("l2")
    NP = nrows // 128
    NTC = seqlen // tchunk
    U = tchunk + 1
    xconv = pr.dram_in("xconv", [nrows, seqlen], BF16)
    ssp_bc = pr.dram_in("ssp_bc", [2 * dstate, seqlen])
    if delta_input:
        deltaT = pr.dram_in("deltaT", [nrows, seqlen])
    else:
        dtraw = pr.dram_in("dtraw", [1, seqlen])
        dt_wT = pr.dram_in("dt_wT", [1, nrows])
        dt_bT = pr.dram_in("dt_bT", [128, NP])
    A_log = pr.dram_in("A_log", [nrows, dstate])
    D_skipT = pr.dram_in("D_skipT", [128, NP])
    o_y = pr.dram_out("o_y", [nrows, seqlen])

    def body(pr):
        s_xc = [pr.sbuf(f"s_xc{j}", [128, seqlen], BF16) for j in range(NP)]
        s_bc = pr.sbuf("s_bc", [2 * dstate, seqlen], BF16)
        s_flat = pr.sbuf("s_flat", [1, 2 * dstate * tchunk], BF16)
        s_onesb = pr.sbuf("s_onesb", [1, 128], BF16)
        s_dtraw = pr.sbuf("s_dtraw", [1, seqlen], F32)
        s_dtw = pr.sbuf("s_dtw", [1, nrows], F32)
        s_dtb = pr.sbuf("s_dtb", [128, NP], F32)
        s_A = pr.sbuf("s_A", [128, NP * dstate], F32)
        s_D = pr.sbuf("s_D", [128, NP], F32)
        s_Bbig = pr.sbuf("s_Bbig", [128, dstate * U], BF16)
        s_Cbig = pr.sbuf("s_Cbig", [128, dstate * U], BF16)
        s_dA = pr.sbuf("s_dA", [128, dstate * U], F32)
        s_dBu = pr.sbuf("s_dBu", [128, dstate * U], BF16)
        s_h = pr.sbuf("s_h", [128, dstate * U], F32)
        s_yp = pr.sbuf("s_yp", [128, dstate * U], BF16)
        s_carry = [pr.sbuf(f"s_carry{j}", [128, dstate], F32) for j in range(NP)]
        s_delta = pr.sbuf("s_delta", [128, tchunk], F32)
        s_u = pr.sbuf("s_u", [128, tchunk], BF16)
        s_sp1 = pr.sbuf("s_sp1", [128, tchunk], F32)
        s_sp2 = pr.sbuf("s_sp2", [128, tchunk], F32)
        s_y = pr.sbuf("s_y", [128, tchunk], F32)
        s_yout = pr.sbuf("s_yout", [128, tchunk], F32)
        p_dl = pr.psum("p_dl", [128, tchunk])
        p_bc = pr.psum("p_bc", [128, tchunk])

        ld_xc = [pr.op("sync", lambda s, j=j: s.dma_start(
            out=s_xc[j][:], in_=xconv[j * 128:(j + 1) * 128, :]), dma=True, group=pr.dma_group("consts"))
            for j in range(NP)]
        ld_bc = pr.op("gpsimd", lambda g: g.dma_start(out=s_bc[:], in_=ssp_bc[:]), dma=True, group=pr.dma_group("gconsts"))
        if not delta_input:
            ld_dtr = pr.op("sync", lambda s: s.dma_start(out=s_dtraw[:], in_=dtraw[:]), dma=True, group=pr.dma_group("consts"))
            ld_dtw = pr.op("sync", lambda s: s.dma_start(out=s_dtw[:], in_=dt_wT[:]), dma=True, group=pr.dma_group("consts"))
            ld_dtb = pr.op("sync", lambda s: s.dma_start(out=s_dtb[:], in_=dt_bT[:]), dma=True, group=pr.dma_group("consts"))
        ld_A = pr.op("sync", lambda s: s.dma_start(
            out=s_A[:, :].rearrange("p (a n) -> p a n", a=NP),
            in_=A_log[:, :].rearrange("(a p) n -> p a n", p=128)), dma=True, group=pr.dma_group("consts"))
        ld_D = pr.op("sync", lambda s: s.dma_start(out=s_D[:], in_=D_skipT[:]), dma=True, group=pr.dma_group("consts"))

        d_Ae = pr.op("scalar", lambda sc: sc.activation(s_A[:], s_A[:], AF.Exp), deps=[ld_A])
        d_An = pr.op("vector", lambda v: v.tensor_scalar(
            s_A[:], s_A[:], -1.0, None, op0=ALU.mult), deps=[d_Ae])
        d_ones = pr.op("gpsimd", lambda g: g.memset(s_onesb[:], 1.0))
        d_z0 = pr.op("vector", lambda v: v.memset(
            s_dA[:, :].rearrange("p (n u) -> p n u", u=U)[:, :, 0:1], 0.0))
        d_zc = [pr.op("vector", lambda v, j=j: v.memset(s_carry[j][:], 0.0)) for j in range(NP)]

        prev_chunk_store = {}
        prev_carry_dep = {j: d_zc[j] for j in range(NP)}
        prev_bcast_users = []
        prev_pt_cycle = []
        for tc in range(NTC):
            co = tc * tchunk
            d_g = pr.op("gpsimd", lambda g, co=co: g.dma_start(
                out=s_flat[0:1, :].rearrange("p (n t) -> p n t", n=2 * dstate),
                in_=s_bc[:, co:co + tchunk]), deps=[ld_bc] + prev_bcast_users, dma=True,
                group=pr.dma_group("flat", stream=True))
            bc_ev = []
            d_mm_prev = None
            for n in range(2 * dstate):
                dst = s_Bbig if n < dstate else s_Cbig
                nn = n % dstate
                d_mm = pr.op("tensor", lambda t, n=n: t.matmul(
                    p_bc[:, 0:tchunk], s_onesb[:], s_flat[0:1, n * tchunk:(n + 1) * tchunk],
                    start=True, stop=True), deps=[d_g, d_ones] + bc_ev[-1:])
                d_mm_last = d_mm
                d_ev = pr.op("scalar", lambda sc, dst=dst, nn=nn: sc.activation(
                    dst[:, nn * U + 1:nn * U + 1 + tchunk], p_bc[:, 0:tchunk], AF.Identity),
                    deps=[d_mm])
                bc_ev.append(d_ev)
            d_bcast_done = bc_ev[-1]
            prev_bcast_users = [d_mm_last]

            for pt in range(NP):
                gate = prev_pt_cycle
                if delta_input:
                    d_dlt = pr.op("sync", lambda s, pt=pt, co=co: s.dma_start(
                        out=s_delta[:], in_=deltaT[pt * 128:(pt + 1) * 128, co:co + tchunk]),
                        deps=gate, dma=True, group=pr.dma_group("dl", stream=True))
                    d_a2 = d_dlt
                else:
                    d_dl = pr.op("tensor", lambda t, pt=pt, co=co: t.matmul(
                        p_dl[:, 0:tchunk], s_dtw[0:1, pt * 128:(pt + 1) * 128],
                        s_dtraw[0:1, co:co + tchunk], start=True, stop=True),
                        deps=[ld_dtw, ld_dtr] + gate)
                    d_a1 = pr.op("scalar", lambda sc, pt=pt: sc.activation(
                        s_sp1[:], p_dl[:, 0:tchunk], AF.Abs, bias=s_dtb[:, pt:pt + 1]),
                        deps=[d_dl, ld_dtb] + gate)
                    d_a2 = pr.op("scalar", lambda sc, pt=pt: sc.activation(
                        s_sp2[:], p_dl[:, 0:tchunk], AF.Relu, bias=s_dtb[:, pt:pt + 1]),
                        deps=[d_dl, ld_dtb] + gate)
                    d_a3 = pr.op("scalar", lambda sc: sc.activation(
                        s_sp1[:], s_sp1[:], AF.Exp, scale=-1.0), deps=[d_a1])
                    d_a4 = pr.op("scalar", lambda sc: sc.activation(
                        s_sp1[:], s_sp1[:], AF.Ln, bias=1.0), deps=[d_a3])
                    d_dlt = pr.op("vector", lambda v: v.tensor_tensor(
                        s_delta[:], s_sp1[:], s_sp2[:], op=ALU.add), deps=[d_a4, d_a2] + gate)
                d_u = pr.op("vector", lambda v, pt=pt, co=co: v.tensor_tensor(
                    s_u[:], s_delta[:], s_xc[pt][:, co:co + tchunk], op=ALU.mult),
                    deps=[d_dlt, ld_xc[pt]])
                d_dAs = []
                for n in range(dstate):
                    d_dAs.append(pr.op("scalar", lambda sc, n=n, pt=pt: sc.activation(
                        s_dA[:, n * U + 1:n * U + 1 + tchunk], s_delta[:], AF.Exp,
                        scale=s_A[:, pt * dstate + n:pt * dstate + n + 1]),
                        deps=[d_dlt, d_An] + gate))
                d_cc = pr.op("vector", lambda v, pt=pt: v.tensor_copy(
                    s_dBu[:, :].rearrange("p (n u) -> p n u", u=U)[:, :, 0:1],
                    s_carry[pt][:, :].rearrange("p (n o) -> p n o", o=1)),
                    deps=[prev_carry_dep[pt]] + gate)
                d_dBus = []
                for n in range(dstate):
                    d_dBus.append(pr.op("vector", lambda v, n=n: v.tensor_tensor(
                        s_dBu[:, n * U + 1:n * U + 1 + tchunk], s_u[:],
                        s_Bbig[:, n * U + 1:n * U + 1 + tchunk], op=ALU.mult),
                        deps=[d_u, d_bcast_done] + gate))
                d_scan = pr.op("vector", lambda v: v.tensor_tensor_scan(
                    s_h[:], s_dA[:], s_dBu[:], 0.0, op0=ALU.mult, op1=ALU.add),
                    deps=d_dAs + d_dBus + [d_cc, d_z0] + gate)
                d_sc = pr.op("vector", lambda v, pt=pt: v.tensor_copy(
                    s_carry[pt][:, :].rearrange("p (n o) -> p n o", o=1),
                    s_h[:, :].rearrange("p (n u) -> p n u", u=U)[:, :, U - 1:U]),
                    deps=[d_scan])
                prev_carry_dep[pt] = d_sc
                d_yp = pr.op("vector", lambda v: v.tensor_tensor(
                    s_yp[:], s_h[:], s_Cbig[:], op=ALU.mult), deps=[d_scan, d_bcast_done])
                d_y = pr.op("vector", lambda v: v.tensor_reduce(
                    s_y[:, :], s_yp[:, :].rearrange("p (n u) -> p u n", u=U)[:, 1:, :],
                    axis=AX.X, op=ALU.add), deps=[d_yp])
                d_yo = pr.op("vector", lambda v, pt=pt, co=co: v.scalar_tensor_tensor(
                    s_yout[:], s_xc[pt][:, co:co + tchunk], s_D[:, pt:pt + 1], s_y[:],
                    op0=ALU.mult, op1=ALU.add), deps=[d_y, ld_D])
                d_st = pr.op("sync", lambda s, pt=pt, co=co: s.dma_start(
                    out=o_y[pt * 128:(pt + 1) * 128, co:co + tchunk], in_=s_yout[:]),
                    deps=[d_yo], dma=True, group=pr.dma_group("yst", stream=True))
                prev_chunk_store[(tc, pt)] = d_st
                prev_pt_cycle = [d_yo, d_scan, d_st, d_a2]
                prev_bcast_users.extend([d_dBus[-1], d_yp])

        pr.op("vector", lambda v: v.engine_nop(),
              deps=[prev_chunk_store[(NTC - 1, j)] for j in range(NP)])

    nc = pr.build(body)
    return nc


def _l2_inmaps(inputs, xconv_bdt, ssp_b33t):
    dt_wT = npT(np.asarray(inputs["L0_attn_dt_w"]))
    dt_b = np.asarray(inputs["L0_attn_dt_b"])
    A_log = np.asarray(inputs["L0_attn_A_log"])
    D_skip = np.asarray(inputs["L0_attn_D_skip"])
    maps = []
    for c in range(8):
        b, dh = c // 2, c % 2
        r0 = dh * 1024
        maps.append({
            "xconv": np.ascontiguousarray(xconv_bdt[b, r0:r0 + 1024]),
            "ssp_bc": np.ascontiguousarray(ssp_b33t[b, 0:32]),
            "dtraw": np.ascontiguousarray(ssp_b33t[b, 32:33]),
            "dt_wT": np.ascontiguousarray(dt_wT[:, r0:r0 + 1024]),
            "dt_bT": npT(dt_b[r0:r0 + 1024].reshape(8, 128)),
            "A_log": np.ascontiguousarray(A_log[r0:r0 + 1024]),
            "D_skipT": npT(D_skip[r0:r0 + 1024].reshape(8, 128)),
        })
    return maps


def run_l2(inputs, xconv, ssp):
    nc = build_l2()
    res = _run(nc, _l2_inmaps(inputs, xconv, ssp))
    y = np.stack([np.concatenate([res[2 * b]["o_y"], res[2 * b + 1]["o_y"]], axis=0) for b in range(B)])
    return y





def build_l3():
    pr = Prog("l3")
    x_in = pr.dram_in("x_in", [TC, D])
    yT = pr.dram_in("yT", [DI, TC])
    zT = pr.dram_in("zT", [DI, TC], BF16)
    out_projT = pr.dram_in("out_projT", [DI, D])
    w_attnT = pr.dram_in("w_attnT", [128, 8])
    q_wT = pr.dram_in("q_wT", [D, D])
    k_wT = pr.dram_in("k_wT", [D, 512])
    v_wT = pr.dram_in("v_wT", [D, 512])
    ident = pr.dram_in("ident", [128, 128], BF16)
    o_s = pr.dram_out("o_s", [TC, D])
    o_QT = pr.dram_out("o_QT", [D, TC], BF16)
    o_KT = pr.dram_out("o_KT", [512, TC], BF16)
    o_V = pr.dram_out("o_V", [TC, 512], BF16)

    def body(pr):
        s_g = [pr.sbuf(f"s_g{j}", [128, TC], BF16) for j in range(16)]
        s_yb = pr.sbuf("s_yb", [128, TC], F32)
        s_zb = pr.sbuf("s_zb", [128, TC], BF16)
        s_sz = pr.sbuf("s_sz", [128, TC], BF16)
        s_opw = [pr.sbuf(f"s_opw{j}", [128, D], BF16) for j in range(16)]
        s_qw = [pr.sbuf(f"s_qw{j}", [128, D], BF16) for j in range(8)]
        s_kw = [pr.sbuf(f"s_kw{j}", [128, 512], BF16) for j in range(8)]
        s_vw = [pr.sbuf(f"s_vw{j}", [128, 512], BF16) for j in range(8)]
        s_x = [pr.sbuf(f"s_x{i}", [128, D], F32) for i in range(3)]
        sb = {"sq": pr.sbuf("s_sq", [128, D], F32), "st": pr.sbuf("s_st", [128, 8], F32),
              "ln": pr.sbuf("s_ln", [128, D], F32)}
        s_cst = pr.sbuf("s_cst", [128, 2], F32)
        s_s = pr.sbuf("s_s", [128, D], F32)
        s_xn2 = pr.sbuf("s_xn2", [128, D], BF16)
        s_wat = pr.sbuf("s_wat", [128, 8], F32)
        s_id = pr.sbuf("s_id", [128, 128], BF16)
        s_xnT = [pr.sbuf(f"s_xnT{j}", [128, TC], BF16) for j in range(8)]
        s_stg = [pr.sbuf(f"s_stg{j}", [128, 512], BF16) for j in range(2)]
        p_tr = pr.psum("p_tr", [128, 128], BF16)
        p_mm = [pr.psum(f"p_mm{i}", [128, 512]) for i in range(2)]

        d_c0 = pr.op("vector", lambda v: v.memset(s_cst[:, 0:1], LN_EPS))
        d_c1 = pr.op("vector", lambda v: v.memset(s_cst[:, 1:2], RMS_EPS))
        ld_id = pr.op("sync", lambda s: s.dma_start(out=s_id[:], in_=ident[:]), dma=True, group=pr.dma_group("consts"))
        ld_wat = pr.op("sync", lambda s: s.dma_start(out=s_wat[:], in_=w_attnT[:]), dma=True, group=pr.dma_group("consts"))
        ld_opw = [pr.op("gpsimd", lambda g, j=j: g.dma_start(
            out=s_opw[j][:], in_=out_projT[j * 128:(j + 1) * 128, :]), dma=True, group=pr.dma_group("wts")) for j in range(16)]
        ld_qw = [pr.op("gpsimd", lambda g, j=j: g.dma_start(
            out=s_qw[j][:], in_=q_wT[j * 128:(j + 1) * 128, :]), dma=True, group=pr.dma_group("wts")) for j in range(8)]
        ld_kw = [pr.op("gpsimd", lambda g, j=j: g.dma_start(
            out=s_kw[j][:], in_=k_wT[j * 128:(j + 1) * 128, :]), dma=True, group=pr.dma_group("wts")) for j in range(8)]
        ld_vw = [pr.op("gpsimd", lambda g, j=j: g.dma_start(
            out=s_vw[j][:], in_=v_wT[j * 128:(j + 1) * 128, :]), dma=True, group=pr.dma_group("wts")) for j in range(8)]

        g_done = []
        prev = []
        for ptile in range(16):
            ld_y = pr.op("sync", lambda s, ptile=ptile: s.dma_start(
                out=s_yb[:], in_=yT[ptile * 128:(ptile + 1) * 128, :]), deps=prev, dma=True,
                group=pr.dma_group("gy", stream=True))
            ld_z = pr.op("sync", lambda s, ptile=ptile: s.dma_start(
                out=s_zb[:], in_=zT[ptile * 128:(ptile + 1) * 128, :]), deps=prev, dma=True,
                group=pr.dma_group("gz", stream=True))
            d_si = pr.op("scalar", lambda sc: sc.activation(s_sz[:], s_zb[:], AF.Silu), deps=[ld_z] + prev)
            d_mul = pr.op("vector", lambda v, ptile=ptile: v.tensor_tensor(
                s_g[ptile][:], s_yb[:], s_sz[:], op=ALU.mult), deps=[d_si, ld_y])
            g_done.append(d_mul)
            prev = [d_mul]
        d_gall = g_done[-1]

        x_cons = {}
        bank_ev = [None, None]
        tp_gate = []
        st_gate = []
        prev_s_st = None
        for ti in range(16):
            ld_x = pr.op("sync", lambda s, ti=ti: s.dma_start(
                out=s_x[ti % 3][:], in_=x_in[ti * 128:(ti + 1) * 128, :]),
                deps=[x_cons[ti - 3]] if (ti - 3) in x_cons else [], dma=True,
                group=pr.dma_group(f"x{ti % 3}", stream=True))
            d_ln = emit_ln_tile(pr, s_x[ti % 3][:, :], 128, sb, [ld_x] + tp_gate, s_cst[:, 0:1])
            x_cons[ti] = d_ln
            ev_dm = []
            for dm in range(2):
                d_mml = None
                for k in range(16):
                    d_mml = pr.op("tensor", lambda t, k=k, ti=ti, dm=dm, first=(k == 0): t.matmul(
                        p_mm[dm][:, :], s_g[k][:, ti * 128:(ti + 1) * 128],
                        s_opw[k][:, dm * 512:(dm + 1) * 512], start=first, stop=(k == 15)),
                        deps=([d_gall, ld_opw[15]] + ([bank_ev[dm]] if bank_ev[dm] else [])) if k == 0 else [d_mml])
                d_ev = pr.op("vector", lambda v, dm=dm: v.tensor_tensor(
                    s_s[:, dm * 512:(dm + 1) * 512], p_mm[dm][:, :],
                    sb["ln"][:, dm * 512:(dm + 1) * 512], op=ALU.add),
                    deps=[d_mml, d_ln] + ([prev_s_st] if prev_s_st else []))
                bank_ev[dm] = d_ev
                ev_dm.append(d_ev)
            d_st = pr.op("sync", lambda s, ti=ti: s.dma_start(
                out=o_s[ti * 128:(ti + 1) * 128, :], in_=s_s[:, :]), deps=ev_dm, dma=True,
                group=pr.dma_group("sst", stream=True))
            prev_s_st = d_st
            st_gate = [d_st]
            d_xn = emit_rms_tile(pr, s_s[:, :], s_xn2[:, :], 128, sb, ev_dm + tp_gate, s_cst[:, 1:2])
            evs, last_ev = emit_transpose_cols(
                pr, lambda j: s_xn2[:, j * 128:(j + 1) * 128], s_xnT, 128, ti * 128,
                p_tr, s_id, [d_xn, ld_id, ld_wat], wcolT=s_wat)
            tp_gate = [evs[-1], d_st]
        d_xnT_all = tp_gate[0]

        outs_st = []
        stg_free = [None, None]
        p_free = [bank_ev[0], bank_ev[1]]
        si = 0
        def qkv_mm(dst_dram, row_off, wtiles, wcol_off, tcw, deps_extra):
            nonlocal si
            d_mml = None
            bank = si % 2
            for k in range(8):
                d_mml = pr.op("tensor", lambda t, k=k, first=(k == 0), bank=bank, wcol_off=wcol_off, tcw=tcw: t.matmul(
                    p_mm[bank][:, 0:512], wtiles[k][:, wcol_off:wcol_off + 128] if False else wtiles[k],
                    s_xnT[k][:, tcw:tcw + 512], start=first, stop=(k == 7)),
                    deps=([d_xnT_all] + deps_extra + ([p_free[bank]] if p_free[bank] else [])) if k == 0 else [d_mml])
            stg = s_stg[si % 2]
            d_ev = pr.op("scalar", lambda sc, stg=stg, bank=bank: sc.activation(
                stg[:], p_mm[bank][:, 0:512], AF.Identity),
                deps=[d_mml] + ([stg_free[si % 2]] if stg_free[si % 2] else []))
            p_free[bank] = d_ev
            d_st = pr.op("sync", lambda s, stg=stg: s.dma_start(
                out=dst_dram, in_=stg[:]), deps=[d_ev], dma=True)
            stg_free[si % 2] = d_st
            outs_st.append(d_st)
            si += 1

        for ptq in range(8):
            qw_slices = [None] * 8
            for tcw in range(0, TC, 512):
                d_mml = None
                bank = si % 2
                for k in range(8):
                    d_mml = pr.op("tensor", lambda t, k=k, ptq=ptq, tcw=tcw, first=(k == 0), bank=bank: t.matmul(
                        p_mm[bank][:, 0:512], s_qw[k][:, ptq * 128:(ptq + 1) * 128],
                        s_xnT[k][:, tcw:tcw + 512], start=first, stop=(k == 7)),
                        deps=([d_xnT_all, ld_qw[7]] + ([p_free[bank]] if p_free[bank] else [])) if k == 0 else [d_mml])
                stg = s_stg[si % 2]
                d_ev = pr.op("scalar", lambda sc, stg=stg, bank=bank: sc.activation(
                    stg[:], p_mm[bank][:, 0:512], AF.Identity),
                    deps=[d_mml] + ([stg_free[si % 2]] if stg_free[si % 2] else []))
                p_free[bank] = d_ev
                d_st = pr.op("sync", lambda s, stg=stg, ptq=ptq, tcw=tcw: s.dma_start(
                    out=o_QT[ptq * 128:(ptq + 1) * 128, tcw:tcw + 512], in_=stg[:]),
                    deps=[d_ev], dma=True, group=pr.dma_group(f"qst{si % 2}", stream=True))
                stg_free[si % 2] = d_st
                outs_st.append(d_st)
                si += 1
        for ptq in range(4):
            for tcw in range(0, TC, 512):
                d_mml = None
                bank = si % 2
                for k in range(8):
                    d_mml = pr.op("tensor", lambda t, k=k, ptq=ptq, tcw=tcw, first=(k == 0), bank=bank: t.matmul(
                        p_mm[bank][:, 0:512], s_kw[k][:, :].rearrange("p (a c) -> p a c", a=4)[:, ptq, :] if False else s_kw[k][:, ptq * 128:(ptq + 1) * 128],
                        s_xnT[k][:, tcw:tcw + 512], start=first, stop=(k == 7)),
                        deps=([d_xnT_all, ld_kw[7]] + ([p_free[bank]] if p_free[bank] else [])) if k == 0 else [d_mml])
                stg = s_stg[si % 2]
                d_ev = pr.op("scalar", lambda sc, stg=stg, bank=bank: sc.activation(
                    stg[:], p_mm[bank][:, 0:512], AF.Identity),
                    deps=[d_mml] + ([stg_free[si % 2]] if stg_free[si % 2] else []))
                p_free[bank] = d_ev
                d_st = pr.op("sync", lambda s, stg=stg, ptq=ptq, tcw=tcw: s.dma_start(
                    out=o_KT[ptq * 128:(ptq + 1) * 128, tcw:tcw + 512], in_=stg[:]),
                    deps=[d_ev], dma=True, group=pr.dma_group(f"qst{si % 2}", stream=True))
                stg_free[si % 2] = d_st
                outs_st.append(d_st)
                si += 1
        for ti in range(16):
            d_mml = None
            bank = si % 2
            for k in range(8):
                d_mml = pr.op("tensor", lambda t, k=k, ti=ti, first=(k == 0), bank=bank: t.matmul(
                    p_mm[bank][:, 0:512], s_xnT[k][:, ti * 128:(ti + 1) * 128],
                    s_vw[k][:, :], start=first, stop=(k == 7)),
                    deps=([d_xnT_all, ld_vw[7]] + ([p_free[bank]] if p_free[bank] else [])) if k == 0 else [d_mml])
            stg = s_stg[si % 2]
            d_ev = pr.op("scalar", lambda sc, stg=stg, bank=bank: sc.activation(
                stg[:], p_mm[bank][:, 0:512], AF.Identity),
                deps=[d_mml] + ([stg_free[si % 2]] if stg_free[si % 2] else []))
            p_free[bank] = d_ev
            d_st = pr.op("sync", lambda s, stg=stg, ti=ti: s.dma_start(
                out=o_V[ti * 128:(ti + 1) * 128, :], in_=stg[:]), deps=[d_ev], dma=True,
                group=pr.dma_group(f"qst{si % 2}", stream=True))
            stg_free[si % 2] = d_st
            outs_st.append(d_st)
            si += 1

        pr.op("vector", lambda v: v.engine_nop(), deps=outs_st[-3:] + st_gate)

    nc = pr.build(body)
    return nc


def _l3_inmaps(inputs, yT_full, zT_full):
    x = np.asarray(inputs["L0_attn_x"])
    common = {
        "out_projT": npT(inputs["L0_attn_out_proj_w"]),
        "w_attnT": npT(np.asarray(inputs["L0_attn_norm_attn_w"]).reshape(8, 128)),
        "q_wT": npT(inputs["L0_attn_q_w"]),
        "k_wT": npT(inputs["L0_attn_k_w"]),
        "v_wT": npT(inputs["L0_attn_v_w"]),
        "ident": IDENT,
    }
    maps = []
    for c in range(8):
        b, h = c // 2, c % 2
        t0 = h * TC
        maps.append({
            "x_in": np.ascontiguousarray(x[b, t0:t0 + TC]),
            "yT": np.ascontiguousarray(yT_full[b, :, t0:t0 + TC]),
            "zT": np.ascontiguousarray(zT_full[b, :, t0:t0 + TC]),
            **common,
        })
    return maps


def run_l3(inputs, yT_full, zT_full):
    nc = build_l3()
    res = _run(nc, _l3_inmaps(inputs, yT_full, zT_full))
    s = np.stack([np.concatenate([res[2 * b]["o_s"], res[2 * b + 1]["o_s"]], axis=0) for b in range(B)])
    QT = np.stack([np.concatenate([res[2 * b]["o_QT"], res[2 * b + 1]["o_QT"]], axis=1) for b in range(B)])
    KT = np.stack([np.concatenate([res[2 * b]["o_KT"], res[2 * b + 1]["o_KT"]], axis=1) for b in range(B)])
    V = np.stack([np.concatenate([res[2 * b]["o_V"], res[2 * b + 1]["o_V"]], axis=0) for b in range(B)])
    return s, QT, KT, V





def build_l4():
    pr = Prog("l4")
    QT = pr.dram_in("QT", [D, TC], BF16)
    KT = pr.dram_in("KT", [512, L], BF16)
    Vt = pr.dram_in("Vt", [L, 512], BF16)
    qpos0 = pr.dram_in("qpos0", [128, 1])
    kpos_in = pr.dram_in("kpos_in", [128, L])
    ident = pr.dram_in("ident", [128, 128], BF16)
    o_ctx = pr.dram_out("o_ctx", [TC, D], BF16)
    if DEBUG_L4:
        o_dbg_row = pr.dram_out("o_dbg_row", [128, L])
        o_dbg_p = pr.dram_out("o_dbg_p", [128, L], BF16)
        o_dbg_sum = pr.dram_out("o_dbg_sum", [128, 2])
        o_dbg_q = pr.dram_out("o_dbg_q", [128, TC], BF16)
        o_dbg_k = pr.dram_out("o_dbg_k", [128, L], BF16)

    def body(pr):
        s_qt = [pr.sbuf(f"s_qt{j}", [128, TC], BF16) for j in range(8)]
        s_kt = [pr.sbuf(f"s_kt{j}", [128, L], BF16) for j in range(4)]
        s_v = [pr.sbuf(f"s_v{j}", [128, 512], BF16) for j in range(32)]
        s_kpos = pr.sbuf("s_kpos", [128, L], F32)
        s_qp0 = pr.sbuf("s_qp0", [128, 1], F32)
        s_qp = pr.sbuf("s_qp", [128, 1], F32)
        s_m = pr.sbuf("s_m", [128, 512], F32)
        s_row = pr.sbuf("s_row", [128, L], F32)
        s_mx = pr.sbuf("s_mx", [128, 2], F32)
        s_sum = pr.sbuf("s_sum", [128, 2], F32)
        s_p = pr.sbuf("s_p", [128, L], BF16)
        s_pt = pr.sbuf("s_pt", [128, L], BF16)
        s_ctx = [pr.sbuf(f"s_ctx{i}", [128, D], BF16) for i in range(2)]
        s_id = pr.sbuf("s_id", [128, 128], BF16)
        p_sc = [pr.psum(f"p_sc{i}", [128, 512]) for i in range(2)]
        p_tr = pr.psum("p_tr", [128, 128], BF16)
        p_av = pr.psum("p_av", [128, 256])
        s_fence = pr.sbuf("s_fence", [1, 2], F32)

        ld_q = [pr.op("sync", lambda s, j=j: s.dma_start(
            out=s_qt[j][:], in_=QT[j * 128:(j + 1) * 128, :]), dma=True, group=pr.dma_group("lds")) for j in range(8)]
        ld_k = [pr.op("sync", lambda s, j=j: s.dma_start(
            out=s_kt[j][:], in_=KT[j * 128:(j + 1) * 128, :]), dma=True, group=pr.dma_group("lds")) for j in range(4)]
        ld_v = [pr.op("sync", lambda s, j=j: s.dma_start(
            out=s_v[j][:], in_=Vt[j * 128:(j + 1) * 128, :]), dma=True, group=pr.dma_group("lds")) for j in range(32)]
        ld_kp = pr.op("sync", lambda s: s.dma_start(out=s_kpos[:], in_=kpos_in[:]), dma=True, group=pr.dma_group("lds"))
        ld_qp = pr.op("sync", lambda s: s.dma_start(out=s_qp0[:], in_=qpos0[:]), dma=True, group=pr.dma_group("lds"))
        ld_id = pr.op("sync", lambda s: s.dma_start(out=s_id[:], in_=ident[:]), dma=True, group=pr.dma_group("consts"))
        ldall = ld_q + ld_k + ld_v + [ld_kp, ld_qp, ld_id]

        ctx_sts = []
        pend = []
        st_parity = [None, None]
        prev_cycle = []
        prev_ctx_st = None
        p_free = [None, None]
        av_free = None
        tr_free = None
        for qi in range(16):
            kcnt = (TC + qi * 128 + 127) // 512 + 1
            kcnt = min(kcnt, 8)
            d_qp = pr.op("vector", lambda v, qi=qi: v.tensor_scalar(
                s_qp[:], s_qp0[:], float(qi * 128), None, op0=ALU.add),
                deps=[ld_qp] + prev_cycle)
            for h in range(4):
                kv = h // 2
                gate = prev_cycle
                mask_evs = []
                for kj in range(kcnt):
                    bank = kj % 2
                    d_mml = None
                    for dk in range(2):
                        d_mml = pr.op("tensor", lambda t, h=h, dk=dk, qi=qi, kj=kj, kv=kv, bank=bank, first=(dk == 0): t.matmul(
                            p_sc[bank][:, :], s_qt[h * 2 + dk][:, qi * 128:(qi + 1) * 128],
                            s_kt[kv * 2 + dk][:, kj * 512:(kj + 1) * 512],
                            start=first, stop=(dk == 1)),
                            deps=(ldall + gate + ([p_free[bank]] if p_free[bank] else [])) if dk == 0 else [d_mml])
                    d_m = pr.op("vector", lambda v, kj=kj: v.tensor_scalar(
                        s_m[:], s_kpos[:, kj * 512:(kj + 1) * 512], s_qp[:, 0:1], MASK_NEG,
                        op0=ALU.is_gt, op1=ALU.mult), deps=[d_qp, ld_kp] + gate)
                    d_ev = pr.op("vector", lambda v, kj=kj, bank=bank: v.scalar_tensor_tensor(
                        s_row[:, kj * 512:(kj + 1) * 512], p_sc[bank][:, :], 0.0625, s_m[:],
                        op0=ALU.mult, op1=ALU.add), deps=[d_mml, d_m])
                    p_free[bank] = d_ev
                    mask_evs.append(d_ev)
                W = kcnt * 512
                d_mx = pr.op("vector", lambda v, W=W: v.tensor_reduce(
                    s_mx[:, 0:1], s_row[:, 0:W], axis=AX.X, op=ALU.max), deps=mask_evs)
                d_nm = pr.op("vector", lambda v: v.tensor_scalar(
                    s_mx[:, 1:2], s_mx[:, 0:1], -1.0, None, op0=ALU.mult), deps=[d_mx])
                d_exp = pr.op("scalar", lambda sc, W=W: sc.activation(
                    s_p[:, 0:W], s_row[:, 0:W], AF.Exp, bias=s_mx[:, 1:2]), deps=[d_nm] + gate)
                d_sm = pr.op("vector", lambda v, W=W: v.tensor_reduce(
                    s_sum[:, 0:1], s_p[:, 0:W], axis=AX.X, op=ALU.add), deps=[d_exp] + gate)
                d_l = pr.op("scalar", lambda sc: sc.activation(
                    s_sum[:, 1:2], s_sum[:, 0:1], AF.Ln), deps=[d_sm])
                d_rc = pr.op("scalar", lambda sc: sc.activation(
                    s_sum[:, 1:2], s_sum[:, 1:2], AF.Exp, scale=-1.0), deps=[d_l])
                d_fence = pr.op("scalar", lambda sc: sc.activation(
                    s_fence[:, 0:1], s_fence[:, 0:1], AF.Identity), deps=[d_exp])
                if DEBUG_L4 and qi == 0 and h == 0:
                    dmp1 = pr.op("sync", lambda s: s.dma_start(out=o_dbg_q[:, :], in_=s_qt[0][:, :]), deps=[d_mx], dma=True)
                    dmp2 = pr.op("sync", lambda s: s.dma_start(out=o_dbg_k[:, :], in_=s_kt[0][:, :]), deps=[d_mx], dma=True)
                    dmp3 = pr.op("sync", lambda s: s.dma_start(out=o_dbg_row[:, :], in_=s_row[:, :]), deps=[d_mx], dma=True)
                    dmp4 = pr.op("sync", lambda s: s.dma_start(out=o_dbg_p[:, :], in_=s_p[:, :]), deps=[d_exp], dma=True)
                    dmp5 = pr.op("sync", lambda s: s.dma_start(out=o_dbg_sum[:, :], in_=s_sum[:, :]), deps=[d_rc], dma=True)
                    prev_cycle = prev_cycle + [dmp1, dmp2, dmp3, dmp4, dmp5]
                tblocks = kcnt * 4
                d_pt_evs = []
                prevt = tr_free
                for tb in range(tblocks):
                    d_tp = pr.op("tensor", lambda t, tb=tb: t.transpose(
                        p_tr[:, :], s_p[:, tb * 128:(tb + 1) * 128], s_id[:]),
                        deps=[d_exp, d_fence, ld_id] + ([prevt] if prevt else []) + gate)
                    d_pe = pr.op("vector", lambda v, tb=tb: v.tensor_copy(
                        s_pt[:, tb * 128:(tb + 1) * 128], p_tr[:, :]), deps=[d_tp])
                    d_pt_evs.append(d_pe)
                    prevt = d_pe
                tr_free = prevt
                d_mml = None
                for tb in range(tblocks):
                    d_mml = pr.op("tensor", lambda t, tb=tb, kv=kv, first=(tb == 0), last=(tb == tblocks - 1): t.matmul(
                        p_av[:, :], s_pt[:, tb * 128:(tb + 1) * 128],
                        s_v[tb][:, kv * 256:(kv + 1) * 256], start=first, stop=last),
                        deps=([d_pt_evs[-1]] + ([av_free] if av_free else [])) if tb == 0 else [d_mml])
                d_cv = pr.op("vector", lambda v, h=h, qi=qi: v.tensor_scalar(
                    s_ctx[qi % 2][:, h * 256:(h + 1) * 256], p_av[:, :], s_sum[:, 1:2], None,
                    op0=ALU.mult), deps=[d_mml, d_rc] + ([st_parity[qi % 2]] if (h == 0 and st_parity[qi % 2]) else []))
                av_free = d_cv
                prev_cycle = [d_cv, d_exp, d_mx]
            pend.append((qi, av_free))
            if len(pend) == 2:
                q0, dep0 = pend.pop(0)
                d_st = pr.op("sync", lambda s, q0=q0: s.dma_start(
                    out=o_ctx[q0 * 128:(q0 + 1) * 128, :], in_=s_ctx[q0 % 2][:]),
                    deps=[dep0, av_free], dma=True,
                    group=pr.dma_group(f"cst{q0 % 2}", stream=True))
                prev_ctx_st = d_st
                st_parity[q0 % 2] = d_st
                ctx_sts.append(d_st)

        d_fl = pr.op("vector", lambda v: v.tensor_copy(s_m[:, 0:4], s_ctx[15 % 2][:, 0:4]),
                     deps=[pend[0][1]])
        d_st = pr.op("sync", lambda s: s.dma_start(
            out=o_ctx[15 * 128:16 * 128, :], in_=s_ctx[15 % 2][:]), deps=[pend[0][1], d_fl],
            dma=True, group=pr.dma_group("cst1", stream=True))
        ctx_sts.append(d_st)
        pr.op("vector", lambda v: v.engine_nop(), deps=ctx_sts[-2:])

    nc = pr.build(body)
    return nc


def _l4_inmaps(QT_full, KT_full, V_full):
    kpos = np.broadcast_to(np.arange(L, dtype=np.float32)[None, :], (128, L))
    kpos = np.ascontiguousarray(kpos)
    maps = []
    for c in range(8):
        b, h = c // 2, c % 2
        t0 = h * TC
        maps.append({
            "QT": np.ascontiguousarray(QT_full[b][:, t0:t0 + TC]),
            "KT": np.ascontiguousarray(KT_full[b]),
            "Vt": np.ascontiguousarray(V_full[b]),
            "qpos0": (t0 + np.arange(128, dtype=np.float32)).reshape(128, 1),
            "kpos_in": kpos,
            "ident": IDENT,
        })
    return maps


def run_l4(QT_full, KT_full, V_full):
    nc = build_l4()
    res = _run(nc, _l4_inmaps(QT_full, KT_full, V_full))
    ctx = np.stack([np.concatenate([res[2 * b]["o_ctx"], res[2 * b + 1]["o_ctx"]], axis=0) for b in range(B)])
    return ctx





def build_l5():
    pr = Prog("l5")
    s_in = pr.dram_in("s_in", [TC, D])
    ctx_in = pr.dram_in("ctx_in", [TC, D], BF16)
    x_in = pr.dram_in("x_in", [TC, D])
    oT = pr.dram_in("oT", [D, D])
    w_ffnT = pr.dram_in("w_ffnT", [128, 8])
    gateT = pr.dram_in("gateT", [D, FH])
    upT = pr.dram_in("upT", [D, FH])
    downT = pr.dram_in("downT", [FH, D])
    fcT = pr.dram_in("fcT", [D, FFH])
    ln_wT = pr.dram_in("ln_wT", [128, 32])
    ln_bT = pr.dram_in("ln_bT", [128, 32])
    projT = pr.dram_in("projT", [FFH, D])
    W_in1 = pr.dram_in("W_in1", [D, 2 * IN2])
    ident = pr.dram_in("ident", [128, 128], BF16)
    o_x2 = pr.dram_out("o_x2", [TC, D])
    o_raw = pr.dram_out("o_raw", [IN2, TC], BF16)
    o_g2 = pr.dram_out("o_g2", [IN2, TC], BF16)
    fcQ = pr.nc.dram_tensor("fcQ", [D, FFH], BF16).ap()
    pjQ = pr.nc.dram_tensor("pjQ", [FFH, D], BF16).ap()
    if DEBUG_L5:
        o_dbg_s2 = pr.dram_out("o_dbg_s2", [128, D])
        o_dbg_x2a = pr.dram_out("o_dbg_x2a", [128, D])
        o_dbg_fcq = pr.dram_out("o_dbg_fcq", [128, FFH], BF16)
        o_dbg_thr = pr.dram_out("o_dbg_thr", [1, 8])
        o_dbg_xn3 = pr.dram_out("o_dbg_xn3", [128, 512], BF16)
        o_dbg_g2 = pr.dram_out("o_dbg_g2", [128, 512], BF16)
        o_dbg_ha = pr.dram_out("o_dbg_ha", [128, D])
        o_dbg_wc = pr.dram_out("o_dbg_wc", [128, D], BF16)
        o_dbg_g2b = pr.dram_out("o_dbg_g2b", [128, 512], BF16)

    FB = fblocks(FH)

    def body(pr):
        s_oT = [pr.sbuf(f"s_oT{j}", [128, D], BF16) for j in range(8)]
        s_xnT = [pr.sbuf(f"s_xnT{j}", [128, 512], BF16) for j in range(8)]
        s_ctxT = pr.sbuf("s_ctxT", [128, 8 * 128], BF16)
        pool = [pr.sbuf(f"pool{j}", [128, 512], BF16) for j in range(32)]
        wch = [pr.sbuf(f"wch{j}", [128, 1024], BF16) for j in range(16)]
        s_s2 = [pr.sbuf(f"s_s2{i}", [128, D], F32) for i in range(4)]
        s_x2a = [pr.sbuf(f"s_x2a{i}", [128, D], F32) for i in range(4)]
        s_x2 = [pr.sbuf(f"s_x2{i}", [128, D], F32) for i in range(4)]
        s_ld = [pr.sbuf(f"s_ldS{i}", [128, D], F32) for i in range(1)]
        s_xr = [pr.sbuf(f"s_xr{i}", [128, D], F32) for i in range(4)]
        s_cx = [pr.sbuf(f"s_cx{i}", [128, D], BF16) for i in range(1)]
        sb = {"sq": pr.sbuf("s_sq", [128, D], F32), "st": pr.sbuf("s_st", [128, 8], F32),
              "ln": pr.sbuf("s_ln", [128, D], F32)}
        s_nbf = pr.sbuf("s_nbf", [128, D], BF16)
        s_cst = pr.sbuf("s_cst", [128, 2], F32)
        s_wff = pr.sbuf("s_wff", [128, 8], F32)
        s_lnw = pr.sbuf("s_lnw", [128, 32], F32)
        s_lnb = pr.sbuf("s_lnb", [128, 32], F32)
        s_id = pr.sbuf("s_id", [128, 128], BF16)
        s_onesb = pr.sbuf("s_onesb", [128, 1], BF16)
        s_ones1 = pr.sbuf("s_ones1", [1, 128], F32)
        s_onc = pr.sbuf("s_onc", [128, 1], F32)
        s_gw = [pr.sbuf(f"s_gw{i}", [128, 8 * 128], BF16) for i in range(2)]
        s_uw = [pr.sbuf(f"s_uw{i}", [128, 8 * 128], BF16) for i in range(2)]
        s_fw = [pr.sbuf(f"s_fw{i}", [128, 8 * 128], BF16) for i in range(2)]
        s_wi = [pr.sbuf(f"s_wi{i}", [128, 8 * 128], BF16) for i in range(2)]
        s_qs = [pr.sbuf(f"s_qs{i}", [128, 512], F32) for i in range(2)]
        s_sg = pr.sbuf("s_sg", [128, 512], BF16)
        s_tsc = pr.sbuf("s_tsc", [128, 512], F32)
        s_mu = pr.sbuf("s_mu", [128, 512], F32)
        s_rs = pr.sbuf("s_rs", [128, 512], F32)
        s_row = pr.sbuf("s_row", [1, 1024], F32)
        s_acc = pr.sbuf("s_acc", [128, 4], F32)
        s_thr = pr.sbuf("s_thr", [128, 4], F32)
        s_g11 = pr.sbuf("s_g11", [1, 8], F32)
        s_stg = [pr.sbuf(f"s_stg{i}", [128, 512], BF16) for i in range(2)]
        p_tr = pr.psum("p_tr", [128, 128], BF16)
        p_mm = [pr.psum(f"p_mm{i}", [128, 512]) for i in range(2)]
        p_st = [pr.psum(f"p_st{i}", [1, 512]) for i in range(2)]
        p_bc = pr.psum("p_bc", [128, 512])
        p_g = pr.psum("p_g", [1, 1])

        d_e0 = pr.op("vector", lambda v: v.memset(s_cst[:, 0:1], LN_EPS))
        d_e1 = pr.op("vector", lambda v: v.memset(s_cst[:, 1:2], RMS_EPS))
        d_ob = pr.op("vector", lambda v: v.memset(s_onesb[:], 1.0))
        d_o1 = pr.op("vector", lambda v: v.memset(s_ones1[:], 1.0))
        d_oc = pr.op("vector", lambda v: v.memset(s_onc[:], 1.0))
        CON = pr.dma_group("consts")
        ld_id = pr.op("sync", lambda s: s.dma_start(out=s_id[:], in_=ident[:]), dma=True, group=CON)
        ld_wff = pr.op("sync", lambda s: s.dma_start(out=s_wff[:], in_=w_ffnT[:]), dma=True, group=CON)
        ld_lnw = pr.op("sync", lambda s: s.dma_start(out=s_lnw[:], in_=ln_wT[:]), dma=True, group=CON)
        ld_lnb = pr.op("sync", lambda s: s.dma_start(out=s_lnb[:], in_=ln_bT[:]), dma=True, group=CON)
        WCON = pr.dma_group("wconsts")
        ld_oT = [pr.op("gpsimd", lambda g, j=j: g.dma_start(
            out=s_oT[j][:], in_=oT[j * 128:(j + 1) * 128, :]), dma=True, group=WCON) for j in range(8)]

        slabs = []
        for j in range(8):
            for c in range(8):
                slabs.append((fcT, fcQ, j * 128, c * 512, 0))
        for j in range(32):
            for c in range(2):
                slabs.append((projT, pjQ, j * 128, c * 512, 1))
        d_z = pr.op("vector", lambda v: v.memset(s_acc[:, :], 0.0))
        prev_use = {0: [], 1: []}
        d_accl = [d_z]
        for i, (mat, dst, r0, c0, mi) in enumerate(slabs):
            qb = s_qs[i % 2]
            ld = pr.op("sync", lambda s, mat=mat, r0=r0, c0=c0, qb=qb: s.dma_start(
                out=qb[:], in_=mat[r0:r0 + 128, c0:c0 + 512]),
                deps=prev_use[i % 2], dma=True, group=pr.dma_group(f"q{i % 2}", stream=True))
            d_r = pr.op("vector", lambda v, qb=qb: v.tensor_reduce(
                s_acc[:, 2:3], qb[:], axis=AX.X, op=ALU.add, apply_absolute_value=True),
                deps=[ld] + d_accl)
            d_a = pr.op("vector", lambda v, mi=mi: v.tensor_tensor(
                s_acc[:, mi:mi + 1], s_acc[:, mi:mi + 1], s_acc[:, 2:3], op=ALU.add), deps=[d_r])
            d_accl = [d_a]
            prev_use[i % 2] = [d_r]
        thr_deps = []
        for mi in range(2):
            d_t = pr.op("tensor", lambda t, mi=mi: t.matmul(
                p_g[:, :], s_acc[:, mi:mi + 1], s_onc[:, 0:1], start=True, stop=True),
                deps=d_accl + [d_oc] + thr_deps)
            d_g1 = pr.op("vector", lambda v, mi=mi: v.tensor_scalar(
                s_g11[0:1, mi:mi + 1], p_g[0:1, 0:1], 1.0 / (D * FFH), 1e-6,
                op0=ALU.mult, op1=ALU.add), deps=[d_t])
            thr_deps.append(d_g1)
        d_th2 = pr.op("vector", lambda v: v.tensor_scalar(
            s_g11[0:1, 4:6], s_g11[0:1, 0:2], 0.5, None, op0=ALU.mult), deps=thr_deps)
        d_th3 = pr.op("vector", lambda v: v.tensor_scalar(
            s_g11[0:1, 6:8], s_g11[0:1, 4:6], -1.0, None, op0=ALU.mult), deps=[d_th2])
        d_tb = pr.op("tensor", lambda t: t.matmul(
            p_bc[:, 0:4], s_ones1[0:1, :], s_g11[0:1, 4:8], start=True, stop=True),
            deps=[d_th3, d_o1])
        d_te = pr.op("vector", lambda v: v.tensor_copy(s_thr[:, :], p_bc[:, 0:4]), deps=[d_tb])
        prev_use = {0: [], 1: []}
        prev_qst = None
        qst_last = None
        for i, (mat, dst, r0, c0, mi) in enumerate(slabs):
            qb = s_qs[i % 2]
            ld = pr.op("sync", lambda s, mat=mat, r0=r0, c0=c0, qb=qb: s.dma_start(
                out=qb[:], in_=mat[r0:r0 + 128, c0:c0 + 512]),
                deps=prev_use[i % 2], dma=True, group=pr.dma_group(f"q{i % 2}", stream=True))
            d_p = pr.op("vector", lambda v, qb=qb, mi=mi: v.tensor_scalar(
                s_mu[:], qb[:], s_thr[:, mi:mi + 1], None, op0=ALU.is_gt), deps=[ld, d_te])
            d_n = pr.op("vector", lambda v, qb=qb, mi=mi: v.tensor_scalar(
                s_rs[:], qb[:], s_thr[:, 2 + mi:3 + mi], None, op0=ALU.is_lt), deps=[ld, d_te])
            d_q = pr.op("vector", lambda v: v.tensor_tensor(
                s_sg[:], s_mu[:], s_rs[:], op=ALU.subtract),
                deps=[d_p, d_n] + ([prev_qst] if prev_qst else []))
            d_s = pr.op("sync", lambda s, dst=dst, r0=r0, c0=c0: s.dma_start(
                out=dst[r0:r0 + 128, c0:c0 + 512], in_=s_sg[:]), deps=[d_q],
                dma=True, group=pr.dma_group("qst", stream=True))
            prev_qst = d_s
            qst_last = d_s
            prev_use[i % 2] = [d_p, d_n]

        if DEBUG_L5:
            dth = pr.op("sync", lambda s: s.dma_start(out=o_dbg_thr[:, :], in_=s_g11[0:1, :]),
                        deps=[d_th3], dma=True, group=pr.dma_group("dbg", stream=True))
            dfq = pr.op("sync", lambda s: s.dma_start(out=o_dbg_fcq[:, :], in_=fcQ[0:128, :]),
                        deps=[qst_last, dth], dma=True, group=pr.dma_group("dbg", stream=True))
        gates = {
            "xnT": [],
            "pool": [],
            "wch": [qst_last],
            "bank": [None, None],
            "stg": [None, None],
            "x2st": [None] * 4,
            "ring": {},
            "stats": [],
            "sg": [qst_last],
            "tr": None,
            "nbf": [],
            "ctxT": [],
        }
        all_stores = []
        WIN_FB = fblocks(2 * IN2)
        _dbg_dd = [None]

        def mk_ld(buf, srcfn, grp, slot_key, extra=(), eng="sync"):
            dep = gates["ring"].get(slot_key, [])
            d = pr.op(eng, srcfn, deps=list(dep) + list(extra), dma=True,
                      group=pr.dma_group(grp, stream=True))
            return d

        for tch in range(4):
            tc0 = tch * 512
            xn_evs = []
            for t4 in range(4):
                ti = tch * 4 + t4
                sl = s_ld[0]; xr = s_xr[t4]; cx = s_cx[0]
                ld_s = mk_ld(sl, (lambda s, sl=sl, ti=ti: s.dma_start(
                    out=sl[:], in_=s_in[ti * 128:(ti + 1) * 128, :])), "lds0", ("s", 0))
                ld_x = mk_ld(xr, (lambda s, xr=xr, ti=ti: s.dma_start(
                    out=xr[:], in_=x_in[ti * 128:(ti + 1) * 128, :])), f"ldx{t4}", ("x", t4))
                ld_c = mk_ld(cx, (lambda s, cx=cx, ti=ti: s.dma_start(
                    out=cx[:], in_=ctx_in[ti * 128:(ti + 1) * 128, :])), "ldc0", ("c", 0))
                ctx_evs = []
                for j in range(8):
                    d_tp = pr.op("tensor", lambda t, j=j, cx=cx: t.transpose(
                        p_tr[:, :], cx[:, j * 128:(j + 1) * 128], s_id[:]),
                        deps=[ld_c, ld_id] + ([gates["tr"]] if gates["tr"] else []) + gates["ctxT"])
                    d_ev = pr.op("vector", lambda v, j=j: v.tensor_copy(
                        s_ctxT[:, j * 128:(j + 1) * 128], p_tr[:, :]), deps=[d_tp])
                    gates["tr"] = d_ev
                    ctx_evs.append(d_ev)
                gates["ring"][("c", 0)] = [ctx_evs[-1]]
                gates["ctxT"] = []
                ev_dm = []
                for dm in range(2):
                    d_mml = None
                    for k in range(8):
                        d_mml = pr.op("tensor", lambda t, k=k, dm=dm, first=(k == 0): t.matmul(
                            p_mm[dm][:, :], s_ctxT[:, k * 128:(k + 1) * 128],
                            s_oT[k][:, dm * 512:(dm + 1) * 512], start=first, stop=(k == 7)),
                            deps=([ctx_evs[-1], ld_oT[-1]] +
                                  ([gates["bank"][dm]] if gates["bank"][dm] else [])) if k == 0 else [d_mml])
                    d_ev = pr.op("vector", lambda v, dm=dm, t4=t4, sl=sl: v.tensor_tensor(
                        s_s2[t4][:, dm * 512:(dm + 1) * 512], p_mm[dm][:, :],
                        sl[:, dm * 512:(dm + 1) * 512], op=ALU.add), deps=[d_mml, ld_s])
                    gates["bank"][dm] = d_ev
                    ev_dm.append(d_ev)
                gates["ctxT"] = [gates["bank"][0], gates["bank"][1]]
                gates["ring"][("s", 0)] = [ev_dm[0], ev_dm[1]]
                if DEBUG_L5 and tch == 0 and t4 == 0:
                    dd = pr.op("sync", lambda s, t4=t4: s.dma_start(out=o_dbg_s2[:, :], in_=s_s2[t4][:, :]),
                               deps=ev_dm + [dfq], dma=True, group=pr.dma_group("dbg", stream=True))
                    _dbg_dd[0] = dd
                    gates["ring"][("s", 0)] = [ev_dm[0], ev_dm[1], dd]
                    gates["ctxT"] = gates["ctxT"] + [dd]
                    all_stores.append(dd)
                d_xn = emit_rms_tile(pr, s_s2[t4][:, :], s_nbf[:, :], 128, sb,
                                     ev_dm + gates["nbf"], s_cst[:, 1:2])
                evs, last_ev = emit_transpose_cols(
                    pr, lambda j, _b=s_nbf: _b[:, j * 128:(j + 1) * 128], s_xnT, 128, t4 * 128,
                    p_tr, s_id, [d_xn, ld_wff] + gates["xnT"], wcolT=s_wff,
                    prev_evac=gates["tr"])
                gates["tr"] = last_ev
                gates["nbf"] = [evs[-1]]
                xn_evs.append(last_ev)
            gates["xnT"] = []
            d_xn_all = xn_evs[-1]
            if DEBUG_L5 and tch == 0:
                ddn = pr.op("sync", lambda s: s.dma_start(out=o_dbg_xn3[:, :], in_=s_xnT[0][:, :]),
                            deps=[d_xn_all, _dbg_dd[0]], dma=True, group=pr.dma_group("dbg", stream=True))
                gates["xnT"] = [ddn]
                all_stores.append(ddn)

            g2_evs = []
            for fbi, (fo, fw) in enumerate(FB):
                gw = s_gw[fbi % 2]; uw = s_uw[fbi % 2]
                ld_g = mk_ld(gw, (lambda s, gw=gw, fo=fo, fw=fw: s.dma_start(
                    out=gw[:, 0:8 * fw].rearrange("p (a c) -> p a c", a=8),
                    in_=gateT[:, fo:fo + fw].rearrange("(a p) c -> p a c", p=128))),
                    f"gw{fbi % 2}", ("gw", fbi % 2), eng="gpsimd")
                ld_u = mk_ld(uw, (lambda s, uw=uw, fo=fo, fw=fw: s.dma_start(
                    out=uw[:, 0:8 * fw].rearrange("p (a c) -> p a c", a=8),
                    in_=upT[:, fo:fo + fw].rearrange("(a p) c -> p a c", p=128))),
                    f"uw{fbi % 2}", ("uw", fbi % 2), eng="gpsimd")
                d_m0 = None
                for k in range(8):
                    d_m0 = pr.op("tensor", lambda t, k=k, fw=fw, gw=gw, first=(k == 0): t.matmul(
                        p_mm[0][0:fw, :], gw[:, k * fw:(k + 1) * fw], s_xnT[k][:, :],
                        start=first, stop=(k == 7)),
                        deps=([ld_g, d_xn_all] + ([gates["bank"][0]] if gates["bank"][0] else [])) if k == 0 else [d_m0])
                d_si = pr.op("scalar", lambda sc, fw=fw: sc.activation(
                    s_sg[0:fw, :], p_mm[0][0:fw, :], AF.Silu), deps=[d_m0] + gates["sg"])
                gates["bank"][0] = d_si
                d_m1 = None
                for k in range(8):
                    d_m1 = pr.op("tensor", lambda t, k=k, fw=fw, uw=uw, first=(k == 0): t.matmul(
                        p_mm[1][0:fw, :], uw[:, k * fw:(k + 1) * fw], s_xnT[k][:, :],
                        start=first, stop=(k == 7)),
                        deps=([ld_u, d_xn_all] + ([gates["bank"][1]] if gates["bank"][1] else [])) if k == 0 else [d_m1])
                d_g2 = pr.op("vector", lambda v, fbi=fbi, fw=fw: v.tensor_tensor(
                    pool[fbi][0:fw, :], s_sg[0:fw, :], p_mm[1][0:fw, :], op=ALU.mult),
                    deps=[d_m1, d_si] + gates["pool"])
                gates["bank"][1] = d_g2
                gates["sg"] = [d_g2]
                gates["ring"][("gw", fbi % 2)] = [d_m0]
                gates["ring"][("uw", fbi % 2)] = [d_m1]
                g2_evs.append(d_g2)
                if DEBUG_L5 and tch == 0 and fbi == 0:
                    ddg = pr.op("sync", lambda s: s.dma_start(out=o_dbg_g2[:, :], in_=pool[0][:, :]),
                                deps=[d_g2, ddn], dma=True, group=pr.dma_group("dbg", stream=True))
                    gates["pool"] = gates["pool"] + [ddg]
                    all_stores.append(ddg)
                if DEBUG_L5 and tch == 0 and fbi == 13:
                    ddg2 = pr.op("sync", lambda s: s.dma_start(out=o_dbg_g2b[:, :], in_=pool[13][:, :]),
                                 deps=[d_g2, ddg], dma=True, group=pr.dma_group("dbg", stream=True))
                    gates["pool"] = gates["pool"] + [ddg2]
                    all_stores.append(ddg2)
            gates["pool"] = []
            d_g2_all = g2_evs[-1]

            for half in range(2):
                fbs = FB[half * 11:(half + 1) * 11]
                dwl = []
                for z, (fo, fw) in enumerate(fbs):
                    dwl.append(pr.op("gpsimd", lambda g, z=z, fo=fo, fw=fw: g.dma_start(
                        out=wch[z][0:fw, :], in_=downT[fo:fo + fw, :]),
                        deps=gates["wch"], dma=True, group=pr.dma_group(f"dw{tch}_{half}")))
                gates["wch"] = []
                if DEBUG_L5 and tch == 0 and half == 0:
                    ddwc = pr.op("sync", lambda s: s.dma_start(out=o_dbg_wc[:, :], in_=wch[0][:, :]),
                                 deps=[dwl[-1], ddg2], dma=True, group=pr.dma_group("dbg", stream=True))
                    gates["wch"] = [ddwc]
                    all_stores.append(ddwc)
                wch_cons = []
                for t4 in range(4):
                    for dm in range(2):
                        d_mml = None
                        for z, (fo, fw) in enumerate(fbs):
                            d_mml = pr.op("tensor", lambda t, z=z, fw=fw, t4=t4, dm=dm, half=half, first=(z == 0), last=(z == len(fbs) - 1): t.matmul(
                                p_mm[dm][:, :], pool[half * 11 + z][0:fw, t4 * 128:(t4 + 1) * 128],
                                wch[z][0:fw, dm * 512:(dm + 1) * 512], start=first, stop=last),
                                deps=([dwl[-1], d_g2_all] + ([gates["bank"][dm]] if gates["bank"][dm] else [])) if z == 0 else [d_mml])
                        wch_cons.append(d_mml)
                        if half == 0:
                            d_ev = pr.op("vector", lambda v, t4=t4, dm=dm: v.tensor_tensor(
                                s_x2a[t4][:, dm * 512:(dm + 1) * 512], p_mm[dm][:, :],
                                s_s2[t4][:, dm * 512:(dm + 1) * 512], op=ALU.add), deps=[d_mml])
                        else:
                            xr = s_xr[t4]
                            d_t1 = pr.op("vector", lambda v, t4=t4, dm=dm: v.tensor_tensor(
                                s_tsc[:, :], p_mm[dm][:, :], s_x2a[t4][:, dm * 512:(dm + 1) * 512],
                                op=ALU.add), deps=[d_mml] + gates["stats"])
                            d_ev = pr.op("vector", lambda v, t4=t4, dm=dm, xr=xr: v.tensor_tensor(
                                s_x2a[t4][:, dm * 512:(dm + 1) * 512], s_tsc[:, :],
                                xr[:, dm * 512:(dm + 1) * 512], op=ALU.add), deps=[d_t1])
                            gates["stats"] = [d_ev]
                            gates["ring"][("x", t4)] = [d_ev]
                        gates["bank"][dm] = d_ev
                if DEBUG_L5 and tch == 0 and half == 0:
                    ddha = pr.op("sync", lambda s: s.dma_start(out=o_dbg_ha[:, :], in_=s_x2a[0][:, :]),
                                 deps=[gates["bank"][0], gates["bank"][1], ddg2], dma=True,
                                 group=pr.dma_group("dbg", stream=True))
                    gates["bank"][0] = ddha
                    all_stores.append(ddha)
                gates["wch"] = wch_cons
            gates["stats"] = []

            if DEBUG_L5 and tch == 0:
                ddx = pr.op("sync", lambda s: s.dma_start(out=o_dbg_x2a[:, :], in_=s_x2a[0][:, :]),
                            deps=[gates["bank"][0], gates["bank"][1], ddha, ddwc], dma=True, group=pr.dma_group("dbg", stream=True))
                gates["nbf"] = gates["nbf"] + [ddx]
                all_stores.append(ddx)
            xn_evs = []
            for t4 in range(4):
                d_ln = emit_ln_tile(pr, s_x2a[t4][:, :], 128, sb,
                                    [gates["bank"][0], gates["bank"][1]] + gates["nbf"], s_cst[:, 0:1])
                d_xb = pr.op("vector", lambda v, t4=t4: v.tensor_copy(s_nbf[:, :], sb["ln"][:, :]),
                             deps=[d_ln] + gates["nbf"])
                evs, last_ev = emit_transpose_cols(
                    pr, lambda j, _b=s_nbf: _b[:, j * 128:(j + 1) * 128], s_xnT, 128, t4 * 128,
                    p_tr, s_id, [d_xb] + gates["xnT"], wcolT=None, prev_evac=gates["tr"])
                gates["tr"] = last_ev
                gates["nbf"] = [evs[-1]]
                xn_evs.append(last_ev)
            gates["xnT"] = []
            d_xn_all = xn_evs[-1]

            stat_mm = [None, None]
            h_evs = []
            sq_gate = gates["sg"]
            for fb in range(32):
                fw_ = s_fw[fb % 2]
                ld_f = mk_ld(fw_, (lambda s, fw_=fw_, fb=fb: s.dma_start(
                    out=fw_[:, :].rearrange("p (a c) -> p a c", a=8),
                    in_=fcQ[:, fb * 128:(fb + 1) * 128].rearrange("(a p) c -> p a c", p=128))),
                    f"fw{fb % 2}", ("fw", fb % 2))
                d_mml = None
                for k in range(8):
                    d_mml = pr.op("tensor", lambda t, k=k, fw_=fw_, fb=fb, first=(k == 0): t.matmul(
                        p_mm[fb % 2][:, :], fw_[:, k * 128:(k + 1) * 128], s_xnT[k][:, :],
                        start=first, stop=(k == 7)),
                        deps=([ld_f, d_xn_all] + ([gates["bank"][fb % 2]] if gates["bank"][fb % 2] else [])) if k == 0 else [d_mml])
                d_h = pr.op("scalar", lambda sc, fb=fb: sc.activation(
                    pool[fb][:, :], p_mm[fb % 2][:, :], AF.Gelu), deps=[d_mml] + gates["pool"])
                gates["bank"][fb % 2] = d_h
                gates["ring"][("fw", fb % 2)] = [d_mml]
                d_s1 = pr.op("tensor", lambda t, fb=fb, first=(fb == 0): t.matmul(
                    p_st[0][:, :], s_onesb[:, 0:1], pool[fb][:, :], start=first, stop=(fb == 31)),
                    deps=[d_h, d_ob] + ([stat_mm[0]] if stat_mm[0] else []) + gates["strow"] if False else [d_h, d_ob] + gates.get("strow0", []))
                stat_mm[0] = d_s1
                d_sq = pr.op("scalar", lambda sc, fb=fb: sc.activation(
                    s_sg[:, :], pool[fb][:, :], AF.Square), deps=[d_h] + sq_gate)
                d_s2m = pr.op("tensor", lambda t, fb=fb, first=(fb == 0): t.matmul(
                    p_st[1][:, :], s_onesb[:, 0:1], s_sg[:, :], start=first, stop=(fb == 31)),
                    deps=[d_sq] + gates.get("strow1", []))
                sq_gate = [d_s2m]
                stat_mm[1] = d_s2m
                h_evs.append(d_h)
            gates["sg"] = sq_gate
            gates["pool"] = []
            d_r0 = pr.op("vector", lambda v: v.tensor_scalar(
                s_row[0:1, 0:512], p_st[0][0:1, :], 1.0 / FFH, None, op0=ALU.mult), deps=[stat_mm[0]])
            d_r1 = pr.op("vector", lambda v: v.tensor_scalar(
                s_row[0:1, 512:1024], p_st[1][0:1, :], 1.0 / FFH, None, op0=ALU.mult), deps=[stat_mm[1]])
            gates["strow0"] = [d_r0]
            gates["strow1"] = [d_r1]
            d_m2 = pr.op("vector", lambda v: v.tensor_tensor(
                s_tsc[0:1, 0:512], s_row[0:1, 0:512], s_row[0:1, 0:512], op=ALU.mult),
                deps=[d_r0] + gates["stats"])
            d_var = pr.op("vector", lambda v: v.tensor_tensor(
                s_tsc[0:1, 0:512], s_row[0:1, 512:1024], s_tsc[0:1, 0:512], op=ALU.subtract), deps=[d_m2, d_r1])
            d_l = pr.op("scalar", lambda sc: sc.activation(
                s_tsc[0:1, 0:512], s_tsc[0:1, 0:512], AF.Ln, bias=s_cst[0:1, 0:1]), deps=[d_var, d_e0])
            d_e = pr.op("scalar", lambda sc: sc.activation(
                s_tsc[0:1, 0:512], s_tsc[0:1, 0:512], AF.Exp, scale=-0.5), deps=[d_l])
            d_bm = pr.op("tensor", lambda t: t.matmul(
                p_bc[:, :], s_ones1[0:1, :], s_row[0:1, 0:512], start=True, stop=True),
                deps=[d_r0, d_o1, d_te])
            d_mb = pr.op("vector", lambda v: v.tensor_copy(s_mu[:, :], p_bc[:, :]), deps=[d_bm, qst_last])
            d_br = pr.op("tensor", lambda t: t.matmul(
                p_bc[:, :], s_ones1[0:1, :], s_tsc[0:1, 0:512], start=True, stop=True),
                deps=[d_e, d_mb])
            d_rb = pr.op("vector", lambda v: v.tensor_copy(s_rs[:, :], p_bc[:, :]), deps=[d_br])
            h2_evs = []
            prev_tsc = []
            for fb in range(32):
                d_1 = pr.op("vector", lambda v, fb=fb: v.tensor_tensor(
                    s_tsc[:, :], pool[fb][:, :], s_mu[:, :], op=ALU.subtract),
                    deps=[h_evs[fb], d_mb, d_br] + prev_tsc)
                d_2 = pr.op("vector", lambda v: v.tensor_tensor(
                    s_tsc[:, :], s_tsc[:, :], s_rs[:, :], op=ALU.mult), deps=[d_1, d_rb])
                d_3 = pr.op("vector", lambda v, fb=fb: v.tensor_scalar(
                    pool[fb][:, :], s_tsc[:, :], s_lnw[:, fb:fb + 1], s_lnb[:, fb:fb + 1],
                    op0=ALU.mult, op1=ALU.add), deps=[d_2, ld_lnw, ld_lnb])
                prev_tsc = [d_3]
                h2_evs.append(d_3)
            d_h2_all = h2_evs[-1]

            for dm in range(2):
                pjl = []
                for k in range(32):
                    pjl.append(pr.op("sync", lambda s, k=k, dm=dm: s.dma_start(
                        out=wch[k // 2][:, (k % 2) * 512:(k % 2 + 1) * 512],
                        in_=pjQ[k * 128:(k + 1) * 128, dm * 512:(dm + 1) * 512]),
                        deps=gates["wch"], dma=True, group=pr.dma_group(f"pj{tch}_{dm}")))
                gates["wch"] = []
                wch_cons = []
                for t4 in range(4):
                    d_mml = None
                    for k in range(32):
                        d_mml = pr.op("tensor", lambda t, k=k, t4=t4, dm=dm, first=(k == 0): t.matmul(
                            p_mm[t4 % 2][:, :], pool[k][:, t4 * 128:(t4 + 1) * 128],
                            wch[k // 2][:, (k % 2) * 512:(k % 2 + 1) * 512],
                            start=first, stop=(k == 31)),
                            deps=([pjl[-1], d_h2_all] + ([gates["bank"][t4 % 2]] if gates["bank"][t4 % 2] else [])) if k == 0 else [d_mml])
                        if k == 0:
                            wch_cons.append(d_mml)
                    d_ev = pr.op("vector", lambda v, t4=t4, dm=dm: v.tensor_tensor(
                        s_x2[t4][:, dm * 512:(dm + 1) * 512], p_mm[t4 % 2][:, :],
                        s_x2a[t4][:, dm * 512:(dm + 1) * 512], op=ALU.add),
                        deps=[d_mml] + ([gates["x2st"][t4]] if gates["x2st"][t4] else []))
                    gates["bank"][t4 % 2] = d_ev
                    wch_cons.append(d_mml)
                    if dm == 1:
                        ti = tch * 4 + t4
                        d_st = pr.op("sync", lambda s, ti=ti, t4=t4: s.dma_start(
                            out=o_x2[ti * 128:(ti + 1) * 128, :], in_=s_x2[t4][:, :]),
                            deps=[gates["bank"][t4 % 2], d_ev], dma=True,
                            group=pr.dma_group(f"sx{t4}", stream=True))
                        gates["x2st"][t4] = d_st
                        all_stores.append(d_st)
                gates["wch"] = wch_cons

            xn_evs = []
            for t4 in range(4):
                d_ln = emit_ln_tile(pr, s_x2[t4][:, :], 128, sb,
                                    [gates["bank"][0], gates["bank"][1]] + gates["nbf"], s_cst[:, 0:1])
                d_xb = pr.op("vector", lambda v, t4=t4: v.tensor_copy(s_nbf[:, :], sb["ln"][:, :]),
                             deps=[d_ln] + gates["nbf"])
                evs, last_ev = emit_transpose_cols(
                    pr, lambda j, _b=s_nbf: _b[:, j * 128:(j + 1) * 128], s_xnT, 128, t4 * 128,
                    p_tr, s_id, [d_xb] + gates["xnT"], wcolT=None, prev_evac=gates["tr"])
                gates["tr"] = last_ev
                gates["nbf"] = [evs[-1]]
                xn_evs.append(last_ev)
            gates["xnT"] = []
            d_xn_all = xn_evs[-1]
            win_cons = []
            for fb in range(32):
                wi = s_wi[fb % 2]
                ld_w = mk_ld(wi, (lambda s, wi=wi, fb=fb: s.dma_start(
                    out=wi[:, :].rearrange("p (a c) -> p a c", a=8),
                    in_=W_in1[:, fb * 128:(fb + 1) * 128].rearrange("(a p) c -> p a c", p=128))),
                    f"wi{fb % 2}", ("wi", fb % 2), eng="gpsimd")
                d_mml = None
                for k in range(8):
                    d_mml = pr.op("tensor", lambda t, k=k, wi=wi, fb=fb, first=(k == 0): t.matmul(
                        p_mm[fb % 2][:, :], wi[:, k * 128:(k + 1) * 128], s_xnT[k][:, :],
                        start=first, stop=(k == 7)),
                        deps=([ld_w, d_xn_all] + ([gates["bank"][fb % 2]] if gates["bank"][fb % 2] else [])) if k == 0 else [d_mml])
                gates["ring"][("wi", fb % 2)] = [d_mml]
                win_cons.append(d_mml)
                stg = s_stg[fb % 2]
                d_ev = pr.op("scalar", lambda sc, stg=stg, fb=fb: sc.activation(
                    stg[:, :], p_mm[fb % 2][:, :], AF.Identity),
                    deps=[d_mml] + ([gates["stg"][fb % 2]] if gates["stg"][fb % 2] else []))
                gates["bank"][fb % 2] = d_ev
                dst = o_raw if fb < 16 else o_g2
                ro = (fb if fb < 16 else fb - 16) * 128
                d_st = pr.op("sync", lambda s, dst=dst, ro=ro, tc0=tc0, stg=stg: s.dma_start(
                    out=dst[ro:ro + 128, tc0:tc0 + 512], in_=stg[:, :]), deps=[d_ev],
                    dma=True, group=pr.dma_group(f"so{fb % 2}", stream=True))
                gates["stg"][fb % 2] = d_st
                all_stores.append(d_st)
            gates["xnT"] = win_cons
            gates["pool"] = []

        dbg_deps = []
        if DEBUG_L5:
            dbg_deps = [dth, dfq]
        pr.op("vector", lambda v: v.engine_nop(), deps=all_stores[-4:] + [gates["x2st"][t] for t in range(4)] + dbg_deps)

    nc = pr.build(body)
    return nc


def _l5_inmaps(inputs, s_full, ctx_full):
    common = {
        "oT": npT(inputs["L0_attn_o_w"]),
        "w_ffnT": npT(np.asarray(inputs["L0_attn_norm_ffn_w"]).reshape(8, 128)),
        "gateT": npT(inputs["L0_attn_gate_w"]),
        "upT": npT(inputs["L0_attn_up_w"]),
        "downT": npT(inputs["L0_attn_down_w"]),
        "fcT": npT(inputs["L0_ffn_W_fc"]),
        "ln_wT": npT(np.asarray(inputs["L0_ffn_ln_w"]).reshape(32, 128)),
        "ln_bT": npT(np.asarray(inputs["L0_ffn_ln_b"]).reshape(32, 128)),
        "projT": npT(inputs["L0_ffn_W_proj"]),
        "W_in1": np.ascontiguousarray(np.asarray(inputs["L1_attn_W_in"])),
        "ident": IDENT,
    }
    x = np.asarray(inputs["L0_attn_x"])
    maps = []
    for c in range(8):
        b, h = c // 2, c % 2
        t0 = h * TC
        maps.append({
            "s_in": np.ascontiguousarray(s_full[b, t0:t0 + TC]),
            "ctx_in": np.ascontiguousarray(ctx_full[b, t0:t0 + TC]),
            "x_in": np.ascontiguousarray(x[b, t0:t0 + TC]),
            **common,
        })
    return maps


def run_l5(inputs, s_full, ctx_full):
    nc = build_l5()
    res = _run(nc, _l5_inmaps(inputs, s_full, ctx_full))
    x2 = np.stack([np.concatenate([res[2 * b]["o_x2"], res[2 * b + 1]["o_x2"]], axis=0) for b in range(B)])
    raw = np.stack([np.concatenate([res[2 * b]["o_raw"], res[2 * b + 1]["o_raw"]], axis=1) for b in range(B)])
    g2 = np.stack([np.concatenate([res[2 * b]["o_g2"], res[2 * b + 1]["o_g2"]], axis=1) for b in range(B)])
    return x2, raw, g2





def build_l6():
    pr = Prog("l6")
    TP = TC + 3
    rawp = pr.dram_in("rawp", [IN2, TP], BF16)
    conv_ck = pr.dram_in("conv_ck", [IN2, 4])
    conv_bk = pr.dram_in("conv_bk", [128, 16])
    W_x = pr.dram_in("W_x", [IN2, 96])
    W_dt = pr.dram_in("W_dt", [DTR, IN2])
    b_dt = pr.dram_in("b_dt", [128, 16])
    o_h = pr.dram_out("o_h", [IN2, TC], BF16)
    o_d = pr.dram_out("o_d", [IN2, TC])
    o_bc = pr.dram_out("o_bc", [32, TC])

    def body(pr):
        s_raw = [pr.sbuf(f"s_raw{i}", [128, TP], BF16) for i in range(2)]
        s_cw = pr.sbuf("s_cw", [128, 4 * 16], F32)
        s_cb = pr.sbuf("s_cb", [128, 16], F32)
        s_conv = pr.sbuf("s_conv", [128, TC], F32)
        s_h = [pr.sbuf(f"s_h{j}", [128, TC], BF16) for j in range(16)]
        s_wx = pr.sbuf("s_wx", [128, 96 * 16], BF16)
        s_wdt = pr.sbuf("s_wdt", [64, IN2], BF16)
        s_ts = pr.sbuf("s_ts", [64, TC], BF16)
        s_bc = pr.sbuf("s_bc", [32, TC], F32)
        s_bdt = pr.sbuf("s_bdt", [128, 16], F32)
        s_sp1 = pr.sbuf("s_sp1", [128, 512], F32)
        s_sp2 = pr.sbuf("s_sp2", [128, 512], F32)
        s_dl = pr.sbuf("s_dl", [128, 512], F32)
        p_sp = pr.psum("p_sp", [128, 512])
        p_dt = pr.psum("p_dt", [128, 512])

        CON = pr.dma_group("consts")
        ld_cw = pr.op("sync", lambda s: s.dma_start(
            out=s_cw[:, :].rearrange("p (a k) -> p a k", a=16),
            in_=conv_ck[:, :].rearrange("(a p) k -> p a k", p=128)), dma=True, group=CON)
        ld_cb = pr.op("sync", lambda s: s.dma_start(out=s_cb[:], in_=conv_bk[:]), dma=True, group=CON)
        ld_bdt = pr.op("sync", lambda s: s.dma_start(out=s_bdt[:], in_=b_dt[:]), dma=True, group=CON)
        GCON = pr.dma_group("gconsts")
        ld_wx = pr.op("gpsimd", lambda g: g.dma_start(
            out=s_wx[:, :].rearrange("p (a c) -> p a c", a=16),
            in_=W_x[:, :].rearrange("(a p) c -> p a c", p=128)), dma=True, group=GCON)
        ld_wdt = pr.op("gpsimd", lambda g: g.dma_start(out=s_wdt[:], in_=W_dt[:]), dma=True, group=GCON)

        h_done = []
        ring = {}
        prev_si = None
        for pt in range(16):
            rb = s_raw[pt % 2]
            ld_r = pr.op("sync", lambda s, rb=rb, pt=pt: s.dma_start(
                out=rb[:], in_=rawp[pt * 128:(pt + 1) * 128, :]),
                deps=ring.get(pt % 2, []), dma=True, group=pr.dma_group(f"r{pt % 2}", stream=True))
            d_c = pr.op("vector", lambda v, rb=rb, pt=pt: v.tensor_scalar(
                s_conv[:, :], rb[:, 0:TC], s_cw[:, pt * 4:pt * 4 + 1], None, op0=ALU.mult),
                deps=[ld_r, ld_cw] + ([prev_si] if prev_si else []))
            for k in range(1, 4):
                d_c = pr.op("vector", lambda v, rb=rb, pt=pt, k=k: v.scalar_tensor_tensor(
                    s_conv[:, :], rb[:, k:k + TC], s_cw[:, pt * 4 + k:pt * 4 + k + 1],
                    s_conv[:, :], op0=ALU.mult, op1=ALU.add), deps=[d_c])
            d_si = pr.op("scalar", lambda sc, pt=pt: sc.activation(
                s_h[pt][:, :], s_conv[:, :], AF.Silu, bias=s_cb[:, pt:pt + 1]),
                deps=[d_c, ld_cb])
            prev_si = d_si
            ring[pt % 2] = [d_c]
            d_st = pr.op("sync", lambda s, pt=pt: s.dma_start(
                out=o_h[pt * 128:(pt + 1) * 128, :], in_=s_h[pt][:, :]), deps=[d_si],
                dma=True, group=pr.dma_group("sth"))
            h_done.append((d_si, d_st))
        d_h_all = h_done[-1][0]

        if L6_PHASES < 2:
            pr.op("vector", lambda v: v.engine_nop(), deps=[h_done[-1][1], h_done[-1][0]])
            return
        sp_ev = []
        prev_mm = None
        for ci in range(4):
            d_mml = None
            for k in range(16):
                d_mml = pr.op("tensor", lambda t, k=k, ci=ci, first=(k == 0): t.matmul(
                    p_sp[0:96, :], s_wx[:, k * 96:(k + 1) * 96], s_h[k][:, ci * 512:(ci + 1) * 512],
                    start=first, stop=(k == 15)),
                    deps=([ld_wx, d_h_all] + sp_ev[-2:]) if k == 0 else [d_mml])
            d_e1 = pr.op("scalar", lambda sc, ci=ci: sc.activation(
                s_ts[:, ci * 512:(ci + 1) * 512], p_sp[0:64, :], AF.Identity), deps=[d_mml])
            d_e2 = pr.op("scalar", lambda sc, ci=ci: sc.activation(
                s_bc[:, ci * 512:(ci + 1) * 512], p_sp[64:96, :], AF.Identity), deps=[d_mml])
            sp_ev.append(d_e2)
            sp_ev.append(d_e1)
        d_bcst = pr.op("sync", lambda s: s.dma_start(out=o_bc[:, :], in_=s_bc[:, :]),
                       deps=sp_ev, dma=True, group=pr.dma_group("stbc"))

        if L6_PHASES < 3:
            pr.op("vector", lambda v: v.engine_nop(), deps=[d_bcst, h_done[-1][1]])
            return
        d_sts = []
        prev_cycle = []
        for pt in range(16):
            for ci in range(4):
                d_mm = pr.op("tensor", lambda t, pt=pt, ci=ci: t.matmul(
                    p_dt[:, :], s_wdt[:, pt * 128:(pt + 1) * 128],
                    s_ts[:, ci * 512:(ci + 1) * 512], start=True, stop=True),
                    deps=[ld_wdt, sp_ev[-1]] + prev_cycle)
                d_a1 = pr.op("scalar", lambda sc, pt=pt: sc.activation(
                    s_sp1[:], p_dt[:, :], AF.Abs, bias=s_bdt[:, pt:pt + 1]),
                    deps=[d_mm, ld_bdt] + prev_cycle)
                d_a2 = pr.op("scalar", lambda sc, pt=pt: sc.activation(
                    s_sp2[:], p_dt[:, :], AF.Relu, bias=s_bdt[:, pt:pt + 1]), deps=[d_mm])
                d_a3 = pr.op("scalar", lambda sc: sc.activation(
                    s_sp1[:], s_sp1[:], AF.Exp, scale=-1.0), deps=[d_a1])
                d_a4 = pr.op("scalar", lambda sc: sc.activation(
                    s_sp1[:], s_sp1[:], AF.Ln, bias=1.0), deps=[d_a3])
                d_d = pr.op("vector", lambda v: v.tensor_tensor(
                    s_dl[:], s_sp1[:], s_sp2[:], op=ALU.add), deps=[d_a4, d_a2] + prev_cycle)
                d_st = pr.op("sync", lambda s, pt=pt, ci=ci: s.dma_start(
                    out=o_d[pt * 128:(pt + 1) * 128, ci * 512:(ci + 1) * 512], in_=s_dl[:]),
                    deps=[d_d], dma=True, group=pr.dma_group("std", stream=True))
                prev_cycle = [d_st, d_d, d_a2]
                d_sts.append(d_st)

        pr.op("vector", lambda v: v.engine_nop(),
              deps=[d_sts[-1], d_bcst, h_done[-1][1]])

    nc = pr.build(body)
    return nc


def _l6_inmaps(inputs, rawT_full):
    common = {
        "conv_ck": npT(np.asarray(inputs["L1_attn_conv_w"])[:, 0, :]),
        "conv_bk": npT(np.asarray(inputs["L1_attn_conv_b"]).reshape(16, 128)),
        "W_x": np.ascontiguousarray(np.asarray(inputs["L1_attn_W_x"])),
        "W_dt": np.ascontiguousarray(np.asarray(inputs["L1_attn_W_dt"])),
        "b_dt": npT(np.asarray(inputs["L1_attn_b_dt"]).reshape(16, 128)),
    }
    maps = []
    for c in range(8):
        b, h = c // 2, c % 2
        t0 = h * TC
        if h == 0:
            rp = np.concatenate([np.zeros((IN2, 3), BF), rawT_full[b][:, 0:TC]], axis=1)
        else:
            rp = rawT_full[b][:, t0 - 3:t0 + TC]
        maps.append({"rawp": np.ascontiguousarray(rp), **common})
    return maps


def run_l6(inputs, rawT_full):
    nc = build_l6()
    res = _run(nc, _l6_inmaps(inputs, rawT_full))
    h = np.stack([np.concatenate([res[2 * b]["o_h"], res[2 * b + 1]["o_h"]], axis=1) for b in range(B)])
    d = np.stack([np.concatenate([res[2 * b]["o_d"], res[2 * b + 1]["o_d"]], axis=1) for b in range(B)])
    bc = np.stack([np.concatenate([res[2 * b]["o_bc"], res[2 * b + 1]["o_bc"]], axis=1) for b in range(B)])
    return h, d, bc


def _l7_inmaps(inputs, hT, dT, bc):
    A_log = np.asarray(inputs["L1_attn_A_log"])
    Dd = np.asarray(inputs["L1_attn_D"])
    maps = []
    for c in range(8):
        b, dh = c // 2, c % 2
        r0 = dh * 1024
        maps.append({
            "xconv": np.ascontiguousarray(hT[b, r0:r0 + 1024]),
            "ssp_bc": np.ascontiguousarray(bc[b]),
            "deltaT": np.ascontiguousarray(dT[b, r0:r0 + 1024]),
            "A_log": np.ascontiguousarray(A_log[r0:r0 + 1024]),
            "D_skipT": npT(Dd[r0:r0 + 1024].reshape(8, 128)),
        })
    return maps


def run_l7(inputs, hT, dT, bc):
    nc = build_l2(delta_input=True)
    res = _run(nc, _l7_inmaps(inputs, hT, dT, bc))
    y = np.stack([np.concatenate([res[2 * b]["o_y"], res[2 * b + 1]["o_y"]], axis=0) for b in range(B)])
    return y





def build_l8():
    pr = Prog("l8")
    x2_in = pr.dram_in("x2_in", [TC, D])
    y2T = pr.dram_in("y2T", [IN2, TC])
    g2T = pr.dram_in("g2T", [IN2, TC], BF16)
    W_out = pr.dram_in("W_out", [IN2, D])
    router_w = pr.dram_in("router_w", [D, E])
    router_b = pr.dram_in("router_b", [1, E])
    w1 = pr.dram_in("w1", [E, D, FF1])
    w2 = pr.dram_in("w2", [E, F2, D])
    ident = pr.dram_in("ident", [128, 128], BF16)
    o_x4 = pr.dram_out("o_x4", [TC, D])

    def body(pr):
        pool = [pr.sbuf(f"pool{j}", [128, 1024], BF16) for j in range(32)]
        wpool = [pr.sbuf(f"wp{j}", [128, 1024], BF16) for j in range(16)]
        s_tok = [pr.sbuf(f"s_tok{j}", [128, D], BF16) for j in range(16)]
        s_xne = [pr.sbuf(f"s_xne{j}", [128, TC], BF16) for j in range(8)]
        s_y2 = pr.sbuf("s_y2", [128, TC], BF16)
        s_g2 = pr.sbuf("s_g2", [128, TC], BF16)
        s_sg2 = pr.sbuf("s_sg2", [128, TC], BF16)
        s_x2 = [pr.sbuf(f"s_x2{i}", [128, D], F32) for i in range(1)]
        s_x3 = pr.sbuf("s_x3", [128, D], F32)
        sb = {"sq": pr.sbuf("s_sq", [128, D], F32), "st": pr.sbuf("s_st", [128, 8], F32),
              "ln": pr.sbuf("s_ln", [128, D], F32)}
        s_nbf2 = pr.sbuf("s_nbf2", [128, D], BF16)
        s_trT = pr.sbuf("s_trT", [128, 8 * 128], BF16)
        s_cst = pr.sbuf("s_cst", [128, 2], F32)
        s_id = pr.sbuf("s_id", [128, 128], BF16)
        s_rw = pr.sbuf("s_rw", [128, 8 * E], BF16)
        s_rb1 = pr.sbuf("s_rb1", [1, E], F32)
        s_rb = pr.sbuf("s_rb", [128, E], F32)
        s_ones1 = pr.sbuf("s_ones1", [1, 128], F32)
        s_lg = pr.sbuf("s_lg", [128, E], F32)
        s_p = pr.sbuf("s_p", [128, E], F32)
        s_eq = pr.sbuf("s_eq", [128, E], F32)
        s_fnd = pr.sbuf("s_fnd", [128, 2], F32)
        s_msk = pr.sbuf("s_msk", [128, 16 * E], F32)
        s_maxp = pr.sbuf("s_maxp", [128, 16], F32)
        s_w1a = [pr.sbuf(f"s_w1a{i}", [128, 8 * 128], BF16) for i in range(2)]
        s_w1g = [pr.sbuf(f"s_w1g{i}", [128, 8 * 128], BF16) for i in range(2)]
        s_sga = pr.sbuf("s_sga", [128, 512], BF16)
        s_stg = [pr.sbuf(f"s_stg{i}", [128, 512], F32) for i in range(2)]
        p_h = [pr.psum(f"p_h{i}", [128, 512]) for i in range(4)]
        p_tr = pr.psum("p_tr", [128, 128], BF16)
        p_r = pr.psum("p_r", [128, E])

        d_e0 = pr.op("vector", lambda v: v.memset(s_cst[:, 0:1], LN_EPS))
        d_o1 = pr.op("vector", lambda v: v.memset(s_ones1[:], 1.0))
        CON = pr.dma_group("consts")
        ld_id = pr.op("sync", lambda s: s.dma_start(out=s_id[:], in_=ident[:]), dma=True, group=CON)
        ld_rb = pr.op("sync", lambda s: s.dma_start(out=s_rb1[:], in_=router_b[:]), dma=True, group=CON)
        GCON = pr.dma_group("gconsts")
        ld_wo = [pr.op("gpsimd", lambda g, j=j: g.dma_start(
            out=wpool[j][:], in_=W_out[j * 128:(j + 1) * 128, :]), dma=True, group=GCON)
            for j in range(16)]
        ld_rw = pr.op("gpsimd", lambda g: g.dma_start(
            out=s_rw[:, :].rearrange("p (a c) -> p a c", a=8),
            in_=router_w[:, :].rearrange("(a p) c -> p a c", p=128)), dma=True, group=GCON)
        d_rbb = pr.op("tensor", lambda t: t.matmul(
            p_r[:, :], s_ones1[0:1, :], s_rb1[0:1, :], start=True, stop=True),
            deps=[ld_rb, d_o1])
        d_rbe = pr.op("vector", lambda v: v.tensor_copy(s_rb[:, :], p_r[:, :]), deps=[d_rbb])

        gates = {"ring": {}, "bank": [None] * 4, "tr": None, "pool": [], "nbf": [],
                 "nbf2": [], "stg": [None] * 4, "wpool": [ld_wo[-1]], "xne": [],
                 "x3": None, "lg": [], "trT": []}
        x3_store = {}

        yg_done = []
        for pt in range(16):
            ld_y = pr.op("gpsimd", lambda g, pt=pt: g.dma_start(
                out=s_y2[:], in_=y2T[pt * 128:(pt + 1) * 128, :]),
                deps=gates["ring"].get("y", []), dma=True, group=pr.dma_group("gy", stream=True))
            ld_g = pr.op("sync", lambda s, pt=pt: s.dma_start(
                out=s_g2[:], in_=g2T[pt * 128:(pt + 1) * 128, :]),
                deps=gates["ring"].get("g", []), dma=True, group=pr.dma_group("gg", stream=True))
            d_si = pr.op("scalar", lambda sc: sc.activation(s_sg2[:], s_g2[:], AF.Silu),
                         deps=[ld_g] + gates["ring"].get("sg", []))
            d_m0 = pr.op("vector", lambda v, pt=pt: v.tensor_tensor(
                pool[2 * pt][:, :], s_y2[:, 0:1024], s_sg2[:, 0:1024], op=ALU.mult),
                deps=[d_si, ld_y] + gates["pool"])
            d_m1 = pr.op("vector", lambda v, pt=pt: v.tensor_tensor(
                pool[2 * pt + 1][:, :], s_y2[:, 1024:2048], s_sg2[:, 1024:2048], op=ALU.mult),
                deps=[d_si, ld_y])
            gates["ring"]["y"] = [d_m0, d_m1]
            gates["ring"]["g"] = [d_si]
            gates["ring"]["sg"] = [d_m0, d_m1]
            yg_done.append(d_m1)
        gates["pool"] = []
        d_yg_all = yg_done[-1]

        for ti in range(16):
            xb = s_x2[0]
            ld_x = pr.op("sync", lambda s, xb=xb, ti=ti: s.dma_start(
                out=xb[:], in_=x2_in[ti * 128:(ti + 1) * 128, :]),
                deps=gates["ring"].get(("x", 0), []), dma=True,
                group=pr.dma_group("gx0", stream=True))
            ev_dm = []
            for dm in range(2):
                d_mml = None
                for k in range(16):
                    po = pool[2 * k + (1 if ti >= 8 else 0)]
                    co = (ti - 8 if ti >= 8 else ti) * 128
                    d_mml = pr.op("tensor", lambda t, k=k, dm=dm, po=po, co=co, first=(k == 0): t.matmul(
                        p_h[dm][:, :], po[:, co:co + 128], wpool[k][:, dm * 512:(dm + 1) * 512],
                        start=first, stop=(k == 15)),
                        deps=([d_yg_all, ld_wo[-1]] + ([gates["bank"][dm]] if gates["bank"][dm] else [])) if k == 0 else [d_mml])
                d_ev = pr.op("vector", lambda v, dm=dm, xb=xb: v.tensor_tensor(
                    s_x3[:, dm * 512:(dm + 1) * 512], p_h[dm][:, :],
                    xb[:, dm * 512:(dm + 1) * 512], op=ALU.add),
                    deps=[d_mml, ld_x] + ([gates["x3"]] if (dm == 0 and gates["x3"]) else []))
                gates["bank"][dm] = d_ev
                ev_dm.append(d_ev)
            gates["ring"][("x", 0)] = [ev_dm[1]]
            d_st = pr.op("sync", lambda s, ti=ti: s.dma_start(
                out=o_x4[ti * 128:(ti + 1) * 128, :], in_=s_x3[:, :]), deps=ev_dm, dma=True,
                group=pr.dma_group(f"sx{ti % 2}", stream=True))
            x3_store[ti] = d_st
            gates["x3"] = d_st
            d_ln = emit_ln_tile(pr, s_x3[:, :], 128, sb, ev_dm + gates["nbf"], s_cst[:, 0:1])
            d_xn = pr.op("vector", lambda v, ti=ti: v.tensor_copy(s_tok[ti][:, :], sb["ln"][:, :]),
                         deps=[d_ln] + gates["pool"] if False else [d_ln])
            gates["nbf"] = [d_xn]
            tr_evs = []
            for j in range(8):
                d_tp = pr.op("tensor", lambda t, j=j, ti=ti: t.transpose(
                    p_tr[:, :], s_tok[ti][:, j * 128:(j + 1) * 128], s_id[:]),
                    deps=[d_xn, ld_id] + ([gates["tr"]] if gates["tr"] else []) + gates["trT"])
                d_te = pr.op("vector", lambda v, j=j: v.tensor_copy(
                    s_trT[:, j * 128:(j + 1) * 128], p_tr[:, :]), deps=[d_tp])
                gates["tr"] = d_te
                tr_evs.append(d_te)
            gates["trT"] = []
            d_rm = None
            for k in range(8):
                d_rm = pr.op("tensor", lambda t, k=k, first=(k == 0): t.matmul(
                    p_r[:, :], s_trT[:, k * 128:(k + 1) * 128], s_rw[:, k * E:(k + 1) * E],
                    start=first, stop=(k == 7)),
                    deps=([tr_evs[-1], ld_rw, d_rbe] + gates["lg"]) if k == 0 else [d_rm])
            gates["trT"] = [d_rm]
            d_lg = pr.op("vector", lambda v: v.tensor_tensor(
                s_lg[:, :], p_r[:, :], s_rb[:, :], op=ALU.add), deps=[d_rm] + gates["lg"])
            d_mx = pr.op("vector", lambda v: v.tensor_reduce(
                sb["st"][:, 0:1], s_lg[:, :], axis=AX.X, op=ALU.max), deps=[d_lg])
            d_nm = pr.op("vector", lambda v: v.tensor_scalar(
                sb["st"][:, 1:2], sb["st"][:, 0:1], -1.0, None, op0=ALU.mult), deps=[d_mx])
            d_ex = pr.op("scalar", lambda sc: sc.activation(
                s_p[:, :], s_lg[:, :], AF.Exp, bias=sb["st"][:, 1:2]), deps=[d_nm] + gates["lg"])
            d_sm = pr.op("vector", lambda v: v.tensor_reduce(
                sb["st"][:, 2:3], s_p[:, :], axis=AX.X, op=ALU.add), deps=[d_ex])
            d_pm = pr.op("vector", lambda v: v.tensor_reduce(
                sb["st"][:, 3:4], s_p[:, :], axis=AX.X, op=ALU.max), deps=[d_ex])
            d_rc = pr.op("vector", lambda v: v.reciprocal(sb["st"][:, 4:5], sb["st"][:, 2:3]),
                         deps=[d_sm])
            d_mp = pr.op("vector", lambda v, ti=ti: v.tensor_tensor(
                s_maxp[:, ti:ti + 1], sb["st"][:, 3:4], sb["st"][:, 4:5], op=ALU.mult),
                deps=[d_pm, d_rc])
            d_eq = pr.op("vector", lambda v: v.tensor_scalar(
                s_eq[:, :], s_p[:, :], sb["st"][:, 3:4], None, op0=ALU.is_ge),
                deps=[d_ex, d_pm])
            d_f = pr.op("vector", lambda v: v.memset(s_fnd[:, 0:1], 0.0), deps=gates["lg"])
            gates["lg"] = [d_eq, d_ex]
            prevm = [d_f, d_eq]
            for e in range(E):
                d_t = pr.op("vector", lambda v: v.tensor_scalar(
                    s_fnd[:, 1:2], s_fnd[:, 0:1], -1.0, 1.0, op0=ALU.mult, op1=ALU.add),
                    deps=prevm)
                d_m = pr.op("vector", lambda v, ti=ti, e=e: v.tensor_tensor(
                    s_msk[:, ti * E + e:ti * E + e + 1], s_eq[:, e:e + 1], s_fnd[:, 1:2],
                    op=ALU.mult), deps=[d_t])
                d_u = pr.op("vector", lambda v, ti=ti, e=e: v.tensor_tensor(
                    s_fnd[:, 0:1], s_fnd[:, 0:1], s_msk[:, ti * E + e:ti * E + e + 1],
                    op=ALU.add), deps=[d_m])
                prevm = [d_u]
            gates["lg"] = gates["lg"] + prevm

        acc_last = {}
        for e in range(E):
            xne_evs = []
            for ti in range(16):
                d_msk = pr.op("vector", lambda v, ti=ti, e=e: v.tensor_scalar(
                    s_nbf2[:, :], s_tok[ti][:, :], s_msk[:, ti * E + e:ti * E + e + 1], None,
                    op0=ALU.mult), deps=gates["nbf2"] + ([prevm[-1]] if e == 0 else []))
                tr_last = None
                for j in range(8):
                    d_tp = pr.op("tensor", lambda t, j=j: t.transpose(
                        p_tr[:, :], s_nbf2[:, j * 128:(j + 1) * 128], s_id[:]),
                        deps=[d_msk, ld_id] + ([gates["tr"]] if gates["tr"] else []) + gates["xne"])
                    d_te = pr.op("vector", lambda v, j=j, ti=ti, e=e: v.tensor_copy(
                        s_xne[j][:, ti * 128:(ti + 1) * 128], p_tr[:, :]), deps=[d_tp])
                    gates["tr"] = d_te
                    tr_last = d_te
                gates["nbf2"] = [d_tp]
                xne_evs.append(tr_last)
            gates["xne"] = []
            d_xne_all = xne_evs[-1]
            for tg in range(2):
                tg0 = tg * 1024
                for jp in range(32):
                    wa = s_w1a[jp % 2]; wg = s_w1g[jp % 2]
                    ld_a = pr.op("gpsimd", lambda g, e=e, jp=jp, wa=wa: g.dma_start(
                        out=wa[:, :].rearrange("p (a c) -> p a c", a=8),
                        in_=w1[e, :, jp * 128:(jp + 1) * 128].rearrange("(a p) c -> p a c", p=128)),
                        deps=gates["ring"].get(("wa", jp % 2), []), dma=True,
                        group=pr.dma_group(f"wa{jp % 2}", stream=True))
                    ld_gw = pr.op("gpsimd", lambda g, e=e, jp=jp, wg=wg: g.dma_start(
                        out=wg[:, :].rearrange("p (a c) -> p a c", a=8),
                        in_=w1[e, :, F2 + jp * 128:F2 + (jp + 1) * 128].rearrange("(a p) c -> p a c", p=128)),
                        deps=gates["ring"].get(("wg", jp % 2), []), dma=True,
                        group=pr.dma_group(f"wg{jp % 2}", stream=True))
                    mm_a = []
                    mm_g = []
                    for half in range(2):
                        d_a = None
                        for k in range(8):
                            d_a = pr.op("tensor", lambda t, k=k, half=half, wa=wa, tg0=tg0, first=(k == 0): t.matmul(
                                p_h[half][:, :], wa[:, k * 128:(k + 1) * 128],
                                s_xne[k][:, tg0 + half * 512:tg0 + (half + 1) * 512],
                                start=first, stop=(k == 7)),
                                deps=([ld_a, d_xne_all] + ([gates["bank"][half]] if gates["bank"][half] else [])) if k == 0 else [d_a])
                        mm_a.append(d_a)
                        d_g = None
                        for k in range(8):
                            d_g = pr.op("tensor", lambda t, k=k, half=half, wg=wg, tg0=tg0, first=(k == 0): t.matmul(
                                p_h[2 + half][:, :], wg[:, k * 128:(k + 1) * 128],
                                s_xne[k][:, tg0 + half * 512:tg0 + (half + 1) * 512],
                                start=first, stop=(k == 7)),
                                deps=([ld_gw, d_xne_all] + ([gates["bank"][2 + half]] if gates["bank"][2 + half] else [])) if k == 0 else [d_g])
                        mm_g.append(d_g)
                    gates["ring"][("wa", jp % 2)] = mm_a
                    gates["ring"][("wg", jp % 2)] = mm_g
                    for half in range(2):
                        d_si = pr.op("scalar", lambda sc, half=half: sc.activation(
                            s_sga[:, :], p_h[half][:, :], AF.Silu),
                            deps=[mm_a[half]] + gates["ring"].get("sga", []))
                        gates["bank"][half] = d_si
                        d_yi = pr.op("vector", lambda v, jp=jp, half=half: v.tensor_tensor(
                            pool[jp][:, half * 512:(half + 1) * 512], s_sga[:, :],
                            p_h[2 + half][:, :], op=ALU.mult),
                            deps=[d_si, mm_g[half]] + (gates["pool"] if jp == 0 and half == 0 else []))
                        gates["bank"][2 + half] = d_yi
                        gates["ring"]["sga"] = [d_yi]
                        if jp == 0 and half == 0:
                            gates["pool"] = []
                w2_cons = []
                for dm in range(2):
                    w2l = []
                    for k in range(32):
                        w2l.append(pr.op("gpsimd", lambda g, e=e, k=k, dm=dm: g.dma_start(
                            out=wpool[k // 2][:, (k % 2) * 512:(k % 2 + 1) * 512],
                            in_=w2[e, k * 128:(k + 1) * 128, dm * 512:(dm + 1) * 512]),
                            deps=gates["wpool"], dma=True,
                            group=pr.dma_group(f"w2_{e}_{tg}_{dm}")))
                    gates["wpool"] = []
                    for tt in range(8):
                        ti = tg * 8 + tt
                        bank = tt % 2
                        d_mml = None
                        for k in range(32):
                            d_mml = pr.op("tensor", lambda t, k=k, tt=tt, dm=dm, bank=bank, first=(k == 0): t.matmul(
                                p_h[bank][:, :], pool[k][:, tt * 128:(tt + 1) * 128],
                                wpool[k // 2][:, (k % 2) * 512:(k % 2 + 1) * 512],
                                start=first, stop=(k == 31)),
                                deps=([w2l[-1], gates["bank"][2], gates["bank"][3]] + ([gates["bank"][bank]] if gates["bank"][bank] else [])) if k == 0 else [d_mml])
                        if k == 31:
                            w2_cons.append(d_mml)
                        sgi = dm
                        stg = s_stg[sgi]
                        d_sc = pr.op("vector", lambda v, stg=stg, ti=ti, bank=bank: v.tensor_scalar(
                            stg[:, :], p_h[bank][:, :], s_maxp[:, ti:ti + 1], None, op0=ALU.mult),
                            deps=[d_mml] + ([gates["stg"][sgi]] if gates["stg"][sgi] else []))
                        gates["bank"][bank] = d_sc
                        d_ac = pr.op("gpsimd", lambda g, stg=stg, ti=ti, dm=dm: g.dma_start(
                            out=o_x4[ti * 128:(ti + 1) * 128, dm * 512:(dm + 1) * 512],
                            in_=stg[:, :], accum_op=ALU.add),
                            deps=[d_sc, x3_store[ti]] + ([acc_last.get((ti, dm))] if acc_last.get((ti, dm)) else []),
                            dma=True, group=pr.dma_group(f"ac{sgi}", stream=True))
                        gates["stg"][sgi] = d_ac
                        acc_last[(ti, dm)] = d_ac
                    gates["wpool"] = w2_cons

        pr.op("vector", lambda v: v.engine_nop(),
              deps=[acc_last[(ti, dm)] for ti in range(16) for dm in range(2)])

    nc = pr.build(body)
    return nc


def _l8_inmaps(inputs, x2_full, y2T_full, g2T_full):
    common = {
        "W_out": np.ascontiguousarray(np.asarray(inputs["L1_attn_W_out"])),
        "router_w": np.ascontiguousarray(np.asarray(inputs["L1_ffn_router_w"])),
        "router_b": np.asarray(inputs["L1_ffn_router_b"]).reshape(1, E),
        "w1": np.ascontiguousarray(np.asarray(inputs["L1_ffn_w1"])),
        "w2": np.ascontiguousarray(np.asarray(inputs["L1_ffn_w2"])),
        "ident": IDENT,
    }
    maps = []
    for c in range(8):
        b, h = c // 2, c % 2
        t0 = h * TC
        maps.append({
            "x2_in": np.ascontiguousarray(x2_full[b, t0:t0 + TC]),
            "y2T": np.ascontiguousarray(y2T_full[b][:, t0:t0 + TC]),
            "g2T": np.ascontiguousarray(g2T_full[b][:, t0:t0 + TC]),
            **common,
        })
    return maps


def run_l8(inputs, x2_full, y2T_full, g2T_full):
    nc = build_l8()
    res = _run(nc, _l8_inmaps(inputs, x2_full, y2T_full, g2T_full))
    x4 = np.stack([np.concatenate([res[2 * b]["o_x4"], res[2 * b + 1]["o_x4"]], axis=0) for b in range(B)])
    return x4


def kernel(**inputs):
    inputs = {k: np.asarray(v) for k, v in inputs.items()}
    xconv, ssp, z = run_l1(inputs)
    y = run_l2(inputs, xconv, ssp)
    s, QT, KT, V = run_l3(inputs, y, z)
    ctx = run_l4(QT, KT, V)
    x2, raw, g2 = run_l5(inputs, s, ctx)
    h1, d1, bc1 = run_l6(inputs, raw)
    y2 = run_l7(inputs, h1, d1, bc1)
    x4 = run_l8(inputs, x2, y2, g2)
    return x4.reshape(B, L, D)


# version 02
# speedup vs baseline: 1.5882x; 1.5882x over previous
import contextlib
import numpy as np
import ml_dtypes
import concourse.bass as bass
import concourse.mybir as mybir
from concourse.bass_utils import run_bass_kernel_spmd

F32 = mybir.dt.float32
BF16 = mybir.dt.bfloat16
AF = mybir.ActivationFunctionType
ALU = mybir.AluOpType
AX = mybir.AxisListType
BF = ml_dtypes.bfloat16

B, L, D = 4, 4096, 1024
TC = 2048
DI = 2048
NST = 16
NH, NKV, DH = 4, 2, 256
FH = 2731
FFH = 4096
IN2 = 2048
DTR = 64
E = 8
F2 = 4096
FF1 = 8192
LN_EPS = 1e-5
RMS_EPS = 1e-6
MASK_NEG = -1e30
DEBUG_L1 = False
DEBUG_L4 = False
DEBUG_L5 = False
L6_PHASES = 3



ENGINES = ("sync", "gpsimd", "tensor", "vector", "scalar")


class Prog:
    def __init__(self, name="prog"):
        self.nc = bass.Bass("TRN2", target_bir_lowering=False)
        self.ops = []
        self.name = name
        self._es = None
        self.groups = {}

    def dma_group(self, name, stream=False):
        if name not in self.groups:
            self.groups[name] = {"stream": stream, "ops": []}
        return name

    def op(self, engine, fn, deps=(), dma=False, group=None):
        gid = None
        if dma:
            assert group is not None, "dma ops must declare a group"
            gid = group
        i = len(self.ops)
        self.ops.append([engine, fn, list(deps), 16 if dma else 1, gid])
        if gid is not None:
            self.groups[gid]["ops"].append(i)
        return i

    def sbuf(self, name, shape, dtype):
        return self._es.enter_context(self.nc.sbuf_tensor(name, shape, dtype))

    def psum(self, name, shape, dtype=F32):
        return self._es.enter_context(self.nc.psum_tensor(name, shape, dtype))

    def dram_in(self, name, shape, dtype=F32):
        return self.nc.dram_tensor(name, shape, dtype, kind="ExternalInput").ap()

    def dram_out(self, name, shape, dtype=F32):
        return self.nc.dram_tensor(name, shape, dtype, kind="ExternalOutput").ap()

    def build(self, body):
        with contextlib.ExitStack() as es:
            self._es = es
            eng_sems = {e: es.enter_context(self.nc.semaphore(f"ch_{e}")) for e in ENGINES}
            body(self)
            grp_sems = {g: es.enter_context(self.nc.semaphore(f"g_{g}")) for g in self.groups}
            nums = sorted([s.num for s in eng_sems.values()] + [s.num for s in grp_sems.values()])
            rng = range(min(nums), max(nums) + 1)
            self.nc.gpsimd.dma_reset(rng)
            self.nc.gpsimd.sem_clear(rng)
            self.nc.all_engine_barrier()

            n = len(self.ops)
            needed = [False] * n
            per_eng = {e: [] for e in ENGINES}
            grp_pos = {}
            for g, info in self.groups.items():
                for j, i in enumerate(info["ops"]):
                    grp_pos[i] = (g, j + 1)
            for i, (e, fn, deps, inc, gid) in enumerate(self.ops):
                per_eng[e].append(i)
                if gid is not None:
                    needed[i] = True
                for d in deps:
                    if (self.ops[d][0] == "tensor" and e == "tensor"
                            and self.ops[d][4] is None and gid is None):
                        assert d < i, f"same-engine dep must precede: {d} -> {i}"
                        continue
                    needed[d] = True
                    if self.ops[d][0] == e and self.ops[d][4] is None and gid is None:
                        assert d < i, f"same-engine dep must precede: {d} -> {i}"
            eng_cnt = {e: 0 for e in ENGINES}
            idx = [0] * n
            for i, (e, fn, deps, inc, gid) in enumerate(self.ops):
                if gid is None and needed[i]:
                    eng_cnt[e] += 1
                idx[i] = eng_cnt[e]

            def wait_val(d):
                e, _, _, inc, gid = self.ops[d]
                if gid is None:
                    return (eng_sems[e], idx[d])
                g, pos = grp_pos[d]
                if self.groups[g]["stream"]:
                    return (grp_sems[g], 16 * pos)
                return (grp_sems[g], 16 * len(self.groups[g]["ops"]))

            blk = es.enter_context(self.nc.Block())
            for e in ENGINES:
                ids = per_eng[e]
                if not ids:
                    continue

                def section(eng, ids=ids, e=e):
                    hi = {}
                    for i in ids:
                        _, fn, deps, inc, gid = self.ops[i]
                        want = {}
                        for d in deps:
                            if (e == "tensor" and self.ops[d][0] == "tensor"
                                    and self.ops[d][4] is None and gid is None):
                                continue
                            sem, v = wait_val(d)
                            want[sem.num] = (sem, max(want[sem.num][1] if sem.num in want else 0, v))
                        for _, (sem, v) in sorted(want.items()):
                            if hi.get(sem.num, -1) < v:
                                eng.wait_ge(sem, v)
                                hi[sem.num] = v
                        instr = fn(eng)
                        if needed[i]:
                            assert instr is not None, f"op {i} on {e} returned None"
                            if gid is not None:
                                instr.then_inc(grp_sems[gid], 16)
                            else:
                                instr.then_inc(eng_sems[e], 1)

                getattr(blk, e)(section)
        return self.nc



def fblocks(n, step=128):
    out = []
    off = 0
    while off < n:
        out.append((off, min(step, n - off)))
        off += step
    return out


def np_bf16(x):
    return np.ascontiguousarray(x, dtype=BF)


def npT(x):
    return np.ascontiguousarray(np.swapaxes(np.asarray(x), -1, -2))


IDENT = np.eye(128, dtype=BF)
ONES128 = np.ones((128, 1), np.float32)
ONESBF = np.ones((128, 1), BF)


def ln_stats_ops(p, sc_sq, x_ap, st, rows, eps, mean_out=True):
    raise NotImplementedError





def emit_ln_tile(pr, x_ap, p, sb, deps, eps_col, sub_mean=True):
    s_sq, s_st, s_ln = sb["sq"], sb["st"], sb["ln"]
    d_sq = pr.op("scalar", lambda sc: sc.activation(s_sq[0:p, :], x_ap, AF.Square), deps=deps)
    d_r1 = pr.op("vector", lambda v: v.tensor_reduce(s_st[0:p, 0:1], x_ap, axis=AX.X, op=ALU.add), deps=deps)
    d_r2 = pr.op("vector", lambda v: v.tensor_reduce(s_st[0:p, 1:2], s_sq[0:p, :], axis=AX.X, op=ALU.add), deps=[d_sq])
    d_m = pr.op("vector", lambda v: v.tensor_scalar(s_st[0:p, 0:1], s_st[0:p, 0:1], 1.0 / D, None, op0=ALU.mult), deps=[d_r1])
    d_v = pr.op("vector", lambda v: v.tensor_scalar(s_st[0:p, 1:2], s_st[0:p, 1:2], 1.0 / D, None, op0=ALU.mult), deps=[d_r2])
    d_m2 = pr.op("vector", lambda v: v.tensor_tensor(s_st[0:p, 2:3], s_st[0:p, 0:1], s_st[0:p, 0:1], op=ALU.mult), deps=[d_m])
    d_var = pr.op("vector", lambda v: v.tensor_tensor(s_st[0:p, 3:4], s_st[0:p, 1:2], s_st[0:p, 2:3], op=ALU.subtract), deps=[d_v, d_m2])
    d_rs = pr.op("scalar", lambda sc: sc.activation(s_st[0:p, 4:5], s_st[0:p, 3:4], AF.Ln, bias=eps_col), deps=[d_var])
    d_rs2 = pr.op("scalar", lambda sc: sc.activation(s_st[0:p, 4:5], s_st[0:p, 4:5], AF.Exp, scale=-0.5), deps=[d_rs])
    d_ln = pr.op("vector", lambda v: v.tensor_scalar(
        s_ln[0:p, :], x_ap, s_st[0:p, 0:1], s_st[0:p, 4:5], op0=ALU.subtract, op1=ALU.mult), deps=[d_rs2, d_m])
    return d_ln


def emit_rms_tile(pr, x_ap, out_ap, p, sb, deps, eps_col):
    s_sq, s_st = sb["sq"], sb["st"]
    d_sq = pr.op("vector", lambda v: v.tensor_tensor(s_sq[0:p, :], x_ap, x_ap, op=ALU.mult), deps=deps)
    d_r = pr.op("vector", lambda v: v.tensor_reduce(s_st[0:p, 5:6], s_sq[0:p, :], axis=AX.X, op=ALU.add), deps=[d_sq])
    d_mm = pr.op("vector", lambda v: v.tensor_scalar(s_st[0:p, 5:6], s_st[0:p, 5:6], 1.0 / D, None, op0=ALU.mult), deps=[d_r])
    d_l = pr.op("scalar", lambda sc: sc.activation(s_st[0:p, 6:7], s_st[0:p, 5:6], AF.Ln, bias=eps_col), deps=[d_mm])
    d_e = pr.op("scalar", lambda sc: sc.activation(s_st[0:p, 6:7], s_st[0:p, 6:7], AF.Exp, scale=-0.5), deps=[d_l])
    d_o = pr.op("vector", lambda v: v.tensor_scalar(out_ap, x_ap, s_st[0:p, 6:7], None, op0=ALU.mult), deps=[d_e] + deps)
    return d_o


def emit_transpose_cols(pr, src_bf_ap_fn, dstT, p, col_off, p_tr, s_id, deps, wcolT=None,
                        nblk=8, prev_evac=None):
    evacs = []
    prev = prev_evac
    for j in range(nblk):
        d_tp = pr.op("tensor", lambda t, j=j: t.transpose(
            p_tr[:, 0:p], src_bf_ap_fn(j), s_id[0:p, 0:p]),
            deps=deps + ([prev] if prev is not None else []))
        if wcolT is not None:
            d_ev = pr.op("vector", lambda v, j=j: v.tensor_scalar(
                dstT[j][:, col_off:col_off + p], p_tr[:, 0:p], wcolT[:, j:j + 1], None,
                op0=ALU.mult), deps=[d_tp])
        else:
            d_ev = pr.op("vector", lambda v, j=j: v.tensor_copy(
                dstT[j][:, col_off:col_off + p], p_tr[:, 0:p]), deps=[d_tp])
        evacs.append(d_ev)
        prev = d_ev
    return evacs, prev



def build_l1():
    pr = Prog("l1")
    TP = TC + 3
    x_pad = pr.dram_in("x_pad", [TP, D])
    w_norm = pr.dram_in("w_norm", [1, D])
    w_normT = pr.dram_in("w_normT", [128, D // 128])
    in_projT = pr.dram_in("in_projT", [D, 2 * DI])
    conv_ck = pr.dram_in("conv_ck", [DI, 4])
    conv_bk = pr.dram_in("conv_bk", [128, DI // 128])
    x_projT = pr.dram_in("x_projT", [DI, 33])
    ident = pr.dram_in("ident", [128, 128], BF16)
    o_xconv = pr.dram_out("o_xconv", [DI, TC], BF16)
    o_ssp = pr.dram_out("o_ssp", [33, TC])
    o_z = pr.dram_out("o_z", [DI, TC], BF16)
    o_dbg = pr.dram_out("o_dbg", [8 * 128, TC + 3], BF16) if DEBUG_L1 else None

    def body(pr):
        nc = pr.nc
        s_x = [pr.sbuf(f"s_x{i}", [128, D], F32) for i in range(3)]
        s_ln = pr.sbuf("s_ln", [128, D], F32)
        s_sq = pr.sbuf("s_sq", [128, D], F32)
        s_xnbf = pr.sbuf("s_xnbf", [128, D], BF16)
        s_st = pr.sbuf("s_st", [128, 8], F32)
        s_cst = pr.sbuf("s_cst", [128, 2], F32)
        s_wnT = pr.sbuf("s_wnT", [128, 8], F32)
        s_id = pr.sbuf("s_id", [128, 128], BF16)
        s_xnT = [pr.sbuf(f"s_xnT{j}", [128, TP], BF16) for j in range(8)]
        s_w = [pr.sbuf(f"s_w{k}", [128, 1024], BF16) for k in range(2)]
        s_fb = pr.sbuf("s_fb", [128, TP], F32)
        s_conv = pr.sbuf("s_conv", [128, TC], F32)
        s_cw = pr.sbuf("s_cw", [128, 4 * 16], F32)
        s_cb = pr.sbuf("s_cb", [128, 16], F32)
        s_xc = [pr.sbuf(f"s_xc{j}", [128, TC], BF16) for j in range(16)]
        s_zb = pr.sbuf("s_zb", [128, TC], BF16)
        s_xp = pr.sbuf("s_xp", [128, 33 * 16], BF16)
        s_ssp = pr.sbuf("s_ssp", [33, TC], F32)
        p_tr = pr.psum("p_tr", [128, 128], BF16)
        p_mm = [pr.psum(f"p_mm{i}", [128, 512]) for i in range(2)]
        p_sp = pr.psum("p_sp", [33, 512])

        tiles = [(0, 3)] + [(3 + 128 * i, 128) for i in range(16)]
        prev_gate = []
        x_consumed = {}

        d_c0s = pr.op("vector", lambda v: v.memset(s_cst[:, 0:1], LN_EPS))
        d_c1s = pr.op("vector", lambda v: v.memset(s_cst[:, 1:2], RMS_EPS))
        ld_wnT = pr.op("sync", lambda s: s.dma_start(
            out=s_wnT[:, :], in_=w_normT[:, :]), dma=True, group=pr.dma_group("consts"))
        ld_id = pr.op("sync", lambda s: s.dma_start(out=s_id[:], in_=ident[:]), dma=True, group=pr.dma_group("consts"))
        ld_cw = pr.op("sync", lambda s: s.dma_start(
            out=s_cw[:, :].rearrange("p (a k) -> p a k", a=16),
            in_=conv_ck[:, :].rearrange("(a p) k -> p a k", p=128)), dma=True, group=pr.dma_group("consts"))
        ld_cb = pr.op("sync", lambda s: s.dma_start(out=s_cb[:, :], in_=conv_bk[:, :]), dma=True, group=pr.dma_group("consts"))
        ld_xp = pr.op("gpsimd", lambda g: g.dma_start(
            out=s_xp[:, :].rearrange("p (a c) -> p a c", a=16),
            in_=x_projT[:, :].rearrange("(a p) c -> p a c", p=128)), dma=True, group=pr.dma_group("gconsts"))

        tr_done = []
        for i, (off, p) in enumerate(tiles):
            xb = s_x[i % 3]
            ld_xi = pr.op("sync", lambda s, xb=xb, off=off, p=p: s.dma_start(
                out=xb[0:p, :], in_=x_pad[off:off + p, :]),
                deps=[x_consumed[i - 3]] if (i - 3) in x_consumed else [], dma=True,
                group=pr.dma_group(f"x{i % 3}", stream=True))
            d_sq = pr.op("scalar", lambda sc, xb=xb, p=p: sc.activation(
                s_sq[0:p, :], xb[0:p, :], AF.Square), deps=[ld_xi] + prev_gate)
            d_r1 = pr.op("vector", lambda v, p=p, xb=xb: v.tensor_reduce(
                s_st[0:p, 0:1], xb[0:p, :], axis=AX.X, op=ALU.add), deps=[ld_xi])
            d_r2 = pr.op("vector", lambda v, p=p: v.tensor_reduce(
                s_st[0:p, 1:2], s_sq[0:p, :], axis=AX.X, op=ALU.add), deps=[d_sq])
            d_m = pr.op("vector", lambda v, p=p: v.tensor_scalar(
                s_st[0:p, 0:1], s_st[0:p, 0:1], 1.0 / D, None, op0=ALU.mult), deps=[d_r1])
            d_v = pr.op("vector", lambda v, p=p: v.tensor_scalar(
                s_st[0:p, 1:2], s_st[0:p, 1:2], 1.0 / D, None, op0=ALU.mult), deps=[d_r2])
            d_m2 = pr.op("vector", lambda v, p=p: v.tensor_tensor(
                s_st[0:p, 2:3], s_st[0:p, 0:1], s_st[0:p, 0:1], op=ALU.mult), deps=[d_m])
            d_var = pr.op("vector", lambda v, p=p: v.tensor_tensor(
                s_st[0:p, 3:4], s_st[0:p, 1:2], s_st[0:p, 2:3], op=ALU.subtract),
                deps=[d_v, d_m2])
            d_rs = pr.op("scalar", lambda sc, p=p: sc.activation(
                s_st[0:p, 4:5], s_st[0:p, 3:4], AF.Ln, bias=s_cst[0:p, 0:1]), deps=[d_var, d_c0s])
            d_rs2 = pr.op("scalar", lambda sc, p=p: sc.activation(
                s_st[0:p, 4:5], s_st[0:p, 4:5], AF.Exp, scale=-0.5), deps=[d_rs])
            d_ln = pr.op("vector", lambda v, p=p, xb=xb: v.tensor_scalar(
                s_ln[0:p, :], xb[0:p, :], s_st[0:p, 0:1], s_st[0:p, 4:5],
                op0=ALU.subtract, op1=ALU.mult), deps=[d_rs2, d_m])
            x_consumed[i] = d_ln
            d_sq2 = pr.op("vector", lambda v, p=p: v.tensor_tensor(
                s_sq[0:p, :], s_ln[0:p, :], s_ln[0:p, :], op=ALU.mult), deps=[d_ln])
            d_r3 = pr.op("vector", lambda v, p=p: v.tensor_reduce(
                s_st[0:p, 5:6], s_sq[0:p, :], axis=AX.X, op=ALU.add), deps=[d_sq2])
            d_mm = pr.op("vector", lambda v, p=p: v.tensor_scalar(
                s_st[0:p, 5:6], s_st[0:p, 5:6], 1.0 / D, None, op0=ALU.mult), deps=[d_r3])
            d_l2 = pr.op("scalar", lambda sc, p=p: sc.activation(
                s_st[0:p, 6:7], s_st[0:p, 5:6], AF.Ln, bias=s_cst[0:p, 1:2]), deps=[d_mm, d_c1s])
            d_e2 = pr.op("scalar", lambda sc, p=p: sc.activation(
                s_st[0:p, 6:7], s_st[0:p, 6:7], AF.Exp, scale=-0.5), deps=[d_l2])
            d_xn = pr.op("vector", lambda v, p=p: v.tensor_scalar(
                s_xnbf[0:p, :], s_ln[0:p, :], s_st[0:p, 6:7], None, op0=ALU.mult),
                deps=[d_e2, d_ln] + prev_gate)
            prev_evacs = []
            for j in range(8):
                d_tp = pr.op("tensor", lambda t, p=p, j=j: t.transpose(
                    p_tr[:, 0:p], s_xnbf[0:p, j * 128:(j + 1) * 128], s_id[0:p, 0:p]),
                    deps=[d_xn, ld_id] + tr_done[-1:])
                d_ev = pr.op("vector", lambda v, p=p, j=j, off=off: v.tensor_scalar(
                    s_xnT[j][:, off:off + p], p_tr[:, 0:p], s_wnT[:, j:j + 1], None,
                    op0=ALU.mult), deps=[d_tp, ld_wnT])
                tr_done.append(d_ev)
            prev_gate = [d_xn, d_tp]
        d_xnT_all = tr_done[-1]

        if DEBUG_L1:
            dbg_st = [pr.op("sync", lambda s, j=j: s.dma_start(
                out=o_dbg[j * 128:(j + 1) * 128, :], in_=s_xnT[j][:, :]),
                deps=[d_xnT_all], dma=True, group=pr.dma_group("st_dbg")) for j in range(8)]

        TCH = [(0, 512), (512, 512), (1024, 512), (1536, 512), (2048, 3)]
        d_wload = {}
        conv_ops = []
        last_mm_on_bank = [None, None]
        prev_fb_scalar_reader = None
        prev_si = None
        prev_zst = None
        for f in range(32):
            wb = s_w[f % 2]
            dep_prev_use = d_wload.get(f - 2, [])
            d_w = pr.op("gpsimd", lambda g, f=f, wb=wb: g.dma_start(
                out=wb[:, :].rearrange("p (a c) -> p a c", a=8),
                in_=in_projT[:, f * 128:(f + 1) * 128].rearrange("(a p) c -> p a c", p=128)),
                deps=dep_prev_use, dma=True, group=pr.dma_group(f"w{f % 2}", stream=True))
            fb_parts = []
            for ci, (co, cw) in enumerate(TCH):
                bank = ci % 2
                mm_deps = [d_w, d_xnT_all]
                if last_mm_on_bank[bank] is not None:
                    mm_deps.append(last_mm_on_bank[bank][1])
                d_mmlast = None
                for k in range(8):
                    d_mmlast = pr.op("tensor", lambda t, wb=wb, k=k, co=co, cw=cw, bank=bank, first=(k == 0): t.matmul(
                        p_mm[bank][:, 0:cw], wb[:, k * 128:(k + 1) * 128],
                        s_xnT[k][:, co:co + cw], start=first, stop=(k == 7)),
                        deps=(mm_deps if k == 0 else [d_mmlast]))
                ev_deps = [d_mmlast]
                if ci == 0 and prev_fb_scalar_reader is not None:
                    ev_deps.append(prev_fb_scalar_reader)
                d_ev = pr.op("vector", lambda v, co=co, cw=cw, bank=bank: v.tensor_copy(
                    s_fb[:, co:co + cw], p_mm[bank][:, 0:cw]), deps=ev_deps)
                last_mm_on_bank[bank] = (d_mmlast, d_ev)
                fb_parts.append(d_ev)
            d_wload[f] = [last_mm_on_bank[0][0], last_mm_on_bank[1][0]]
            if f < 16:
                d_c0 = pr.op("vector", lambda v, f=f: v.tensor_scalar(
                    s_conv[:, :], s_fb[:, 0:TC], s_cw[:, f * 4:f * 4 + 1], None,
                    op0=ALU.mult), deps=fb_parts + [ld_cw] + ([prev_si] if prev_si else []))
                d_ck = d_c0
                for k in range(1, 4):
                    d_ck = pr.op("vector", lambda v, f=f, k=k: v.scalar_tensor_tensor(
                        s_conv[:, :], s_fb[:, k:k + TC], s_cw[:, f * 4 + k:f * 4 + k + 1],
                        s_conv[:, :], op0=ALU.mult, op1=ALU.add), deps=[d_ck])
                d_si = pr.op("scalar", lambda sc, f=f: sc.activation(
                    s_xc[f][:, :], s_conv[:, :], AF.Silu, bias=s_cb[:, f:f + 1]),
                    deps=[d_ck, ld_cb])
                prev_si = d_si
                d_st = pr.op("sync", lambda s, f=f: s.dma_start(
                    out=o_xconv[f * 128:(f + 1) * 128, :], in_=s_xc[f][:, :]),
                    deps=[d_si], dma=True, group=pr.dma_group("st_xc"))
                conv_ops.append((d_si, d_st))
            else:
                d_zb = pr.op("scalar", lambda sc: sc.activation(
                    s_zb[:, :], s_fb[:, 3:3 + TC], AF.Identity),
                    deps=fb_parts + ([prev_zst] if prev_zst else []))
                prev_fb_scalar_reader = d_zb
                d_st = pr.op("sync", lambda s, f=f: s.dma_start(
                    out=o_z[(f - 16) * 128:(f - 15) * 128, :], in_=s_zb[:, :]),
                    deps=[d_zb], dma=True, group=pr.dma_group("zst", stream=True))
                prev_zst = d_st
                conv_ops.append((d_zb, d_st))

        sp_evac = []
        for ci, (co, cw) in enumerate(fblocks(TC, 512)):
            d_mmlast = None
            for k in range(16):
                d_mmlast = pr.op("tensor", lambda t, k=k, co=co, cw=cw, first=(k == 0): t.matmul(
                    p_sp[:, 0:cw], s_xp[:, k * 33:(k + 1) * 33], s_xc[k][:, co:co + cw],
                    start=first, stop=(k == 15)),
                    deps=([ld_xp, conv_ops[15][0]] + sp_evac[-1:]) if k == 0 else [d_mmlast])
            d_ev = pr.op("vector", lambda v, co=co, cw=cw: v.tensor_copy(
                s_ssp[:, co:co + cw], p_sp[:, 0:cw]), deps=[d_mmlast])
            sp_evac.append(d_ev)
        d_spst = pr.op("sync", lambda s: s.dma_start(out=o_ssp[:, :], in_=s_ssp[:, :]),
                       deps=sp_evac, dma=True, group=pr.dma_group("st_sp"))

        all_st = [d_spst] + [st for (_x, st) in conv_ops] + (dbg_st if DEBUG_L1 else [])
        pr.op("vector", lambda v: v.engine_nop(), deps=all_st)

    nc = pr.build(body)
    return nc



def _run(nc, in_maps):
    res = run_bass_kernel_spmd(nc, in_maps, core_ids=list(range(8)))
    return res.results


def _l1_inmaps(inputs):
    x = np.asarray(inputs["L0_attn_x"])
    wT = {
        "w_norm": np.asarray(inputs["L0_attn_norm_m_w"]).reshape(1, D),
        "w_normT": npT(np.asarray(inputs["L0_attn_norm_m_w"]).reshape(D // 128, 128)),
        "in_projT": npT(inputs["L0_attn_in_proj_w"]),
        "conv_ck": np.ascontiguousarray(np.asarray(inputs["L0_attn_conv_w"])[:, 0, :]),
        "conv_bk": npT(np.asarray(inputs["L0_attn_conv_b"]).reshape(DI // 128, 128)),
        "x_projT": npT(inputs["L0_attn_x_proj_w"]),
        "ident": IDENT,
    }
    maps = []
    for c in range(8):
        b, h = c // 2, c % 2
        t0 = h * TC
        if h == 0:
            xp = np.concatenate([np.zeros((3, D), np.float32), x[b, 0:TC]], axis=0)
        else:
            xp = x[b, t0 - 3:t0 + TC]
        maps.append({"x_pad": np.ascontiguousarray(xp), **wT})
    return maps


def run_l1(inputs):
    nc = build_l1()
    res = _run(nc, _l1_inmaps(inputs))
    xconv = np.stack([np.concatenate([res[2 * b]["o_xconv"], res[2 * b + 1]["o_xconv"]], axis=1) for b in range(B)])
    ssp = np.stack([np.concatenate([res[2 * b]["o_ssp"], res[2 * b + 1]["o_ssp"]], axis=1) for b in range(B)])
    z = np.stack([np.concatenate([res[2 * b]["o_z"], res[2 * b + 1]["o_z"]], axis=1) for b in range(B)])
    return xconv, ssp, z





def build_l2(nrows=1024, seqlen=L, dstate=16, tchunk=256, delta_input=False):
    pr = Prog("l2")
    NP = nrows // 128
    NTC = seqlen // tchunk
    U = tchunk + 1
    xconv = pr.dram_in("xconv", [nrows, seqlen], BF16)
    ssp_bc = pr.dram_in("ssp_bc", [2 * dstate, seqlen])
    if delta_input:
        deltaT = pr.dram_in("deltaT", [nrows, seqlen])
    else:
        dtraw = pr.dram_in("dtraw", [1, seqlen])
        dt_wT = pr.dram_in("dt_wT", [1, nrows])
        dt_bT = pr.dram_in("dt_bT", [128, NP])
    A_log = pr.dram_in("A_log", [nrows, dstate])
    D_skipT = pr.dram_in("D_skipT", [128, NP])
    o_y = pr.dram_out("o_y", [nrows, seqlen])

    def body(pr):
        s_xc = [pr.sbuf(f"s_xc{j}", [128, seqlen], BF16) for j in range(NP)]
        s_bc = pr.sbuf("s_bc", [2 * dstate, seqlen], BF16)
        s_flat = pr.sbuf("s_flat", [1, 2 * dstate * tchunk], BF16)
        s_onesb = pr.sbuf("s_onesb", [1, 128], BF16)
        s_dtraw = pr.sbuf("s_dtraw", [1, seqlen], F32)
        s_dtw = pr.sbuf("s_dtw", [1, nrows], F32)
        s_dtb = pr.sbuf("s_dtb", [128, NP], F32)
        s_A = pr.sbuf("s_A", [128, NP * dstate], F32)
        s_D = pr.sbuf("s_D", [128, NP], F32)
        s_Bbig = pr.sbuf("s_Bbig", [128, dstate * U], BF16)
        s_Cbig = pr.sbuf("s_Cbig", [128, dstate * U], BF16)
        s_dA = pr.sbuf("s_dA", [128, dstate * U], F32)
        s_dBu = pr.sbuf("s_dBu", [128, dstate * U], BF16)
        s_h = pr.sbuf("s_h", [128, dstate * U], F32)
        s_yp = pr.sbuf("s_yp", [128, dstate * U], BF16)
        s_carry = [pr.sbuf(f"s_carry{j}", [128, dstate], F32) for j in range(NP)]
        s_delta = pr.sbuf("s_delta", [128, tchunk], F32)
        s_u = pr.sbuf("s_u", [128, tchunk], BF16)
        s_sp1 = pr.sbuf("s_sp1", [128, tchunk], F32)
        s_sp2 = pr.sbuf("s_sp2", [128, tchunk], F32)
        s_y = pr.sbuf("s_y", [128, tchunk], F32)
        s_yout = pr.sbuf("s_yout", [128, tchunk], F32)
        p_dl = pr.psum("p_dl", [128, tchunk])
        p_bc = pr.psum("p_bc", [128, tchunk])

        ld_xc = [pr.op("sync", lambda s, j=j: s.dma_start(
            out=s_xc[j][:], in_=xconv[j * 128:(j + 1) * 128, :]), dma=True, group=pr.dma_group("consts"))
            for j in range(NP)]
        ld_bc = pr.op("gpsimd", lambda g: g.dma_start(out=s_bc[:], in_=ssp_bc[:]), dma=True, group=pr.dma_group("gconsts"))
        if not delta_input:
            ld_dtr = pr.op("sync", lambda s: s.dma_start(out=s_dtraw[:], in_=dtraw[:]), dma=True, group=pr.dma_group("consts"))
            ld_dtw = pr.op("sync", lambda s: s.dma_start(out=s_dtw[:], in_=dt_wT[:]), dma=True, group=pr.dma_group("consts"))
            ld_dtb = pr.op("sync", lambda s: s.dma_start(out=s_dtb[:], in_=dt_bT[:]), dma=True, group=pr.dma_group("consts"))
        ld_A = pr.op("sync", lambda s: s.dma_start(
            out=s_A[:, :].rearrange("p (a n) -> p a n", a=NP),
            in_=A_log[:, :].rearrange("(a p) n -> p a n", p=128)), dma=True, group=pr.dma_group("consts"))
        ld_D = pr.op("sync", lambda s: s.dma_start(out=s_D[:], in_=D_skipT[:]), dma=True, group=pr.dma_group("consts"))

        d_Ae = pr.op("scalar", lambda sc: sc.activation(s_A[:], s_A[:], AF.Exp), deps=[ld_A])
        d_An = pr.op("vector", lambda v: v.tensor_scalar(
            s_A[:], s_A[:], -1.0, None, op0=ALU.mult), deps=[d_Ae])
        d_ones = pr.op("gpsimd", lambda g: g.memset(s_onesb[:], 1.0))
        d_z0 = pr.op("vector", lambda v: v.memset(
            s_dA[:, :].rearrange("p (n u) -> p n u", u=U)[:, :, 0:1], 0.0))
        d_zc = [pr.op("vector", lambda v, j=j: v.memset(s_carry[j][:], 0.0)) for j in range(NP)]

        prev_chunk_store = {}
        prev_carry_dep = {j: d_zc[j] for j in range(NP)}
        prev_bcast_users = []
        prev_pt_cycle = []
        for tc in range(NTC):
            co = tc * tchunk
            d_g = pr.op("gpsimd", lambda g, co=co: g.dma_start(
                out=s_flat[0:1, :].rearrange("p (n t) -> p n t", n=2 * dstate),
                in_=s_bc[:, co:co + tchunk]), deps=[ld_bc] + prev_bcast_users, dma=True,
                group=pr.dma_group("flat", stream=True))
            bc_ev = []
            d_mm_prev = None
            for n in range(2 * dstate):
                dst = s_Bbig if n < dstate else s_Cbig
                nn = n % dstate
                d_mm = pr.op("tensor", lambda t, n=n: t.matmul(
                    p_bc[:, 0:tchunk], s_onesb[:], s_flat[0:1, n * tchunk:(n + 1) * tchunk],
                    start=True, stop=True), deps=[d_g, d_ones] + bc_ev[-1:])
                d_mm_last = d_mm
                d_ev = pr.op("scalar", lambda sc, dst=dst, nn=nn: sc.activation(
                    dst[:, nn * U + 1:nn * U + 1 + tchunk], p_bc[:, 0:tchunk], AF.Identity),
                    deps=[d_mm])
                bc_ev.append(d_ev)
            d_bcast_done = bc_ev[-1]
            prev_bcast_users = [d_mm_last]

            for pt in range(NP):
                gate = prev_pt_cycle
                if delta_input:
                    d_dlt = pr.op("sync", lambda s, pt=pt, co=co: s.dma_start(
                        out=s_delta[:], in_=deltaT[pt * 128:(pt + 1) * 128, co:co + tchunk]),
                        deps=gate, dma=True, group=pr.dma_group("dl", stream=True))
                    d_a2 = d_dlt
                else:
                    d_dl = pr.op("tensor", lambda t, pt=pt, co=co: t.matmul(
                        p_dl[:, 0:tchunk], s_dtw[0:1, pt * 128:(pt + 1) * 128],
                        s_dtraw[0:1, co:co + tchunk], start=True, stop=True),
                        deps=[ld_dtw, ld_dtr] + gate)
                    d_a1 = pr.op("scalar", lambda sc, pt=pt: sc.activation(
                        s_sp1[:], p_dl[:, 0:tchunk], AF.Abs, bias=s_dtb[:, pt:pt + 1]),
                        deps=[d_dl, ld_dtb] + gate)
                    d_a2 = pr.op("scalar", lambda sc, pt=pt: sc.activation(
                        s_sp2[:], p_dl[:, 0:tchunk], AF.Relu, bias=s_dtb[:, pt:pt + 1]),
                        deps=[d_dl, ld_dtb] + gate)
                    d_a3 = pr.op("scalar", lambda sc: sc.activation(
                        s_sp1[:], s_sp1[:], AF.Exp, scale=-1.0), deps=[d_a1])
                    d_a4 = pr.op("scalar", lambda sc: sc.activation(
                        s_sp1[:], s_sp1[:], AF.Ln, bias=1.0), deps=[d_a3])
                    d_dlt = pr.op("vector", lambda v: v.tensor_tensor(
                        s_delta[:], s_sp1[:], s_sp2[:], op=ALU.add), deps=[d_a4, d_a2] + gate)
                d_u = pr.op("vector", lambda v, pt=pt, co=co: v.tensor_tensor(
                    s_u[:], s_delta[:], s_xc[pt][:, co:co + tchunk], op=ALU.mult),
                    deps=[d_dlt, ld_xc[pt]])
                d_dAs = []
                for n in range(dstate):
                    d_dAs.append(pr.op("scalar", lambda sc, n=n, pt=pt: sc.activation(
                        s_dA[:, n * U + 1:n * U + 1 + tchunk], s_delta[:], AF.Exp,
                        scale=s_A[:, pt * dstate + n:pt * dstate + n + 1]),
                        deps=[d_dlt, d_An] + gate))
                d_cc = pr.op("vector", lambda v, pt=pt: v.tensor_copy(
                    s_dBu[:, :].rearrange("p (n u) -> p n u", u=U)[:, :, 0:1],
                    s_carry[pt][:, :].rearrange("p (n o) -> p n o", o=1)),
                    deps=[prev_carry_dep[pt]] + gate)
                d_dBus = []
                for n in range(dstate):
                    d_dBus.append(pr.op("vector", lambda v, n=n: v.tensor_tensor(
                        s_dBu[:, n * U + 1:n * U + 1 + tchunk], s_u[:],
                        s_Bbig[:, n * U + 1:n * U + 1 + tchunk], op=ALU.mult),
                        deps=[d_u, d_bcast_done] + gate))
                d_scan = pr.op("vector", lambda v: v.tensor_tensor_scan(
                    s_h[:], s_dA[:], s_dBu[:], 0.0, op0=ALU.mult, op1=ALU.add),
                    deps=d_dAs + d_dBus + [d_cc, d_z0] + gate)
                d_sc = pr.op("vector", lambda v, pt=pt: v.tensor_copy(
                    s_carry[pt][:, :].rearrange("p (n o) -> p n o", o=1),
                    s_h[:, :].rearrange("p (n u) -> p n u", u=U)[:, :, U - 1:U]),
                    deps=[d_scan])
                prev_carry_dep[pt] = d_sc
                d_yp = pr.op("vector", lambda v: v.tensor_tensor(
                    s_yp[:], s_h[:], s_Cbig[:], op=ALU.mult), deps=[d_scan, d_bcast_done])
                d_y = pr.op("vector", lambda v: v.tensor_reduce(
                    s_y[:, :], s_yp[:, :].rearrange("p (n u) -> p u n", u=U)[:, 1:, :],
                    axis=AX.X, op=ALU.add), deps=[d_yp])
                d_yo = pr.op("vector", lambda v, pt=pt, co=co: v.scalar_tensor_tensor(
                    s_yout[:], s_xc[pt][:, co:co + tchunk], s_D[:, pt:pt + 1], s_y[:],
                    op0=ALU.mult, op1=ALU.add), deps=[d_y, ld_D])
                d_st = pr.op("sync", lambda s, pt=pt, co=co: s.dma_start(
                    out=o_y[pt * 128:(pt + 1) * 128, co:co + tchunk], in_=s_yout[:]),
                    deps=[d_yo], dma=True, group=pr.dma_group("yst", stream=True))
                prev_chunk_store[(tc, pt)] = d_st
                prev_pt_cycle = [d_yo, d_scan, d_st, d_a2]
                prev_bcast_users.extend([d_dBus[-1], d_yp])

        pr.op("vector", lambda v: v.engine_nop(),
              deps=[prev_chunk_store[(NTC - 1, j)] for j in range(NP)])

    nc = pr.build(body)
    return nc


def _l2_inmaps(inputs, xconv_bdt, ssp_b33t):
    dt_wT = npT(np.asarray(inputs["L0_attn_dt_w"]))
    dt_b = np.asarray(inputs["L0_attn_dt_b"])
    A_log = np.asarray(inputs["L0_attn_A_log"])
    D_skip = np.asarray(inputs["L0_attn_D_skip"])
    maps = []
    for c in range(8):
        b, dh = c // 2, c % 2
        r0 = dh * 1024
        maps.append({
            "xconv": np.ascontiguousarray(xconv_bdt[b, r0:r0 + 1024]),
            "ssp_bc": np.ascontiguousarray(ssp_b33t[b, 0:32]),
            "dtraw": np.ascontiguousarray(ssp_b33t[b, 32:33]),
            "dt_wT": np.ascontiguousarray(dt_wT[:, r0:r0 + 1024]),
            "dt_bT": npT(dt_b[r0:r0 + 1024].reshape(8, 128)),
            "A_log": np.ascontiguousarray(A_log[r0:r0 + 1024]),
            "D_skipT": npT(D_skip[r0:r0 + 1024].reshape(8, 128)),
        })
    return maps


def run_l2(inputs, xconv, ssp):
    nc = build_l2()
    res = _run(nc, _l2_inmaps(inputs, xconv, ssp))
    y = np.stack([np.concatenate([res[2 * b]["o_y"], res[2 * b + 1]["o_y"]], axis=0) for b in range(B)])
    return y





def build_l3():
    pr = Prog("l3")
    x_in = pr.dram_in("x_in", [TC, D])
    yT = pr.dram_in("yT", [DI, TC])
    zT = pr.dram_in("zT", [DI, TC], BF16)
    out_projT = pr.dram_in("out_projT", [DI, D])
    w_attnT = pr.dram_in("w_attnT", [128, 8])
    q_wT = pr.dram_in("q_wT", [D, D])
    k_wT = pr.dram_in("k_wT", [D, 512])
    v_wT = pr.dram_in("v_wT", [D, 512])
    ident = pr.dram_in("ident", [128, 128], BF16)
    o_s = pr.dram_out("o_s", [TC, D])
    o_QT = pr.dram_out("o_QT", [D, TC], BF16)
    o_KT = pr.dram_out("o_KT", [512, TC], BF16)
    o_V = pr.dram_out("o_V", [TC, 512], BF16)

    def body(pr):
        s_g = [pr.sbuf(f"s_g{j}", [128, TC], BF16) for j in range(16)]
        s_yb = pr.sbuf("s_yb", [128, TC], F32)
        s_zb = pr.sbuf("s_zb", [128, TC], BF16)
        s_sz = pr.sbuf("s_sz", [128, TC], BF16)
        s_opw = [pr.sbuf(f"s_opw{j}", [128, D], BF16) for j in range(16)]
        s_qw = [pr.sbuf(f"s_qw{j}", [128, D], BF16) for j in range(8)]
        s_kw = [pr.sbuf(f"s_kw{j}", [128, 512], BF16) for j in range(8)]
        s_vw = [pr.sbuf(f"s_vw{j}", [128, 512], BF16) for j in range(8)]
        s_x = [pr.sbuf(f"s_x{i}", [128, D], F32) for i in range(3)]
        sb = {"sq": pr.sbuf("s_sq", [128, D], F32), "st": pr.sbuf("s_st", [128, 8], F32),
              "ln": pr.sbuf("s_ln", [128, D], F32)}
        s_cst = pr.sbuf("s_cst", [128, 2], F32)
        s_s = pr.sbuf("s_s", [128, D], F32)
        s_xn2 = pr.sbuf("s_xn2", [128, D], BF16)
        s_wat = pr.sbuf("s_wat", [128, 8], F32)
        s_id = pr.sbuf("s_id", [128, 128], BF16)
        s_xnT = [pr.sbuf(f"s_xnT{j}", [128, TC], BF16) for j in range(8)]
        s_stg = [pr.sbuf(f"s_stg{j}", [128, 512], BF16) for j in range(2)]
        p_tr = pr.psum("p_tr", [128, 128], BF16)
        p_mm = [pr.psum(f"p_mm{i}", [128, 512]) for i in range(2)]

        d_c0 = pr.op("vector", lambda v: v.memset(s_cst[:, 0:1], LN_EPS))
        d_c1 = pr.op("vector", lambda v: v.memset(s_cst[:, 1:2], RMS_EPS))
        ld_id = pr.op("sync", lambda s: s.dma_start(out=s_id[:], in_=ident[:]), dma=True, group=pr.dma_group("consts"))
        ld_wat = pr.op("sync", lambda s: s.dma_start(out=s_wat[:], in_=w_attnT[:]), dma=True, group=pr.dma_group("consts"))
        ld_opw = [pr.op("gpsimd", lambda g, j=j: g.dma_start(
            out=s_opw[j][:], in_=out_projT[j * 128:(j + 1) * 128, :]), dma=True, group=pr.dma_group("wts")) for j in range(16)]
        ld_qw = [pr.op("gpsimd", lambda g, j=j: g.dma_start(
            out=s_qw[j][:], in_=q_wT[j * 128:(j + 1) * 128, :]), dma=True, group=pr.dma_group("wts")) for j in range(8)]
        ld_kw = [pr.op("gpsimd", lambda g, j=j: g.dma_start(
            out=s_kw[j][:], in_=k_wT[j * 128:(j + 1) * 128, :]), dma=True, group=pr.dma_group("wts")) for j in range(8)]
        ld_vw = [pr.op("gpsimd", lambda g, j=j: g.dma_start(
            out=s_vw[j][:], in_=v_wT[j * 128:(j + 1) * 128, :]), dma=True, group=pr.dma_group("wts")) for j in range(8)]

        g_done = []
        prev = []
        for ptile in range(16):
            ld_y = pr.op("sync", lambda s, ptile=ptile: s.dma_start(
                out=s_yb[:], in_=yT[ptile * 128:(ptile + 1) * 128, :]), deps=prev, dma=True,
                group=pr.dma_group("gy", stream=True))
            ld_z = pr.op("sync", lambda s, ptile=ptile: s.dma_start(
                out=s_zb[:], in_=zT[ptile * 128:(ptile + 1) * 128, :]), deps=prev, dma=True,
                group=pr.dma_group("gz", stream=True))
            d_si = pr.op("scalar", lambda sc: sc.activation(s_sz[:], s_zb[:], AF.Silu), deps=[ld_z] + prev)
            d_mul = pr.op("vector", lambda v, ptile=ptile: v.tensor_tensor(
                s_g[ptile][:], s_yb[:], s_sz[:], op=ALU.mult), deps=[d_si, ld_y])
            g_done.append(d_mul)
            prev = [d_mul]
        d_gall = g_done[-1]

        x_cons = {}
        bank_ev = [None, None]
        tp_gate = []
        st_gate = []
        prev_s_st = None
        for ti in range(16):
            ld_x = pr.op("sync", lambda s, ti=ti: s.dma_start(
                out=s_x[ti % 3][:], in_=x_in[ti * 128:(ti + 1) * 128, :]),
                deps=[x_cons[ti - 3]] if (ti - 3) in x_cons else [], dma=True,
                group=pr.dma_group(f"x{ti % 3}", stream=True))
            d_ln = emit_ln_tile(pr, s_x[ti % 3][:, :], 128, sb, [ld_x] + tp_gate, s_cst[:, 0:1])
            x_cons[ti] = d_ln
            ev_dm = []
            for dm in range(2):
                d_mml = None
                for k in range(16):
                    d_mml = pr.op("tensor", lambda t, k=k, ti=ti, dm=dm, first=(k == 0): t.matmul(
                        p_mm[dm][:, :], s_g[k][:, ti * 128:(ti + 1) * 128],
                        s_opw[k][:, dm * 512:(dm + 1) * 512], start=first, stop=(k == 15)),
                        deps=([d_gall, ld_opw[15]] + ([bank_ev[dm]] if bank_ev[dm] else [])) if k == 0 else [d_mml])
                d_ev = pr.op("vector", lambda v, dm=dm: v.tensor_tensor(
                    s_s[:, dm * 512:(dm + 1) * 512], p_mm[dm][:, :],
                    sb["ln"][:, dm * 512:(dm + 1) * 512], op=ALU.add),
                    deps=[d_mml, d_ln] + ([prev_s_st] if prev_s_st else []))
                bank_ev[dm] = d_ev
                ev_dm.append(d_ev)
            d_st = pr.op("sync", lambda s, ti=ti: s.dma_start(
                out=o_s[ti * 128:(ti + 1) * 128, :], in_=s_s[:, :]), deps=ev_dm, dma=True,
                group=pr.dma_group("sst", stream=True))
            prev_s_st = d_st
            st_gate = [d_st]
            d_xn = emit_rms_tile(pr, s_s[:, :], s_xn2[:, :], 128, sb, ev_dm + tp_gate, s_cst[:, 1:2])
            evs, last_ev = emit_transpose_cols(
                pr, lambda j: s_xn2[:, j * 128:(j + 1) * 128], s_xnT, 128, ti * 128,
                p_tr, s_id, [d_xn, ld_id, ld_wat], wcolT=s_wat)
            tp_gate = [evs[-1], d_st]
        d_xnT_all = tp_gate[0]

        outs_st = []
        stg_free = [None, None]
        p_free = [bank_ev[0], bank_ev[1]]
        si = 0
        def qkv_mm(dst_dram, row_off, wtiles, wcol_off, tcw, deps_extra):
            nonlocal si
            d_mml = None
            bank = si % 2
            for k in range(8):
                d_mml = pr.op("tensor", lambda t, k=k, first=(k == 0), bank=bank, wcol_off=wcol_off, tcw=tcw: t.matmul(
                    p_mm[bank][:, 0:512], wtiles[k][:, wcol_off:wcol_off + 128] if False else wtiles[k],
                    s_xnT[k][:, tcw:tcw + 512], start=first, stop=(k == 7)),
                    deps=([d_xnT_all] + deps_extra + ([p_free[bank]] if p_free[bank] else [])) if k == 0 else [d_mml])
            stg = s_stg[si % 2]
            d_ev = pr.op("scalar", lambda sc, stg=stg, bank=bank: sc.activation(
                stg[:], p_mm[bank][:, 0:512], AF.Identity),
                deps=[d_mml] + ([stg_free[si % 2]] if stg_free[si % 2] else []))
            p_free[bank] = d_ev
            d_st = pr.op("sync", lambda s, stg=stg: s.dma_start(
                out=dst_dram, in_=stg[:]), deps=[d_ev], dma=True)
            stg_free[si % 2] = d_st
            outs_st.append(d_st)
            si += 1

        for ptq in range(8):
            qw_slices = [None] * 8
            for tcw in range(0, TC, 512):
                d_mml = None
                bank = si % 2
                for k in range(8):
                    d_mml = pr.op("tensor", lambda t, k=k, ptq=ptq, tcw=tcw, first=(k == 0), bank=bank: t.matmul(
                        p_mm[bank][:, 0:512], s_qw[k][:, ptq * 128:(ptq + 1) * 128],
                        s_xnT[k][:, tcw:tcw + 512], start=first, stop=(k == 7)),
                        deps=([d_xnT_all, ld_qw[7]] + ([p_free[bank]] if p_free[bank] else [])) if k == 0 else [d_mml])
                stg = s_stg[si % 2]
                d_ev = pr.op("scalar", lambda sc, stg=stg, bank=bank: sc.activation(
                    stg[:], p_mm[bank][:, 0:512], AF.Identity),
                    deps=[d_mml] + ([stg_free[si % 2]] if stg_free[si % 2] else []))
                p_free[bank] = d_ev
                d_st = pr.op("sync", lambda s, stg=stg, ptq=ptq, tcw=tcw: s.dma_start(
                    out=o_QT[ptq * 128:(ptq + 1) * 128, tcw:tcw + 512], in_=stg[:]),
                    deps=[d_ev], dma=True, group=pr.dma_group(f"qst{si % 2}", stream=True))
                stg_free[si % 2] = d_st
                outs_st.append(d_st)
                si += 1
        for ptq in range(4):
            for tcw in range(0, TC, 512):
                d_mml = None
                bank = si % 2
                for k in range(8):
                    d_mml = pr.op("tensor", lambda t, k=k, ptq=ptq, tcw=tcw, first=(k == 0), bank=bank: t.matmul(
                        p_mm[bank][:, 0:512], s_kw[k][:, :].rearrange("p (a c) -> p a c", a=4)[:, ptq, :] if False else s_kw[k][:, ptq * 128:(ptq + 1) * 128],
                        s_xnT[k][:, tcw:tcw + 512], start=first, stop=(k == 7)),
                        deps=([d_xnT_all, ld_kw[7]] + ([p_free[bank]] if p_free[bank] else [])) if k == 0 else [d_mml])
                stg = s_stg[si % 2]
                d_ev = pr.op("scalar", lambda sc, stg=stg, bank=bank: sc.activation(
                    stg[:], p_mm[bank][:, 0:512], AF.Identity),
                    deps=[d_mml] + ([stg_free[si % 2]] if stg_free[si % 2] else []))
                p_free[bank] = d_ev
                d_st = pr.op("sync", lambda s, stg=stg, ptq=ptq, tcw=tcw: s.dma_start(
                    out=o_KT[ptq * 128:(ptq + 1) * 128, tcw:tcw + 512], in_=stg[:]),
                    deps=[d_ev], dma=True, group=pr.dma_group(f"qst{si % 2}", stream=True))
                stg_free[si % 2] = d_st
                outs_st.append(d_st)
                si += 1
        for ti in range(16):
            d_mml = None
            bank = si % 2
            for k in range(8):
                d_mml = pr.op("tensor", lambda t, k=k, ti=ti, first=(k == 0), bank=bank: t.matmul(
                    p_mm[bank][:, 0:512], s_xnT[k][:, ti * 128:(ti + 1) * 128],
                    s_vw[k][:, :], start=first, stop=(k == 7)),
                    deps=([d_xnT_all, ld_vw[7]] + ([p_free[bank]] if p_free[bank] else [])) if k == 0 else [d_mml])
            stg = s_stg[si % 2]
            d_ev = pr.op("scalar", lambda sc, stg=stg, bank=bank: sc.activation(
                stg[:], p_mm[bank][:, 0:512], AF.Identity),
                deps=[d_mml] + ([stg_free[si % 2]] if stg_free[si % 2] else []))
            p_free[bank] = d_ev
            d_st = pr.op("sync", lambda s, stg=stg, ti=ti: s.dma_start(
                out=o_V[ti * 128:(ti + 1) * 128, :], in_=stg[:]), deps=[d_ev], dma=True,
                group=pr.dma_group(f"qst{si % 2}", stream=True))
            stg_free[si % 2] = d_st
            outs_st.append(d_st)
            si += 1

        pr.op("vector", lambda v: v.engine_nop(), deps=outs_st[-3:] + st_gate)

    nc = pr.build(body)
    return nc


def _l3_inmaps(inputs, yT_full, zT_full):
    x = np.asarray(inputs["L0_attn_x"])
    common = {
        "out_projT": npT(inputs["L0_attn_out_proj_w"]),
        "w_attnT": npT(np.asarray(inputs["L0_attn_norm_attn_w"]).reshape(8, 128)),
        "q_wT": npT(inputs["L0_attn_q_w"]),
        "k_wT": npT(inputs["L0_attn_k_w"]),
        "v_wT": npT(inputs["L0_attn_v_w"]),
        "ident": IDENT,
    }
    maps = []
    for c in range(8):
        b, h = c // 2, c % 2
        t0 = h * TC
        maps.append({
            "x_in": np.ascontiguousarray(x[b, t0:t0 + TC]),
            "yT": np.ascontiguousarray(yT_full[b, :, t0:t0 + TC]),
            "zT": np.ascontiguousarray(zT_full[b, :, t0:t0 + TC]),
            **common,
        })
    return maps


def run_l3(inputs, yT_full, zT_full):
    nc = build_l3()
    res = _run(nc, _l3_inmaps(inputs, yT_full, zT_full))
    s = np.stack([np.concatenate([res[2 * b]["o_s"], res[2 * b + 1]["o_s"]], axis=0) for b in range(B)])
    QT = np.stack([np.concatenate([res[2 * b]["o_QT"], res[2 * b + 1]["o_QT"]], axis=1) for b in range(B)])
    KT = np.stack([np.concatenate([res[2 * b]["o_KT"], res[2 * b + 1]["o_KT"]], axis=1) for b in range(B)])
    V = np.stack([np.concatenate([res[2 * b]["o_V"], res[2 * b + 1]["o_V"]], axis=0) for b in range(B)])
    return s, QT, KT, V





def build_l4():
    pr = Prog("l4")
    QT = pr.dram_in("QT", [D, TC], BF16)
    KT = pr.dram_in("KT", [512, L], BF16)
    Vt = pr.dram_in("Vt", [L, 512], BF16)
    qpos0 = pr.dram_in("qpos0", [128, 1])
    kpos_in = pr.dram_in("kpos_in", [128, L])
    ident = pr.dram_in("ident", [128, 128], BF16)
    o_ctx = pr.dram_out("o_ctx", [TC, D], BF16)
    if DEBUG_L4:
        o_dbg_row = pr.dram_out("o_dbg_row", [128, L])
        o_dbg_p = pr.dram_out("o_dbg_p", [128, L], BF16)
        o_dbg_sum = pr.dram_out("o_dbg_sum", [128, 2])
        o_dbg_q = pr.dram_out("o_dbg_q", [128, TC], BF16)
        o_dbg_k = pr.dram_out("o_dbg_k", [128, L], BF16)

    def body(pr):
        s_qt = [pr.sbuf(f"s_qt{j}", [128, TC], BF16) for j in range(8)]
        s_kt = [pr.sbuf(f"s_kt{j}", [128, L], BF16) for j in range(4)]
        s_v = [pr.sbuf(f"s_v{j}", [128, 512], BF16) for j in range(32)]
        s_kpos = pr.sbuf("s_kpos", [128, L], F32)
        s_qp0 = pr.sbuf("s_qp0", [128, 1], F32)
        s_qp = pr.sbuf("s_qp", [128, 1], F32)
        s_m = pr.sbuf("s_m", [128, 512], F32)
        s_row = pr.sbuf("s_row", [128, L], F32)
        s_mx = pr.sbuf("s_mx", [128, 2], F32)
        s_sum = pr.sbuf("s_sum", [128, 2], F32)
        s_p = pr.sbuf("s_p", [128, L], BF16)
        s_pt = pr.sbuf("s_pt", [128, L], BF16)
        s_ctx = [pr.sbuf(f"s_ctx{i}", [128, D], BF16) for i in range(2)]
        s_id = pr.sbuf("s_id", [128, 128], BF16)
        p_sc = [pr.psum(f"p_sc{i}", [128, 512]) for i in range(2)]
        p_tr = pr.psum("p_tr", [128, 128], BF16)
        p_av = pr.psum("p_av", [128, 256])
        s_fence = pr.sbuf("s_fence", [1, 2], F32)

        ld_q = [pr.op("sync", lambda s, j=j: s.dma_start(
            out=s_qt[j][:], in_=QT[j * 128:(j + 1) * 128, :]), dma=True, group=pr.dma_group("lds")) for j in range(8)]
        ld_k = [pr.op("sync", lambda s, j=j: s.dma_start(
            out=s_kt[j][:], in_=KT[j * 128:(j + 1) * 128, :]), dma=True, group=pr.dma_group("lds")) for j in range(4)]
        ld_v = [pr.op("sync", lambda s, j=j: s.dma_start(
            out=s_v[j][:], in_=Vt[j * 128:(j + 1) * 128, :]), dma=True, group=pr.dma_group("lds")) for j in range(32)]
        ld_kp = pr.op("sync", lambda s: s.dma_start(out=s_kpos[:], in_=kpos_in[:]), dma=True, group=pr.dma_group("lds"))
        ld_qp = pr.op("sync", lambda s: s.dma_start(out=s_qp0[:], in_=qpos0[:]), dma=True, group=pr.dma_group("lds"))
        ld_id = pr.op("sync", lambda s: s.dma_start(out=s_id[:], in_=ident[:]), dma=True, group=pr.dma_group("consts"))
        ldall = ld_q + ld_k + ld_v + [ld_kp, ld_qp, ld_id]

        ctx_sts = []
        pend = []
        st_parity = [None, None]
        prev_cycle = []
        prev_ctx_st = None
        p_free = [None, None]
        av_free = None
        tr_free = None
        for qi in range(16):
            kcnt = (TC + qi * 128 + 127) // 512 + 1
            kcnt = min(kcnt, 8)
            d_qp = pr.op("vector", lambda v, qi=qi: v.tensor_scalar(
                s_qp[:], s_qp0[:], float(qi * 128), None, op0=ALU.add),
                deps=[ld_qp] + prev_cycle)
            for h in range(4):
                kv = h // 2
                gate = prev_cycle
                mask_evs = []
                for kj in range(kcnt):
                    bank = kj % 2
                    d_mml = None
                    for dk in range(2):
                        d_mml = pr.op("tensor", lambda t, h=h, dk=dk, qi=qi, kj=kj, kv=kv, bank=bank, first=(dk == 0): t.matmul(
                            p_sc[bank][:, :], s_qt[h * 2 + dk][:, qi * 128:(qi + 1) * 128],
                            s_kt[kv * 2 + dk][:, kj * 512:(kj + 1) * 512],
                            start=first, stop=(dk == 1)),
                            deps=(ldall + gate + ([p_free[bank]] if p_free[bank] else [])) if dk == 0 else [d_mml])
                    d_m = pr.op("vector", lambda v, kj=kj: v.tensor_scalar(
                        s_m[:], s_kpos[:, kj * 512:(kj + 1) * 512], s_qp[:, 0:1], MASK_NEG,
                        op0=ALU.is_gt, op1=ALU.mult), deps=[d_qp, ld_kp] + gate)
                    d_ev = pr.op("vector", lambda v, kj=kj, bank=bank: v.scalar_tensor_tensor(
                        s_row[:, kj * 512:(kj + 1) * 512], p_sc[bank][:, :], 0.0625, s_m[:],
                        op0=ALU.mult, op1=ALU.add), deps=[d_mml, d_m])
                    p_free[bank] = d_ev
                    mask_evs.append(d_ev)
                W = kcnt * 512
                d_mx = pr.op("vector", lambda v, W=W: v.tensor_reduce(
                    s_mx[:, 0:1], s_row[:, 0:W], axis=AX.X, op=ALU.max), deps=mask_evs)
                d_nm = pr.op("vector", lambda v: v.tensor_scalar(
                    s_mx[:, 1:2], s_mx[:, 0:1], -1.0, None, op0=ALU.mult), deps=[d_mx])
                d_exp = pr.op("scalar", lambda sc, W=W: sc.activation(
                    s_p[:, 0:W], s_row[:, 0:W], AF.Exp, bias=s_mx[:, 1:2]), deps=[d_nm] + gate)
                d_sm = pr.op("vector", lambda v, W=W: v.tensor_reduce(
                    s_sum[:, 0:1], s_p[:, 0:W], axis=AX.X, op=ALU.add), deps=[d_exp] + gate)
                d_l = pr.op("scalar", lambda sc: sc.activation(
                    s_sum[:, 1:2], s_sum[:, 0:1], AF.Ln), deps=[d_sm])
                d_rc = pr.op("scalar", lambda sc: sc.activation(
                    s_sum[:, 1:2], s_sum[:, 1:2], AF.Exp, scale=-1.0), deps=[d_l])
                d_fence = pr.op("scalar", lambda sc: sc.activation(
                    s_fence[:, 0:1], s_fence[:, 0:1], AF.Identity), deps=[d_exp])
                if DEBUG_L4 and qi == 0 and h == 0:
                    dmp1 = pr.op("sync", lambda s: s.dma_start(out=o_dbg_q[:, :], in_=s_qt[0][:, :]), deps=[d_mx], dma=True)
                    dmp2 = pr.op("sync", lambda s: s.dma_start(out=o_dbg_k[:, :], in_=s_kt[0][:, :]), deps=[d_mx], dma=True)
                    dmp3 = pr.op("sync", lambda s: s.dma_start(out=o_dbg_row[:, :], in_=s_row[:, :]), deps=[d_mx], dma=True)
                    dmp4 = pr.op("sync", lambda s: s.dma_start(out=o_dbg_p[:, :], in_=s_p[:, :]), deps=[d_exp], dma=True)
                    dmp5 = pr.op("sync", lambda s: s.dma_start(out=o_dbg_sum[:, :], in_=s_sum[:, :]), deps=[d_rc], dma=True)
                    prev_cycle = prev_cycle + [dmp1, dmp2, dmp3, dmp4, dmp5]
                tblocks = kcnt * 4
                d_pt_evs = []
                prevt = tr_free
                for tb in range(tblocks):
                    d_tp = pr.op("tensor", lambda t, tb=tb: t.transpose(
                        p_tr[:, :], s_p[:, tb * 128:(tb + 1) * 128], s_id[:]),
                        deps=[d_exp, d_fence, ld_id] + ([prevt] if prevt else []) + gate)
                    d_pe = pr.op("vector", lambda v, tb=tb: v.tensor_copy(
                        s_pt[:, tb * 128:(tb + 1) * 128], p_tr[:, :]), deps=[d_tp])
                    d_pt_evs.append(d_pe)
                    prevt = d_pe
                tr_free = prevt
                d_mml = None
                for tb in range(tblocks):
                    d_mml = pr.op("tensor", lambda t, tb=tb, kv=kv, first=(tb == 0), last=(tb == tblocks - 1): t.matmul(
                        p_av[:, :], s_pt[:, tb * 128:(tb + 1) * 128],
                        s_v[tb][:, kv * 256:(kv + 1) * 256], start=first, stop=last),
                        deps=([d_pt_evs[-1]] + ([av_free] if av_free else [])) if tb == 0 else [d_mml])
                d_cv = pr.op("vector", lambda v, h=h, qi=qi: v.tensor_scalar(
                    s_ctx[qi % 2][:, h * 256:(h + 1) * 256], p_av[:, :], s_sum[:, 1:2], None,
                    op0=ALU.mult), deps=[d_mml, d_rc] + ([st_parity[qi % 2]] if (h == 0 and st_parity[qi % 2]) else []))
                av_free = d_cv
                prev_cycle = [d_cv, d_exp, d_mx]
            pend.append((qi, av_free))
            if len(pend) == 2:
                q0, dep0 = pend.pop(0)
                d_st = pr.op("sync", lambda s, q0=q0: s.dma_start(
                    out=o_ctx[q0 * 128:(q0 + 1) * 128, :], in_=s_ctx[q0 % 2][:]),
                    deps=[dep0, av_free], dma=True,
                    group=pr.dma_group(f"cst{q0 % 2}", stream=True))
                prev_ctx_st = d_st
                st_parity[q0 % 2] = d_st
                ctx_sts.append(d_st)

        d_fl = pr.op("vector", lambda v: v.tensor_copy(s_m[:, 0:4], s_ctx[15 % 2][:, 0:4]),
                     deps=[pend[0][1]])
        d_st = pr.op("sync", lambda s: s.dma_start(
            out=o_ctx[15 * 128:16 * 128, :], in_=s_ctx[15 % 2][:]), deps=[pend[0][1], d_fl],
            dma=True, group=pr.dma_group("cst1", stream=True))
        ctx_sts.append(d_st)
        pr.op("vector", lambda v: v.engine_nop(), deps=ctx_sts[-2:])

    nc = pr.build(body)
    return nc


def _l4_inmaps(QT_full, KT_full, V_full):
    kpos = np.broadcast_to(np.arange(L, dtype=np.float32)[None, :], (128, L))
    kpos = np.ascontiguousarray(kpos)
    maps = []
    for c in range(8):
        b, h = c // 2, c % 2
        t0 = h * TC
        maps.append({
            "QT": np.ascontiguousarray(QT_full[b][:, t0:t0 + TC]),
            "KT": np.ascontiguousarray(KT_full[b]),
            "Vt": np.ascontiguousarray(V_full[b]),
            "qpos0": (t0 + np.arange(128, dtype=np.float32)).reshape(128, 1),
            "kpos_in": kpos,
            "ident": IDENT,
        })
    return maps


def run_l4(QT_full, KT_full, V_full):
    nc = build_l4()
    res = _run(nc, _l4_inmaps(QT_full, KT_full, V_full))
    ctx = np.stack([np.concatenate([res[2 * b]["o_ctx"], res[2 * b + 1]["o_ctx"]], axis=0) for b in range(B)])
    return ctx





def build_l5():
    pr = Prog("l5")
    s_in = pr.dram_in("s_in", [TC, D])
    ctx_in = pr.dram_in("ctx_in", [TC, D], BF16)
    x_in = pr.dram_in("x_in", [TC, D])
    oT = pr.dram_in("oT", [D, D])
    w_ffnT = pr.dram_in("w_ffnT", [128, 8])
    gateT = pr.dram_in("gateT", [D, FH])
    upT = pr.dram_in("upT", [D, FH])
    downT = pr.dram_in("downT", [FH, D])
    fcT = pr.dram_in("fcT", [D, FFH])
    ln_wT = pr.dram_in("ln_wT", [128, 32])
    ln_bT = pr.dram_in("ln_bT", [128, 32])
    projT = pr.dram_in("projT", [FFH, D])
    W_in1 = pr.dram_in("W_in1", [D, 2 * IN2])
    ident = pr.dram_in("ident", [128, 128], BF16)
    o_x2 = pr.dram_out("o_x2", [TC, D])
    o_raw = pr.dram_out("o_raw", [IN2, TC], BF16)
    o_g2 = pr.dram_out("o_g2", [IN2, TC], BF16)
    fcQ = pr.nc.dram_tensor("fcQ", [D, FFH], BF16).ap()
    pjQ = pr.nc.dram_tensor("pjQ", [FFH, D], BF16).ap()
    if DEBUG_L5:
        o_dbg_s2 = pr.dram_out("o_dbg_s2", [128, D])
        o_dbg_x2a = pr.dram_out("o_dbg_x2a", [128, D])
        o_dbg_fcq = pr.dram_out("o_dbg_fcq", [128, FFH], BF16)
        o_dbg_thr = pr.dram_out("o_dbg_thr", [1, 8])
        o_dbg_xn3 = pr.dram_out("o_dbg_xn3", [128, 512], BF16)
        o_dbg_g2 = pr.dram_out("o_dbg_g2", [128, 512], BF16)
        o_dbg_ha = pr.dram_out("o_dbg_ha", [128, D])
        o_dbg_wc = pr.dram_out("o_dbg_wc", [128, D], BF16)
        o_dbg_g2b = pr.dram_out("o_dbg_g2b", [128, 512], BF16)

    FB = fblocks(FH)

    def body(pr):
        s_oT = [pr.sbuf(f"s_oT{j}", [128, D], BF16) for j in range(8)]
        s_xnT = [pr.sbuf(f"s_xnT{j}", [128, 512], BF16) for j in range(8)]
        s_ctxT = pr.sbuf("s_ctxT", [128, 8 * 128], BF16)
        pool = [pr.sbuf(f"pool{j}", [128, 512], BF16) for j in range(32)]
        wch = [pr.sbuf(f"wch{j}", [128, 1024], BF16) for j in range(16)]
        s_s2 = [pr.sbuf(f"s_s2{i}", [128, D], F32) for i in range(4)]
        s_x2a = [pr.sbuf(f"s_x2a{i}", [128, D], F32) for i in range(4)]
        s_x2 = [pr.sbuf(f"s_x2{i}", [128, D], F32) for i in range(4)]
        s_ld = [pr.sbuf(f"s_ldS{i}", [128, D], F32) for i in range(1)]
        s_xr = [pr.sbuf(f"s_xr{i}", [128, D], F32) for i in range(4)]
        s_cx = [pr.sbuf(f"s_cx{i}", [128, D], BF16) for i in range(1)]
        sb = {"sq": pr.sbuf("s_sq", [128, D], F32), "st": pr.sbuf("s_st", [128, 8], F32),
              "ln": pr.sbuf("s_ln", [128, D], F32)}
        s_nbf = pr.sbuf("s_nbf", [128, D], BF16)
        s_cst = pr.sbuf("s_cst", [128, 2], F32)
        s_wff = pr.sbuf("s_wff", [128, 8], F32)
        s_lnw = pr.sbuf("s_lnw", [128, 32], F32)
        s_lnb = pr.sbuf("s_lnb", [128, 32], F32)
        s_id = pr.sbuf("s_id", [128, 128], BF16)
        s_onesb = pr.sbuf("s_onesb", [128, 1], BF16)
        s_ones1 = pr.sbuf("s_ones1", [1, 128], F32)
        s_onc = pr.sbuf("s_onc", [128, 1], F32)
        s_gw = [pr.sbuf(f"s_gw{i}", [128, 8 * 128], BF16) for i in range(2)]
        s_uw = [pr.sbuf(f"s_uw{i}", [128, 8 * 128], BF16) for i in range(2)]
        s_fw = [pr.sbuf(f"s_fw{i}", [128, 8 * 128], BF16) for i in range(2)]
        s_wi = [pr.sbuf(f"s_wi{i}", [128, 8 * 128], BF16) for i in range(2)]
        s_qs = [pr.sbuf(f"s_qs{i}", [128, 512], F32) for i in range(2)]
        s_sg = pr.sbuf("s_sg", [128, 512], BF16)
        s_tsc = pr.sbuf("s_tsc", [128, 512], F32)
        s_mu = pr.sbuf("s_mu", [128, 512], F32)
        s_rs = pr.sbuf("s_rs", [128, 512], F32)
        s_row = pr.sbuf("s_row", [1, 1024], F32)
        s_acc = pr.sbuf("s_acc", [128, 4], F32)
        s_thr = pr.sbuf("s_thr", [128, 4], F32)
        s_g11 = pr.sbuf("s_g11", [1, 8], F32)
        s_stg = [pr.sbuf(f"s_stg{i}", [128, 512], BF16) for i in range(2)]
        p_tr = pr.psum("p_tr", [128, 128], BF16)
        p_mm = [pr.psum(f"p_mm{i}", [128, 512]) for i in range(2)]
        p_st = [pr.psum(f"p_st{i}", [1, 512]) for i in range(2)]
        p_bc = pr.psum("p_bc", [128, 512])
        p_g = pr.psum("p_g", [1, 1])

        d_e0 = pr.op("vector", lambda v: v.memset(s_cst[:, 0:1], LN_EPS))
        d_e1 = pr.op("vector", lambda v: v.memset(s_cst[:, 1:2], RMS_EPS))
        d_ob = pr.op("vector", lambda v: v.memset(s_onesb[:], 1.0))
        d_o1 = pr.op("vector", lambda v: v.memset(s_ones1[:], 1.0))
        d_oc = pr.op("vector", lambda v: v.memset(s_onc[:], 1.0))
        CON = pr.dma_group("consts")
        ld_id = pr.op("sync", lambda s: s.dma_start(out=s_id[:], in_=ident[:]), dma=True, group=CON)
        ld_wff = pr.op("sync", lambda s: s.dma_start(out=s_wff[:], in_=w_ffnT[:]), dma=True, group=CON)
        ld_lnw = pr.op("sync", lambda s: s.dma_start(out=s_lnw[:], in_=ln_wT[:]), dma=True, group=CON)
        ld_lnb = pr.op("sync", lambda s: s.dma_start(out=s_lnb[:], in_=ln_bT[:]), dma=True, group=CON)
        WCON = pr.dma_group("wconsts")
        ld_oT = [pr.op("gpsimd", lambda g, j=j: g.dma_start(
            out=s_oT[j][:], in_=oT[j * 128:(j + 1) * 128, :]), dma=True, group=WCON) for j in range(8)]

        slabs = []
        for j in range(8):
            for c in range(8):
                slabs.append((fcT, fcQ, j * 128, c * 512, 0))
        for j in range(32):
            for c in range(2):
                slabs.append((projT, pjQ, j * 128, c * 512, 1))
        d_z = pr.op("vector", lambda v: v.memset(s_acc[:, :], 0.0))
        prev_use = {0: [], 1: []}
        d_accl = [d_z]
        for i, (mat, dst, r0, c0, mi) in enumerate(slabs):
            qb = s_qs[i % 2]
            ld = pr.op("sync", lambda s, mat=mat, r0=r0, c0=c0, qb=qb: s.dma_start(
                out=qb[:], in_=mat[r0:r0 + 128, c0:c0 + 512]),
                deps=prev_use[i % 2], dma=True, group=pr.dma_group(f"q{i % 2}", stream=True))
            d_r = pr.op("vector", lambda v, qb=qb: v.tensor_reduce(
                s_acc[:, 2:3], qb[:], axis=AX.X, op=ALU.add, apply_absolute_value=True),
                deps=[ld] + d_accl)
            d_a = pr.op("vector", lambda v, mi=mi: v.tensor_tensor(
                s_acc[:, mi:mi + 1], s_acc[:, mi:mi + 1], s_acc[:, 2:3], op=ALU.add), deps=[d_r])
            d_accl = [d_a]
            prev_use[i % 2] = [d_r]
        thr_deps = []
        for mi in range(2):
            d_t = pr.op("tensor", lambda t, mi=mi: t.matmul(
                p_g[:, :], s_acc[:, mi:mi + 1], s_onc[:, 0:1], start=True, stop=True),
                deps=d_accl + [d_oc] + thr_deps)
            d_g1 = pr.op("vector", lambda v, mi=mi: v.tensor_scalar(
                s_g11[0:1, mi:mi + 1], p_g[0:1, 0:1], 1.0 / (D * FFH), 1e-6,
                op0=ALU.mult, op1=ALU.add), deps=[d_t])
            thr_deps.append(d_g1)
        d_th2 = pr.op("vector", lambda v: v.tensor_scalar(
            s_g11[0:1, 4:6], s_g11[0:1, 0:2], 0.5, None, op0=ALU.mult), deps=thr_deps)
        d_th3 = pr.op("vector", lambda v: v.tensor_scalar(
            s_g11[0:1, 6:8], s_g11[0:1, 4:6], -1.0, None, op0=ALU.mult), deps=[d_th2])
        d_tb = pr.op("tensor", lambda t: t.matmul(
            p_bc[:, 0:4], s_ones1[0:1, :], s_g11[0:1, 4:8], start=True, stop=True),
            deps=[d_th3, d_o1])
        d_te = pr.op("vector", lambda v: v.tensor_copy(s_thr[:, :], p_bc[:, 0:4]), deps=[d_tb])
        prev_use = {0: [], 1: []}
        prev_qst = None
        qst_last = None
        for i, (mat, dst, r0, c0, mi) in enumerate(slabs):
            qb = s_qs[i % 2]
            ld = pr.op("sync", lambda s, mat=mat, r0=r0, c0=c0, qb=qb: s.dma_start(
                out=qb[:], in_=mat[r0:r0 + 128, c0:c0 + 512]),
                deps=prev_use[i % 2], dma=True, group=pr.dma_group(f"q{i % 2}", stream=True))
            d_p = pr.op("vector", lambda v, qb=qb, mi=mi: v.tensor_scalar(
                s_mu[:], qb[:], s_thr[:, mi:mi + 1], None, op0=ALU.is_gt), deps=[ld, d_te])
            d_n = pr.op("vector", lambda v, qb=qb, mi=mi: v.tensor_scalar(
                s_rs[:], qb[:], s_thr[:, 2 + mi:3 + mi], None, op0=ALU.is_lt), deps=[ld, d_te])
            d_q = pr.op("vector", lambda v: v.tensor_tensor(
                s_sg[:], s_mu[:], s_rs[:], op=ALU.subtract),
                deps=[d_p, d_n] + ([prev_qst] if prev_qst else []))
            d_s = pr.op("sync", lambda s, dst=dst, r0=r0, c0=c0: s.dma_start(
                out=dst[r0:r0 + 128, c0:c0 + 512], in_=s_sg[:]), deps=[d_q],
                dma=True, group=pr.dma_group("qst", stream=True))
            prev_qst = d_s
            qst_last = d_s
            prev_use[i % 2] = [d_p, d_n]

        if DEBUG_L5:
            dth = pr.op("sync", lambda s: s.dma_start(out=o_dbg_thr[:, :], in_=s_g11[0:1, :]),
                        deps=[d_th3], dma=True, group=pr.dma_group("dbg", stream=True))
            dfq = pr.op("sync", lambda s: s.dma_start(out=o_dbg_fcq[:, :], in_=fcQ[0:128, :]),
                        deps=[qst_last, dth], dma=True, group=pr.dma_group("dbg", stream=True))
        gates = {
            "xnT": [],
            "pool": [],
            "wch": [qst_last],
            "bank": [None, None],
            "stg": [None, None],
            "x2st": [None] * 4,
            "ring": {},
            "stats": [],
            "sg": [qst_last],
            "tr": None,
            "nbf": [],
            "ctxT": [],
        }
        all_stores = []
        WIN_FB = fblocks(2 * IN2)
        _dbg_dd = [None]

        def mk_ld(buf, srcfn, grp, slot_key, extra=(), eng="sync"):
            dep = gates["ring"].get(slot_key, [])
            d = pr.op(eng, srcfn, deps=list(dep) + list(extra), dma=True,
                      group=pr.dma_group(grp, stream=True))
            return d

        for tch in range(4):
            tc0 = tch * 512
            xn_evs = []
            for t4 in range(4):
                ti = tch * 4 + t4
                sl = s_ld[0]; xr = s_xr[t4]; cx = s_cx[0]
                ld_s = mk_ld(sl, (lambda s, sl=sl, ti=ti: s.dma_start(
                    out=sl[:], in_=s_in[ti * 128:(ti + 1) * 128, :])), "lds0", ("s", 0))
                ld_x = mk_ld(xr, (lambda s, xr=xr, ti=ti: s.dma_start(
                    out=xr[:], in_=x_in[ti * 128:(ti + 1) * 128, :])), f"ldx{t4}", ("x", t4))
                ld_c = mk_ld(cx, (lambda s, cx=cx, ti=ti: s.dma_start(
                    out=cx[:], in_=ctx_in[ti * 128:(ti + 1) * 128, :])), "ldc0", ("c", 0))
                ctx_evs = []
                for j in range(8):
                    d_tp = pr.op("tensor", lambda t, j=j, cx=cx: t.transpose(
                        p_tr[:, :], cx[:, j * 128:(j + 1) * 128], s_id[:]),
                        deps=[ld_c, ld_id] + ([gates["tr"]] if gates["tr"] else []) + gates["ctxT"])
                    d_ev = pr.op("vector", lambda v, j=j: v.tensor_copy(
                        s_ctxT[:, j * 128:(j + 1) * 128], p_tr[:, :]), deps=[d_tp])
                    gates["tr"] = d_ev
                    ctx_evs.append(d_ev)
                gates["ring"][("c", 0)] = [ctx_evs[-1]]
                gates["ctxT"] = []
                ev_dm = []
                for dm in range(2):
                    d_mml = None
                    for k in range(8):
                        d_mml = pr.op("tensor", lambda t, k=k, dm=dm, first=(k == 0): t.matmul(
                            p_mm[dm][:, :], s_ctxT[:, k * 128:(k + 1) * 128],
                            s_oT[k][:, dm * 512:(dm + 1) * 512], start=first, stop=(k == 7)),
                            deps=([ctx_evs[-1], ld_oT[-1]] +
                                  ([gates["bank"][dm]] if gates["bank"][dm] else [])) if k == 0 else [d_mml])
                    d_ev = pr.op("vector", lambda v, dm=dm, t4=t4, sl=sl: v.tensor_tensor(
                        s_s2[t4][:, dm * 512:(dm + 1) * 512], p_mm[dm][:, :],
                        sl[:, dm * 512:(dm + 1) * 512], op=ALU.add), deps=[d_mml, ld_s])
                    gates["bank"][dm] = d_ev
                    ev_dm.append(d_ev)
                gates["ctxT"] = [gates["bank"][0], gates["bank"][1]]
                gates["ring"][("s", 0)] = [ev_dm[0], ev_dm[1]]
                if DEBUG_L5 and tch == 0 and t4 == 0:
                    dd = pr.op("sync", lambda s, t4=t4: s.dma_start(out=o_dbg_s2[:, :], in_=s_s2[t4][:, :]),
                               deps=ev_dm + [dfq], dma=True, group=pr.dma_group("dbg", stream=True))
                    _dbg_dd[0] = dd
                    gates["ring"][("s", 0)] = [ev_dm[0], ev_dm[1], dd]
                    gates["ctxT"] = gates["ctxT"] + [dd]
                    all_stores.append(dd)
                d_xn = emit_rms_tile(pr, s_s2[t4][:, :], s_nbf[:, :], 128, sb,
                                     ev_dm + gates["nbf"], s_cst[:, 1:2])
                evs, last_ev = emit_transpose_cols(
                    pr, lambda j, _b=s_nbf: _b[:, j * 128:(j + 1) * 128], s_xnT, 128, t4 * 128,
                    p_tr, s_id, [d_xn, ld_wff] + gates["xnT"], wcolT=s_wff,
                    prev_evac=gates["tr"])
                gates["tr"] = last_ev
                gates["nbf"] = [evs[-1]]
                xn_evs.append(last_ev)
            gates["xnT"] = []
            d_xn_all = xn_evs[-1]
            if DEBUG_L5 and tch == 0:
                ddn = pr.op("sync", lambda s: s.dma_start(out=o_dbg_xn3[:, :], in_=s_xnT[0][:, :]),
                            deps=[d_xn_all, _dbg_dd[0]], dma=True, group=pr.dma_group("dbg", stream=True))
                gates["xnT"] = [ddn]
                all_stores.append(ddn)

            g2_evs = []
            for fbi, (fo, fw) in enumerate(FB):
                gw = s_gw[fbi % 2]; uw = s_uw[fbi % 2]
                ld_g = mk_ld(gw, (lambda s, gw=gw, fo=fo, fw=fw: s.dma_start(
                    out=gw[:, 0:8 * fw].rearrange("p (a c) -> p a c", a=8),
                    in_=gateT[:, fo:fo + fw].rearrange("(a p) c -> p a c", p=128))),
                    f"gw{fbi % 2}", ("gw", fbi % 2), eng="gpsimd")
                ld_u = mk_ld(uw, (lambda s, uw=uw, fo=fo, fw=fw: s.dma_start(
                    out=uw[:, 0:8 * fw].rearrange("p (a c) -> p a c", a=8),
                    in_=upT[:, fo:fo + fw].rearrange("(a p) c -> p a c", p=128))),
                    f"uw{fbi % 2}", ("uw", fbi % 2), eng="gpsimd")
                d_m0 = None
                for k in range(8):
                    d_m0 = pr.op("tensor", lambda t, k=k, fw=fw, gw=gw, first=(k == 0): t.matmul(
                        p_mm[0][0:fw, :], gw[:, k * fw:(k + 1) * fw], s_xnT[k][:, :],
                        start=first, stop=(k == 7)),
                        deps=([ld_g, d_xn_all] + ([gates["bank"][0]] if gates["bank"][0] else [])) if k == 0 else [d_m0])
                d_si = pr.op("scalar", lambda sc, fw=fw: sc.activation(
                    s_sg[0:fw, :], p_mm[0][0:fw, :], AF.Silu), deps=[d_m0] + gates["sg"])
                gates["bank"][0] = d_si
                d_m1 = None
                for k in range(8):
                    d_m1 = pr.op("tensor", lambda t, k=k, fw=fw, uw=uw, first=(k == 0): t.matmul(
                        p_mm[1][0:fw, :], uw[:, k * fw:(k + 1) * fw], s_xnT[k][:, :],
                        start=first, stop=(k == 7)),
                        deps=([ld_u, d_xn_all] + ([gates["bank"][1]] if gates["bank"][1] else [])) if k == 0 else [d_m1])
                d_g2 = pr.op("vector", lambda v, fbi=fbi, fw=fw: v.tensor_tensor(
                    pool[fbi][0:fw, :], s_sg[0:fw, :], p_mm[1][0:fw, :], op=ALU.mult),
                    deps=[d_m1, d_si] + gates["pool"])
                gates["bank"][1] = d_g2
                gates["sg"] = [d_g2]
                gates["ring"][("gw", fbi % 2)] = [d_m0]
                gates["ring"][("uw", fbi % 2)] = [d_m1]
                g2_evs.append(d_g2)
                if DEBUG_L5 and tch == 0 and fbi == 0:
                    ddg = pr.op("sync", lambda s: s.dma_start(out=o_dbg_g2[:, :], in_=pool[0][:, :]),
                                deps=[d_g2, ddn], dma=True, group=pr.dma_group("dbg", stream=True))
                    gates["pool"] = gates["pool"] + [ddg]
                    all_stores.append(ddg)
                if DEBUG_L5 and tch == 0 and fbi == 13:
                    ddg2 = pr.op("sync", lambda s: s.dma_start(out=o_dbg_g2b[:, :], in_=pool[13][:, :]),
                                 deps=[d_g2, ddg], dma=True, group=pr.dma_group("dbg", stream=True))
                    gates["pool"] = gates["pool"] + [ddg2]
                    all_stores.append(ddg2)
            gates["pool"] = []
            d_g2_all = g2_evs[-1]

            for half in range(2):
                fbs = FB[half * 11:(half + 1) * 11]
                dwl = []
                for z, (fo, fw) in enumerate(fbs):
                    dwl.append(pr.op("gpsimd", lambda g, z=z, fo=fo, fw=fw: g.dma_start(
                        out=wch[z][0:fw, :], in_=downT[fo:fo + fw, :]),
                        deps=gates["wch"], dma=True, group=pr.dma_group(f"dw{tch}_{half}")))
                gates["wch"] = []
                if DEBUG_L5 and tch == 0 and half == 0:
                    ddwc = pr.op("sync", lambda s: s.dma_start(out=o_dbg_wc[:, :], in_=wch[0][:, :]),
                                 deps=[dwl[-1], ddg2], dma=True, group=pr.dma_group("dbg", stream=True))
                    gates["wch"] = [ddwc]
                    all_stores.append(ddwc)
                wch_cons = []
                for t4 in range(4):
                    for dm in range(2):
                        d_mml = None
                        for z, (fo, fw) in enumerate(fbs):
                            d_mml = pr.op("tensor", lambda t, z=z, fw=fw, t4=t4, dm=dm, half=half, first=(z == 0), last=(z == len(fbs) - 1): t.matmul(
                                p_mm[dm][:, :], pool[half * 11 + z][0:fw, t4 * 128:(t4 + 1) * 128],
                                wch[z][0:fw, dm * 512:(dm + 1) * 512], start=first, stop=last),
                                deps=([dwl[-1], d_g2_all] + ([gates["bank"][dm]] if gates["bank"][dm] else [])) if z == 0 else [d_mml])
                        wch_cons.append(d_mml)
                        if half == 0:
                            d_ev = pr.op("vector", lambda v, t4=t4, dm=dm: v.tensor_tensor(
                                s_x2a[t4][:, dm * 512:(dm + 1) * 512], p_mm[dm][:, :],
                                s_s2[t4][:, dm * 512:(dm + 1) * 512], op=ALU.add), deps=[d_mml])
                        else:
                            xr = s_xr[t4]
                            d_t1 = pr.op("vector", lambda v, t4=t4, dm=dm: v.tensor_tensor(
                                s_tsc[:, :], p_mm[dm][:, :], s_x2a[t4][:, dm * 512:(dm + 1) * 512],
                                op=ALU.add), deps=[d_mml] + gates["stats"])
                            d_ev = pr.op("vector", lambda v, t4=t4, dm=dm, xr=xr: v.tensor_tensor(
                                s_x2a[t4][:, dm * 512:(dm + 1) * 512], s_tsc[:, :],
                                xr[:, dm * 512:(dm + 1) * 512], op=ALU.add), deps=[d_t1])
                            gates["stats"] = [d_ev]
                            gates["ring"][("x", t4)] = [d_ev]
                        gates["bank"][dm] = d_ev
                if DEBUG_L5 and tch == 0 and half == 0:
                    ddha = pr.op("sync", lambda s: s.dma_start(out=o_dbg_ha[:, :], in_=s_x2a[0][:, :]),
                                 deps=[gates["bank"][0], gates["bank"][1], ddg2], dma=True,
                                 group=pr.dma_group("dbg", stream=True))
                    gates["bank"][0] = ddha
                    all_stores.append(ddha)
                gates["wch"] = wch_cons
            gates["stats"] = []

            if DEBUG_L5 and tch == 0:
                ddx = pr.op("sync", lambda s: s.dma_start(out=o_dbg_x2a[:, :], in_=s_x2a[0][:, :]),
                            deps=[gates["bank"][0], gates["bank"][1], ddha, ddwc], dma=True, group=pr.dma_group("dbg", stream=True))
                gates["nbf"] = gates["nbf"] + [ddx]
                all_stores.append(ddx)
            xn_evs = []
            for t4 in range(4):
                d_ln = emit_ln_tile(pr, s_x2a[t4][:, :], 128, sb,
                                    [gates["bank"][0], gates["bank"][1]] + gates["nbf"], s_cst[:, 0:1])
                d_xb = pr.op("vector", lambda v, t4=t4: v.tensor_copy(s_nbf[:, :], sb["ln"][:, :]),
                             deps=[d_ln] + gates["nbf"])
                evs, last_ev = emit_transpose_cols(
                    pr, lambda j, _b=s_nbf: _b[:, j * 128:(j + 1) * 128], s_xnT, 128, t4 * 128,
                    p_tr, s_id, [d_xb] + gates["xnT"], wcolT=None, prev_evac=gates["tr"])
                gates["tr"] = last_ev
                gates["nbf"] = [evs[-1]]
                xn_evs.append(last_ev)
            gates["xnT"] = []
            d_xn_all = xn_evs[-1]

            stat_mm = [None, None]
            h_evs = []
            sq_gate = gates["sg"]
            for fb in range(32):
                fw_ = s_fw[fb % 2]
                ld_f = mk_ld(fw_, (lambda s, fw_=fw_, fb=fb: s.dma_start(
                    out=fw_[:, :].rearrange("p (a c) -> p a c", a=8),
                    in_=fcQ[:, fb * 128:(fb + 1) * 128].rearrange("(a p) c -> p a c", p=128))),
                    f"fw{fb % 2}", ("fw", fb % 2))
                d_mml = None
                for k in range(8):
                    d_mml = pr.op("tensor", lambda t, k=k, fw_=fw_, fb=fb, first=(k == 0): t.matmul(
                        p_mm[fb % 2][:, :], fw_[:, k * 128:(k + 1) * 128], s_xnT[k][:, :],
                        start=first, stop=(k == 7)),
                        deps=([ld_f, d_xn_all] + ([gates["bank"][fb % 2]] if gates["bank"][fb % 2] else [])) if k == 0 else [d_mml])
                d_h = pr.op("scalar", lambda sc, fb=fb: sc.activation(
                    pool[fb][:, :], p_mm[fb % 2][:, :], AF.Gelu), deps=[d_mml] + gates["pool"])
                gates["bank"][fb % 2] = d_h
                gates["ring"][("fw", fb % 2)] = [d_mml]
                d_s1 = pr.op("tensor", lambda t, fb=fb, first=(fb == 0): t.matmul(
                    p_st[0][:, :], s_onesb[:, 0:1], pool[fb][:, :], start=first, stop=(fb == 31)),
                    deps=[d_h, d_ob] + ([stat_mm[0]] if stat_mm[0] else []) + gates["strow"] if False else [d_h, d_ob] + gates.get("strow0", []))
                stat_mm[0] = d_s1
                d_sq = pr.op("scalar", lambda sc, fb=fb: sc.activation(
                    s_sg[:, :], pool[fb][:, :], AF.Square), deps=[d_h] + sq_gate)
                d_s2m = pr.op("tensor", lambda t, fb=fb, first=(fb == 0): t.matmul(
                    p_st[1][:, :], s_onesb[:, 0:1], s_sg[:, :], start=first, stop=(fb == 31)),
                    deps=[d_sq] + gates.get("strow1", []))
                sq_gate = [d_s2m]
                stat_mm[1] = d_s2m
                h_evs.append(d_h)
            gates["sg"] = sq_gate
            gates["pool"] = []
            d_r0 = pr.op("vector", lambda v: v.tensor_scalar(
                s_row[0:1, 0:512], p_st[0][0:1, :], 1.0 / FFH, None, op0=ALU.mult), deps=[stat_mm[0]])
            d_r1 = pr.op("vector", lambda v: v.tensor_scalar(
                s_row[0:1, 512:1024], p_st[1][0:1, :], 1.0 / FFH, None, op0=ALU.mult), deps=[stat_mm[1]])
            gates["strow0"] = [d_r0]
            gates["strow1"] = [d_r1]
            d_m2 = pr.op("vector", lambda v: v.tensor_tensor(
                s_tsc[0:1, 0:512], s_row[0:1, 0:512], s_row[0:1, 0:512], op=ALU.mult),
                deps=[d_r0] + gates["stats"])
            d_var = pr.op("vector", lambda v: v.tensor_tensor(
                s_tsc[0:1, 0:512], s_row[0:1, 512:1024], s_tsc[0:1, 0:512], op=ALU.subtract), deps=[d_m2, d_r1])
            d_l = pr.op("scalar", lambda sc: sc.activation(
                s_tsc[0:1, 0:512], s_tsc[0:1, 0:512], AF.Ln, bias=s_cst[0:1, 0:1]), deps=[d_var, d_e0])
            d_e = pr.op("scalar", lambda sc: sc.activation(
                s_tsc[0:1, 0:512], s_tsc[0:1, 0:512], AF.Exp, scale=-0.5), deps=[d_l])
            d_bm = pr.op("tensor", lambda t: t.matmul(
                p_bc[:, :], s_ones1[0:1, :], s_row[0:1, 0:512], start=True, stop=True),
                deps=[d_r0, d_o1, d_te])
            d_mb = pr.op("vector", lambda v: v.tensor_copy(s_mu[:, :], p_bc[:, :]), deps=[d_bm, qst_last])
            d_br = pr.op("tensor", lambda t: t.matmul(
                p_bc[:, :], s_ones1[0:1, :], s_tsc[0:1, 0:512], start=True, stop=True),
                deps=[d_e, d_mb])
            d_rb = pr.op("vector", lambda v: v.tensor_copy(s_rs[:, :], p_bc[:, :]), deps=[d_br])
            h2_evs = []
            prev_tsc = []
            for fb in range(32):
                d_1 = pr.op("vector", lambda v, fb=fb: v.tensor_tensor(
                    s_tsc[:, :], pool[fb][:, :], s_mu[:, :], op=ALU.subtract),
                    deps=[h_evs[fb], d_mb, d_br] + prev_tsc)
                d_2 = pr.op("vector", lambda v: v.tensor_tensor(
                    s_tsc[:, :], s_tsc[:, :], s_rs[:, :], op=ALU.mult), deps=[d_1, d_rb])
                d_3 = pr.op("vector", lambda v, fb=fb: v.tensor_scalar(
                    pool[fb][:, :], s_tsc[:, :], s_lnw[:, fb:fb + 1], s_lnb[:, fb:fb + 1],
                    op0=ALU.mult, op1=ALU.add), deps=[d_2, ld_lnw, ld_lnb])
                prev_tsc = [d_3]
                h2_evs.append(d_3)
            d_h2_all = h2_evs[-1]

            for dm in range(2):
                pjl = []
                for k in range(32):
                    pjl.append(pr.op("sync", lambda s, k=k, dm=dm: s.dma_start(
                        out=wch[k // 2][:, (k % 2) * 512:(k % 2 + 1) * 512],
                        in_=pjQ[k * 128:(k + 1) * 128, dm * 512:(dm + 1) * 512]),
                        deps=gates["wch"], dma=True, group=pr.dma_group(f"pj{tch}_{dm}")))
                gates["wch"] = []
                wch_cons = []
                for t4 in range(4):
                    d_mml = None
                    for k in range(32):
                        d_mml = pr.op("tensor", lambda t, k=k, t4=t4, dm=dm, first=(k == 0): t.matmul(
                            p_mm[t4 % 2][:, :], pool[k][:, t4 * 128:(t4 + 1) * 128],
                            wch[k // 2][:, (k % 2) * 512:(k % 2 + 1) * 512],
                            start=first, stop=(k == 31)),
                            deps=([pjl[-1], d_h2_all] + ([gates["bank"][t4 % 2]] if gates["bank"][t4 % 2] else [])) if k == 0 else [d_mml])
                        if k == 0:
                            wch_cons.append(d_mml)
                    d_ev = pr.op("vector", lambda v, t4=t4, dm=dm: v.tensor_tensor(
                        s_x2[t4][:, dm * 512:(dm + 1) * 512], p_mm[t4 % 2][:, :],
                        s_x2a[t4][:, dm * 512:(dm + 1) * 512], op=ALU.add),
                        deps=[d_mml] + ([gates["x2st"][t4]] if gates["x2st"][t4] else []))
                    gates["bank"][t4 % 2] = d_ev
                    wch_cons.append(d_mml)
                    if dm == 1:
                        ti = tch * 4 + t4
                        d_st = pr.op("sync", lambda s, ti=ti, t4=t4: s.dma_start(
                            out=o_x2[ti * 128:(ti + 1) * 128, :], in_=s_x2[t4][:, :]),
                            deps=[gates["bank"][t4 % 2], d_ev], dma=True,
                            group=pr.dma_group(f"sx{t4}", stream=True))
                        gates["x2st"][t4] = d_st
                        all_stores.append(d_st)
                gates["wch"] = wch_cons

            xn_evs = []
            for t4 in range(4):
                d_ln = emit_ln_tile(pr, s_x2[t4][:, :], 128, sb,
                                    [gates["bank"][0], gates["bank"][1]] + gates["nbf"], s_cst[:, 0:1])
                d_xb = pr.op("vector", lambda v, t4=t4: v.tensor_copy(s_nbf[:, :], sb["ln"][:, :]),
                             deps=[d_ln] + gates["nbf"])
                evs, last_ev = emit_transpose_cols(
                    pr, lambda j, _b=s_nbf: _b[:, j * 128:(j + 1) * 128], s_xnT, 128, t4 * 128,
                    p_tr, s_id, [d_xb] + gates["xnT"], wcolT=None, prev_evac=gates["tr"])
                gates["tr"] = last_ev
                gates["nbf"] = [evs[-1]]
                xn_evs.append(last_ev)
            gates["xnT"] = []
            d_xn_all = xn_evs[-1]
            win_cons = []
            for fb in range(32):
                wi = s_wi[fb % 2]
                ld_w = mk_ld(wi, (lambda s, wi=wi, fb=fb: s.dma_start(
                    out=wi[:, :].rearrange("p (a c) -> p a c", a=8),
                    in_=W_in1[:, fb * 128:(fb + 1) * 128].rearrange("(a p) c -> p a c", p=128))),
                    f"wi{fb % 2}", ("wi", fb % 2), eng="gpsimd")
                d_mml = None
                for k in range(8):
                    d_mml = pr.op("tensor", lambda t, k=k, wi=wi, fb=fb, first=(k == 0): t.matmul(
                        p_mm[fb % 2][:, :], wi[:, k * 128:(k + 1) * 128], s_xnT[k][:, :],
                        start=first, stop=(k == 7)),
                        deps=([ld_w, d_xn_all] + ([gates["bank"][fb % 2]] if gates["bank"][fb % 2] else [])) if k == 0 else [d_mml])
                gates["ring"][("wi", fb % 2)] = [d_mml]
                win_cons.append(d_mml)
                stg = s_stg[fb % 2]
                d_ev = pr.op("scalar", lambda sc, stg=stg, fb=fb: sc.activation(
                    stg[:, :], p_mm[fb % 2][:, :], AF.Identity),
                    deps=[d_mml] + ([gates["stg"][fb % 2]] if gates["stg"][fb % 2] else []))
                gates["bank"][fb % 2] = d_ev
                dst = o_raw if fb < 16 else o_g2
                ro = (fb if fb < 16 else fb - 16) * 128
                d_st = pr.op("sync", lambda s, dst=dst, ro=ro, tc0=tc0, stg=stg: s.dma_start(
                    out=dst[ro:ro + 128, tc0:tc0 + 512], in_=stg[:, :]), deps=[d_ev],
                    dma=True, group=pr.dma_group(f"so{fb % 2}", stream=True))
                gates["stg"][fb % 2] = d_st
                all_stores.append(d_st)
            gates["xnT"] = win_cons
            gates["pool"] = []

        dbg_deps = []
        if DEBUG_L5:
            dbg_deps = [dth, dfq]
        pr.op("vector", lambda v: v.engine_nop(), deps=all_stores[-4:] + [gates["x2st"][t] for t in range(4)] + dbg_deps)

    nc = pr.build(body)
    return nc


def _l5_inmaps(inputs, s_full, ctx_full):
    common = {
        "oT": npT(inputs["L0_attn_o_w"]),
        "w_ffnT": npT(np.asarray(inputs["L0_attn_norm_ffn_w"]).reshape(8, 128)),
        "gateT": npT(inputs["L0_attn_gate_w"]),
        "upT": npT(inputs["L0_attn_up_w"]),
        "downT": npT(inputs["L0_attn_down_w"]),
        "fcT": npT(inputs["L0_ffn_W_fc"]),
        "ln_wT": npT(np.asarray(inputs["L0_ffn_ln_w"]).reshape(32, 128)),
        "ln_bT": npT(np.asarray(inputs["L0_ffn_ln_b"]).reshape(32, 128)),
        "projT": npT(inputs["L0_ffn_W_proj"]),
        "W_in1": np.ascontiguousarray(np.asarray(inputs["L1_attn_W_in"])),
        "ident": IDENT,
    }
    x = np.asarray(inputs["L0_attn_x"])
    maps = []
    for c in range(8):
        b, h = c // 2, c % 2
        t0 = h * TC
        maps.append({
            "s_in": np.ascontiguousarray(s_full[b, t0:t0 + TC]),
            "ctx_in": np.ascontiguousarray(ctx_full[b, t0:t0 + TC]),
            "x_in": np.ascontiguousarray(x[b, t0:t0 + TC]),
            **common,
        })
    return maps


def run_l5(inputs, s_full, ctx_full):
    nc = build_l5()
    res = _run(nc, _l5_inmaps(inputs, s_full, ctx_full))
    x2 = np.stack([np.concatenate([res[2 * b]["o_x2"], res[2 * b + 1]["o_x2"]], axis=0) for b in range(B)])
    raw = np.stack([np.concatenate([res[2 * b]["o_raw"], res[2 * b + 1]["o_raw"]], axis=1) for b in range(B)])
    g2 = np.stack([np.concatenate([res[2 * b]["o_g2"], res[2 * b + 1]["o_g2"]], axis=1) for b in range(B)])
    return x2, raw, g2





def build_l6():
    pr = Prog("l6")
    TP = TC + 3
    rawp = pr.dram_in("rawp", [IN2, TP], BF16)
    conv_ck = pr.dram_in("conv_ck", [IN2, 4])
    conv_bk = pr.dram_in("conv_bk", [128, 16])
    W_x = pr.dram_in("W_x", [IN2, 96])
    W_dt = pr.dram_in("W_dt", [DTR, IN2])
    b_dt = pr.dram_in("b_dt", [128, 16])
    o_h = pr.dram_out("o_h", [IN2, TC], BF16)
    o_d = pr.dram_out("o_d", [IN2, TC])
    o_bc = pr.dram_out("o_bc", [32, TC])

    def body(pr):
        s_raw = [pr.sbuf(f"s_raw{i}", [128, TP], BF16) for i in range(2)]
        s_cw = pr.sbuf("s_cw", [128, 4 * 16], F32)
        s_cb = pr.sbuf("s_cb", [128, 16], F32)
        s_conv = pr.sbuf("s_conv", [128, TC], F32)
        s_h = [pr.sbuf(f"s_h{j}", [128, TC], BF16) for j in range(16)]
        s_wx = pr.sbuf("s_wx", [128, 96 * 16], BF16)
        s_wdt = pr.sbuf("s_wdt", [64, IN2], BF16)
        s_ts = pr.sbuf("s_ts", [64, TC], BF16)
        s_bc = pr.sbuf("s_bc", [32, TC], F32)
        s_bdt = pr.sbuf("s_bdt", [128, 16], F32)
        s_sp1 = pr.sbuf("s_sp1", [128, 512], F32)
        s_sp2 = pr.sbuf("s_sp2", [128, 512], F32)
        s_dl = pr.sbuf("s_dl", [128, 512], F32)
        p_sp = pr.psum("p_sp", [128, 512])
        p_dt = pr.psum("p_dt", [128, 512])

        CON = pr.dma_group("consts")
        ld_cw = pr.op("sync", lambda s: s.dma_start(
            out=s_cw[:, :].rearrange("p (a k) -> p a k", a=16),
            in_=conv_ck[:, :].rearrange("(a p) k -> p a k", p=128)), dma=True, group=CON)
        ld_cb = pr.op("sync", lambda s: s.dma_start(out=s_cb[:], in_=conv_bk[:]), dma=True, group=CON)
        ld_bdt = pr.op("sync", lambda s: s.dma_start(out=s_bdt[:], in_=b_dt[:]), dma=True, group=CON)
        GCON = pr.dma_group("gconsts")
        ld_wx = pr.op("gpsimd", lambda g: g.dma_start(
            out=s_wx[:, :].rearrange("p (a c) -> p a c", a=16),
            in_=W_x[:, :].rearrange("(a p) c -> p a c", p=128)), dma=True, group=GCON)
        ld_wdt = pr.op("gpsimd", lambda g: g.dma_start(out=s_wdt[:], in_=W_dt[:]), dma=True, group=GCON)

        h_done = []
        ring = {}
        prev_si = None
        for pt in range(16):
            rb = s_raw[pt % 2]
            ld_r = pr.op("sync", lambda s, rb=rb, pt=pt: s.dma_start(
                out=rb[:], in_=rawp[pt * 128:(pt + 1) * 128, :]),
                deps=ring.get(pt % 2, []), dma=True, group=pr.dma_group(f"r{pt % 2}", stream=True))
            d_c = pr.op("vector", lambda v, rb=rb, pt=pt: v.tensor_scalar(
                s_conv[:, :], rb[:, 0:TC], s_cw[:, pt * 4:pt * 4 + 1], None, op0=ALU.mult),
                deps=[ld_r, ld_cw] + ([prev_si] if prev_si else []))
            for k in range(1, 4):
                d_c = pr.op("vector", lambda v, rb=rb, pt=pt, k=k: v.scalar_tensor_tensor(
                    s_conv[:, :], rb[:, k:k + TC], s_cw[:, pt * 4 + k:pt * 4 + k + 1],
                    s_conv[:, :], op0=ALU.mult, op1=ALU.add), deps=[d_c])
            d_si = pr.op("scalar", lambda sc, pt=pt: sc.activation(
                s_h[pt][:, :], s_conv[:, :], AF.Silu, bias=s_cb[:, pt:pt + 1]),
                deps=[d_c, ld_cb])
            prev_si = d_si
            ring[pt % 2] = [d_c]
            d_st = pr.op("sync", lambda s, pt=pt: s.dma_start(
                out=o_h[pt * 128:(pt + 1) * 128, :], in_=s_h[pt][:, :]), deps=[d_si],
                dma=True, group=pr.dma_group("sth"))
            h_done.append((d_si, d_st))
        d_h_all = h_done[-1][0]

        if L6_PHASES < 2:
            pr.op("vector", lambda v: v.engine_nop(), deps=[h_done[-1][1], h_done[-1][0]])
            return
        sp_ev = []
        prev_mm = None
        for ci in range(4):
            d_mml = None
            for k in range(16):
                d_mml = pr.op("tensor", lambda t, k=k, ci=ci, first=(k == 0): t.matmul(
                    p_sp[0:96, :], s_wx[:, k * 96:(k + 1) * 96], s_h[k][:, ci * 512:(ci + 1) * 512],
                    start=first, stop=(k == 15)),
                    deps=([ld_wx, d_h_all] + sp_ev[-2:]) if k == 0 else [d_mml])
            d_e1 = pr.op("scalar", lambda sc, ci=ci: sc.activation(
                s_ts[:, ci * 512:(ci + 1) * 512], p_sp[0:64, :], AF.Identity), deps=[d_mml])
            d_e2 = pr.op("scalar", lambda sc, ci=ci: sc.activation(
                s_bc[:, ci * 512:(ci + 1) * 512], p_sp[64:96, :], AF.Identity), deps=[d_mml])
            sp_ev.append(d_e2)
            sp_ev.append(d_e1)
        d_bcst = pr.op("sync", lambda s: s.dma_start(out=o_bc[:, :], in_=s_bc[:, :]),
                       deps=sp_ev, dma=True, group=pr.dma_group("stbc"))

        if L6_PHASES < 3:
            pr.op("vector", lambda v: v.engine_nop(), deps=[d_bcst, h_done[-1][1]])
            return
        d_sts = []
        prev_cycle = []
        for pt in range(16):
            for ci in range(4):
                d_mm = pr.op("tensor", lambda t, pt=pt, ci=ci: t.matmul(
                    p_dt[:, :], s_wdt[:, pt * 128:(pt + 1) * 128],
                    s_ts[:, ci * 512:(ci + 1) * 512], start=True, stop=True),
                    deps=[ld_wdt, sp_ev[-1]] + prev_cycle)
                d_a1 = pr.op("scalar", lambda sc, pt=pt: sc.activation(
                    s_sp1[:], p_dt[:, :], AF.Abs, bias=s_bdt[:, pt:pt + 1]),
                    deps=[d_mm, ld_bdt] + prev_cycle)
                d_a2 = pr.op("scalar", lambda sc, pt=pt: sc.activation(
                    s_sp2[:], p_dt[:, :], AF.Relu, bias=s_bdt[:, pt:pt + 1]), deps=[d_mm])
                d_a3 = pr.op("scalar", lambda sc: sc.activation(
                    s_sp1[:], s_sp1[:], AF.Exp, scale=-1.0), deps=[d_a1])
                d_a4 = pr.op("scalar", lambda sc: sc.activation(
                    s_sp1[:], s_sp1[:], AF.Ln, bias=1.0), deps=[d_a3])
                d_d = pr.op("vector", lambda v: v.tensor_tensor(
                    s_dl[:], s_sp1[:], s_sp2[:], op=ALU.add), deps=[d_a4, d_a2] + prev_cycle)
                d_st = pr.op("sync", lambda s, pt=pt, ci=ci: s.dma_start(
                    out=o_d[pt * 128:(pt + 1) * 128, ci * 512:(ci + 1) * 512], in_=s_dl[:]),
                    deps=[d_d], dma=True, group=pr.dma_group("std", stream=True))
                prev_cycle = [d_st, d_d, d_a2]
                d_sts.append(d_st)

        pr.op("vector", lambda v: v.engine_nop(),
              deps=[d_sts[-1], d_bcst, h_done[-1][1]])

    nc = pr.build(body)
    return nc


def _l6_inmaps(inputs, rawT_full):
    common = {
        "conv_ck": npT(np.asarray(inputs["L1_attn_conv_w"])[:, 0, :]),
        "conv_bk": npT(np.asarray(inputs["L1_attn_conv_b"]).reshape(16, 128)),
        "W_x": np.ascontiguousarray(np.asarray(inputs["L1_attn_W_x"])),
        "W_dt": np.ascontiguousarray(np.asarray(inputs["L1_attn_W_dt"])),
        "b_dt": npT(np.asarray(inputs["L1_attn_b_dt"]).reshape(16, 128)),
    }
    maps = []
    for c in range(8):
        b, h = c // 2, c % 2
        t0 = h * TC
        if h == 0:
            rp = np.concatenate([np.zeros((IN2, 3), BF), rawT_full[b][:, 0:TC]], axis=1)
        else:
            rp = rawT_full[b][:, t0 - 3:t0 + TC]
        maps.append({"rawp": np.ascontiguousarray(rp), **common})
    return maps


def run_l6(inputs, rawT_full):
    nc = build_l6()
    res = _run(nc, _l6_inmaps(inputs, rawT_full))
    h = np.stack([np.concatenate([res[2 * b]["o_h"], res[2 * b + 1]["o_h"]], axis=1) for b in range(B)])
    d = np.stack([np.concatenate([res[2 * b]["o_d"], res[2 * b + 1]["o_d"]], axis=1) for b in range(B)])
    bc = np.stack([np.concatenate([res[2 * b]["o_bc"], res[2 * b + 1]["o_bc"]], axis=1) for b in range(B)])
    return h, d, bc


def _l7_inmaps(inputs, hT, dT, bc):
    A_log = np.asarray(inputs["L1_attn_A_log"])
    Dd = np.asarray(inputs["L1_attn_D"])
    maps = []
    for c in range(8):
        b, dh = c // 2, c % 2
        r0 = dh * 1024
        maps.append({
            "xconv": np.ascontiguousarray(hT[b, r0:r0 + 1024]),
            "ssp_bc": np.ascontiguousarray(bc[b]),
            "deltaT": np.ascontiguousarray(dT[b, r0:r0 + 1024]),
            "A_log": np.ascontiguousarray(A_log[r0:r0 + 1024]),
            "D_skipT": npT(Dd[r0:r0 + 1024].reshape(8, 128)),
        })
    return maps


def run_l7(inputs, hT, dT, bc):
    nc = build_l2(delta_input=True)
    res = _run(nc, _l7_inmaps(inputs, hT, dT, bc))
    y = np.stack([np.concatenate([res[2 * b]["o_y"], res[2 * b + 1]["o_y"]], axis=0) for b in range(B)])
    return y





def build_l8():
    pr = Prog("l8")
    x2_in = pr.dram_in("x2_in", [TC, D])
    y2T = pr.dram_in("y2T", [IN2, TC])
    g2T = pr.dram_in("g2T", [IN2, TC], BF16)
    W_out = pr.dram_in("W_out", [IN2, D])
    router_w = pr.dram_in("router_w", [D, E])
    router_b = pr.dram_in("router_b", [1, E])
    w1 = pr.dram_in("w1", [E, D, FF1])
    w2 = pr.dram_in("w2", [E, F2, D])
    ident = pr.dram_in("ident", [128, 128], BF16)
    o_x4 = pr.dram_out("o_x4", [TC, D])

    def body(pr):
        pool = [pr.sbuf(f"pool{j}", [128, 1024], BF16) for j in range(32)]
        wpool = [pr.sbuf(f"wp{j}", [128, 1024], BF16) for j in range(16)]
        s_tok = [pr.sbuf(f"s_tok{j}", [128, D], BF16) for j in range(16)]
        s_xne = [pr.sbuf(f"s_xne{j}", [128, TC], BF16) for j in range(8)]
        s_y2 = pr.sbuf("s_y2", [128, TC], BF16)
        s_g2 = pr.sbuf("s_g2", [128, TC], BF16)
        s_sg2 = pr.sbuf("s_sg2", [128, TC], BF16)
        s_x2 = [pr.sbuf(f"s_x2{i}", [128, D], F32) for i in range(1)]
        s_x3 = pr.sbuf("s_x3", [128, D], F32)
        sb = {"sq": pr.sbuf("s_sq", [128, D], F32), "st": pr.sbuf("s_st", [128, 8], F32),
              "ln": pr.sbuf("s_ln", [128, D], F32)}
        s_nbf2 = pr.sbuf("s_nbf2", [128, D], BF16)
        s_trT = pr.sbuf("s_trT", [128, 8 * 128], BF16)
        s_cst = pr.sbuf("s_cst", [128, 2], F32)
        s_id = pr.sbuf("s_id", [128, 128], BF16)
        s_rw = pr.sbuf("s_rw", [128, 8 * E], BF16)
        s_rb1 = pr.sbuf("s_rb1", [1, E], F32)
        s_rb = pr.sbuf("s_rb", [128, E], F32)
        s_ones1 = pr.sbuf("s_ones1", [1, 128], F32)
        s_lg = pr.sbuf("s_lg", [128, E], F32)
        s_p = pr.sbuf("s_p", [128, E], F32)
        s_eq = pr.sbuf("s_eq", [128, E], F32)
        s_fnd = pr.sbuf("s_fnd", [128, 2], F32)
        s_msk = pr.sbuf("s_msk", [128, 16 * E], F32)
        s_maxp = pr.sbuf("s_maxp", [128, 16], F32)
        s_w1a = [pr.sbuf(f"s_w1a{i}", [128, 8 * 128], BF16) for i in range(2)]
        s_w1g = [pr.sbuf(f"s_w1g{i}", [128, 8 * 128], BF16) for i in range(2)]
        s_sga = pr.sbuf("s_sga", [128, 512], BF16)
        s_stg = [pr.sbuf(f"s_stg{i}", [128, 512], F32) for i in range(2)]
        p_h = [pr.psum(f"p_h{i}", [128, 512]) for i in range(4)]
        p_trs = [pr.psum("p_tr", [128, 128], BF16), pr.psum("p_trb", [128, 128], BF16)]
        p_r = pr.psum("p_r", [128, E])

        d_e0 = pr.op("vector", lambda v: v.memset(s_cst[:, 0:1], LN_EPS))
        d_o1 = pr.op("vector", lambda v: v.memset(s_ones1[:], 1.0))
        CON = pr.dma_group("consts")
        ld_id = pr.op("sync", lambda s: s.dma_start(out=s_id[:], in_=ident[:]), dma=True, group=CON)
        ld_rb = pr.op("sync", lambda s: s.dma_start(out=s_rb1[:], in_=router_b[:]), dma=True, group=CON)
        GCON = pr.dma_group("gconsts")
        ld_wo = [pr.op("gpsimd", lambda g, j=j: g.dma_start(
            out=wpool[j][:], in_=W_out[j * 128:(j + 1) * 128, :]), dma=True, group=GCON)
            for j in range(16)]
        ld_rw = pr.op("gpsimd", lambda g: g.dma_start(
            out=s_rw[:, :].rearrange("p (a c) -> p a c", a=8),
            in_=router_w[:, :].rearrange("(a p) c -> p a c", p=128)), dma=True, group=GCON)
        d_rbb = pr.op("tensor", lambda t: t.matmul(
            p_r[:, :], s_ones1[0:1, :], s_rb1[0:1, :], start=True, stop=True),
            deps=[ld_rb, d_o1])
        d_rbe = pr.op("vector", lambda v: v.tensor_copy(s_rb[:, :], p_r[:, :]), deps=[d_rbb])

        gates = {"ring": {}, "bank": [None] * 4, "tr": [None, None], "pool": [], "nbf": [],
                 "nbf2": [], "stg": [None] * 4, "wpool": [ld_wo[-1]], "xne": [],
                 "x3": None, "lg": [], "trT": []}
        x3_store = {}

        yg_done = []
        for pt in range(16):
            ld_y = pr.op("gpsimd", lambda g, pt=pt: g.dma_start(
                out=s_y2[:], in_=y2T[pt * 128:(pt + 1) * 128, :]),
                deps=gates["ring"].get("y", []), dma=True, group=pr.dma_group("gy", stream=True))
            ld_g = pr.op("sync", lambda s, pt=pt: s.dma_start(
                out=s_g2[:], in_=g2T[pt * 128:(pt + 1) * 128, :]),
                deps=gates["ring"].get("g", []), dma=True, group=pr.dma_group("gg", stream=True))
            d_si = pr.op("scalar", lambda sc: sc.activation(s_sg2[:], s_g2[:], AF.Silu),
                         deps=[ld_g] + gates["ring"].get("sg", []))
            d_m0 = pr.op("vector", lambda v, pt=pt: v.tensor_tensor(
                pool[2 * pt][:, :], s_y2[:, 0:1024], s_sg2[:, 0:1024], op=ALU.mult),
                deps=[d_si, ld_y] + gates["pool"])
            d_m1 = pr.op("vector", lambda v, pt=pt: v.tensor_tensor(
                pool[2 * pt + 1][:, :], s_y2[:, 1024:2048], s_sg2[:, 1024:2048], op=ALU.mult),
                deps=[d_si, ld_y])
            gates["ring"]["y"] = [d_m0, d_m1]
            gates["ring"]["g"] = [d_si]
            gates["ring"]["sg"] = [d_m0, d_m1]
            yg_done.append(d_m1)
        gates["pool"] = []
        d_yg_all = yg_done[-1]

        for ti in range(16):
            xb = s_x2[0]
            ld_x = pr.op("sync", lambda s, xb=xb, ti=ti: s.dma_start(
                out=xb[:], in_=x2_in[ti * 128:(ti + 1) * 128, :]),
                deps=gates["ring"].get(("x", 0), []), dma=True,
                group=pr.dma_group("gx0", stream=True))
            ev_dm = []
            for dm in range(2):
                d_mml = None
                for k in range(16):
                    po = pool[2 * k + (1 if ti >= 8 else 0)]
                    co = (ti - 8 if ti >= 8 else ti) * 128
                    d_mml = pr.op("tensor", lambda t, k=k, dm=dm, po=po, co=co, first=(k == 0): t.matmul(
                        p_h[dm][:, :], po[:, co:co + 128], wpool[k][:, dm * 512:(dm + 1) * 512],
                        start=first, stop=(k == 15)),
                        deps=([d_yg_all, ld_wo[-1]] + ([gates["bank"][dm]] if gates["bank"][dm] else [])) if k == 0 else [d_mml])
                d_ev = pr.op("vector", lambda v, dm=dm, xb=xb: v.tensor_tensor(
                    s_x3[:, dm * 512:(dm + 1) * 512], p_h[dm][:, :],
                    xb[:, dm * 512:(dm + 1) * 512], op=ALU.add),
                    deps=[d_mml, ld_x] + ([gates["x3"]] if (dm == 0 and gates["x3"]) else []))
                gates["bank"][dm] = d_ev
                ev_dm.append(d_ev)
            gates["ring"][("x", 0)] = [ev_dm[1]]
            d_st = pr.op("sync", lambda s, ti=ti: s.dma_start(
                out=o_x4[ti * 128:(ti + 1) * 128, :], in_=s_x3[:, :]), deps=ev_dm, dma=True,
                group=pr.dma_group(f"sx{ti % 2}", stream=True))
            x3_store[ti] = d_st
            gates["x3"] = d_st
            d_ln = emit_ln_tile(pr, s_x3[:, :], 128, sb, ev_dm + gates["nbf"], s_cst[:, 0:1])
            d_xn = pr.op("vector", lambda v, ti=ti: v.tensor_copy(s_tok[ti][:, :], sb["ln"][:, :]),
                         deps=[d_ln] + gates["pool"] if False else [d_ln])
            gates["nbf"] = [d_xn]
            tr_evs = []
            for j in range(8):
                ptr = p_trs[j % 2]
                d_tp = pr.op("tensor", lambda t, j=j, ti=ti, ptr=ptr: t.transpose(
                    ptr[:, :], s_tok[ti][:, j * 128:(j + 1) * 128], s_id[:]),
                    deps=[d_xn, ld_id] + ([gates["tr"][j % 2]] if gates["tr"][j % 2] else []) + gates["trT"])
                d_te = pr.op("vector", lambda v, j=j, ptr=ptr: v.tensor_copy(
                    s_trT[:, j * 128:(j + 1) * 128], ptr[:, :]), deps=[d_tp])
                gates["tr"][j % 2] = d_te
                tr_evs.append(d_te)
            gates["trT"] = []
            d_rm = None
            for k in range(8):
                d_rm = pr.op("tensor", lambda t, k=k, first=(k == 0): t.matmul(
                    p_r[:, :], s_trT[:, k * 128:(k + 1) * 128], s_rw[:, k * E:(k + 1) * E],
                    start=first, stop=(k == 7)),
                    deps=([tr_evs[-1], ld_rw, d_rbe] + gates["lg"]) if k == 0 else [d_rm])
            gates["trT"] = [d_rm]
            d_lg = pr.op("vector", lambda v: v.tensor_tensor(
                s_lg[:, :], p_r[:, :], s_rb[:, :], op=ALU.add), deps=[d_rm] + gates["lg"])
            d_mx = pr.op("vector", lambda v: v.tensor_reduce(
                sb["st"][:, 0:1], s_lg[:, :], axis=AX.X, op=ALU.max), deps=[d_lg])
            d_nm = pr.op("vector", lambda v: v.tensor_scalar(
                sb["st"][:, 1:2], sb["st"][:, 0:1], -1.0, None, op0=ALU.mult), deps=[d_mx])
            d_ex = pr.op("scalar", lambda sc: sc.activation(
                s_p[:, :], s_lg[:, :], AF.Exp, bias=sb["st"][:, 1:2]), deps=[d_nm] + gates["lg"])
            d_sm = pr.op("vector", lambda v: v.tensor_reduce(
                sb["st"][:, 2:3], s_p[:, :], axis=AX.X, op=ALU.add), deps=[d_ex])
            d_pm = pr.op("vector", lambda v: v.tensor_reduce(
                sb["st"][:, 3:4], s_p[:, :], axis=AX.X, op=ALU.max), deps=[d_ex])
            d_rc = pr.op("vector", lambda v: v.reciprocal(sb["st"][:, 4:5], sb["st"][:, 2:3]),
                         deps=[d_sm])
            d_mp = pr.op("vector", lambda v, ti=ti: v.tensor_tensor(
                s_maxp[:, ti:ti + 1], sb["st"][:, 3:4], sb["st"][:, 4:5], op=ALU.mult),
                deps=[d_pm, d_rc])
            d_eq = pr.op("vector", lambda v: v.tensor_scalar(
                s_eq[:, :], s_p[:, :], sb["st"][:, 3:4], None, op0=ALU.is_ge),
                deps=[d_ex, d_pm])
            d_f = pr.op("vector", lambda v: v.memset(s_fnd[:, 0:1], 0.0), deps=gates["lg"])
            gates["lg"] = [d_eq, d_ex]
            prevm = [d_f, d_eq]
            for e in range(E):
                d_t = pr.op("vector", lambda v: v.tensor_scalar(
                    s_fnd[:, 1:2], s_fnd[:, 0:1], -1.0, 1.0, op0=ALU.mult, op1=ALU.add),
                    deps=prevm)
                d_m = pr.op("vector", lambda v, ti=ti, e=e: v.tensor_tensor(
                    s_msk[:, ti * E + e:ti * E + e + 1], s_eq[:, e:e + 1], s_fnd[:, 1:2],
                    op=ALU.mult), deps=[d_t])
                d_u = pr.op("vector", lambda v, ti=ti, e=e: v.tensor_tensor(
                    s_fnd[:, 0:1], s_fnd[:, 0:1], s_msk[:, ti * E + e:ti * E + e + 1],
                    op=ALU.add), deps=[d_m])
                prevm = [d_u]
            gates["lg"] = gates["lg"] + prevm

        acc_last = {}
        for e in range(E):
            xne_evs = []
            for ti in range(16):
                d_msk = pr.op("vector", lambda v, ti=ti, e=e: v.tensor_scalar(
                    s_nbf2[:, :], s_tok[ti][:, :], s_msk[:, ti * E + e:ti * E + e + 1], None,
                    op0=ALU.mult), deps=gates["nbf2"] + ([prevm[-1]] if e == 0 else []))
                tr_last = None
                for j in range(8):
                    ptr = p_trs[j % 2]
                    d_tp = pr.op("tensor", lambda t, j=j, ptr=ptr: t.transpose(
                        ptr[:, :], s_nbf2[:, j * 128:(j + 1) * 128], s_id[:]),
                        deps=[d_msk, ld_id] + ([gates["tr"][j % 2]] if gates["tr"][j % 2] else []) + gates["xne"])
                    d_te = pr.op("vector", lambda v, j=j, ti=ti, e=e, ptr=ptr: v.tensor_copy(
                        s_xne[j][:, ti * 128:(ti + 1) * 128], ptr[:, :]), deps=[d_tp])
                    gates["tr"][j % 2] = d_te
                    tr_last = d_te
                gates["nbf2"] = [d_tp]
                xne_evs.append(tr_last)
            gates["xne"] = []
            d_xne_all = xne_evs[-1]
            for tg in range(2):
                tg0 = tg * 1024
                w2l0 = []
                for k in range(32):
                    w2l0.append(pr.op("gpsimd", lambda g, e=e, k=k: g.dma_start(
                        out=wpool[k // 2][:, (k % 2) * 512:(k % 2 + 1) * 512],
                        in_=w2[e, k * 128:(k + 1) * 128, 0:512]),
                        deps=gates["wpool"], dma=True,
                        group=pr.dma_group(f"w2_{e}_{tg}_0")))
                gates["wpool"] = []
                for jp in range(32):
                    wa = s_w1a[jp % 2]; wg = s_w1g[jp % 2]
                    ld_a = pr.op("gpsimd", lambda g, e=e, jp=jp, wa=wa: g.dma_start(
                        out=wa[:, :].rearrange("p (a c) -> p a c", a=8),
                        in_=w1[e, :, jp * 128:(jp + 1) * 128].rearrange("(a p) c -> p a c", p=128)),
                        deps=gates["ring"].get(("wa", jp % 2), []), dma=True,
                        group=pr.dma_group(f"wa{jp % 2}", stream=True))
                    ld_gw = pr.op("gpsimd", lambda g, e=e, jp=jp, wg=wg: g.dma_start(
                        out=wg[:, :].rearrange("p (a c) -> p a c", a=8),
                        in_=w1[e, :, F2 + jp * 128:F2 + (jp + 1) * 128].rearrange("(a p) c -> p a c", p=128)),
                        deps=gates["ring"].get(("wg", jp % 2), []), dma=True,
                        group=pr.dma_group(f"wg{jp % 2}", stream=True))
                    mm_a = []
                    mm_g = []
                    for half in range(2):
                        d_a = None
                        for k in range(8):
                            d_a = pr.op("tensor", lambda t, k=k, half=half, wa=wa, tg0=tg0, first=(k == 0): t.matmul(
                                p_h[half][:, :], wa[:, k * 128:(k + 1) * 128],
                                s_xne[k][:, tg0 + half * 512:tg0 + (half + 1) * 512],
                                start=first, stop=(k == 7)),
                                deps=([ld_a, d_xne_all] + ([gates["bank"][half]] if gates["bank"][half] else [])) if k == 0 else [d_a])
                        mm_a.append(d_a)
                        d_g = None
                        for k in range(8):
                            d_g = pr.op("tensor", lambda t, k=k, half=half, wg=wg, tg0=tg0, first=(k == 0): t.matmul(
                                p_h[2 + half][:, :], wg[:, k * 128:(k + 1) * 128],
                                s_xne[k][:, tg0 + half * 512:tg0 + (half + 1) * 512],
                                start=first, stop=(k == 7)),
                                deps=([ld_gw, d_xne_all] + ([gates["bank"][2 + half]] if gates["bank"][2 + half] else [])) if k == 0 else [d_g])
                        mm_g.append(d_g)
                    gates["ring"][("wa", jp % 2)] = mm_a
                    gates["ring"][("wg", jp % 2)] = mm_g
                    for half in range(2):
                        d_si = pr.op("scalar", lambda sc, half=half: sc.activation(
                            s_sga[:, :], p_h[half][:, :], AF.Silu),
                            deps=[mm_a[half]] + gates["ring"].get("sga", []))
                        gates["bank"][half] = d_si
                        d_yi = pr.op("vector", lambda v, jp=jp, half=half: v.tensor_tensor(
                            pool[jp][:, half * 512:(half + 1) * 512], s_sga[:, :],
                            p_h[2 + half][:, :], op=ALU.mult),
                            deps=[d_si, mm_g[half]] + (gates["pool"] if jp == 0 and half == 0 else []))
                        gates["bank"][2 + half] = d_yi
                        gates["ring"]["sga"] = [d_yi]
                        if jp == 0 and half == 0:
                            gates["pool"] = []
                w2_cons = []
                for dm in range(2):
                    if dm == 0:
                        w2l = w2l0
                    else:
                        w2l = []
                        for k in range(32):
                            w2l.append(pr.op("gpsimd", lambda g, e=e, k=k, dm=dm: g.dma_start(
                                out=wpool[k // 2][:, (k % 2) * 512:(k % 2 + 1) * 512],
                                in_=w2[e, k * 128:(k + 1) * 128, dm * 512:(dm + 1) * 512]),
                                deps=gates["wpool"], dma=True,
                                group=pr.dma_group(f"w2_{e}_{tg}_{dm}")))
                        gates["wpool"] = []
                    for tt in range(8):
                        ti = tg * 8 + tt
                        bank = tt % 4
                        d_mml = None
                        for k in range(32):
                            d_mml = pr.op("tensor", lambda t, k=k, tt=tt, dm=dm, bank=bank, first=(k == 0): t.matmul(
                                p_h[bank][:, :], pool[k][:, tt * 128:(tt + 1) * 128],
                                wpool[k // 2][:, (k % 2) * 512:(k % 2 + 1) * 512],
                                start=first, stop=(k == 31)),
                                deps=([w2l[-1], gates["bank"][2], gates["bank"][3]] + ([gates["bank"][bank]] if gates["bank"][bank] else [])) if k == 0 else [d_mml])
                        if k == 31:
                            w2_cons.append(d_mml)
                        sgi = dm
                        stg = s_stg[sgi]
                        d_sc = pr.op("scalar", lambda sc, stg=stg, ti=ti, bank=bank: sc.activation(
                            stg[:, :], p_h[bank][:, :], AF.Identity, scale=s_maxp[:, ti:ti + 1]),
                            deps=[d_mml] + ([gates["stg"][sgi]] if gates["stg"][sgi] else []))
                        gates["bank"][bank] = d_sc
                        d_ac = pr.op("gpsimd", lambda g, stg=stg, ti=ti, dm=dm: g.dma_start(
                            out=o_x4[ti * 128:(ti + 1) * 128, dm * 512:(dm + 1) * 512],
                            in_=stg[:, :], accum_op=ALU.add),
                            deps=[d_sc, x3_store[ti]] + ([acc_last.get((ti, dm))] if acc_last.get((ti, dm)) else []),
                            dma=True, group=pr.dma_group(f"ac{sgi}", stream=True))
                        gates["stg"][sgi] = d_ac
                        acc_last[(ti, dm)] = d_ac
                    gates["wpool"] = w2_cons

        pr.op("vector", lambda v: v.engine_nop(),
              deps=[acc_last[(ti, dm)] for ti in range(16) for dm in range(2)])

    nc = pr.build(body)
    return nc


def _l8_inmaps(inputs, x2_full, y2T_full, g2T_full):
    common = {
        "W_out": np.ascontiguousarray(np.asarray(inputs["L1_attn_W_out"])),
        "router_w": np.ascontiguousarray(np.asarray(inputs["L1_ffn_router_w"])),
        "router_b": np.asarray(inputs["L1_ffn_router_b"]).reshape(1, E),
        "w1": np.ascontiguousarray(np.asarray(inputs["L1_ffn_w1"])),
        "w2": np.ascontiguousarray(np.asarray(inputs["L1_ffn_w2"])),
        "ident": IDENT,
    }
    maps = []
    for c in range(8):
        b, h = c // 2, c % 2
        t0 = h * TC
        maps.append({
            "x2_in": np.ascontiguousarray(x2_full[b, t0:t0 + TC]),
            "y2T": np.ascontiguousarray(y2T_full[b][:, t0:t0 + TC]),
            "g2T": np.ascontiguousarray(g2T_full[b][:, t0:t0 + TC]),
            **common,
        })
    return maps


def run_l8(inputs, x2_full, y2T_full, g2T_full):
    nc = build_l8()
    res = _run(nc, _l8_inmaps(inputs, x2_full, y2T_full, g2T_full))
    x4 = np.stack([np.concatenate([res[2 * b]["o_x4"], res[2 * b + 1]["o_x4"]], axis=0) for b in range(B)])
    return x4


def kernel(**inputs):
    inputs = {k: np.asarray(v) for k, v in inputs.items()}
    xconv, ssp, z = run_l1(inputs)
    y = run_l2(inputs, xconv, ssp)
    s, QT, KT, V = run_l3(inputs, y, z)
    ctx = run_l4(QT, KT, V)
    x2, raw, g2 = run_l5(inputs, s, ctx)
    h1, d1, bc1 = run_l6(inputs, raw)
    y2 = run_l7(inputs, h1, d1, bc1)
    x4 = run_l8(inputs, x2, y2, g2)
    return x4.reshape(B, L, D)


# version 03
# speedup vs baseline: 1.7148x; 1.0797x over previous
import contextlib
import numpy as np
import ml_dtypes
import concourse.bass as bass
import concourse.mybir as mybir
from concourse.bass_utils import run_bass_kernel_spmd

F32 = mybir.dt.float32
BF16 = mybir.dt.bfloat16
AF = mybir.ActivationFunctionType
ALU = mybir.AluOpType
AX = mybir.AxisListType
BF = ml_dtypes.bfloat16

B, L, D = 4, 4096, 1024
TC = 2048
DI = 2048
NST = 16
NH, NKV, DH = 4, 2, 256
FH = 2731
FFH = 4096
IN2 = 2048
DTR = 64
E = 8
F2 = 4096
FF1 = 8192
LN_EPS = 1e-5
RMS_EPS = 1e-6
MASK_NEG = -1e30
DEBUG_L1 = False
DEBUG_L4 = False
DEBUG_L5 = False
L6_PHASES = 3



ENGINES = ("sync", "gpsimd", "tensor", "vector", "scalar")


class Prog:
    def __init__(self, name="prog"):
        self.nc = bass.Bass("TRN2", target_bir_lowering=False)
        self.ops = []
        self.name = name
        self._es = None
        self.groups = {}

    def dma_group(self, name, stream=False):
        if name not in self.groups:
            self.groups[name] = {"stream": stream, "ops": []}
        return name

    def op(self, engine, fn, deps=(), dma=False, group=None):
        gid = None
        if dma:
            assert group is not None, "dma ops must declare a group"
            gid = group
        i = len(self.ops)
        self.ops.append([engine, fn, list(deps), 16 if dma else 1, gid])
        if gid is not None:
            self.groups[gid]["ops"].append(i)
        return i

    def sbuf(self, name, shape, dtype):
        return self._es.enter_context(self.nc.sbuf_tensor(name, shape, dtype))

    def psum(self, name, shape, dtype=F32):
        return self._es.enter_context(self.nc.psum_tensor(name, shape, dtype))

    def dram_in(self, name, shape, dtype=F32):
        return self.nc.dram_tensor(name, shape, dtype, kind="ExternalInput").ap()

    def dram_out(self, name, shape, dtype=F32):
        return self.nc.dram_tensor(name, shape, dtype, kind="ExternalOutput").ap()

    def build(self, body):
        with contextlib.ExitStack() as es:
            self._es = es
            eng_sems = {e: es.enter_context(self.nc.semaphore(f"ch_{e}")) for e in ENGINES}
            body(self)
            grp_sems = {g: es.enter_context(self.nc.semaphore(f"g_{g}")) for g in self.groups}
            nums = sorted([s.num for s in eng_sems.values()] + [s.num for s in grp_sems.values()])
            rng = range(min(nums), max(nums) + 1)
            self.nc.gpsimd.dma_reset(rng)
            self.nc.gpsimd.sem_clear(rng)
            self.nc.all_engine_barrier()

            n = len(self.ops)
            needed = [False] * n
            per_eng = {e: [] for e in ENGINES}
            grp_pos = {}
            for g, info in self.groups.items():
                for j, i in enumerate(info["ops"]):
                    grp_pos[i] = (g, j + 1)
            for i, (e, fn, deps, inc, gid) in enumerate(self.ops):
                per_eng[e].append(i)
                if gid is not None:
                    needed[i] = True
                for d in deps:
                    if (self.ops[d][0] == "tensor" and e == "tensor"
                            and self.ops[d][4] is None and gid is None):
                        assert d < i, f"same-engine dep must precede: {d} -> {i}"
                        continue
                    needed[d] = True
                    if self.ops[d][0] == e and self.ops[d][4] is None and gid is None:
                        assert d < i, f"same-engine dep must precede: {d} -> {i}"
            eng_cnt = {e: 0 for e in ENGINES}
            idx = [0] * n
            for i, (e, fn, deps, inc, gid) in enumerate(self.ops):
                if gid is None and needed[i]:
                    eng_cnt[e] += 1
                idx[i] = eng_cnt[e]

            def wait_val(d):
                e, _, _, inc, gid = self.ops[d]
                if gid is None:
                    return (eng_sems[e], idx[d])
                g, pos = grp_pos[d]
                if self.groups[g]["stream"]:
                    return (grp_sems[g], 16 * pos)
                return (grp_sems[g], 16 * len(self.groups[g]["ops"]))

            blk = es.enter_context(self.nc.Block())
            for e in ENGINES:
                ids = per_eng[e]
                if not ids:
                    continue

                def section(eng, ids=ids, e=e):
                    hi = {}
                    for i in ids:
                        _, fn, deps, inc, gid = self.ops[i]
                        want = {}
                        for d in deps:
                            if (e == "tensor" and self.ops[d][0] == "tensor"
                                    and self.ops[d][4] is None and gid is None):
                                continue
                            sem, v = wait_val(d)
                            want[sem.num] = (sem, max(want[sem.num][1] if sem.num in want else 0, v))
                        for _, (sem, v) in sorted(want.items()):
                            if hi.get(sem.num, -1) < v:
                                eng.wait_ge(sem, v)
                                hi[sem.num] = v
                        instr = fn(eng)
                        if needed[i]:
                            assert instr is not None, f"op {i} on {e} returned None"
                            if gid is not None:
                                instr.then_inc(grp_sems[gid], 16)
                            else:
                                instr.then_inc(eng_sems[e], 1)

                getattr(blk, e)(section)
        return self.nc



def fblocks(n, step=128):
    out = []
    off = 0
    while off < n:
        out.append((off, min(step, n - off)))
        off += step
    return out


def np_bf16(x):
    return np.ascontiguousarray(x, dtype=BF)


def npT(x):
    return np.ascontiguousarray(np.swapaxes(np.asarray(x), -1, -2))


IDENT = np.eye(128, dtype=BF)
ONES128 = np.ones((128, 1), np.float32)
ONESBF = np.ones((128, 1), BF)


def ln_stats_ops(p, sc_sq, x_ap, st, rows, eps, mean_out=True):
    raise NotImplementedError





def emit_ln_tile(pr, x_ap, p, sb, deps, eps_col, sub_mean=True):
    s_sq, s_st, s_ln = sb["sq"], sb["st"], sb["ln"]
    d_sq = pr.op("scalar", lambda sc: sc.activation(s_sq[0:p, :], x_ap, AF.Square), deps=deps)
    d_r1 = pr.op("vector", lambda v: v.tensor_reduce(s_st[0:p, 0:1], x_ap, axis=AX.X, op=ALU.add), deps=deps)
    d_r2 = pr.op("vector", lambda v: v.tensor_reduce(s_st[0:p, 1:2], s_sq[0:p, :], axis=AX.X, op=ALU.add), deps=[d_sq])
    d_m = pr.op("vector", lambda v: v.tensor_scalar(s_st[0:p, 0:1], s_st[0:p, 0:1], 1.0 / D, None, op0=ALU.mult), deps=[d_r1])
    d_v = pr.op("vector", lambda v: v.tensor_scalar(s_st[0:p, 1:2], s_st[0:p, 1:2], 1.0 / D, None, op0=ALU.mult), deps=[d_r2])
    d_m2 = pr.op("vector", lambda v: v.tensor_tensor(s_st[0:p, 2:3], s_st[0:p, 0:1], s_st[0:p, 0:1], op=ALU.mult), deps=[d_m])
    d_var = pr.op("vector", lambda v: v.tensor_tensor(s_st[0:p, 3:4], s_st[0:p, 1:2], s_st[0:p, 2:3], op=ALU.subtract), deps=[d_v, d_m2])
    d_rs = pr.op("scalar", lambda sc: sc.activation(s_st[0:p, 4:5], s_st[0:p, 3:4], AF.Ln, bias=eps_col), deps=[d_var])
    d_rs2 = pr.op("scalar", lambda sc: sc.activation(s_st[0:p, 4:5], s_st[0:p, 4:5], AF.Exp, scale=-0.5), deps=[d_rs])
    d_ln = pr.op("vector", lambda v: v.tensor_scalar(
        s_ln[0:p, :], x_ap, s_st[0:p, 0:1], s_st[0:p, 4:5], op0=ALU.subtract, op1=ALU.mult), deps=[d_rs2, d_m])
    return d_ln


def emit_rms_tile(pr, x_ap, out_ap, p, sb, deps, eps_col):
    s_sq, s_st = sb["sq"], sb["st"]
    d_sq = pr.op("vector", lambda v: v.tensor_tensor(s_sq[0:p, :], x_ap, x_ap, op=ALU.mult), deps=deps)
    d_r = pr.op("vector", lambda v: v.tensor_reduce(s_st[0:p, 5:6], s_sq[0:p, :], axis=AX.X, op=ALU.add), deps=[d_sq])
    d_mm = pr.op("vector", lambda v: v.tensor_scalar(s_st[0:p, 5:6], s_st[0:p, 5:6], 1.0 / D, None, op0=ALU.mult), deps=[d_r])
    d_l = pr.op("scalar", lambda sc: sc.activation(s_st[0:p, 6:7], s_st[0:p, 5:6], AF.Ln, bias=eps_col), deps=[d_mm])
    d_e = pr.op("scalar", lambda sc: sc.activation(s_st[0:p, 6:7], s_st[0:p, 6:7], AF.Exp, scale=-0.5), deps=[d_l])
    d_o = pr.op("vector", lambda v: v.tensor_scalar(out_ap, x_ap, s_st[0:p, 6:7], None, op0=ALU.mult), deps=[d_e] + deps)
    return d_o


def emit_transpose_cols(pr, src_bf_ap_fn, dstT, p, col_off, p_tr, s_id, deps, wcolT=None,
                        nblk=8, prev_evac=None):
    evacs = []
    prev = prev_evac
    for j in range(nblk):
        d_tp = pr.op("tensor", lambda t, j=j: t.transpose(
            p_tr[:, 0:p], src_bf_ap_fn(j), s_id[0:p, 0:p]),
            deps=deps + ([prev] if prev is not None else []))
        if wcolT is not None:
            d_ev = pr.op("vector", lambda v, j=j: v.tensor_scalar(
                dstT[j][:, col_off:col_off + p], p_tr[:, 0:p], wcolT[:, j:j + 1], None,
                op0=ALU.mult), deps=[d_tp])
        else:
            d_ev = pr.op("vector", lambda v, j=j: v.tensor_copy(
                dstT[j][:, col_off:col_off + p], p_tr[:, 0:p]), deps=[d_tp])
        evacs.append(d_ev)
        prev = d_ev
    return evacs, prev



def build_l1():
    pr = Prog("l1")
    TP = TC + 3
    x_pad = pr.dram_in("x_pad", [TP, D])
    w_norm = pr.dram_in("w_norm", [1, D])
    w_normT = pr.dram_in("w_normT", [128, D // 128])
    in_projT = pr.dram_in("in_projT", [D, 2 * DI])
    conv_ck = pr.dram_in("conv_ck", [DI, 4])
    conv_bk = pr.dram_in("conv_bk", [128, DI // 128])
    x_projT = pr.dram_in("x_projT", [DI, 33])
    ident = pr.dram_in("ident", [128, 128], BF16)
    o_xconv = pr.dram_out("o_xconv", [DI, TC], BF16)
    o_ssp = pr.dram_out("o_ssp", [33, TC])
    o_z = pr.dram_out("o_z", [DI, TC], BF16)
    o_dbg = pr.dram_out("o_dbg", [8 * 128, TC + 3], BF16) if DEBUG_L1 else None

    def body(pr):
        nc = pr.nc
        s_x = [pr.sbuf(f"s_x{i}", [128, D], F32) for i in range(3)]
        s_ln = pr.sbuf("s_ln", [128, D], F32)
        s_sq = pr.sbuf("s_sq", [128, D], F32)
        s_xnbf = pr.sbuf("s_xnbf", [128, D], BF16)
        s_st = pr.sbuf("s_st", [128, 8], F32)
        s_cst = pr.sbuf("s_cst", [128, 2], F32)
        s_wnT = pr.sbuf("s_wnT", [128, 8], F32)
        s_id = pr.sbuf("s_id", [128, 128], BF16)
        s_xnT = [pr.sbuf(f"s_xnT{j}", [128, TP], BF16) for j in range(8)]
        s_w = [pr.sbuf(f"s_w{k}", [128, 1024], BF16) for k in range(2)]
        s_fb = pr.sbuf("s_fb", [128, TP], F32)
        s_conv = pr.sbuf("s_conv", [128, TC], F32)
        s_cw = pr.sbuf("s_cw", [128, 4 * 16], F32)
        s_cb = pr.sbuf("s_cb", [128, 16], F32)
        s_xc = [pr.sbuf(f"s_xc{j}", [128, TC], BF16) for j in range(16)]
        s_zb = pr.sbuf("s_zb", [128, TC], BF16)
        s_xp = pr.sbuf("s_xp", [128, 33 * 16], BF16)
        s_ssp = pr.sbuf("s_ssp", [33, TC], F32)
        p_tr = pr.psum("p_tr", [128, 128], BF16)
        p_mm = [pr.psum(f"p_mm{i}", [128, 512]) for i in range(2)]
        p_sp = pr.psum("p_sp", [33, 512])

        tiles = [(0, 3)] + [(3 + 128 * i, 128) for i in range(16)]
        prev_gate = []
        x_consumed = {}

        d_c0s = pr.op("vector", lambda v: v.memset(s_cst[:, 0:1], LN_EPS))
        d_c1s = pr.op("vector", lambda v: v.memset(s_cst[:, 1:2], RMS_EPS))
        ld_wnT = pr.op("sync", lambda s: s.dma_start(
            out=s_wnT[:, :], in_=w_normT[:, :]), dma=True, group=pr.dma_group("consts"))
        ld_id = pr.op("sync", lambda s: s.dma_start(out=s_id[:], in_=ident[:]), dma=True, group=pr.dma_group("consts"))
        ld_cw = pr.op("sync", lambda s: s.dma_start(
            out=s_cw[:, :].rearrange("p (a k) -> p a k", a=16),
            in_=conv_ck[:, :].rearrange("(a p) k -> p a k", p=128)), dma=True, group=pr.dma_group("consts"))
        ld_cb = pr.op("sync", lambda s: s.dma_start(out=s_cb[:, :], in_=conv_bk[:, :]), dma=True, group=pr.dma_group("consts"))
        ld_xp = pr.op("gpsimd", lambda g: g.dma_start(
            out=s_xp[:, :].rearrange("p (a c) -> p a c", a=16),
            in_=x_projT[:, :].rearrange("(a p) c -> p a c", p=128)), dma=True, group=pr.dma_group("gconsts"))

        tr_done = []
        for i, (off, p) in enumerate(tiles):
            xb = s_x[i % 3]
            ld_xi = pr.op("sync", lambda s, xb=xb, off=off, p=p: s.dma_start(
                out=xb[0:p, :], in_=x_pad[off:off + p, :]),
                deps=[x_consumed[i - 3]] if (i - 3) in x_consumed else [], dma=True,
                group=pr.dma_group(f"x{i % 3}", stream=True))
            d_sq = pr.op("scalar", lambda sc, xb=xb, p=p: sc.activation(
                s_sq[0:p, :], xb[0:p, :], AF.Square), deps=[ld_xi] + prev_gate)
            d_r1 = pr.op("vector", lambda v, p=p, xb=xb: v.tensor_reduce(
                s_st[0:p, 0:1], xb[0:p, :], axis=AX.X, op=ALU.add), deps=[ld_xi])
            d_r2 = pr.op("vector", lambda v, p=p: v.tensor_reduce(
                s_st[0:p, 1:2], s_sq[0:p, :], axis=AX.X, op=ALU.add), deps=[d_sq])
            d_m = pr.op("vector", lambda v, p=p: v.tensor_scalar(
                s_st[0:p, 0:1], s_st[0:p, 0:1], 1.0 / D, None, op0=ALU.mult), deps=[d_r1])
            d_v = pr.op("vector", lambda v, p=p: v.tensor_scalar(
                s_st[0:p, 1:2], s_st[0:p, 1:2], 1.0 / D, None, op0=ALU.mult), deps=[d_r2])
            d_m2 = pr.op("vector", lambda v, p=p: v.tensor_tensor(
                s_st[0:p, 2:3], s_st[0:p, 0:1], s_st[0:p, 0:1], op=ALU.mult), deps=[d_m])
            d_var = pr.op("vector", lambda v, p=p: v.tensor_tensor(
                s_st[0:p, 3:4], s_st[0:p, 1:2], s_st[0:p, 2:3], op=ALU.subtract),
                deps=[d_v, d_m2])
            d_rs = pr.op("scalar", lambda sc, p=p: sc.activation(
                s_st[0:p, 4:5], s_st[0:p, 3:4], AF.Ln, bias=s_cst[0:p, 0:1]), deps=[d_var, d_c0s])
            d_rs2 = pr.op("scalar", lambda sc, p=p: sc.activation(
                s_st[0:p, 4:5], s_st[0:p, 4:5], AF.Exp, scale=-0.5), deps=[d_rs])
            d_ln = pr.op("vector", lambda v, p=p, xb=xb: v.tensor_scalar(
                s_ln[0:p, :], xb[0:p, :], s_st[0:p, 0:1], s_st[0:p, 4:5],
                op0=ALU.subtract, op1=ALU.mult), deps=[d_rs2, d_m])
            x_consumed[i] = d_ln
            d_sq2 = pr.op("vector", lambda v, p=p: v.tensor_tensor(
                s_sq[0:p, :], s_ln[0:p, :], s_ln[0:p, :], op=ALU.mult), deps=[d_ln])
            d_r3 = pr.op("vector", lambda v, p=p: v.tensor_reduce(
                s_st[0:p, 5:6], s_sq[0:p, :], axis=AX.X, op=ALU.add), deps=[d_sq2])
            d_mm = pr.op("vector", lambda v, p=p: v.tensor_scalar(
                s_st[0:p, 5:6], s_st[0:p, 5:6], 1.0 / D, None, op0=ALU.mult), deps=[d_r3])
            d_l2 = pr.op("scalar", lambda sc, p=p: sc.activation(
                s_st[0:p, 6:7], s_st[0:p, 5:6], AF.Ln, bias=s_cst[0:p, 1:2]), deps=[d_mm, d_c1s])
            d_e2 = pr.op("scalar", lambda sc, p=p: sc.activation(
                s_st[0:p, 6:7], s_st[0:p, 6:7], AF.Exp, scale=-0.5), deps=[d_l2])
            d_xn = pr.op("vector", lambda v, p=p: v.tensor_scalar(
                s_xnbf[0:p, :], s_ln[0:p, :], s_st[0:p, 6:7], None, op0=ALU.mult),
                deps=[d_e2, d_ln] + prev_gate)
            prev_evacs = []
            for j in range(8):
                d_tp = pr.op("tensor", lambda t, p=p, j=j: t.transpose(
                    p_tr[:, 0:p], s_xnbf[0:p, j * 128:(j + 1) * 128], s_id[0:p, 0:p]),
                    deps=[d_xn, ld_id] + tr_done[-1:])
                d_ev = pr.op("vector", lambda v, p=p, j=j, off=off: v.tensor_scalar(
                    s_xnT[j][:, off:off + p], p_tr[:, 0:p], s_wnT[:, j:j + 1], None,
                    op0=ALU.mult), deps=[d_tp, ld_wnT])
                tr_done.append(d_ev)
            prev_gate = [d_xn, d_tp]
        d_xnT_all = tr_done[-1]

        if DEBUG_L1:
            dbg_st = [pr.op("sync", lambda s, j=j: s.dma_start(
                out=o_dbg[j * 128:(j + 1) * 128, :], in_=s_xnT[j][:, :]),
                deps=[d_xnT_all], dma=True, group=pr.dma_group("st_dbg")) for j in range(8)]

        TCH = [(0, 512), (512, 512), (1024, 512), (1536, 512), (2048, 3)]
        d_wload = {}
        conv_ops = []
        last_mm_on_bank = [None, None]
        prev_fb_scalar_reader = None
        prev_si = None
        prev_zst = None
        for f in range(32):
            wb = s_w[f % 2]
            dep_prev_use = d_wload.get(f - 2, [])
            d_w = pr.op("gpsimd", lambda g, f=f, wb=wb: g.dma_start(
                out=wb[:, :].rearrange("p (a c) -> p a c", a=8),
                in_=in_projT[:, f * 128:(f + 1) * 128].rearrange("(a p) c -> p a c", p=128)),
                deps=dep_prev_use, dma=True, group=pr.dma_group(f"w{f % 2}", stream=True))
            fb_parts = []
            for ci, (co, cw) in enumerate(TCH):
                bank = ci % 2
                mm_deps = [d_w, d_xnT_all]
                if last_mm_on_bank[bank] is not None:
                    mm_deps.append(last_mm_on_bank[bank][1])
                d_mmlast = None
                for k in range(8):
                    d_mmlast = pr.op("tensor", lambda t, wb=wb, k=k, co=co, cw=cw, bank=bank, first=(k == 0): t.matmul(
                        p_mm[bank][:, 0:cw], wb[:, k * 128:(k + 1) * 128],
                        s_xnT[k][:, co:co + cw], start=first, stop=(k == 7)),
                        deps=(mm_deps if k == 0 else [d_mmlast]))
                ev_deps = [d_mmlast]
                if ci == 0 and prev_fb_scalar_reader is not None:
                    ev_deps.append(prev_fb_scalar_reader)
                d_ev = pr.op("vector", lambda v, co=co, cw=cw, bank=bank: v.tensor_copy(
                    s_fb[:, co:co + cw], p_mm[bank][:, 0:cw]), deps=ev_deps)
                last_mm_on_bank[bank] = (d_mmlast, d_ev)
                fb_parts.append(d_ev)
            d_wload[f] = [last_mm_on_bank[0][0], last_mm_on_bank[1][0]]
            if f < 16:
                d_c0 = pr.op("vector", lambda v, f=f: v.tensor_scalar(
                    s_conv[:, :], s_fb[:, 0:TC], s_cw[:, f * 4:f * 4 + 1], None,
                    op0=ALU.mult), deps=fb_parts + [ld_cw] + ([prev_si] if prev_si else []))
                d_ck = d_c0
                for k in range(1, 4):
                    d_ck = pr.op("vector", lambda v, f=f, k=k: v.scalar_tensor_tensor(
                        s_conv[:, :], s_fb[:, k:k + TC], s_cw[:, f * 4 + k:f * 4 + k + 1],
                        s_conv[:, :], op0=ALU.mult, op1=ALU.add), deps=[d_ck])
                d_si = pr.op("scalar", lambda sc, f=f: sc.activation(
                    s_xc[f][:, :], s_conv[:, :], AF.Silu, bias=s_cb[:, f:f + 1]),
                    deps=[d_ck, ld_cb])
                prev_si = d_si
                d_st = pr.op("sync", lambda s, f=f: s.dma_start(
                    out=o_xconv[f * 128:(f + 1) * 128, :], in_=s_xc[f][:, :]),
                    deps=[d_si], dma=True, group=pr.dma_group("st_xc"))
                conv_ops.append((d_si, d_st))
            else:
                d_zb = pr.op("scalar", lambda sc: sc.activation(
                    s_zb[:, :], s_fb[:, 3:3 + TC], AF.Identity),
                    deps=fb_parts + ([prev_zst] if prev_zst else []))
                prev_fb_scalar_reader = d_zb
                d_st = pr.op("sync", lambda s, f=f: s.dma_start(
                    out=o_z[(f - 16) * 128:(f - 15) * 128, :], in_=s_zb[:, :]),
                    deps=[d_zb], dma=True, group=pr.dma_group("zst", stream=True))
                prev_zst = d_st
                conv_ops.append((d_zb, d_st))

        sp_evac = []
        for ci, (co, cw) in enumerate(fblocks(TC, 512)):
            d_mmlast = None
            for k in range(16):
                d_mmlast = pr.op("tensor", lambda t, k=k, co=co, cw=cw, first=(k == 0): t.matmul(
                    p_sp[:, 0:cw], s_xp[:, k * 33:(k + 1) * 33], s_xc[k][:, co:co + cw],
                    start=first, stop=(k == 15)),
                    deps=([ld_xp, conv_ops[15][0]] + sp_evac[-1:]) if k == 0 else [d_mmlast])
            d_ev = pr.op("vector", lambda v, co=co, cw=cw: v.tensor_copy(
                s_ssp[:, co:co + cw], p_sp[:, 0:cw]), deps=[d_mmlast])
            sp_evac.append(d_ev)
        d_spst = pr.op("sync", lambda s: s.dma_start(out=o_ssp[:, :], in_=s_ssp[:, :]),
                       deps=sp_evac, dma=True, group=pr.dma_group("st_sp"))

        all_st = [d_spst] + [st for (_x, st) in conv_ops] + (dbg_st if DEBUG_L1 else [])
        pr.op("vector", lambda v: v.engine_nop(), deps=all_st)

    nc = pr.build(body)
    return nc



def _run(nc, in_maps):
    res = run_bass_kernel_spmd(nc, in_maps, core_ids=list(range(8)))
    return res.results


def _l1_inmaps(inputs):
    x = np.asarray(inputs["L0_attn_x"])
    wT = {
        "w_norm": np.asarray(inputs["L0_attn_norm_m_w"]).reshape(1, D),
        "w_normT": npT(np.asarray(inputs["L0_attn_norm_m_w"]).reshape(D // 128, 128)),
        "in_projT": npT(inputs["L0_attn_in_proj_w"]),
        "conv_ck": np.ascontiguousarray(np.asarray(inputs["L0_attn_conv_w"])[:, 0, :]),
        "conv_bk": npT(np.asarray(inputs["L0_attn_conv_b"]).reshape(DI // 128, 128)),
        "x_projT": npT(inputs["L0_attn_x_proj_w"]),
        "ident": IDENT,
    }
    maps = []
    for c in range(8):
        b, h = c // 2, c % 2
        t0 = h * TC
        if h == 0:
            xp = np.concatenate([np.zeros((3, D), np.float32), x[b, 0:TC]], axis=0)
        else:
            xp = x[b, t0 - 3:t0 + TC]
        maps.append({"x_pad": np.ascontiguousarray(xp), **wT})
    return maps


def run_l1(inputs):
    nc = build_l1()
    res = _run(nc, _l1_inmaps(inputs))
    xconv = np.stack([np.concatenate([res[2 * b]["o_xconv"], res[2 * b + 1]["o_xconv"]], axis=1) for b in range(B)])
    ssp = np.stack([np.concatenate([res[2 * b]["o_ssp"], res[2 * b + 1]["o_ssp"]], axis=1) for b in range(B)])
    z = np.stack([np.concatenate([res[2 * b]["o_z"], res[2 * b + 1]["o_z"]], axis=1) for b in range(B)])
    return xconv, ssp, z





def build_l2(nrows=1024, seqlen=L, dstate=16, tchunk=256, delta_input=False):
    pr = Prog("l2")
    NP = nrows // 128
    NTC = seqlen // tchunk
    U = tchunk + 1
    xconv = pr.dram_in("xconv", [nrows, seqlen], BF16)
    ssp_bc = pr.dram_in("ssp_bc", [2 * dstate, seqlen])
    if delta_input:
        deltaT = pr.dram_in("deltaT", [nrows, seqlen])
    else:
        dtraw = pr.dram_in("dtraw", [1, seqlen])
        dt_wT = pr.dram_in("dt_wT", [1, nrows])
        dt_bT = pr.dram_in("dt_bT", [128, NP])
    A_log = pr.dram_in("A_log", [nrows, dstate])
    D_skipT = pr.dram_in("D_skipT", [128, NP])
    o_y = pr.dram_out("o_y", [nrows, seqlen])

    def body(pr):
        s_xc = [pr.sbuf(f"s_xc{j}", [128, seqlen], BF16) for j in range(NP)]
        s_bc = pr.sbuf("s_bc", [2 * dstate, seqlen], BF16)
        s_flat = pr.sbuf("s_flat", [1, 2 * dstate * tchunk], BF16)
        s_onesb = pr.sbuf("s_onesb", [1, 128], BF16)
        s_dtraw = pr.sbuf("s_dtraw", [1, seqlen], F32)
        s_dtw = pr.sbuf("s_dtw", [1, nrows], F32)
        s_dtb = pr.sbuf("s_dtb", [128, NP], F32)
        s_A = pr.sbuf("s_A", [128, NP * dstate], F32)
        s_D = pr.sbuf("s_D", [128, NP], F32)
        s_Bbig = pr.sbuf("s_Bbig", [128, dstate * U], BF16)
        s_Cbig = pr.sbuf("s_Cbig", [128, dstate * U], BF16)
        s_dA = [pr.sbuf(f"s_dA{i}", [128, dstate * U], F32) for i in range(2)]
        s_dBu = pr.sbuf("s_dBu", [128, dstate * U], BF16)
        s_h = pr.sbuf("s_h", [128, dstate * U], F32)
        s_yp = pr.sbuf("s_yp", [128, dstate * U], BF16)
        s_carry = [pr.sbuf(f"s_carry{j}", [128, dstate], F32) for j in range(NP)]
        s_delta = [pr.sbuf(f"s_delta{i}", [128, tchunk], F32) for i in range(2)]
        s_u = pr.sbuf("s_u", [128, tchunk], BF16)
        s_sp1 = [pr.sbuf(f"s_sp1{i}", [128, tchunk], F32) for i in range(2)]
        s_sp2 = [pr.sbuf(f"s_sp2{i}", [128, tchunk], F32) for i in range(2)]
        s_y = pr.sbuf("s_y", [128, tchunk], F32)
        s_yout = [pr.sbuf(f"s_yout{i}", [128, tchunk], F32) for i in range(2)]
        p_dl = [pr.psum(f"p_dl{i}", [128, tchunk]) for i in range(2)]
        p_bc = [pr.psum(f"p_bc{i}", [128, tchunk]) for i in range(2)]

        ld_xc = [pr.op("sync", lambda s, j=j: s.dma_start(
            out=s_xc[j][:], in_=xconv[j * 128:(j + 1) * 128, :]), dma=True, group=pr.dma_group("consts"))
            for j in range(NP)]
        ld_bc = pr.op("gpsimd", lambda g: g.dma_start(out=s_bc[:], in_=ssp_bc[:]), dma=True, group=pr.dma_group("gconsts"))
        if not delta_input:
            ld_dtr = pr.op("sync", lambda s: s.dma_start(out=s_dtraw[:], in_=dtraw[:]), dma=True, group=pr.dma_group("consts"))
            ld_dtw = pr.op("sync", lambda s: s.dma_start(out=s_dtw[:], in_=dt_wT[:]), dma=True, group=pr.dma_group("consts"))
            ld_dtb = pr.op("sync", lambda s: s.dma_start(out=s_dtb[:], in_=dt_bT[:]), dma=True, group=pr.dma_group("consts"))
        ld_A = pr.op("sync", lambda s: s.dma_start(
            out=s_A[:, :].rearrange("p (a n) -> p a n", a=NP),
            in_=A_log[:, :].rearrange("(a p) n -> p a n", p=128)), dma=True, group=pr.dma_group("consts"))
        ld_D = pr.op("sync", lambda s: s.dma_start(out=s_D[:], in_=D_skipT[:]), dma=True, group=pr.dma_group("consts"))

        d_Ae = pr.op("scalar", lambda sc: sc.activation(s_A[:], s_A[:], AF.Exp), deps=[ld_A])
        d_An = pr.op("vector", lambda v: v.tensor_scalar(
            s_A[:], s_A[:], -1.0, None, op0=ALU.mult), deps=[d_Ae])
        d_ones = pr.op("gpsimd", lambda g: g.memset(s_onesb[:], 1.0))
        d_z0 = [pr.op("vector", lambda v, i=i: v.memset(
            s_dA[i][:, :].rearrange("p (n u) -> p n u", u=U)[:, :, 0:1], 0.0)) for i in range(2)]
        d_zc = [pr.op("vector", lambda v, j=j: v.memset(s_carry[j][:], 0.0)) for j in range(NP)]

        prev_chunk_store = {}
        prev_carry_dep = {j: d_zc[j] for j in range(NP)}
        prev_bcast_users = []
        pc = {0: [], 1: []}
        it = 0
        for tc in range(NTC):
            co = tc * tchunk
            d_g = pr.op("gpsimd", lambda g, co=co: g.dma_start(
                out=s_flat[0:1, :].rearrange("p (n t) -> p n t", n=2 * dstate),
                in_=s_bc[:, co:co + tchunk]), deps=[ld_bc] + prev_bcast_users, dma=True,
                group=pr.dma_group("flat", stream=True))
            bc_ev = []
            for n in range(2 * dstate):
                dst = s_Bbig if n < dstate else s_Cbig
                nn = n % dstate
                bk = n % 2
                d_mm = pr.op("tensor", lambda t, n=n, bk=bk: t.matmul(
                    p_bc[bk][:, 0:tchunk], s_onesb[:], s_flat[0:1, n * tchunk:(n + 1) * tchunk],
                    start=True, stop=True), deps=[d_g, d_ones] + (bc_ev[-2:-1] if n >= 2 else []))
                d_mm_last = d_mm
                d_ev = pr.op("scalar", lambda sc, dst=dst, nn=nn, bk=bk: sc.activation(
                    dst[:, nn * U + 1:nn * U + 1 + tchunk], p_bc[bk][:, 0:tchunk], AF.Identity),
                    deps=[d_mm])
                bc_ev.append(d_ev)
            d_bcast_done = bc_ev[-1]
            prev_bcast_users = [d_mm_last, d_bcast_done]

            for pt in range(NP):
                P = it % 2
                it += 1
                gate = pc[P]
                if delta_input:
                    d_dlt = pr.op("sync", lambda s, pt=pt, co=co, P=P: s.dma_start(
                        out=s_delta[P][:], in_=deltaT[pt * 128:(pt + 1) * 128, co:co + tchunk]),
                        deps=gate, dma=True, group=pr.dma_group(f"dl{P}", stream=True))
                    d_a2 = d_dlt
                else:
                    d_dl = pr.op("tensor", lambda t, pt=pt, co=co, P=P: t.matmul(
                        p_dl[P][:, 0:tchunk], s_dtw[0:1, pt * 128:(pt + 1) * 128],
                        s_dtraw[0:1, co:co + tchunk], start=True, stop=True),
                        deps=[ld_dtw, ld_dtr] + gate)
                    d_a1 = pr.op("scalar", lambda sc, pt=pt, P=P: sc.activation(
                        s_sp1[P][:], p_dl[P][:, 0:tchunk], AF.Abs, bias=s_dtb[:, pt:pt + 1]),
                        deps=[d_dl, ld_dtb] + gate)
                    d_a2 = pr.op("scalar", lambda sc, pt=pt, P=P: sc.activation(
                        s_sp2[P][:], p_dl[P][:, 0:tchunk], AF.Relu, bias=s_dtb[:, pt:pt + 1]),
                        deps=[d_dl, ld_dtb] + gate)
                    d_a3 = pr.op("scalar", lambda sc, P=P: sc.activation(
                        s_sp1[P][:], s_sp1[P][:], AF.Exp, scale=-1.0), deps=[d_a1])
                    d_a4 = pr.op("scalar", lambda sc, P=P: sc.activation(
                        s_sp1[P][:], s_sp1[P][:], AF.Ln, bias=1.0), deps=[d_a3])
                    d_dlt = pr.op("gpsimd", lambda g, P=P: g.tensor_tensor(
                        s_delta[P][:], s_sp1[P][:], s_sp2[P][:], op=ALU.add),
                        deps=[d_a4, d_a2] + gate)
                d_dAs = []
                for n in range(dstate):
                    d_dAs.append(pr.op("scalar", lambda sc, n=n, pt=pt, P=P: sc.activation(
                        s_dA[P][:, n * U + 1:n * U + 1 + tchunk], s_delta[P][:], AF.Exp,
                        scale=s_A[:, pt * dstate + n:pt * dstate + n + 1]),
                        deps=[d_dlt, d_An] + gate))
                d_u = pr.op("vector", lambda v, pt=pt, co=co, P=P: v.tensor_tensor(
                    s_u[:], s_delta[P][:], s_xc[pt][:, co:co + tchunk], op=ALU.mult),
                    deps=[d_dlt, ld_xc[pt]])
                d_cc = pr.op("vector", lambda v, pt=pt: v.tensor_copy(
                    s_dBu[:, :].rearrange("p (n u) -> p n u", u=U)[:, :, 0:1],
                    s_carry[pt][:, :].rearrange("p (n o) -> p n o", o=1)),
                    deps=[prev_carry_dep[pt]])
                d_dBus = []
                for n in range(dstate):
                    d_dBus.append(pr.op("vector", lambda v, n=n: v.tensor_tensor(
                        s_dBu[:, n * U + 1:n * U + 1 + tchunk], s_u[:],
                        s_Bbig[:, n * U + 1:n * U + 1 + tchunk], op=ALU.mult),
                        deps=[d_u, d_bcast_done]))
                d_scan = pr.op("vector", lambda v, P=P: v.tensor_tensor_scan(
                    s_h[:], s_dA[P][:], s_dBu[:], 0.0, op0=ALU.mult, op1=ALU.add),
                    deps=d_dAs + d_dBus + [d_cc, d_z0[P]])
                d_sc = pr.op("vector", lambda v, pt=pt: v.tensor_copy(
                    s_carry[pt][:, :].rearrange("p (n o) -> p n o", o=1),
                    s_h[:, :].rearrange("p (n u) -> p n u", u=U)[:, :, U - 1:U]),
                    deps=[d_scan])
                prev_carry_dep[pt] = d_sc
                d_yp = pr.op("vector", lambda v: v.tensor_tensor(
                    s_yp[:], s_h[:], s_Cbig[:], op=ALU.mult), deps=[d_scan, d_bcast_done])
                d_y = pr.op("vector", lambda v: v.tensor_reduce(
                    s_y[:, :], s_yp[:, :].rearrange("p (n u) -> p u n", u=U)[:, 1:, :],
                    axis=AX.X, op=ALU.add), deps=[d_yp])
                d_yo = pr.op("vector", lambda v, pt=pt, co=co, P=P: v.scalar_tensor_tensor(
                    s_yout[P][:], s_xc[pt][:, co:co + tchunk], s_D[:, pt:pt + 1], s_y[:],
                    op0=ALU.mult, op1=ALU.add), deps=[d_y, ld_D] + gate)
                d_st = pr.op("sync", lambda s, pt=pt, co=co, P=P: s.dma_start(
                    out=o_y[pt * 128:(pt + 1) * 128, co:co + tchunk], in_=s_yout[P][:]),
                    deps=[d_yo], dma=True, group=pr.dma_group(f"yst{P}", stream=True))
                prev_chunk_store[(tc, pt)] = d_st
                pc[P] = [d_yo, d_scan, d_st, d_a2]
                prev_bcast_users.extend([d_dBus[-1], d_yp])

        pr.op("vector", lambda v: v.engine_nop(),
              deps=[prev_chunk_store[(NTC - 1, j)] for j in range(NP)])

    nc = pr.build(body)
    return nc


def _l2_inmaps(inputs, xconv_bdt, ssp_b33t):
    dt_wT = npT(np.asarray(inputs["L0_attn_dt_w"]))
    dt_b = np.asarray(inputs["L0_attn_dt_b"])
    A_log = np.asarray(inputs["L0_attn_A_log"])
    D_skip = np.asarray(inputs["L0_attn_D_skip"])
    maps = []
    for c in range(8):
        b, dh = c // 2, c % 2
        r0 = dh * 1024
        maps.append({
            "xconv": np.ascontiguousarray(xconv_bdt[b, r0:r0 + 1024]),
            "ssp_bc": np.ascontiguousarray(ssp_b33t[b, 0:32]),
            "dtraw": np.ascontiguousarray(ssp_b33t[b, 32:33]),
            "dt_wT": np.ascontiguousarray(dt_wT[:, r0:r0 + 1024]),
            "dt_bT": npT(dt_b[r0:r0 + 1024].reshape(8, 128)),
            "A_log": np.ascontiguousarray(A_log[r0:r0 + 1024]),
            "D_skipT": npT(D_skip[r0:r0 + 1024].reshape(8, 128)),
        })
    return maps


def run_l2(inputs, xconv, ssp):
    nc = build_l2()
    res = _run(nc, _l2_inmaps(inputs, xconv, ssp))
    y = np.stack([np.concatenate([res[2 * b]["o_y"], res[2 * b + 1]["o_y"]], axis=0) for b in range(B)])
    return y





def build_l3():
    pr = Prog("l3")
    x_in = pr.dram_in("x_in", [TC, D])
    yT = pr.dram_in("yT", [DI, TC])
    zT = pr.dram_in("zT", [DI, TC], BF16)
    out_projT = pr.dram_in("out_projT", [DI, D])
    w_attnT = pr.dram_in("w_attnT", [128, 8])
    q_wT = pr.dram_in("q_wT", [D, D])
    k_wT = pr.dram_in("k_wT", [D, 512])
    v_wT = pr.dram_in("v_wT", [D, 512])
    ident = pr.dram_in("ident", [128, 128], BF16)
    o_s = pr.dram_out("o_s", [TC, D])
    o_QT = pr.dram_out("o_QT", [D, TC], BF16)
    o_KT = pr.dram_out("o_KT", [512, TC], BF16)
    o_V = pr.dram_out("o_V", [TC, 512], BF16)

    def body(pr):
        s_g = [pr.sbuf(f"s_g{j}", [128, TC], BF16) for j in range(16)]
        s_yb = pr.sbuf("s_yb", [128, TC], F32)
        s_zb = pr.sbuf("s_zb", [128, TC], BF16)
        s_sz = pr.sbuf("s_sz", [128, TC], BF16)
        s_opw = [pr.sbuf(f"s_opw{j}", [128, D], BF16) for j in range(16)]
        s_qw = [pr.sbuf(f"s_qw{j}", [128, D], BF16) for j in range(8)]
        s_kw = [pr.sbuf(f"s_kw{j}", [128, 512], BF16) for j in range(8)]
        s_vw = [pr.sbuf(f"s_vw{j}", [128, 512], BF16) for j in range(8)]
        s_x = [pr.sbuf(f"s_x{i}", [128, D], F32) for i in range(3)]
        sb = {"sq": pr.sbuf("s_sq", [128, D], F32), "st": pr.sbuf("s_st", [128, 8], F32),
              "ln": pr.sbuf("s_ln", [128, D], F32)}
        s_cst = pr.sbuf("s_cst", [128, 2], F32)
        s_s = pr.sbuf("s_s", [128, D], F32)
        s_xn2 = pr.sbuf("s_xn2", [128, D], BF16)
        s_wat = pr.sbuf("s_wat", [128, 8], F32)
        s_id = pr.sbuf("s_id", [128, 128], BF16)
        s_xnT = [pr.sbuf(f"s_xnT{j}", [128, TC], BF16) for j in range(8)]
        s_stg = [pr.sbuf(f"s_stg{j}", [128, 512], BF16) for j in range(2)]
        p_tr = pr.psum("p_tr", [128, 128], BF16)
        p_mm = [pr.psum(f"p_mm{i}", [128, 512]) for i in range(2)]

        d_c0 = pr.op("vector", lambda v: v.memset(s_cst[:, 0:1], LN_EPS))
        d_c1 = pr.op("vector", lambda v: v.memset(s_cst[:, 1:2], RMS_EPS))
        ld_id = pr.op("sync", lambda s: s.dma_start(out=s_id[:], in_=ident[:]), dma=True, group=pr.dma_group("consts"))
        ld_wat = pr.op("sync", lambda s: s.dma_start(out=s_wat[:], in_=w_attnT[:]), dma=True, group=pr.dma_group("consts"))
        ld_opw = [pr.op("gpsimd", lambda g, j=j: g.dma_start(
            out=s_opw[j][:], in_=out_projT[j * 128:(j + 1) * 128, :]), dma=True, group=pr.dma_group("wts")) for j in range(16)]
        ld_qw = [pr.op("gpsimd", lambda g, j=j: g.dma_start(
            out=s_qw[j][:], in_=q_wT[j * 128:(j + 1) * 128, :]), dma=True, group=pr.dma_group("wts")) for j in range(8)]
        ld_kw = [pr.op("gpsimd", lambda g, j=j: g.dma_start(
            out=s_kw[j][:], in_=k_wT[j * 128:(j + 1) * 128, :]), dma=True, group=pr.dma_group("wts")) for j in range(8)]
        ld_vw = [pr.op("gpsimd", lambda g, j=j: g.dma_start(
            out=s_vw[j][:], in_=v_wT[j * 128:(j + 1) * 128, :]), dma=True, group=pr.dma_group("wts")) for j in range(8)]

        g_done = []
        prev = []
        for ptile in range(16):
            ld_y = pr.op("sync", lambda s, ptile=ptile: s.dma_start(
                out=s_yb[:], in_=yT[ptile * 128:(ptile + 1) * 128, :]), deps=prev, dma=True,
                group=pr.dma_group("gy", stream=True))
            ld_z = pr.op("sync", lambda s, ptile=ptile: s.dma_start(
                out=s_zb[:], in_=zT[ptile * 128:(ptile + 1) * 128, :]), deps=prev, dma=True,
                group=pr.dma_group("gz", stream=True))
            d_si = pr.op("scalar", lambda sc: sc.activation(s_sz[:], s_zb[:], AF.Silu), deps=[ld_z] + prev)
            d_mul = pr.op("vector", lambda v, ptile=ptile: v.tensor_tensor(
                s_g[ptile][:], s_yb[:], s_sz[:], op=ALU.mult), deps=[d_si, ld_y])
            g_done.append(d_mul)
            prev = [d_mul]
        d_gall = g_done[-1]

        x_cons = {}
        bank_ev = [None, None]
        tp_gate = []
        st_gate = []
        prev_s_st = None
        for ti in range(16):
            ld_x = pr.op("sync", lambda s, ti=ti: s.dma_start(
                out=s_x[ti % 3][:], in_=x_in[ti * 128:(ti + 1) * 128, :]),
                deps=[x_cons[ti - 3]] if (ti - 3) in x_cons else [], dma=True,
                group=pr.dma_group(f"x{ti % 3}", stream=True))
            d_ln = emit_ln_tile(pr, s_x[ti % 3][:, :], 128, sb, [ld_x] + tp_gate, s_cst[:, 0:1])
            x_cons[ti] = d_ln
            ev_dm = []
            for dm in range(2):
                d_mml = None
                for k in range(16):
                    d_mml = pr.op("tensor", lambda t, k=k, ti=ti, dm=dm, first=(k == 0): t.matmul(
                        p_mm[dm][:, :], s_g[k][:, ti * 128:(ti + 1) * 128],
                        s_opw[k][:, dm * 512:(dm + 1) * 512], start=first, stop=(k == 15)),
                        deps=([d_gall, ld_opw[15]] + ([bank_ev[dm]] if bank_ev[dm] else [])) if k == 0 else [d_mml])
                d_ev = pr.op("vector", lambda v, dm=dm: v.tensor_tensor(
                    s_s[:, dm * 512:(dm + 1) * 512], p_mm[dm][:, :],
                    sb["ln"][:, dm * 512:(dm + 1) * 512], op=ALU.add),
                    deps=[d_mml, d_ln] + ([prev_s_st] if prev_s_st else []))
                bank_ev[dm] = d_ev
                ev_dm.append(d_ev)
            d_st = pr.op("sync", lambda s, ti=ti: s.dma_start(
                out=o_s[ti * 128:(ti + 1) * 128, :], in_=s_s[:, :]), deps=ev_dm, dma=True,
                group=pr.dma_group("sst", stream=True))
            prev_s_st = d_st
            st_gate = [d_st]
            d_xn = emit_rms_tile(pr, s_s[:, :], s_xn2[:, :], 128, sb, ev_dm + tp_gate, s_cst[:, 1:2])
            evs, last_ev = emit_transpose_cols(
                pr, lambda j: s_xn2[:, j * 128:(j + 1) * 128], s_xnT, 128, ti * 128,
                p_tr, s_id, [d_xn, ld_id, ld_wat], wcolT=s_wat)
            tp_gate = [evs[-1], d_st]
        d_xnT_all = tp_gate[0]

        outs_st = []
        stg_free = [None, None]
        p_free = [bank_ev[0], bank_ev[1]]
        si = 0
        def qkv_mm(dst_dram, row_off, wtiles, wcol_off, tcw, deps_extra):
            nonlocal si
            d_mml = None
            bank = si % 2
            for k in range(8):
                d_mml = pr.op("tensor", lambda t, k=k, first=(k == 0), bank=bank, wcol_off=wcol_off, tcw=tcw: t.matmul(
                    p_mm[bank][:, 0:512], wtiles[k][:, wcol_off:wcol_off + 128] if False else wtiles[k],
                    s_xnT[k][:, tcw:tcw + 512], start=first, stop=(k == 7)),
                    deps=([d_xnT_all] + deps_extra + ([p_free[bank]] if p_free[bank] else [])) if k == 0 else [d_mml])
            stg = s_stg[si % 2]
            d_ev = pr.op("scalar", lambda sc, stg=stg, bank=bank: sc.activation(
                stg[:], p_mm[bank][:, 0:512], AF.Identity),
                deps=[d_mml] + ([stg_free[si % 2]] if stg_free[si % 2] else []))
            p_free[bank] = d_ev
            d_st = pr.op("sync", lambda s, stg=stg: s.dma_start(
                out=dst_dram, in_=stg[:]), deps=[d_ev], dma=True)
            stg_free[si % 2] = d_st
            outs_st.append(d_st)
            si += 1

        for ptq in range(8):
            qw_slices = [None] * 8
            for tcw in range(0, TC, 512):
                d_mml = None
                bank = si % 2
                for k in range(8):
                    d_mml = pr.op("tensor", lambda t, k=k, ptq=ptq, tcw=tcw, first=(k == 0), bank=bank: t.matmul(
                        p_mm[bank][:, 0:512], s_qw[k][:, ptq * 128:(ptq + 1) * 128],
                        s_xnT[k][:, tcw:tcw + 512], start=first, stop=(k == 7)),
                        deps=([d_xnT_all, ld_qw[7]] + ([p_free[bank]] if p_free[bank] else [])) if k == 0 else [d_mml])
                stg = s_stg[si % 2]
                d_ev = pr.op("scalar", lambda sc, stg=stg, bank=bank: sc.activation(
                    stg[:], p_mm[bank][:, 0:512], AF.Identity),
                    deps=[d_mml] + ([stg_free[si % 2]] if stg_free[si % 2] else []))
                p_free[bank] = d_ev
                d_st = pr.op("sync", lambda s, stg=stg, ptq=ptq, tcw=tcw: s.dma_start(
                    out=o_QT[ptq * 128:(ptq + 1) * 128, tcw:tcw + 512], in_=stg[:]),
                    deps=[d_ev], dma=True, group=pr.dma_group(f"qst{si % 2}", stream=True))
                stg_free[si % 2] = d_st
                outs_st.append(d_st)
                si += 1
        for ptq in range(4):
            for tcw in range(0, TC, 512):
                d_mml = None
                bank = si % 2
                for k in range(8):
                    d_mml = pr.op("tensor", lambda t, k=k, ptq=ptq, tcw=tcw, first=(k == 0), bank=bank: t.matmul(
                        p_mm[bank][:, 0:512], s_kw[k][:, :].rearrange("p (a c) -> p a c", a=4)[:, ptq, :] if False else s_kw[k][:, ptq * 128:(ptq + 1) * 128],
                        s_xnT[k][:, tcw:tcw + 512], start=first, stop=(k == 7)),
                        deps=([d_xnT_all, ld_kw[7]] + ([p_free[bank]] if p_free[bank] else [])) if k == 0 else [d_mml])
                stg = s_stg[si % 2]
                d_ev = pr.op("scalar", lambda sc, stg=stg, bank=bank: sc.activation(
                    stg[:], p_mm[bank][:, 0:512], AF.Identity),
                    deps=[d_mml] + ([stg_free[si % 2]] if stg_free[si % 2] else []))
                p_free[bank] = d_ev
                d_st = pr.op("sync", lambda s, stg=stg, ptq=ptq, tcw=tcw: s.dma_start(
                    out=o_KT[ptq * 128:(ptq + 1) * 128, tcw:tcw + 512], in_=stg[:]),
                    deps=[d_ev], dma=True, group=pr.dma_group(f"qst{si % 2}", stream=True))
                stg_free[si % 2] = d_st
                outs_st.append(d_st)
                si += 1
        for ti in range(16):
            d_mml = None
            bank = si % 2
            for k in range(8):
                d_mml = pr.op("tensor", lambda t, k=k, ti=ti, first=(k == 0), bank=bank: t.matmul(
                    p_mm[bank][:, 0:512], s_xnT[k][:, ti * 128:(ti + 1) * 128],
                    s_vw[k][:, :], start=first, stop=(k == 7)),
                    deps=([d_xnT_all, ld_vw[7]] + ([p_free[bank]] if p_free[bank] else [])) if k == 0 else [d_mml])
            stg = s_stg[si % 2]
            d_ev = pr.op("scalar", lambda sc, stg=stg, bank=bank: sc.activation(
                stg[:], p_mm[bank][:, 0:512], AF.Identity),
                deps=[d_mml] + ([stg_free[si % 2]] if stg_free[si % 2] else []))
            p_free[bank] = d_ev
            d_st = pr.op("sync", lambda s, stg=stg, ti=ti: s.dma_start(
                out=o_V[ti * 128:(ti + 1) * 128, :], in_=stg[:]), deps=[d_ev], dma=True,
                group=pr.dma_group(f"qst{si % 2}", stream=True))
            stg_free[si % 2] = d_st
            outs_st.append(d_st)
            si += 1

        pr.op("vector", lambda v: v.engine_nop(), deps=outs_st[-3:] + st_gate)

    nc = pr.build(body)
    return nc


def _l3_inmaps(inputs, yT_full, zT_full):
    x = np.asarray(inputs["L0_attn_x"])
    common = {
        "out_projT": npT(inputs["L0_attn_out_proj_w"]),
        "w_attnT": npT(np.asarray(inputs["L0_attn_norm_attn_w"]).reshape(8, 128)),
        "q_wT": npT(inputs["L0_attn_q_w"]),
        "k_wT": npT(inputs["L0_attn_k_w"]),
        "v_wT": npT(inputs["L0_attn_v_w"]),
        "ident": IDENT,
    }
    maps = []
    for c in range(8):
        b, h = c // 2, c % 2
        t0 = h * TC
        maps.append({
            "x_in": np.ascontiguousarray(x[b, t0:t0 + TC]),
            "yT": np.ascontiguousarray(yT_full[b, :, t0:t0 + TC]),
            "zT": np.ascontiguousarray(zT_full[b, :, t0:t0 + TC]),
            **common,
        })
    return maps


def run_l3(inputs, yT_full, zT_full):
    nc = build_l3()
    res = _run(nc, _l3_inmaps(inputs, yT_full, zT_full))
    s = np.stack([np.concatenate([res[2 * b]["o_s"], res[2 * b + 1]["o_s"]], axis=0) for b in range(B)])
    QT = np.stack([np.concatenate([res[2 * b]["o_QT"], res[2 * b + 1]["o_QT"]], axis=1) for b in range(B)])
    KT = np.stack([np.concatenate([res[2 * b]["o_KT"], res[2 * b + 1]["o_KT"]], axis=1) for b in range(B)])
    V = np.stack([np.concatenate([res[2 * b]["o_V"], res[2 * b + 1]["o_V"]], axis=0) for b in range(B)])
    return s, QT, KT, V





def build_l4():
    pr = Prog("l4")
    QT = pr.dram_in("QT", [D, TC], BF16)
    KT = pr.dram_in("KT", [512, L], BF16)
    Vt = pr.dram_in("Vt", [L, 512], BF16)
    qpos0 = pr.dram_in("qpos0", [128, 1])
    kpos_in = pr.dram_in("kpos_in", [128, L])
    ident = pr.dram_in("ident", [128, 128], BF16)
    o_ctx = pr.dram_out("o_ctx", [TC, D], BF16)
    if DEBUG_L4:
        o_dbg_row = pr.dram_out("o_dbg_row", [128, L])
        o_dbg_p = pr.dram_out("o_dbg_p", [128, L], BF16)
        o_dbg_sum = pr.dram_out("o_dbg_sum", [128, 2])
        o_dbg_q = pr.dram_out("o_dbg_q", [128, TC], BF16)
        o_dbg_k = pr.dram_out("o_dbg_k", [128, L], BF16)

    def body(pr):
        s_qt = [pr.sbuf(f"s_qt{j}", [128, TC], BF16) for j in range(8)]
        s_kt = [pr.sbuf(f"s_kt{j}", [128, L], BF16) for j in range(4)]
        s_v = [pr.sbuf(f"s_v{j}", [128, 512], BF16) for j in range(32)]
        s_kpos = pr.sbuf("s_kpos", [128, L], F32)
        s_qp0 = pr.sbuf("s_qp0", [128, 1], F32)
        s_qp = pr.sbuf("s_qp", [128, 1], F32)
        s_m = pr.sbuf("s_m", [128, 512], F32)
        s_row = pr.sbuf("s_row", [128, L], F32)
        s_mx = pr.sbuf("s_mx", [128, 2], F32)
        s_sum = pr.sbuf("s_sum", [128, 2], F32)
        s_p = pr.sbuf("s_p", [128, L], BF16)
        s_pt = pr.sbuf("s_pt", [128, L], BF16)
        s_ctx = [pr.sbuf(f"s_ctx{i}", [128, D], BF16) for i in range(2)]
        s_id = pr.sbuf("s_id", [128, 128], BF16)
        p_sc = [pr.psum(f"p_sc{i}", [128, 512]) for i in range(2)]
        p_tr = pr.psum("p_tr", [128, 128], BF16)
        p_av = pr.psum("p_av", [128, 256])
        s_fence = pr.sbuf("s_fence", [1, 2], F32)

        ld_q = [pr.op("sync", lambda s, j=j: s.dma_start(
            out=s_qt[j][:], in_=QT[j * 128:(j + 1) * 128, :]), dma=True, group=pr.dma_group("lds")) for j in range(8)]
        ld_k = [pr.op("sync", lambda s, j=j: s.dma_start(
            out=s_kt[j][:], in_=KT[j * 128:(j + 1) * 128, :]), dma=True, group=pr.dma_group("lds")) for j in range(4)]
        ld_v = [pr.op("sync", lambda s, j=j: s.dma_start(
            out=s_v[j][:], in_=Vt[j * 128:(j + 1) * 128, :]), dma=True, group=pr.dma_group("lds")) for j in range(32)]
        ld_kp = pr.op("sync", lambda s: s.dma_start(out=s_kpos[:], in_=kpos_in[:]), dma=True, group=pr.dma_group("lds"))
        ld_qp = pr.op("sync", lambda s: s.dma_start(out=s_qp0[:], in_=qpos0[:]), dma=True, group=pr.dma_group("lds"))
        ld_id = pr.op("sync", lambda s: s.dma_start(out=s_id[:], in_=ident[:]), dma=True, group=pr.dma_group("consts"))
        ldall = ld_q + ld_k + ld_v + [ld_kp, ld_qp, ld_id]

        ctx_sts = []
        pend = []
        st_parity = [None, None]
        prev_cycle = []
        prev_ctx_st = None
        p_free = [None, None]
        av_free = None
        tr_free = None
        for qi in range(16):
            kcnt = (TC + qi * 128 + 127) // 512 + 1
            kcnt = min(kcnt, 8)
            d_qp = pr.op("vector", lambda v, qi=qi: v.tensor_scalar(
                s_qp[:], s_qp0[:], float(qi * 128), None, op0=ALU.add),
                deps=[ld_qp] + prev_cycle)
            for h in range(4):
                kv = h // 2
                gate = prev_cycle
                mask_evs = []
                for kj in range(kcnt):
                    bank = kj % 2
                    d_mml = None
                    for dk in range(2):
                        d_mml = pr.op("tensor", lambda t, h=h, dk=dk, qi=qi, kj=kj, kv=kv, bank=bank, first=(dk == 0): t.matmul(
                            p_sc[bank][:, :], s_qt[h * 2 + dk][:, qi * 128:(qi + 1) * 128],
                            s_kt[kv * 2 + dk][:, kj * 512:(kj + 1) * 512],
                            start=first, stop=(dk == 1)),
                            deps=(ldall + gate + ([p_free[bank]] if p_free[bank] else [])) if dk == 0 else [d_mml])
                    d_m = pr.op("vector", lambda v, kj=kj: v.tensor_scalar(
                        s_m[:], s_kpos[:, kj * 512:(kj + 1) * 512], s_qp[:, 0:1], MASK_NEG,
                        op0=ALU.is_gt, op1=ALU.mult), deps=[d_qp, ld_kp] + gate)
                    d_ev = pr.op("vector", lambda v, kj=kj, bank=bank: v.scalar_tensor_tensor(
                        s_row[:, kj * 512:(kj + 1) * 512], p_sc[bank][:, :], 0.0625, s_m[:],
                        op0=ALU.mult, op1=ALU.add), deps=[d_mml, d_m])
                    p_free[bank] = d_ev
                    mask_evs.append(d_ev)
                W = kcnt * 512
                d_mx = pr.op("vector", lambda v, W=W: v.tensor_reduce(
                    s_mx[:, 0:1], s_row[:, 0:W], axis=AX.X, op=ALU.max), deps=mask_evs)
                d_nm = pr.op("vector", lambda v: v.tensor_scalar(
                    s_mx[:, 1:2], s_mx[:, 0:1], -1.0, None, op0=ALU.mult), deps=[d_mx])
                d_exp = pr.op("scalar", lambda sc, W=W: sc.activation(
                    s_p[:, 0:W], s_row[:, 0:W], AF.Exp, bias=s_mx[:, 1:2]), deps=[d_nm] + gate)
                d_sm = pr.op("vector", lambda v, W=W: v.tensor_reduce(
                    s_sum[:, 0:1], s_p[:, 0:W], axis=AX.X, op=ALU.add), deps=[d_exp] + gate)
                d_l = pr.op("scalar", lambda sc: sc.activation(
                    s_sum[:, 1:2], s_sum[:, 0:1], AF.Ln), deps=[d_sm])
                d_rc = pr.op("scalar", lambda sc: sc.activation(
                    s_sum[:, 1:2], s_sum[:, 1:2], AF.Exp, scale=-1.0), deps=[d_l])
                d_fence = pr.op("scalar", lambda sc: sc.activation(
                    s_fence[:, 0:1], s_fence[:, 0:1], AF.Identity), deps=[d_exp])
                if DEBUG_L4 and qi == 0 and h == 0:
                    dmp1 = pr.op("sync", lambda s: s.dma_start(out=o_dbg_q[:, :], in_=s_qt[0][:, :]), deps=[d_mx], dma=True)
                    dmp2 = pr.op("sync", lambda s: s.dma_start(out=o_dbg_k[:, :], in_=s_kt[0][:, :]), deps=[d_mx], dma=True)
                    dmp3 = pr.op("sync", lambda s: s.dma_start(out=o_dbg_row[:, :], in_=s_row[:, :]), deps=[d_mx], dma=True)
                    dmp4 = pr.op("sync", lambda s: s.dma_start(out=o_dbg_p[:, :], in_=s_p[:, :]), deps=[d_exp], dma=True)
                    dmp5 = pr.op("sync", lambda s: s.dma_start(out=o_dbg_sum[:, :], in_=s_sum[:, :]), deps=[d_rc], dma=True)
                    prev_cycle = prev_cycle + [dmp1, dmp2, dmp3, dmp4, dmp5]
                tblocks = kcnt * 4
                d_pt_evs = []
                prevt = tr_free
                for tb in range(tblocks):
                    d_tp = pr.op("tensor", lambda t, tb=tb: t.transpose(
                        p_tr[:, :], s_p[:, tb * 128:(tb + 1) * 128], s_id[:]),
                        deps=[d_exp, d_fence, ld_id] + ([prevt] if prevt else []) + gate)
                    d_pe = pr.op("vector", lambda v, tb=tb: v.tensor_copy(
                        s_pt[:, tb * 128:(tb + 1) * 128], p_tr[:, :]), deps=[d_tp])
                    d_pt_evs.append(d_pe)
                    prevt = d_pe
                tr_free = prevt
                d_mml = None
                for tb in range(tblocks):
                    d_mml = pr.op("tensor", lambda t, tb=tb, kv=kv, first=(tb == 0), last=(tb == tblocks - 1): t.matmul(
                        p_av[:, :], s_pt[:, tb * 128:(tb + 1) * 128],
                        s_v[tb][:, kv * 256:(kv + 1) * 256], start=first, stop=last),
                        deps=([d_pt_evs[-1]] + ([av_free] if av_free else [])) if tb == 0 else [d_mml])
                d_cv = pr.op("vector", lambda v, h=h, qi=qi: v.tensor_scalar(
                    s_ctx[qi % 2][:, h * 256:(h + 1) * 256], p_av[:, :], s_sum[:, 1:2], None,
                    op0=ALU.mult), deps=[d_mml, d_rc] + ([st_parity[qi % 2]] if (h == 0 and st_parity[qi % 2]) else []))
                av_free = d_cv
                prev_cycle = [d_cv, d_exp, d_mx]
            pend.append((qi, av_free))
            if len(pend) == 2:
                q0, dep0 = pend.pop(0)
                d_st = pr.op("sync", lambda s, q0=q0: s.dma_start(
                    out=o_ctx[q0 * 128:(q0 + 1) * 128, :], in_=s_ctx[q0 % 2][:]),
                    deps=[dep0, av_free], dma=True,
                    group=pr.dma_group(f"cst{q0 % 2}", stream=True))
                prev_ctx_st = d_st
                st_parity[q0 % 2] = d_st
                ctx_sts.append(d_st)

        d_fl = pr.op("vector", lambda v: v.tensor_copy(s_m[:, 0:4], s_ctx[15 % 2][:, 0:4]),
                     deps=[pend[0][1]])
        d_st = pr.op("sync", lambda s: s.dma_start(
            out=o_ctx[15 * 128:16 * 128, :], in_=s_ctx[15 % 2][:]), deps=[pend[0][1], d_fl],
            dma=True, group=pr.dma_group("cst1", stream=True))
        ctx_sts.append(d_st)
        pr.op("vector", lambda v: v.engine_nop(), deps=ctx_sts[-2:])

    nc = pr.build(body)
    return nc


def _l4_inmaps(QT_full, KT_full, V_full):
    kpos = np.broadcast_to(np.arange(L, dtype=np.float32)[None, :], (128, L))
    kpos = np.ascontiguousarray(kpos)
    maps = []
    for c in range(8):
        b, h = c // 2, c % 2
        t0 = h * TC
        maps.append({
            "QT": np.ascontiguousarray(QT_full[b][:, t0:t0 + TC]),
            "KT": np.ascontiguousarray(KT_full[b]),
            "Vt": np.ascontiguousarray(V_full[b]),
            "qpos0": (t0 + np.arange(128, dtype=np.float32)).reshape(128, 1),
            "kpos_in": kpos,
            "ident": IDENT,
        })
    return maps


def run_l4(QT_full, KT_full, V_full):
    nc = build_l4()
    res = _run(nc, _l4_inmaps(QT_full, KT_full, V_full))
    ctx = np.stack([np.concatenate([res[2 * b]["o_ctx"], res[2 * b + 1]["o_ctx"]], axis=0) for b in range(B)])
    return ctx





def build_l5():
    pr = Prog("l5")
    s_in = pr.dram_in("s_in", [TC, D])
    ctx_in = pr.dram_in("ctx_in", [TC, D], BF16)
    x_in = pr.dram_in("x_in", [TC, D])
    oT = pr.dram_in("oT", [D, D])
    w_ffnT = pr.dram_in("w_ffnT", [128, 8])
    gateT = pr.dram_in("gateT", [D, FH])
    upT = pr.dram_in("upT", [D, FH])
    downT = pr.dram_in("downT", [FH, D])
    fcT = pr.dram_in("fcT", [D, FFH])
    ln_wT = pr.dram_in("ln_wT", [128, 32])
    ln_bT = pr.dram_in("ln_bT", [128, 32])
    projT = pr.dram_in("projT", [FFH, D])
    W_in1 = pr.dram_in("W_in1", [D, 2 * IN2])
    ident = pr.dram_in("ident", [128, 128], BF16)
    o_x2 = pr.dram_out("o_x2", [TC, D])
    o_raw = pr.dram_out("o_raw", [IN2, TC], BF16)
    o_g2 = pr.dram_out("o_g2", [IN2, TC], BF16)
    fcQ = pr.nc.dram_tensor("fcQ", [D, FFH], BF16).ap()
    pjQ = pr.nc.dram_tensor("pjQ", [FFH, D], BF16).ap()
    if DEBUG_L5:
        o_dbg_s2 = pr.dram_out("o_dbg_s2", [128, D])
        o_dbg_x2a = pr.dram_out("o_dbg_x2a", [128, D])
        o_dbg_fcq = pr.dram_out("o_dbg_fcq", [128, FFH], BF16)
        o_dbg_thr = pr.dram_out("o_dbg_thr", [1, 8])
        o_dbg_xn3 = pr.dram_out("o_dbg_xn3", [128, 512], BF16)
        o_dbg_g2 = pr.dram_out("o_dbg_g2", [128, 512], BF16)
        o_dbg_ha = pr.dram_out("o_dbg_ha", [128, D])
        o_dbg_wc = pr.dram_out("o_dbg_wc", [128, D], BF16)
        o_dbg_g2b = pr.dram_out("o_dbg_g2b", [128, 512], BF16)

    FB = fblocks(FH)

    def body(pr):
        s_oT = [pr.sbuf(f"s_oT{j}", [128, D], BF16) for j in range(8)]
        s_xnT = [pr.sbuf(f"s_xnT{j}", [128, 512], BF16) for j in range(8)]
        s_ctxT = pr.sbuf("s_ctxT", [128, 8 * 128], BF16)
        pool = [pr.sbuf(f"pool{j}", [128, 512], BF16) for j in range(32)]
        wch = [pr.sbuf(f"wch{j}", [128, 1024], BF16) for j in range(16)]
        s_s2 = [pr.sbuf(f"s_s2{i}", [128, D], F32) for i in range(4)]
        s_x2a = [pr.sbuf(f"s_x2a{i}", [128, D], F32) for i in range(4)]
        s_x2 = [pr.sbuf(f"s_x2{i}", [128, D], F32) for i in range(4)]
        s_ld = [pr.sbuf(f"s_ldS{i}", [128, D], F32) for i in range(1)]
        s_xr = [pr.sbuf(f"s_xr{i}", [128, D], F32) for i in range(4)]
        s_cx = [pr.sbuf(f"s_cx{i}", [128, D], BF16) for i in range(1)]
        sb = {"sq": pr.sbuf("s_sq", [128, D], F32), "st": pr.sbuf("s_st", [128, 8], F32),
              "ln": pr.sbuf("s_ln", [128, D], F32)}
        s_nbf = pr.sbuf("s_nbf", [128, D], BF16)
        s_cst = pr.sbuf("s_cst", [128, 2], F32)
        s_wff = pr.sbuf("s_wff", [128, 8], F32)
        s_lnw = pr.sbuf("s_lnw", [128, 32], F32)
        s_lnb = pr.sbuf("s_lnb", [128, 32], F32)
        s_id = pr.sbuf("s_id", [128, 128], BF16)
        s_onesb = pr.sbuf("s_onesb", [128, 1], BF16)
        s_ones1 = pr.sbuf("s_ones1", [1, 128], F32)
        s_onc = pr.sbuf("s_onc", [128, 1], F32)
        s_gw = [pr.sbuf(f"s_gw{i}", [128, 8 * 128], BF16) for i in range(2)]
        s_uw = [pr.sbuf(f"s_uw{i}", [128, 8 * 128], BF16) for i in range(2)]
        s_fw = [pr.sbuf(f"s_fw{i}", [128, 8 * 128], BF16) for i in range(2)]
        s_wi = [pr.sbuf(f"s_wi{i}", [128, 8 * 128], BF16) for i in range(2)]
        s_qs = [pr.sbuf(f"s_qs{i}", [128, 512], F32) for i in range(2)]
        s_sg = pr.sbuf("s_sg", [128, 512], BF16)
        s_tsc = pr.sbuf("s_tsc", [128, 512], F32)
        s_mu = pr.sbuf("s_mu", [128, 512], F32)
        s_rs = pr.sbuf("s_rs", [128, 512], F32)
        s_row = pr.sbuf("s_row", [1, 1024], F32)
        s_acc = pr.sbuf("s_acc", [128, 4], F32)
        s_thr = pr.sbuf("s_thr", [128, 4], F32)
        s_g11 = pr.sbuf("s_g11", [1, 8], F32)
        s_stg = [pr.sbuf(f"s_stg{i}", [128, 512], BF16) for i in range(2)]
        p_tr = pr.psum("p_tr", [128, 128], BF16)
        p_mm = [pr.psum(f"p_mm{i}", [128, 512]) for i in range(2)]
        p_st = [pr.psum(f"p_st{i}", [1, 512]) for i in range(2)]
        p_bc = pr.psum("p_bc", [128, 512])
        p_g = pr.psum("p_g", [1, 1])

        d_e0 = pr.op("vector", lambda v: v.memset(s_cst[:, 0:1], LN_EPS))
        d_e1 = pr.op("vector", lambda v: v.memset(s_cst[:, 1:2], RMS_EPS))
        d_ob = pr.op("vector", lambda v: v.memset(s_onesb[:], 1.0))
        d_o1 = pr.op("vector", lambda v: v.memset(s_ones1[:], 1.0))
        d_oc = pr.op("vector", lambda v: v.memset(s_onc[:], 1.0))
        CON = pr.dma_group("consts")
        ld_id = pr.op("sync", lambda s: s.dma_start(out=s_id[:], in_=ident[:]), dma=True, group=CON)
        ld_wff = pr.op("sync", lambda s: s.dma_start(out=s_wff[:], in_=w_ffnT[:]), dma=True, group=CON)
        ld_lnw = pr.op("sync", lambda s: s.dma_start(out=s_lnw[:], in_=ln_wT[:]), dma=True, group=CON)
        ld_lnb = pr.op("sync", lambda s: s.dma_start(out=s_lnb[:], in_=ln_bT[:]), dma=True, group=CON)
        WCON = pr.dma_group("wconsts")
        ld_oT = [pr.op("gpsimd", lambda g, j=j: g.dma_start(
            out=s_oT[j][:], in_=oT[j * 128:(j + 1) * 128, :]), dma=True, group=WCON) for j in range(8)]

        slabs = []
        for j in range(8):
            for c in range(8):
                slabs.append((fcT, fcQ, j * 128, c * 512, 0))
        for j in range(32):
            for c in range(2):
                slabs.append((projT, pjQ, j * 128, c * 512, 1))
        d_z = pr.op("vector", lambda v: v.memset(s_acc[:, :], 0.0))
        prev_use = {0: [], 1: []}
        d_accl = [d_z]
        for i, (mat, dst, r0, c0, mi) in enumerate(slabs):
            qb = s_qs[i % 2]
            ld = pr.op("sync", lambda s, mat=mat, r0=r0, c0=c0, qb=qb: s.dma_start(
                out=qb[:], in_=mat[r0:r0 + 128, c0:c0 + 512]),
                deps=prev_use[i % 2], dma=True, group=pr.dma_group(f"q{i % 2}", stream=True))
            d_r = pr.op("vector", lambda v, qb=qb: v.tensor_reduce(
                s_acc[:, 2:3], qb[:], axis=AX.X, op=ALU.add, apply_absolute_value=True),
                deps=[ld] + d_accl)
            d_a = pr.op("vector", lambda v, mi=mi: v.tensor_tensor(
                s_acc[:, mi:mi + 1], s_acc[:, mi:mi + 1], s_acc[:, 2:3], op=ALU.add), deps=[d_r])
            d_accl = [d_a]
            prev_use[i % 2] = [d_r]
        thr_deps = []
        for mi in range(2):
            d_t = pr.op("tensor", lambda t, mi=mi: t.matmul(
                p_g[:, :], s_acc[:, mi:mi + 1], s_onc[:, 0:1], start=True, stop=True),
                deps=d_accl + [d_oc] + thr_deps)
            d_g1 = pr.op("vector", lambda v, mi=mi: v.tensor_scalar(
                s_g11[0:1, mi:mi + 1], p_g[0:1, 0:1], 1.0 / (D * FFH), 1e-6,
                op0=ALU.mult, op1=ALU.add), deps=[d_t])
            thr_deps.append(d_g1)
        d_th2 = pr.op("vector", lambda v: v.tensor_scalar(
            s_g11[0:1, 4:6], s_g11[0:1, 0:2], 0.5, None, op0=ALU.mult), deps=thr_deps)
        d_th3 = pr.op("vector", lambda v: v.tensor_scalar(
            s_g11[0:1, 6:8], s_g11[0:1, 4:6], -1.0, None, op0=ALU.mult), deps=[d_th2])
        d_tb = pr.op("tensor", lambda t: t.matmul(
            p_bc[:, 0:4], s_ones1[0:1, :], s_g11[0:1, 4:8], start=True, stop=True),
            deps=[d_th3, d_o1])
        d_te = pr.op("vector", lambda v: v.tensor_copy(s_thr[:, :], p_bc[:, 0:4]), deps=[d_tb])
        prev_use = {0: [], 1: []}
        prev_qst = None
        qst_last = None
        for i, (mat, dst, r0, c0, mi) in enumerate(slabs):
            qb = s_qs[i % 2]
            ld = pr.op("sync", lambda s, mat=mat, r0=r0, c0=c0, qb=qb: s.dma_start(
                out=qb[:], in_=mat[r0:r0 + 128, c0:c0 + 512]),
                deps=prev_use[i % 2], dma=True, group=pr.dma_group(f"q{i % 2}", stream=True))
            d_p = pr.op("vector", lambda v, qb=qb, mi=mi: v.tensor_scalar(
                s_mu[:], qb[:], s_thr[:, mi:mi + 1], None, op0=ALU.is_gt), deps=[ld, d_te])
            d_n = pr.op("vector", lambda v, qb=qb, mi=mi: v.tensor_scalar(
                s_rs[:], qb[:], s_thr[:, 2 + mi:3 + mi], None, op0=ALU.is_lt), deps=[ld, d_te])
            d_q = pr.op("vector", lambda v: v.tensor_tensor(
                s_sg[:], s_mu[:], s_rs[:], op=ALU.subtract),
                deps=[d_p, d_n] + ([prev_qst] if prev_qst else []))
            d_s = pr.op("sync", lambda s, dst=dst, r0=r0, c0=c0: s.dma_start(
                out=dst[r0:r0 + 128, c0:c0 + 512], in_=s_sg[:]), deps=[d_q],
                dma=True, group=pr.dma_group("qst", stream=True))
            prev_qst = d_s
            qst_last = d_s
            prev_use[i % 2] = [d_p, d_n]

        if DEBUG_L5:
            dth = pr.op("sync", lambda s: s.dma_start(out=o_dbg_thr[:, :], in_=s_g11[0:1, :]),
                        deps=[d_th3], dma=True, group=pr.dma_group("dbg", stream=True))
            dfq = pr.op("sync", lambda s: s.dma_start(out=o_dbg_fcq[:, :], in_=fcQ[0:128, :]),
                        deps=[qst_last, dth], dma=True, group=pr.dma_group("dbg", stream=True))
        gates = {
            "xnT": [],
            "pool": [],
            "wch": [qst_last],
            "bank": [None, None],
            "stg": [None, None],
            "x2st": [None] * 4,
            "ring": {},
            "stats": [],
            "sg": [qst_last],
            "tr": None,
            "nbf": [],
            "ctxT": [],
        }
        all_stores = []
        WIN_FB = fblocks(2 * IN2)
        _dbg_dd = [None]

        def mk_ld(buf, srcfn, grp, slot_key, extra=(), eng="sync"):
            dep = gates["ring"].get(slot_key, [])
            d = pr.op(eng, srcfn, deps=list(dep) + list(extra), dma=True,
                      group=pr.dma_group(grp, stream=True))
            return d

        for tch in range(4):
            tc0 = tch * 512
            xn_evs = []
            for t4 in range(4):
                ti = tch * 4 + t4
                sl = s_ld[0]; xr = s_xr[t4]; cx = s_cx[0]
                ld_s = mk_ld(sl, (lambda s, sl=sl, ti=ti: s.dma_start(
                    out=sl[:], in_=s_in[ti * 128:(ti + 1) * 128, :])), "lds0", ("s", 0))
                ld_x = mk_ld(xr, (lambda s, xr=xr, ti=ti: s.dma_start(
                    out=xr[:], in_=x_in[ti * 128:(ti + 1) * 128, :])), f"ldx{t4}", ("x", t4))
                ld_c = mk_ld(cx, (lambda s, cx=cx, ti=ti: s.dma_start(
                    out=cx[:], in_=ctx_in[ti * 128:(ti + 1) * 128, :])), "ldc0", ("c", 0))
                ctx_evs = []
                for j in range(8):
                    d_tp = pr.op("tensor", lambda t, j=j, cx=cx: t.transpose(
                        p_tr[:, :], cx[:, j * 128:(j + 1) * 128], s_id[:]),
                        deps=[ld_c, ld_id] + ([gates["tr"]] if gates["tr"] else []) + gates["ctxT"])
                    d_ev = pr.op("vector", lambda v, j=j: v.tensor_copy(
                        s_ctxT[:, j * 128:(j + 1) * 128], p_tr[:, :]), deps=[d_tp])
                    gates["tr"] = d_ev
                    ctx_evs.append(d_ev)
                gates["ring"][("c", 0)] = [ctx_evs[-1]]
                gates["ctxT"] = []
                ev_dm = []
                for dm in range(2):
                    d_mml = None
                    for k in range(8):
                        d_mml = pr.op("tensor", lambda t, k=k, dm=dm, first=(k == 0): t.matmul(
                            p_mm[dm][:, :], s_ctxT[:, k * 128:(k + 1) * 128],
                            s_oT[k][:, dm * 512:(dm + 1) * 512], start=first, stop=(k == 7)),
                            deps=([ctx_evs[-1], ld_oT[-1]] +
                                  ([gates["bank"][dm]] if gates["bank"][dm] else [])) if k == 0 else [d_mml])
                    d_ev = pr.op("vector", lambda v, dm=dm, t4=t4, sl=sl: v.tensor_tensor(
                        s_s2[t4][:, dm * 512:(dm + 1) * 512], p_mm[dm][:, :],
                        sl[:, dm * 512:(dm + 1) * 512], op=ALU.add), deps=[d_mml, ld_s])
                    gates["bank"][dm] = d_ev
                    ev_dm.append(d_ev)
                gates["ctxT"] = [gates["bank"][0], gates["bank"][1]]
                gates["ring"][("s", 0)] = [ev_dm[0], ev_dm[1]]
                if DEBUG_L5 and tch == 0 and t4 == 0:
                    dd = pr.op("sync", lambda s, t4=t4: s.dma_start(out=o_dbg_s2[:, :], in_=s_s2[t4][:, :]),
                               deps=ev_dm + [dfq], dma=True, group=pr.dma_group("dbg", stream=True))
                    _dbg_dd[0] = dd
                    gates["ring"][("s", 0)] = [ev_dm[0], ev_dm[1], dd]
                    gates["ctxT"] = gates["ctxT"] + [dd]
                    all_stores.append(dd)
                d_xn = emit_rms_tile(pr, s_s2[t4][:, :], s_nbf[:, :], 128, sb,
                                     ev_dm + gates["nbf"], s_cst[:, 1:2])
                evs, last_ev = emit_transpose_cols(
                    pr, lambda j, _b=s_nbf: _b[:, j * 128:(j + 1) * 128], s_xnT, 128, t4 * 128,
                    p_tr, s_id, [d_xn, ld_wff] + gates["xnT"], wcolT=s_wff,
                    prev_evac=gates["tr"])
                gates["tr"] = last_ev
                gates["nbf"] = [evs[-1]]
                xn_evs.append(last_ev)
            gates["xnT"] = []
            d_xn_all = xn_evs[-1]
            if DEBUG_L5 and tch == 0:
                ddn = pr.op("sync", lambda s: s.dma_start(out=o_dbg_xn3[:, :], in_=s_xnT[0][:, :]),
                            deps=[d_xn_all, _dbg_dd[0]], dma=True, group=pr.dma_group("dbg", stream=True))
                gates["xnT"] = [ddn]
                all_stores.append(ddn)

            g2_evs = []
            for fbi, (fo, fw) in enumerate(FB):
                gw = s_gw[fbi % 2]; uw = s_uw[fbi % 2]
                ld_g = mk_ld(gw, (lambda s, gw=gw, fo=fo, fw=fw: s.dma_start(
                    out=gw[:, 0:8 * fw].rearrange("p (a c) -> p a c", a=8),
                    in_=gateT[:, fo:fo + fw].rearrange("(a p) c -> p a c", p=128))),
                    f"gw{fbi % 2}", ("gw", fbi % 2), eng="gpsimd")
                ld_u = mk_ld(uw, (lambda s, uw=uw, fo=fo, fw=fw: s.dma_start(
                    out=uw[:, 0:8 * fw].rearrange("p (a c) -> p a c", a=8),
                    in_=upT[:, fo:fo + fw].rearrange("(a p) c -> p a c", p=128))),
                    f"uw{fbi % 2}", ("uw", fbi % 2), eng="gpsimd")
                d_m0 = None
                for k in range(8):
                    d_m0 = pr.op("tensor", lambda t, k=k, fw=fw, gw=gw, first=(k == 0): t.matmul(
                        p_mm[0][0:fw, :], gw[:, k * fw:(k + 1) * fw], s_xnT[k][:, :],
                        start=first, stop=(k == 7)),
                        deps=([ld_g, d_xn_all] + ([gates["bank"][0]] if gates["bank"][0] else [])) if k == 0 else [d_m0])
                d_si = pr.op("scalar", lambda sc, fw=fw: sc.activation(
                    s_sg[0:fw, :], p_mm[0][0:fw, :], AF.Silu), deps=[d_m0] + gates["sg"])
                gates["bank"][0] = d_si
                d_m1 = None
                for k in range(8):
                    d_m1 = pr.op("tensor", lambda t, k=k, fw=fw, uw=uw, first=(k == 0): t.matmul(
                        p_mm[1][0:fw, :], uw[:, k * fw:(k + 1) * fw], s_xnT[k][:, :],
                        start=first, stop=(k == 7)),
                        deps=([ld_u, d_xn_all] + ([gates["bank"][1]] if gates["bank"][1] else [])) if k == 0 else [d_m1])
                d_g2 = pr.op("vector", lambda v, fbi=fbi, fw=fw: v.tensor_tensor(
                    pool[fbi][0:fw, :], s_sg[0:fw, :], p_mm[1][0:fw, :], op=ALU.mult),
                    deps=[d_m1, d_si] + gates["pool"])
                gates["bank"][1] = d_g2
                gates["sg"] = [d_g2]
                gates["ring"][("gw", fbi % 2)] = [d_m0]
                gates["ring"][("uw", fbi % 2)] = [d_m1]
                g2_evs.append(d_g2)
                if DEBUG_L5 and tch == 0 and fbi == 0:
                    ddg = pr.op("sync", lambda s: s.dma_start(out=o_dbg_g2[:, :], in_=pool[0][:, :]),
                                deps=[d_g2, ddn], dma=True, group=pr.dma_group("dbg", stream=True))
                    gates["pool"] = gates["pool"] + [ddg]
                    all_stores.append(ddg)
                if DEBUG_L5 and tch == 0 and fbi == 13:
                    ddg2 = pr.op("sync", lambda s: s.dma_start(out=o_dbg_g2b[:, :], in_=pool[13][:, :]),
                                 deps=[d_g2, ddg], dma=True, group=pr.dma_group("dbg", stream=True))
                    gates["pool"] = gates["pool"] + [ddg2]
                    all_stores.append(ddg2)
            gates["pool"] = []
            d_g2_all = g2_evs[-1]

            for half in range(2):
                fbs = FB[half * 11:(half + 1) * 11]
                dwl = []
                for z, (fo, fw) in enumerate(fbs):
                    dwl.append(pr.op("gpsimd", lambda g, z=z, fo=fo, fw=fw: g.dma_start(
                        out=wch[z][0:fw, :], in_=downT[fo:fo + fw, :]),
                        deps=gates["wch"], dma=True, group=pr.dma_group(f"dw{tch}_{half}")))
                gates["wch"] = []
                if DEBUG_L5 and tch == 0 and half == 0:
                    ddwc = pr.op("sync", lambda s: s.dma_start(out=o_dbg_wc[:, :], in_=wch[0][:, :]),
                                 deps=[dwl[-1], ddg2], dma=True, group=pr.dma_group("dbg", stream=True))
                    gates["wch"] = [ddwc]
                    all_stores.append(ddwc)
                wch_cons = []
                for t4 in range(4):
                    for dm in range(2):
                        d_mml = None
                        for z, (fo, fw) in enumerate(fbs):
                            d_mml = pr.op("tensor", lambda t, z=z, fw=fw, t4=t4, dm=dm, half=half, first=(z == 0), last=(z == len(fbs) - 1): t.matmul(
                                p_mm[dm][:, :], pool[half * 11 + z][0:fw, t4 * 128:(t4 + 1) * 128],
                                wch[z][0:fw, dm * 512:(dm + 1) * 512], start=first, stop=last),
                                deps=([dwl[-1], d_g2_all] + ([gates["bank"][dm]] if gates["bank"][dm] else [])) if z == 0 else [d_mml])
                        wch_cons.append(d_mml)
                        if half == 0:
                            d_ev = pr.op("vector", lambda v, t4=t4, dm=dm: v.tensor_tensor(
                                s_x2a[t4][:, dm * 512:(dm + 1) * 512], p_mm[dm][:, :],
                                s_s2[t4][:, dm * 512:(dm + 1) * 512], op=ALU.add), deps=[d_mml])
                        else:
                            xr = s_xr[t4]
                            d_t1 = pr.op("vector", lambda v, t4=t4, dm=dm: v.tensor_tensor(
                                s_tsc[:, :], p_mm[dm][:, :], s_x2a[t4][:, dm * 512:(dm + 1) * 512],
                                op=ALU.add), deps=[d_mml] + gates["stats"])
                            d_ev = pr.op("vector", lambda v, t4=t4, dm=dm, xr=xr: v.tensor_tensor(
                                s_x2a[t4][:, dm * 512:(dm + 1) * 512], s_tsc[:, :],
                                xr[:, dm * 512:(dm + 1) * 512], op=ALU.add), deps=[d_t1])
                            gates["stats"] = [d_ev]
                            gates["ring"][("x", t4)] = [d_ev]
                        gates["bank"][dm] = d_ev
                if DEBUG_L5 and tch == 0 and half == 0:
                    ddha = pr.op("sync", lambda s: s.dma_start(out=o_dbg_ha[:, :], in_=s_x2a[0][:, :]),
                                 deps=[gates["bank"][0], gates["bank"][1], ddg2], dma=True,
                                 group=pr.dma_group("dbg", stream=True))
                    gates["bank"][0] = ddha
                    all_stores.append(ddha)
                gates["wch"] = wch_cons
            gates["stats"] = []

            if DEBUG_L5 and tch == 0:
                ddx = pr.op("sync", lambda s: s.dma_start(out=o_dbg_x2a[:, :], in_=s_x2a[0][:, :]),
                            deps=[gates["bank"][0], gates["bank"][1], ddha, ddwc], dma=True, group=pr.dma_group("dbg", stream=True))
                gates["nbf"] = gates["nbf"] + [ddx]
                all_stores.append(ddx)
            xn_evs = []
            for t4 in range(4):
                d_ln = emit_ln_tile(pr, s_x2a[t4][:, :], 128, sb,
                                    [gates["bank"][0], gates["bank"][1]] + gates["nbf"], s_cst[:, 0:1])
                d_xb = pr.op("vector", lambda v, t4=t4: v.tensor_copy(s_nbf[:, :], sb["ln"][:, :]),
                             deps=[d_ln] + gates["nbf"])
                evs, last_ev = emit_transpose_cols(
                    pr, lambda j, _b=s_nbf: _b[:, j * 128:(j + 1) * 128], s_xnT, 128, t4 * 128,
                    p_tr, s_id, [d_xb] + gates["xnT"], wcolT=None, prev_evac=gates["tr"])
                gates["tr"] = last_ev
                gates["nbf"] = [evs[-1]]
                xn_evs.append(last_ev)
            gates["xnT"] = []
            d_xn_all = xn_evs[-1]

            stat_mm = [None, None]
            h_evs = []
            sq_gate = gates["sg"]
            for fb in range(32):
                fw_ = s_fw[fb % 2]
                ld_f = mk_ld(fw_, (lambda s, fw_=fw_, fb=fb: s.dma_start(
                    out=fw_[:, :].rearrange("p (a c) -> p a c", a=8),
                    in_=fcQ[:, fb * 128:(fb + 1) * 128].rearrange("(a p) c -> p a c", p=128))),
                    f"fw{fb % 2}", ("fw", fb % 2))
                d_mml = None
                for k in range(8):
                    d_mml = pr.op("tensor", lambda t, k=k, fw_=fw_, fb=fb, first=(k == 0): t.matmul(
                        p_mm[fb % 2][:, :], fw_[:, k * 128:(k + 1) * 128], s_xnT[k][:, :],
                        start=first, stop=(k == 7)),
                        deps=([ld_f, d_xn_all] + ([gates["bank"][fb % 2]] if gates["bank"][fb % 2] else [])) if k == 0 else [d_mml])
                d_h = pr.op("scalar", lambda sc, fb=fb: sc.activation(
                    pool[fb][:, :], p_mm[fb % 2][:, :], AF.Gelu), deps=[d_mml] + gates["pool"])
                gates["bank"][fb % 2] = d_h
                gates["ring"][("fw", fb % 2)] = [d_mml]
                d_s1 = pr.op("tensor", lambda t, fb=fb, first=(fb == 0): t.matmul(
                    p_st[0][:, :], s_onesb[:, 0:1], pool[fb][:, :], start=first, stop=(fb == 31)),
                    deps=[d_h, d_ob] + ([stat_mm[0]] if stat_mm[0] else []) + gates["strow"] if False else [d_h, d_ob] + gates.get("strow0", []))
                stat_mm[0] = d_s1
                d_sq = pr.op("scalar", lambda sc, fb=fb: sc.activation(
                    s_sg[:, :], pool[fb][:, :], AF.Square), deps=[d_h] + sq_gate)
                d_s2m = pr.op("tensor", lambda t, fb=fb, first=(fb == 0): t.matmul(
                    p_st[1][:, :], s_onesb[:, 0:1], s_sg[:, :], start=first, stop=(fb == 31)),
                    deps=[d_sq] + gates.get("strow1", []))
                sq_gate = [d_s2m]
                stat_mm[1] = d_s2m
                h_evs.append(d_h)
            gates["sg"] = sq_gate
            gates["pool"] = []
            d_r0 = pr.op("vector", lambda v: v.tensor_scalar(
                s_row[0:1, 0:512], p_st[0][0:1, :], 1.0 / FFH, None, op0=ALU.mult), deps=[stat_mm[0]])
            d_r1 = pr.op("vector", lambda v: v.tensor_scalar(
                s_row[0:1, 512:1024], p_st[1][0:1, :], 1.0 / FFH, None, op0=ALU.mult), deps=[stat_mm[1]])
            gates["strow0"] = [d_r0]
            gates["strow1"] = [d_r1]
            d_m2 = pr.op("vector", lambda v: v.tensor_tensor(
                s_tsc[0:1, 0:512], s_row[0:1, 0:512], s_row[0:1, 0:512], op=ALU.mult),
                deps=[d_r0] + gates["stats"])
            d_var = pr.op("vector", lambda v: v.tensor_tensor(
                s_tsc[0:1, 0:512], s_row[0:1, 512:1024], s_tsc[0:1, 0:512], op=ALU.subtract), deps=[d_m2, d_r1])
            d_l = pr.op("scalar", lambda sc: sc.activation(
                s_tsc[0:1, 0:512], s_tsc[0:1, 0:512], AF.Ln, bias=s_cst[0:1, 0:1]), deps=[d_var, d_e0])
            d_e = pr.op("scalar", lambda sc: sc.activation(
                s_tsc[0:1, 0:512], s_tsc[0:1, 0:512], AF.Exp, scale=-0.5), deps=[d_l])
            d_bm = pr.op("tensor", lambda t: t.matmul(
                p_bc[:, :], s_ones1[0:1, :], s_row[0:1, 0:512], start=True, stop=True),
                deps=[d_r0, d_o1, d_te])
            d_mb = pr.op("vector", lambda v: v.tensor_copy(s_mu[:, :], p_bc[:, :]), deps=[d_bm, qst_last])
            d_br = pr.op("tensor", lambda t: t.matmul(
                p_bc[:, :], s_ones1[0:1, :], s_tsc[0:1, 0:512], start=True, stop=True),
                deps=[d_e, d_mb])
            d_rb = pr.op("vector", lambda v: v.tensor_copy(s_rs[:, :], p_bc[:, :]), deps=[d_br])
            h2_evs = []
            prev_tsc = []
            for fb in range(32):
                d_1 = pr.op("vector", lambda v, fb=fb: v.tensor_tensor(
                    s_tsc[:, :], pool[fb][:, :], s_mu[:, :], op=ALU.subtract),
                    deps=[h_evs[fb], d_mb, d_br] + prev_tsc)
                d_2 = pr.op("vector", lambda v: v.tensor_tensor(
                    s_tsc[:, :], s_tsc[:, :], s_rs[:, :], op=ALU.mult), deps=[d_1, d_rb])
                d_3 = pr.op("vector", lambda v, fb=fb: v.tensor_scalar(
                    pool[fb][:, :], s_tsc[:, :], s_lnw[:, fb:fb + 1], s_lnb[:, fb:fb + 1],
                    op0=ALU.mult, op1=ALU.add), deps=[d_2, ld_lnw, ld_lnb])
                prev_tsc = [d_3]
                h2_evs.append(d_3)
            d_h2_all = h2_evs[-1]

            for dm in range(2):
                pjl = []
                for k in range(32):
                    pjl.append(pr.op("sync", lambda s, k=k, dm=dm: s.dma_start(
                        out=wch[k // 2][:, (k % 2) * 512:(k % 2 + 1) * 512],
                        in_=pjQ[k * 128:(k + 1) * 128, dm * 512:(dm + 1) * 512]),
                        deps=gates["wch"], dma=True, group=pr.dma_group(f"pj{tch}_{dm}")))
                gates["wch"] = []
                wch_cons = []
                for t4 in range(4):
                    d_mml = None
                    for k in range(32):
                        d_mml = pr.op("tensor", lambda t, k=k, t4=t4, dm=dm, first=(k == 0): t.matmul(
                            p_mm[t4 % 2][:, :], pool[k][:, t4 * 128:(t4 + 1) * 128],
                            wch[k // 2][:, (k % 2) * 512:(k % 2 + 1) * 512],
                            start=first, stop=(k == 31)),
                            deps=([pjl[-1], d_h2_all] + ([gates["bank"][t4 % 2]] if gates["bank"][t4 % 2] else [])) if k == 0 else [d_mml])
                        if k == 0:
                            wch_cons.append(d_mml)
                    d_ev = pr.op("vector", lambda v, t4=t4, dm=dm: v.tensor_tensor(
                        s_x2[t4][:, dm * 512:(dm + 1) * 512], p_mm[t4 % 2][:, :],
                        s_x2a[t4][:, dm * 512:(dm + 1) * 512], op=ALU.add),
                        deps=[d_mml] + ([gates["x2st"][t4]] if gates["x2st"][t4] else []))
                    gates["bank"][t4 % 2] = d_ev
                    wch_cons.append(d_mml)
                    if dm == 1:
                        ti = tch * 4 + t4
                        d_st = pr.op("sync", lambda s, ti=ti, t4=t4: s.dma_start(
                            out=o_x2[ti * 128:(ti + 1) * 128, :], in_=s_x2[t4][:, :]),
                            deps=[gates["bank"][t4 % 2], d_ev], dma=True,
                            group=pr.dma_group(f"sx{t4}", stream=True))
                        gates["x2st"][t4] = d_st
                        all_stores.append(d_st)
                gates["wch"] = wch_cons

            xn_evs = []
            for t4 in range(4):
                d_ln = emit_ln_tile(pr, s_x2[t4][:, :], 128, sb,
                                    [gates["bank"][0], gates["bank"][1]] + gates["nbf"], s_cst[:, 0:1])
                d_xb = pr.op("vector", lambda v, t4=t4: v.tensor_copy(s_nbf[:, :], sb["ln"][:, :]),
                             deps=[d_ln] + gates["nbf"])
                evs, last_ev = emit_transpose_cols(
                    pr, lambda j, _b=s_nbf: _b[:, j * 128:(j + 1) * 128], s_xnT, 128, t4 * 128,
                    p_tr, s_id, [d_xb] + gates["xnT"], wcolT=None, prev_evac=gates["tr"])
                gates["tr"] = last_ev
                gates["nbf"] = [evs[-1]]
                xn_evs.append(last_ev)
            gates["xnT"] = []
            d_xn_all = xn_evs[-1]
            win_cons = []
            for fb in range(32):
                wi = s_wi[fb % 2]
                ld_w = mk_ld(wi, (lambda s, wi=wi, fb=fb: s.dma_start(
                    out=wi[:, :].rearrange("p (a c) -> p a c", a=8),
                    in_=W_in1[:, fb * 128:(fb + 1) * 128].rearrange("(a p) c -> p a c", p=128))),
                    f"wi{fb % 2}", ("wi", fb % 2), eng="gpsimd")
                d_mml = None
                for k in range(8):
                    d_mml = pr.op("tensor", lambda t, k=k, wi=wi, fb=fb, first=(k == 0): t.matmul(
                        p_mm[fb % 2][:, :], wi[:, k * 128:(k + 1) * 128], s_xnT[k][:, :],
                        start=first, stop=(k == 7)),
                        deps=([ld_w, d_xn_all] + ([gates["bank"][fb % 2]] if gates["bank"][fb % 2] else [])) if k == 0 else [d_mml])
                gates["ring"][("wi", fb % 2)] = [d_mml]
                win_cons.append(d_mml)
                stg = s_stg[fb % 2]
                d_ev = pr.op("scalar", lambda sc, stg=stg, fb=fb: sc.activation(
                    stg[:, :], p_mm[fb % 2][:, :], AF.Identity),
                    deps=[d_mml] + ([gates["stg"][fb % 2]] if gates["stg"][fb % 2] else []))
                gates["bank"][fb % 2] = d_ev
                dst = o_raw if fb < 16 else o_g2
                ro = (fb if fb < 16 else fb - 16) * 128
                d_st = pr.op("sync", lambda s, dst=dst, ro=ro, tc0=tc0, stg=stg: s.dma_start(
                    out=dst[ro:ro + 128, tc0:tc0 + 512], in_=stg[:, :]), deps=[d_ev],
                    dma=True, group=pr.dma_group(f"so{fb % 2}", stream=True))
                gates["stg"][fb % 2] = d_st
                all_stores.append(d_st)
            gates["xnT"] = win_cons
            gates["pool"] = []

        dbg_deps = []
        if DEBUG_L5:
            dbg_deps = [dth, dfq]
        pr.op("vector", lambda v: v.engine_nop(), deps=all_stores[-4:] + [gates["x2st"][t] for t in range(4)] + dbg_deps)

    nc = pr.build(body)
    return nc


def _l5_inmaps(inputs, s_full, ctx_full):
    common = {
        "oT": npT(inputs["L0_attn_o_w"]),
        "w_ffnT": npT(np.asarray(inputs["L0_attn_norm_ffn_w"]).reshape(8, 128)),
        "gateT": npT(inputs["L0_attn_gate_w"]),
        "upT": npT(inputs["L0_attn_up_w"]),
        "downT": npT(inputs["L0_attn_down_w"]),
        "fcT": npT(inputs["L0_ffn_W_fc"]),
        "ln_wT": npT(np.asarray(inputs["L0_ffn_ln_w"]).reshape(32, 128)),
        "ln_bT": npT(np.asarray(inputs["L0_ffn_ln_b"]).reshape(32, 128)),
        "projT": npT(inputs["L0_ffn_W_proj"]),
        "W_in1": np.ascontiguousarray(np.asarray(inputs["L1_attn_W_in"])),
        "ident": IDENT,
    }
    x = np.asarray(inputs["L0_attn_x"])
    maps = []
    for c in range(8):
        b, h = c // 2, c % 2
        t0 = h * TC
        maps.append({
            "s_in": np.ascontiguousarray(s_full[b, t0:t0 + TC]),
            "ctx_in": np.ascontiguousarray(ctx_full[b, t0:t0 + TC]),
            "x_in": np.ascontiguousarray(x[b, t0:t0 + TC]),
            **common,
        })
    return maps


def run_l5(inputs, s_full, ctx_full):
    nc = build_l5()
    res = _run(nc, _l5_inmaps(inputs, s_full, ctx_full))
    x2 = np.stack([np.concatenate([res[2 * b]["o_x2"], res[2 * b + 1]["o_x2"]], axis=0) for b in range(B)])
    raw = np.stack([np.concatenate([res[2 * b]["o_raw"], res[2 * b + 1]["o_raw"]], axis=1) for b in range(B)])
    g2 = np.stack([np.concatenate([res[2 * b]["o_g2"], res[2 * b + 1]["o_g2"]], axis=1) for b in range(B)])
    return x2, raw, g2





def build_l6():
    pr = Prog("l6")
    TP = TC + 3
    rawp = pr.dram_in("rawp", [IN2, TP], BF16)
    conv_ck = pr.dram_in("conv_ck", [IN2, 4])
    conv_bk = pr.dram_in("conv_bk", [128, 16])
    W_x = pr.dram_in("W_x", [IN2, 96])
    W_dt = pr.dram_in("W_dt", [DTR, IN2])
    b_dt = pr.dram_in("b_dt", [128, 16])
    o_h = pr.dram_out("o_h", [IN2, TC], BF16)
    o_d = pr.dram_out("o_d", [IN2, TC])
    o_bc = pr.dram_out("o_bc", [32, TC])

    def body(pr):
        s_raw = [pr.sbuf(f"s_raw{i}", [128, TP], BF16) for i in range(2)]
        s_cw = pr.sbuf("s_cw", [128, 4 * 16], F32)
        s_cb = pr.sbuf("s_cb", [128, 16], F32)
        s_conv = pr.sbuf("s_conv", [128, TC], F32)
        s_h = [pr.sbuf(f"s_h{j}", [128, TC], BF16) for j in range(16)]
        s_wx = pr.sbuf("s_wx", [128, 96 * 16], BF16)
        s_wdt = pr.sbuf("s_wdt", [64, IN2], BF16)
        s_ts = pr.sbuf("s_ts", [64, TC], BF16)
        s_bc = pr.sbuf("s_bc", [32, TC], F32)
        s_bdt = pr.sbuf("s_bdt", [128, 16], F32)
        s_sp1 = pr.sbuf("s_sp1", [128, 512], F32)
        s_sp2 = pr.sbuf("s_sp2", [128, 512], F32)
        s_dl = pr.sbuf("s_dl", [128, 512], F32)
        p_sp = pr.psum("p_sp", [128, 512])
        p_dt = pr.psum("p_dt", [128, 512])

        CON = pr.dma_group("consts")
        ld_cw = pr.op("sync", lambda s: s.dma_start(
            out=s_cw[:, :].rearrange("p (a k) -> p a k", a=16),
            in_=conv_ck[:, :].rearrange("(a p) k -> p a k", p=128)), dma=True, group=CON)
        ld_cb = pr.op("sync", lambda s: s.dma_start(out=s_cb[:], in_=conv_bk[:]), dma=True, group=CON)
        ld_bdt = pr.op("sync", lambda s: s.dma_start(out=s_bdt[:], in_=b_dt[:]), dma=True, group=CON)
        GCON = pr.dma_group("gconsts")
        ld_wx = pr.op("gpsimd", lambda g: g.dma_start(
            out=s_wx[:, :].rearrange("p (a c) -> p a c", a=16),
            in_=W_x[:, :].rearrange("(a p) c -> p a c", p=128)), dma=True, group=GCON)
        ld_wdt = pr.op("gpsimd", lambda g: g.dma_start(out=s_wdt[:], in_=W_dt[:]), dma=True, group=GCON)

        h_done = []
        ring = {}
        prev_si = None
        for pt in range(16):
            rb = s_raw[pt % 2]
            ld_r = pr.op("sync", lambda s, rb=rb, pt=pt: s.dma_start(
                out=rb[:], in_=rawp[pt * 128:(pt + 1) * 128, :]),
                deps=ring.get(pt % 2, []), dma=True, group=pr.dma_group(f"r{pt % 2}", stream=True))
            d_c = pr.op("vector", lambda v, rb=rb, pt=pt: v.tensor_scalar(
                s_conv[:, :], rb[:, 0:TC], s_cw[:, pt * 4:pt * 4 + 1], None, op0=ALU.mult),
                deps=[ld_r, ld_cw] + ([prev_si] if prev_si else []))
            for k in range(1, 4):
                d_c = pr.op("vector", lambda v, rb=rb, pt=pt, k=k: v.scalar_tensor_tensor(
                    s_conv[:, :], rb[:, k:k + TC], s_cw[:, pt * 4 + k:pt * 4 + k + 1],
                    s_conv[:, :], op0=ALU.mult, op1=ALU.add), deps=[d_c])
            d_si = pr.op("scalar", lambda sc, pt=pt: sc.activation(
                s_h[pt][:, :], s_conv[:, :], AF.Silu, bias=s_cb[:, pt:pt + 1]),
                deps=[d_c, ld_cb])
            prev_si = d_si
            ring[pt % 2] = [d_c]
            d_st = pr.op("sync", lambda s, pt=pt: s.dma_start(
                out=o_h[pt * 128:(pt + 1) * 128, :], in_=s_h[pt][:, :]), deps=[d_si],
                dma=True, group=pr.dma_group("sth"))
            h_done.append((d_si, d_st))
        d_h_all = h_done[-1][0]

        if L6_PHASES < 2:
            pr.op("vector", lambda v: v.engine_nop(), deps=[h_done[-1][1], h_done[-1][0]])
            return
        sp_ev = []
        prev_mm = None
        for ci in range(4):
            d_mml = None
            for k in range(16):
                d_mml = pr.op("tensor", lambda t, k=k, ci=ci, first=(k == 0): t.matmul(
                    p_sp[0:96, :], s_wx[:, k * 96:(k + 1) * 96], s_h[k][:, ci * 512:(ci + 1) * 512],
                    start=first, stop=(k == 15)),
                    deps=([ld_wx, d_h_all] + sp_ev[-2:]) if k == 0 else [d_mml])
            d_e1 = pr.op("scalar", lambda sc, ci=ci: sc.activation(
                s_ts[:, ci * 512:(ci + 1) * 512], p_sp[0:64, :], AF.Identity), deps=[d_mml])
            d_e2 = pr.op("scalar", lambda sc, ci=ci: sc.activation(
                s_bc[:, ci * 512:(ci + 1) * 512], p_sp[64:96, :], AF.Identity), deps=[d_mml])
            sp_ev.append(d_e2)
            sp_ev.append(d_e1)
        d_bcst = pr.op("sync", lambda s: s.dma_start(out=o_bc[:, :], in_=s_bc[:, :]),
                       deps=sp_ev, dma=True, group=pr.dma_group("stbc"))

        if L6_PHASES < 3:
            pr.op("vector", lambda v: v.engine_nop(), deps=[d_bcst, h_done[-1][1]])
            return
        d_sts = []
        prev_cycle = []
        for pt in range(16):
            for ci in range(4):
                d_mm = pr.op("tensor", lambda t, pt=pt, ci=ci: t.matmul(
                    p_dt[:, :], s_wdt[:, pt * 128:(pt + 1) * 128],
                    s_ts[:, ci * 512:(ci + 1) * 512], start=True, stop=True),
                    deps=[ld_wdt, sp_ev[-1]] + prev_cycle)
                d_a1 = pr.op("scalar", lambda sc, pt=pt: sc.activation(
                    s_sp1[:], p_dt[:, :], AF.Abs, bias=s_bdt[:, pt:pt + 1]),
                    deps=[d_mm, ld_bdt] + prev_cycle)
                d_a2 = pr.op("scalar", lambda sc, pt=pt: sc.activation(
                    s_sp2[:], p_dt[:, :], AF.Relu, bias=s_bdt[:, pt:pt + 1]), deps=[d_mm])
                d_a3 = pr.op("scalar", lambda sc: sc.activation(
                    s_sp1[:], s_sp1[:], AF.Exp, scale=-1.0), deps=[d_a1])
                d_a4 = pr.op("scalar", lambda sc: sc.activation(
                    s_sp1[:], s_sp1[:], AF.Ln, bias=1.0), deps=[d_a3])
                d_d = pr.op("vector", lambda v: v.tensor_tensor(
                    s_dl[:], s_sp1[:], s_sp2[:], op=ALU.add), deps=[d_a4, d_a2] + prev_cycle)
                d_st = pr.op("sync", lambda s, pt=pt, ci=ci: s.dma_start(
                    out=o_d[pt * 128:(pt + 1) * 128, ci * 512:(ci + 1) * 512], in_=s_dl[:]),
                    deps=[d_d], dma=True, group=pr.dma_group("std", stream=True))
                prev_cycle = [d_st, d_d, d_a2]
                d_sts.append(d_st)

        pr.op("vector", lambda v: v.engine_nop(),
              deps=[d_sts[-1], d_bcst, h_done[-1][1]])

    nc = pr.build(body)
    return nc


def _l6_inmaps(inputs, rawT_full):
    common = {
        "conv_ck": npT(np.asarray(inputs["L1_attn_conv_w"])[:, 0, :]),
        "conv_bk": npT(np.asarray(inputs["L1_attn_conv_b"]).reshape(16, 128)),
        "W_x": np.ascontiguousarray(np.asarray(inputs["L1_attn_W_x"])),
        "W_dt": np.ascontiguousarray(np.asarray(inputs["L1_attn_W_dt"])),
        "b_dt": npT(np.asarray(inputs["L1_attn_b_dt"]).reshape(16, 128)),
    }
    maps = []
    for c in range(8):
        b, h = c // 2, c % 2
        t0 = h * TC
        if h == 0:
            rp = np.concatenate([np.zeros((IN2, 3), BF), rawT_full[b][:, 0:TC]], axis=1)
        else:
            rp = rawT_full[b][:, t0 - 3:t0 + TC]
        maps.append({"rawp": np.ascontiguousarray(rp), **common})
    return maps


def run_l6(inputs, rawT_full):
    nc = build_l6()
    res = _run(nc, _l6_inmaps(inputs, rawT_full))
    h = np.stack([np.concatenate([res[2 * b]["o_h"], res[2 * b + 1]["o_h"]], axis=1) for b in range(B)])
    d = np.stack([np.concatenate([res[2 * b]["o_d"], res[2 * b + 1]["o_d"]], axis=1) for b in range(B)])
    bc = np.stack([np.concatenate([res[2 * b]["o_bc"], res[2 * b + 1]["o_bc"]], axis=1) for b in range(B)])
    return h, d, bc


def _l7_inmaps(inputs, hT, dT, bc):
    A_log = np.asarray(inputs["L1_attn_A_log"])
    Dd = np.asarray(inputs["L1_attn_D"])
    maps = []
    for c in range(8):
        b, dh = c // 2, c % 2
        r0 = dh * 1024
        maps.append({
            "xconv": np.ascontiguousarray(hT[b, r0:r0 + 1024]),
            "ssp_bc": np.ascontiguousarray(bc[b]),
            "deltaT": np.ascontiguousarray(dT[b, r0:r0 + 1024]),
            "A_log": np.ascontiguousarray(A_log[r0:r0 + 1024]),
            "D_skipT": npT(Dd[r0:r0 + 1024].reshape(8, 128)),
        })
    return maps


def run_l7(inputs, hT, dT, bc):
    nc = build_l2(delta_input=True)
    res = _run(nc, _l7_inmaps(inputs, hT, dT, bc))
    y = np.stack([np.concatenate([res[2 * b]["o_y"], res[2 * b + 1]["o_y"]], axis=0) for b in range(B)])
    return y





def build_l8():
    pr = Prog("l8")
    x2_in = pr.dram_in("x2_in", [TC, D])
    y2T = pr.dram_in("y2T", [IN2, TC])
    g2T = pr.dram_in("g2T", [IN2, TC], BF16)
    W_out = pr.dram_in("W_out", [IN2, D])
    router_w = pr.dram_in("router_w", [D, E])
    router_b = pr.dram_in("router_b", [1, E])
    w1 = pr.dram_in("w1", [E, D, FF1])
    w2 = pr.dram_in("w2", [E, F2, D])
    ident = pr.dram_in("ident", [128, 128], BF16)
    o_x4 = pr.dram_out("o_x4", [TC, D])

    def body(pr):
        pool = [pr.sbuf(f"pool{j}", [128, 1024], BF16) for j in range(32)]
        wpool = [pr.sbuf(f"wp{j}", [128, 1024], BF16) for j in range(16)]
        s_tok = [pr.sbuf(f"s_tok{j}", [128, D], BF16) for j in range(16)]
        s_xne = [pr.sbuf(f"s_xne{j}", [128, TC], BF16) for j in range(8)]
        s_y2 = pr.sbuf("s_y2", [128, TC], BF16)
        s_g2 = pr.sbuf("s_g2", [128, TC], BF16)
        s_sg2 = pr.sbuf("s_sg2", [128, TC], BF16)
        s_x2 = [pr.sbuf(f"s_x2{i}", [128, D], F32) for i in range(1)]
        s_x3 = pr.sbuf("s_x3", [128, D], F32)
        sb = {"sq": pr.sbuf("s_sq", [128, D], F32), "st": pr.sbuf("s_st", [128, 8], F32),
              "ln": pr.sbuf("s_ln", [128, D], F32)}
        s_nbf2 = pr.sbuf("s_nbf2", [128, D], BF16)
        s_trT = pr.sbuf("s_trT", [128, 8 * 128], BF16)
        s_cst = pr.sbuf("s_cst", [128, 2], F32)
        s_id = pr.sbuf("s_id", [128, 128], BF16)
        s_rw = pr.sbuf("s_rw", [128, 8 * E], BF16)
        s_rb1 = pr.sbuf("s_rb1", [1, E], F32)
        s_rb = pr.sbuf("s_rb", [128, E], F32)
        s_ones1 = pr.sbuf("s_ones1", [1, 128], F32)
        s_lg = pr.sbuf("s_lg", [128, E], F32)
        s_p = pr.sbuf("s_p", [128, E], F32)
        s_eq = pr.sbuf("s_eq", [128, E], F32)
        s_fnd = pr.sbuf("s_fnd", [128, 2], F32)
        s_msk = pr.sbuf("s_msk", [128, 16 * E], F32)
        s_maxp = pr.sbuf("s_maxp", [128, 16], F32)
        s_w1a = [pr.sbuf(f"s_w1a{i}", [128, 8 * 128], BF16) for i in range(2)]
        s_w1g = [pr.sbuf(f"s_w1g{i}", [128, 8 * 128], BF16) for i in range(2)]
        s_sga = pr.sbuf("s_sga", [128, 512], BF16)
        s_stg = [pr.sbuf(f"s_stg{i}", [128, 512], F32) for i in range(2)]
        p_h = [pr.psum(f"p_h{i}", [128, 512]) for i in range(4)]
        p_trs = [pr.psum("p_tr", [128, 128], BF16), pr.psum("p_trb", [128, 128], BF16)]
        p_r = pr.psum("p_r", [128, E])

        d_e0 = pr.op("vector", lambda v: v.memset(s_cst[:, 0:1], LN_EPS))
        d_o1 = pr.op("vector", lambda v: v.memset(s_ones1[:], 1.0))
        CON = pr.dma_group("consts")
        ld_id = pr.op("sync", lambda s: s.dma_start(out=s_id[:], in_=ident[:]), dma=True, group=CON)
        ld_rb = pr.op("sync", lambda s: s.dma_start(out=s_rb1[:], in_=router_b[:]), dma=True, group=CON)
        GCON = pr.dma_group("gconsts")
        ld_wo = [pr.op("gpsimd", lambda g, j=j: g.dma_start(
            out=wpool[j][:], in_=W_out[j * 128:(j + 1) * 128, :]), dma=True, group=GCON)
            for j in range(16)]
        ld_rw = pr.op("gpsimd", lambda g: g.dma_start(
            out=s_rw[:, :].rearrange("p (a c) -> p a c", a=8),
            in_=router_w[:, :].rearrange("(a p) c -> p a c", p=128)), dma=True, group=GCON)
        d_rbb = pr.op("tensor", lambda t: t.matmul(
            p_r[:, :], s_ones1[0:1, :], s_rb1[0:1, :], start=True, stop=True),
            deps=[ld_rb, d_o1])
        d_rbe = pr.op("vector", lambda v: v.tensor_copy(s_rb[:, :], p_r[:, :]), deps=[d_rbb])

        gates = {"ring": {}, "bank": [None] * 4, "tr": [None, None], "pool": [], "nbf": [],
                 "nbf2": [], "stg": [None] * 4, "wpool": [ld_wo[-1]], "xne": [],
                 "x3": None, "lg": [], "trT": []}
        x3_store = {}

        yg_done = []
        for pt in range(16):
            ld_y = pr.op("gpsimd", lambda g, pt=pt: g.dma_start(
                out=s_y2[:], in_=y2T[pt * 128:(pt + 1) * 128, :]),
                deps=gates["ring"].get("y", []), dma=True, group=pr.dma_group("gy", stream=True))
            ld_g = pr.op("sync", lambda s, pt=pt: s.dma_start(
                out=s_g2[:], in_=g2T[pt * 128:(pt + 1) * 128, :]),
                deps=gates["ring"].get("g", []), dma=True, group=pr.dma_group("gg", stream=True))
            d_si = pr.op("scalar", lambda sc: sc.activation(s_sg2[:], s_g2[:], AF.Silu),
                         deps=[ld_g] + gates["ring"].get("sg", []))
            d_m0 = pr.op("vector", lambda v, pt=pt: v.tensor_tensor(
                pool[2 * pt][:, :], s_y2[:, 0:1024], s_sg2[:, 0:1024], op=ALU.mult),
                deps=[d_si, ld_y] + gates["pool"])
            d_m1 = pr.op("vector", lambda v, pt=pt: v.tensor_tensor(
                pool[2 * pt + 1][:, :], s_y2[:, 1024:2048], s_sg2[:, 1024:2048], op=ALU.mult),
                deps=[d_si, ld_y])
            gates["ring"]["y"] = [d_m0, d_m1]
            gates["ring"]["g"] = [d_si]
            gates["ring"]["sg"] = [d_m0, d_m1]
            yg_done.append(d_m1)
        gates["pool"] = []
        d_yg_all = yg_done[-1]

        for ti in range(16):
            xb = s_x2[0]
            ld_x = pr.op("sync", lambda s, xb=xb, ti=ti: s.dma_start(
                out=xb[:], in_=x2_in[ti * 128:(ti + 1) * 128, :]),
                deps=gates["ring"].get(("x", 0), []), dma=True,
                group=pr.dma_group("gx0", stream=True))
            ev_dm = []
            for dm in range(2):
                d_mml = None
                for k in range(16):
                    po = pool[2 * k + (1 if ti >= 8 else 0)]
                    co = (ti - 8 if ti >= 8 else ti) * 128
                    d_mml = pr.op("tensor", lambda t, k=k, dm=dm, po=po, co=co, first=(k == 0): t.matmul(
                        p_h[dm][:, :], po[:, co:co + 128], wpool[k][:, dm * 512:(dm + 1) * 512],
                        start=first, stop=(k == 15)),
                        deps=([d_yg_all, ld_wo[-1]] + ([gates["bank"][dm]] if gates["bank"][dm] else [])) if k == 0 else [d_mml])
                d_ev = pr.op("vector", lambda v, dm=dm, xb=xb: v.tensor_tensor(
                    s_x3[:, dm * 512:(dm + 1) * 512], p_h[dm][:, :],
                    xb[:, dm * 512:(dm + 1) * 512], op=ALU.add),
                    deps=[d_mml, ld_x] + ([gates["x3"]] if (dm == 0 and gates["x3"]) else []))
                gates["bank"][dm] = d_ev
                ev_dm.append(d_ev)
            gates["ring"][("x", 0)] = [ev_dm[1]]
            d_st = pr.op("sync", lambda s, ti=ti: s.dma_start(
                out=o_x4[ti * 128:(ti + 1) * 128, :], in_=s_x3[:, :]), deps=ev_dm, dma=True,
                group=pr.dma_group(f"sx{ti % 2}", stream=True))
            x3_store[ti] = d_st
            gates["x3"] = d_st
            d_ln = emit_ln_tile(pr, s_x3[:, :], 128, sb, ev_dm + gates["nbf"], s_cst[:, 0:1])
            d_xn = pr.op("vector", lambda v, ti=ti: v.tensor_copy(s_tok[ti][:, :], sb["ln"][:, :]),
                         deps=[d_ln] + gates["pool"] if False else [d_ln])
            gates["nbf"] = [d_xn]
            tr_evs = []
            for j in range(8):
                ptr = p_trs[j % 2]
                d_tp = pr.op("tensor", lambda t, j=j, ti=ti, ptr=ptr: t.transpose(
                    ptr[:, :], s_tok[ti][:, j * 128:(j + 1) * 128], s_id[:]),
                    deps=[d_xn, ld_id] + ([gates["tr"][j % 2]] if gates["tr"][j % 2] else []) + gates["trT"])
                d_te = pr.op("vector", lambda v, j=j, ptr=ptr: v.tensor_copy(
                    s_trT[:, j * 128:(j + 1) * 128], ptr[:, :]), deps=[d_tp])
                gates["tr"][j % 2] = d_te
                tr_evs.append(d_te)
            gates["trT"] = []
            d_rm = None
            for k in range(8):
                d_rm = pr.op("tensor", lambda t, k=k, first=(k == 0): t.matmul(
                    p_r[:, :], s_trT[:, k * 128:(k + 1) * 128], s_rw[:, k * E:(k + 1) * E],
                    start=first, stop=(k == 7)),
                    deps=([tr_evs[-1], ld_rw, d_rbe] + gates["lg"]) if k == 0 else [d_rm])
            gates["trT"] = [d_rm]
            d_lg = pr.op("vector", lambda v: v.tensor_tensor(
                s_lg[:, :], p_r[:, :], s_rb[:, :], op=ALU.add), deps=[d_rm] + gates["lg"])
            d_mx = pr.op("vector", lambda v: v.tensor_reduce(
                sb["st"][:, 0:1], s_lg[:, :], axis=AX.X, op=ALU.max), deps=[d_lg])
            d_nm = pr.op("vector", lambda v: v.tensor_scalar(
                sb["st"][:, 1:2], sb["st"][:, 0:1], -1.0, None, op0=ALU.mult), deps=[d_mx])
            d_ex = pr.op("scalar", lambda sc: sc.activation(
                s_p[:, :], s_lg[:, :], AF.Exp, bias=sb["st"][:, 1:2]), deps=[d_nm] + gates["lg"])
            d_sm = pr.op("vector", lambda v: v.tensor_reduce(
                sb["st"][:, 2:3], s_p[:, :], axis=AX.X, op=ALU.add), deps=[d_ex])
            d_pm = pr.op("vector", lambda v: v.tensor_reduce(
                sb["st"][:, 3:4], s_p[:, :], axis=AX.X, op=ALU.max), deps=[d_ex])
            d_rc = pr.op("vector", lambda v: v.reciprocal(sb["st"][:, 4:5], sb["st"][:, 2:3]),
                         deps=[d_sm])
            d_mp = pr.op("vector", lambda v, ti=ti: v.tensor_tensor(
                s_maxp[:, ti:ti + 1], sb["st"][:, 3:4], sb["st"][:, 4:5], op=ALU.mult),
                deps=[d_pm, d_rc])
            d_eq = pr.op("vector", lambda v: v.tensor_scalar(
                s_eq[:, :], s_p[:, :], sb["st"][:, 3:4], None, op0=ALU.is_ge),
                deps=[d_ex, d_pm])
            d_f = pr.op("vector", lambda v: v.memset(s_fnd[:, 0:1], 0.0), deps=gates["lg"])
            gates["lg"] = [d_eq, d_ex]
            prevm = [d_f, d_eq]
            for e in range(E):
                d_t = pr.op("vector", lambda v: v.tensor_scalar(
                    s_fnd[:, 1:2], s_fnd[:, 0:1], -1.0, 1.0, op0=ALU.mult, op1=ALU.add),
                    deps=prevm)
                d_m = pr.op("vector", lambda v, ti=ti, e=e: v.tensor_tensor(
                    s_msk[:, ti * E + e:ti * E + e + 1], s_eq[:, e:e + 1], s_fnd[:, 1:2],
                    op=ALU.mult), deps=[d_t])
                d_u = pr.op("vector", lambda v, ti=ti, e=e: v.tensor_tensor(
                    s_fnd[:, 0:1], s_fnd[:, 0:1], s_msk[:, ti * E + e:ti * E + e + 1],
                    op=ALU.add), deps=[d_m])
                prevm = [d_u]
            gates["lg"] = gates["lg"] + prevm

        acc_last = {}
        for e in range(E):
            xne_evs = []
            for ti in range(16):
                d_msk = pr.op("vector", lambda v, ti=ti, e=e: v.tensor_scalar(
                    s_nbf2[:, :], s_tok[ti][:, :], s_msk[:, ti * E + e:ti * E + e + 1], None,
                    op0=ALU.mult), deps=gates["nbf2"] + ([prevm[-1]] if e == 0 else []))
                tr_last = None
                for j in range(8):
                    ptr = p_trs[j % 2]
                    d_tp = pr.op("tensor", lambda t, j=j, ptr=ptr: t.transpose(
                        ptr[:, :], s_nbf2[:, j * 128:(j + 1) * 128], s_id[:]),
                        deps=[d_msk, ld_id] + ([gates["tr"][j % 2]] if gates["tr"][j % 2] else []) + gates["xne"])
                    d_te = pr.op("vector", lambda v, j=j, ti=ti, e=e, ptr=ptr: v.tensor_copy(
                        s_xne[j][:, ti * 128:(ti + 1) * 128], ptr[:, :]), deps=[d_tp])
                    gates["tr"][j % 2] = d_te
                    tr_last = d_te
                gates["nbf2"] = [d_tp]
                xne_evs.append(tr_last)
            gates["xne"] = []
            d_xne_all = xne_evs[-1]
            for tg in range(2):
                tg0 = tg * 1024
                w2l0 = []
                for k in range(32):
                    w2l0.append(pr.op("gpsimd", lambda g, e=e, k=k: g.dma_start(
                        out=wpool[k // 2][:, (k % 2) * 512:(k % 2 + 1) * 512],
                        in_=w2[e, k * 128:(k + 1) * 128, 0:512]),
                        deps=gates["wpool"], dma=True,
                        group=pr.dma_group(f"w2_{e}_{tg}_0")))
                gates["wpool"] = []
                for jp in range(32):
                    wa = s_w1a[jp % 2]; wg = s_w1g[jp % 2]
                    ld_a = pr.op("gpsimd", lambda g, e=e, jp=jp, wa=wa: g.dma_start(
                        out=wa[:, :].rearrange("p (a c) -> p a c", a=8),
                        in_=w1[e, :, jp * 128:(jp + 1) * 128].rearrange("(a p) c -> p a c", p=128)),
                        deps=gates["ring"].get(("wa", jp % 2), []), dma=True,
                        group=pr.dma_group(f"wa{jp % 2}", stream=True))
                    ld_gw = pr.op("gpsimd", lambda g, e=e, jp=jp, wg=wg: g.dma_start(
                        out=wg[:, :].rearrange("p (a c) -> p a c", a=8),
                        in_=w1[e, :, F2 + jp * 128:F2 + (jp + 1) * 128].rearrange("(a p) c -> p a c", p=128)),
                        deps=gates["ring"].get(("wg", jp % 2), []), dma=True,
                        group=pr.dma_group(f"wg{jp % 2}", stream=True))
                    mm_a = []
                    mm_g = []
                    for half in range(2):
                        d_a = None
                        for k in range(8):
                            d_a = pr.op("tensor", lambda t, k=k, half=half, wa=wa, tg0=tg0, first=(k == 0): t.matmul(
                                p_h[half][:, :], wa[:, k * 128:(k + 1) * 128],
                                s_xne[k][:, tg0 + half * 512:tg0 + (half + 1) * 512],
                                start=first, stop=(k == 7)),
                                deps=([ld_a, d_xne_all] + ([gates["bank"][half]] if gates["bank"][half] else [])) if k == 0 else [d_a])
                        mm_a.append(d_a)
                        d_g = None
                        for k in range(8):
                            d_g = pr.op("tensor", lambda t, k=k, half=half, wg=wg, tg0=tg0, first=(k == 0): t.matmul(
                                p_h[2 + half][:, :], wg[:, k * 128:(k + 1) * 128],
                                s_xne[k][:, tg0 + half * 512:tg0 + (half + 1) * 512],
                                start=first, stop=(k == 7)),
                                deps=([ld_gw, d_xne_all] + ([gates["bank"][2 + half]] if gates["bank"][2 + half] else [])) if k == 0 else [d_g])
                        mm_g.append(d_g)
                    gates["ring"][("wa", jp % 2)] = mm_a
                    gates["ring"][("wg", jp % 2)] = mm_g
                    for half in range(2):
                        d_si = pr.op("scalar", lambda sc, half=half: sc.activation(
                            s_sga[:, :], p_h[half][:, :], AF.Silu),
                            deps=[mm_a[half]] + gates["ring"].get("sga", []))
                        gates["bank"][half] = d_si
                        d_yi = pr.op("vector", lambda v, jp=jp, half=half: v.tensor_tensor(
                            pool[jp][:, half * 512:(half + 1) * 512], s_sga[:, :],
                            p_h[2 + half][:, :], op=ALU.mult),
                            deps=[d_si, mm_g[half]] + (gates["pool"] if jp == 0 and half == 0 else []))
                        gates["bank"][2 + half] = d_yi
                        gates["ring"]["sga"] = [d_yi]
                        if jp == 0 and half == 0:
                            gates["pool"] = []
                w2_cons = []
                for dm in range(2):
                    if dm == 0:
                        w2l = w2l0
                    else:
                        w2l = []
                        for k in range(32):
                            w2l.append(pr.op("gpsimd", lambda g, e=e, k=k, dm=dm: g.dma_start(
                                out=wpool[k // 2][:, (k % 2) * 512:(k % 2 + 1) * 512],
                                in_=w2[e, k * 128:(k + 1) * 128, dm * 512:(dm + 1) * 512]),
                                deps=gates["wpool"], dma=True,
                                group=pr.dma_group(f"w2_{e}_{tg}_{dm}")))
                        gates["wpool"] = []
                    for tt in range(8):
                        ti = tg * 8 + tt
                        bank = tt % 4
                        d_mml = None
                        for k in range(32):
                            d_mml = pr.op("tensor", lambda t, k=k, tt=tt, dm=dm, bank=bank, first=(k == 0): t.matmul(
                                p_h[bank][:, :], pool[k][:, tt * 128:(tt + 1) * 128],
                                wpool[k // 2][:, (k % 2) * 512:(k % 2 + 1) * 512],
                                start=first, stop=(k == 31)),
                                deps=([w2l[-1], gates["bank"][2], gates["bank"][3]] + ([gates["bank"][bank]] if gates["bank"][bank] else [])) if k == 0 else [d_mml])
                        if k == 31:
                            w2_cons.append(d_mml)
                        sgi = dm
                        stg = s_stg[sgi]
                        d_sc = pr.op("scalar", lambda sc, stg=stg, ti=ti, bank=bank: sc.activation(
                            stg[:, :], p_h[bank][:, :], AF.Identity, scale=s_maxp[:, ti:ti + 1]),
                            deps=[d_mml] + ([gates["stg"][sgi]] if gates["stg"][sgi] else []))
                        gates["bank"][bank] = d_sc
                        d_ac = pr.op("gpsimd", lambda g, stg=stg, ti=ti, dm=dm: g.dma_start(
                            out=o_x4[ti * 128:(ti + 1) * 128, dm * 512:(dm + 1) * 512],
                            in_=stg[:, :], accum_op=ALU.add),
                            deps=[d_sc, x3_store[ti]] + ([acc_last.get((ti, dm))] if acc_last.get((ti, dm)) else []),
                            dma=True, group=pr.dma_group(f"ac{sgi}", stream=True))
                        gates["stg"][sgi] = d_ac
                        acc_last[(ti, dm)] = d_ac
                    gates["wpool"] = w2_cons

        pr.op("vector", lambda v: v.engine_nop(),
              deps=[acc_last[(ti, dm)] for ti in range(16) for dm in range(2)])

    nc = pr.build(body)
    return nc


def _l8_inmaps(inputs, x2_full, y2T_full, g2T_full):
    common = {
        "W_out": np.ascontiguousarray(np.asarray(inputs["L1_attn_W_out"])),
        "router_w": np.ascontiguousarray(np.asarray(inputs["L1_ffn_router_w"])),
        "router_b": np.asarray(inputs["L1_ffn_router_b"]).reshape(1, E),
        "w1": np.ascontiguousarray(np.asarray(inputs["L1_ffn_w1"])),
        "w2": np.ascontiguousarray(np.asarray(inputs["L1_ffn_w2"])),
        "ident": IDENT,
    }
    maps = []
    for c in range(8):
        b, h = c // 2, c % 2
        t0 = h * TC
        maps.append({
            "x2_in": np.ascontiguousarray(x2_full[b, t0:t0 + TC]),
            "y2T": np.ascontiguousarray(y2T_full[b][:, t0:t0 + TC]),
            "g2T": np.ascontiguousarray(g2T_full[b][:, t0:t0 + TC]),
            **common,
        })
    return maps


def run_l8(inputs, x2_full, y2T_full, g2T_full):
    nc = build_l8()
    res = _run(nc, _l8_inmaps(inputs, x2_full, y2T_full, g2T_full))
    x4 = np.stack([np.concatenate([res[2 * b]["o_x4"], res[2 * b + 1]["o_x4"]], axis=0) for b in range(B)])
    return x4


def kernel(**inputs):
    inputs = {k: np.asarray(v) for k, v in inputs.items()}
    xconv, ssp, z = run_l1(inputs)
    y = run_l2(inputs, xconv, ssp)
    s, QT, KT, V = run_l3(inputs, y, z)
    ctx = run_l4(QT, KT, V)
    x2, raw, g2 = run_l5(inputs, s, ctx)
    h1, d1, bc1 = run_l6(inputs, raw)
    y2 = run_l7(inputs, h1, d1, bc1)
    x4 = run_l8(inputs, x2, y2, g2)
    return x4.reshape(B, L, D)
